# version 06
# baseline (speedup 1.0000x reference)
import contextlib
import numpy as np
import ml_dtypes
import concourse.bass as bass
import concourse.mybir as mybir
from concourse.bass_utils import run_bass_kernel_spmd

F32 = mybir.dt.float32
BF16 = mybir.dt.bfloat16
I32 = mybir.dt.int32
AF = mybir.ActivationFunctionType
ALU = mybir.AluOpType
AX = mybir.AxisListType
bf16 = ml_dtypes.bfloat16

N_CORES = 8
NTOK = 4096
D = 1024
CH = 8
RSTR = 4104
RCOLS = CH * RSTR

ENGS = ("sync", "tensor", "vector", "scalar", "gpsimd")


class _Chan:
    __slots__ = ("name", "n", "sem")

    def __init__(self, name):
        self.name = name
        self.n = 0
        self.sem = None


class Prog:

    def __init__(self, nc):
        self.nc = nc
        self.ops = {e: [] for e in ENGS}
        self.chans = {}
        self.scope = None

    def ch(self, name):
        if name not in self.chans:
            self.chans[name] = _Chan(name)
        return self.chans[name]

    def emit(self, eng, fn, wait=(), inc=(), inc_amt=1):
        waits = []
        for ch, v in wait:
            v = ch.n if v is None else v
            assert v <= ch.n, f"wait on future count of {ch.name}: {v} > {ch.n}"
            if v > 0:
                waits.append((ch, v))
        if isinstance(inc, _Chan):
            inc = (inc,)
        incs = []
        for ch in inc:
            ch.n += inc_amt
            incs.append((ch, inc_amt))
        self.ops[eng].append((waits, fn, incs, self.scope))

    def build(self):
        nc = self.nc
        with contextlib.ExitStack() as stack:
            for name in sorted(self.chans):
                ch = self.chans[name]
                ch.sem = stack.enter_context(nc.semaphore("ch_" + name))
            with nc.Block() as block:
                for eng in ENGS:
                    if not self.ops[eng]:
                        continue

                    ck = self.chans.get("CK")

                    def make(eng, ck=ck):
                        def emit_all(e):
                            last = {}
                            cur = [None, None]
                            if ck is not None and eng != "sync":
                                e.wait_ge(ck.sem, ck.n)
                                last = {ck: ck.n}
                            for waits, fn, incs, scope in self.ops[eng]:
                                if scope != cur[0]:
                                    if cur[1] is not None:
                                        nc.leave_named_scope(cur[0], cur[1], False)
                                        cur[1] = None
                                    cur[0] = scope
                                    if scope is not None:
                                        cur[1], _ = nc.enter_named_scope(
                                            scope, False)
                                for ch, v in waits:
                                    if last.get(ch, 0) >= v:
                                        continue
                                    e.wait_ge(ch.sem, v)
                                    last[ch] = v
                                inst = fn(e)
                                for ch, amt in incs:
                                    inst = inst.then_inc(ch.sem, amt)
                            if cur[1] is not None:
                                nc.leave_named_scope(cur[0], cur[1], False)
                            return None
                        return emit_all
                    getattr(block, eng)(make(eng))


class DmaRing:

    def __init__(self, P, name, k=4, eng="sync"):
        self.P, self.k, self.eng = P, k, eng
        self.chs = [P.ch(f"{name}{i}") for i in range(k)]
        self.i = 0

    def dma(self, fn, wait=()):
        ch = self.chs[self.i % self.k]
        w = list(wait)
        if ch.n > 0:
            w.append((ch, ch.n))
        self.P.emit(self.eng, fn, wait=w, inc=ch, inc_amt=16)
        self.i += 1
        return (ch, ch.n)

    def all_done(self):
        return [(ch, ch.n) for ch in self.chs]


class _View:
    __slots__ = ("_ap",)

    def __init__(self, ap):
        self._ap = ap

    def __getitem__(self, k):
        return self._ap[k]

    def ap(self):
        return self._ap

    def rearrange(self, *a, **k):
        return self._ap.rearrange(*a, **k)


ARENA_TAIL = {"p2_mrow", "p2_qrow", "p2_vrow", "p2_rrow", "p2_mrrow",
              "p2_rrowbf", "p2_mrrowbf", "p2_ssb0", "p2_ssb1", "p2_pbf0",
              "p2_pbf1", "p2_pT0", "p2_pT1", "p2_mx0", "p2_mx1",
              "p2_dn0", "p2_dn1"}
ARENA_REAL = {"p2_m640", "p2_ones8", "p2_ones1", "p2_onescol"}
ARENA_BYTES = 141 * 1024
ARENA_TAIL_BYTES = 26 * 1024
_DT_BYTES = {F32: 4, BF16: 2, I32: 4}


class IO:

    def __init__(self, nc, internal=(), use_arena=False):
        self.nc = nc
        self.dram = {}
        self.kinds = {}
        self.sb = {}
        self.internal = set(internal)
        self.use_arena = use_arena
        self.arena = None
        self.abump = {}

    def din(self, name, shape, dtype):
        if name not in self.dram:
            self.dram[name] = self.nc.dram_tensor(name, list(shape), dtype,
                                                  kind="ExternalInput").ap()
            self.kinds[name] = "in"
        return self.dram[name]

    def dout(self, name, shape, dtype):
        if name not in self.dram:
            kind = "Internal" if name in self.internal else "ExternalOutput"
            self.dram[name] = self.nc.dram_tensor(name, list(shape), dtype,
                                                  kind=kind).ap()
            self.kinds[name] = "internal" if name in self.internal else "out"
        return self.dram[name]

    def dint(self, name, shape, dtype):
        if name not in self.dram:
            self.dram[name] = self.nc.dram_tensor(name, list(shape), dtype,
                                                  kind="Internal").ap()
            self.kinds[name] = "internal"
        return self.dram[name]

    def sbuf(self, name, shape, dtype):
        if name in self.sb:
            return self.sb[name]
        arena_ok = (self.use_arena and name[0] == "p" and name[2] == "_"
                    and name not in ARENA_REAL)
        if not arena_ok:
            self.sb[name] = self.nc.alloc_sbuf_tensor(name, list(shape), dtype)
            return self.sb[name]
        if self.arena is None:
            self.arena = self.nc.alloc_sbuf_tensor(
                "ARENA", [128, ARENA_BYTES // 2], BF16)
        region = "tail" if name in ARENA_TAIL else name[:2]
        base = self.abump.get(region)
        if base is None:
            base = (ARENA_BYTES - ARENA_TAIL_BYTES) if region == "tail" else 0
            self.abump[region] = base
        nbytes = shape[1] * _DT_BYTES[dtype]
        nbytes = (nbytes + 63) & ~63
        limit = ARENA_BYTES if region == "tail" else ARENA_BYTES - ARENA_TAIL_BYTES
        assert base + nbytes <= limit, \
            f"arena overflow: {name} region {region} base {base} + {nbytes} > {limit}"
        self.abump[region] = base + nbytes
        col0 = base // 2
        view = self.arena.ap()[:, col0: col0 + shape[1] * _DT_BYTES[dtype] // 2]
        if dtype != BF16:
            view = view.bitcast(dtype)
        if shape[0] != 128:
            view = view[0:shape[0], :]
        self.sb[name] = _View(view)
        return self.sb[name]

    def psum(self, name, shape, dtype=F32):
        if name not in self.sb:
            self.sb[name] = self.nc.alloc_psum_tensor(name, list(shape), dtype)
        return self.sb[name]


def _r_slice(R, c, t0, n):
    return R[:, c * RSTR + 1 + t0: c * RSTR + 1 + t0 + n]


def _r_shift_slice(R, c, t0, n):
    off = 0 if c >= 4 else 1
    return R[:, c * RSTR + off + t0: c * RSTR + off + t0 + n]


CONST_VALS = (1e-5, 0.125, 1.0 / 127.0, 8.0, -1.0, 1e-6, 0.5)


def emit_consts(P, io):
    nc = P.nc
    identin = io.din("identin", (128, 128), BF16)
    ident = io.sbuf("ident", (128, 128), BF16)
    K = P.ch("CK")
    P.emit("sync", lambda e: e.dma_start(ident[:], identin), inc=K, inc_amt=16)
    constsin = io.din("constsin", (128, len(CONST_VALS)), F32)
    ct = io.sbuf("consts", (128, len(CONST_VALS)), F32)
    P.emit("sync", lambda e: e.dma_start(ct[:], constsin), inc=K, inc_amt=16)
    for i, v in enumerate(CONST_VALS):
        nc.const_aps.register(F32, float(v), ct[:, i:i + 1])


def emit_ph0(P, io):
    nc = P.nc
    x_tm = io.din("x_tm", (NTOK, D), F32)
    R = io.sbuf("R", (128, RCOLS), BF16)
    ident = io.sbuf("ident", (128, 128), BF16)

    xld = [io.sbuf(f"p0_x{p}", (128, D), F32) for p in range(2)]
    xn = [io.sbuf(f"p0_xn{p}", (128, D), BF16) for p in range(2)]
    dump = io.sbuf("p0_dump", (128, D), BF16)
    cs = io.sbuf("p0_cs", (128, 1), F32)
    cmean = io.sbuf("p0_cmean", (128, 1), F32)
    cq = io.sbuf("p0_cq", (128, 1), F32)
    cmsq = io.sbuf("p0_cmsq", (128, 1), F32)
    cm2 = io.sbuf("p0_cm2", (128, 1), F32)
    cvar = io.sbuf("p0_cvar", (128, 1), F32)
    cstd = io.sbuf("p0_cstd", (128, 1), F32)
    crstd = io.sbuf("p0_crstd", (128, 1), F32)
    A1 = io.psum("p1_A1", (128, 2048), F32)
    psT = [A1.ap()[:, 0:512].bitcast(BF16), A1.ap()[:, 512:1024].bitcast(BF16)]

    VT, SQ, VV, ST, TT, EVT, VS = (P.ch(x) for x in
                                   ("p0VT", "qSQA", "qRED", "qLNX", "p0TT",
                                    "p0EVT", "qVS"))
    ring = DmaRing(P, "Ls", k=4, eng="sync")

    P.emit("gpsimd", lambda e: e.memset(R[:], 0.0), inc=P.ch("p0G"))

    nsub = NTOK // 128
    for s in range(nsub):
        p = s % 2
        tok = ring.dma(lambda e, s=s, p=p: e.dma_start(
            xld[p][:], x_tm[s * 128:(s + 1) * 128, :]),
            wait=[(VT, max(0, s - 1))])
        P.emit("vector", lambda e, p=p: e.reduce_sum(cs[:, 0:1], xld[p][:], axis=AX.X),
               wait=[tok], inc=VS)
        P.emit("vector", lambda e: e.tensor_scalar_mul(cmean[:, 0:1], cs[:, 0:1], 1.0 / D),
               wait=[(VS, None)], inc=VS)
        P.emit("scalar", lambda e, p=p: e.activation(
            dump[:], xld[p][:], AF.Square),
            wait=[tok, (VT, max(0, s))], inc=SQ)
        P.emit("vector", lambda e: e.reduce_sum(cq[:, 0:1], dump[:], axis=AX.X),
               wait=[(SQ, None)], inc=VS)
        P.emit("vector", lambda e: e.tensor_scalar_mul(cmsq[:, 0:1], cq[:, 0:1], 1.0 / D),
               wait=[(VS, None)], inc=VS)
        P.emit("vector", lambda e: e.tensor_mul(cm2[:, 0:1], cmean[:, 0:1], cmean[:, 0:1]),
               inc=VS)
        P.emit("vector", lambda e: e.tensor_sub(cvar[:, 0:1], cmsq[:, 0:1], cm2[:, 0:1]),
               wait=[(VS, None)], inc=VV)
        P.emit("scalar", lambda e: e.activation(cstd[:, 0:1], cvar[:, 0:1], AF.Sqrt,
                                                bias=1e-5),
               wait=[(VV, None)], inc=ST)
        P.emit("vector", lambda e: e.reciprocal(crstd[:, 0:1], cstd[:, 0:1]),
               wait=[(ST, None)], inc=VS)
        P.emit("vector", lambda e, p=p: e.tensor_scalar(
            xn[p][:], xld[p][:], cmean[:, 0:1], crstd[:, 0:1], ALU.subtract, ALU.mult),
            wait=[(TT, max(0, s - 1)), (VS, None)], inc=VT)
        for c in range(CH):
            P.emit("tensor", lambda e, p=p, c=c: e.transpose(
                psT[p][:, c * 128:(c + 1) * 128], xn[p][:, c * 128:(c + 1) * 128],
                ident[:]),
                wait=([(VT, None), (EVT, max(0, s - 1))] if c == 0 else []),
                inc=(TT if c == CH - 1 else ()))
        P.emit("vector", lambda e, s=s, p=p: e.tensor_copy(
            R.ap().rearrange("p (c t) -> p c t", c=CH, t=RSTR)
             [:, :, 1 + s * 128: 1 + (s + 1) * 128],
            psT[p].rearrange("p (c t) -> p c t", c=CH, t=128)),
            wait=[(TT, None)] + ([(P.ch("p0G"), 1)] if s == 0 else []), inc=EVT)


def emit_ph1(P, io, after=()):
    R = io.sbuf("R", (128, RCOLS), BF16)
    wq_h = io.din("l0Wq", (D, D), F32)
    wk_h = io.din("l0Wk", (D, D), F32)
    wv_h = io.din("l0Wv", (D, D), F32)
    wo_h = io.din("l0Wo", (D, D), F32)
    bo_h = io.din("l0bo", (128, CH), F32)
    esel_h = io.din("l0esel", (16, 1024), BF16)
    attn0 = io.dout("attn0", (D, NTOK), BF16)

    wq = io.sbuf("p1_wq", (128, CH * 1024), BF16)
    wk = io.sbuf("p1_wk", (128, CH * 1024), BF16)
    wv = io.sbuf("p1_wv", (128, CH * 1024), BF16)
    wo = io.sbuf("p1_wo", (128, CH * 1024), BF16)
    bo = io.sbuf("p1_bo", (128, CH), F32)
    esel = io.sbuf("p1_esel", (16, 1024), BF16)

    kbf = [io.sbuf(f"p1_kbf{p}", (128, 1024), BF16) for p in range(2)]
    vaug = [io.sbuf(f"p1_vaug{p}", (128, 1040), BF16) for p in range(2)]
    mt = io.sbuf("p1_mt", (128, 1024), BF16)
    et = io.sbuf("p1_et", (128, 1024), BF16)
    kvsb = io.sbuf("p1_kvsb", (64, 1040), BF16)
    kvacc = io.sbuf("p1_kvacc", (64, 1040), F32)
    ksbd = io.sbuf("p1_ksbd", (128, 128), BF16)
    kvbd = io.sbuf("p1_kvbd", (128, 1024), BF16)
    qbf = io.sbuf("p1_qbf", (128, 4096), BF16)
    obf = io.sbuf("p1_obf", (128, 4096), BF16)
    zs = io.sbuf("p1_zs", (16, 512), F32)
    zib = io.sbuf("p1_zib", (16, 512), BF16)
    zbsb = io.sbuf("p1_zbsb", (128, 512), BF16)
    att = [io.sbuf("p1_att0", (128, CH * 512), BF16)] * 2

    A1 = io.psum("p1_A1", (128, 2048), F32)
    A2 = io.psum("p1_A2", (128, 2048), F32)
    ps_k = A1[:, 0:1024]
    ps_v = A1[:, 1024:2048]
    ps_kv = [A2[0:64, 128 * h: 128 * h + 65] for h in range(16)]
    ps_q = [A1[:, 0:512], A1[:, 512:1024]]
    ps_o = [A1[:, 1024:1536], A1[:, 1536:2048]]
    ps_z = A2[0:16, 0:512]
    ps_zb = A2[:, 512:1024]
    ps_w = [A2[:, 1024:1536], A2[:, 1536:2048]]

    wring = DmaRing(P, "Lw", k=6, eng="gpsimd")
    oring = DmaRing(P, "Lo", k=2, eng="sync")
    PEk = P.ch("p1PEk")
    PEv = P.ch("p1PEv")
    PEkv = P.ch("p1PEkv")
    KVA = P.ch("p1KVA")
    VEk = P.ch("p1VEk")
    AEv = P.ch("p1AEv")
    MN = P.ch("p1MN")
    EX = P.ch("qAEx")
    VS = P.ch("qVS")
    BD = P.ch("p1BD")
    PEq = P.ch("p1PEq")
    VEq = P.ch("p1VEq")
    PEz = P.ch("p1PEz")
    VEz = P.ch("p1VEz")
    PEo = P.ch("p1PEo")
    PEzb = P.ch("p1PEzb")
    VEo = P.ch("p1VEo")
    VEzb = P.ch("p1VEzb")
    PEw = P.ch("p1PEw")
    AEw = P.ch("p1AEw")
    ONE = P.ch("p1ONE")
    C = P.ch("p1C")

    wtoks = {}
    for name, hbm, sb in (("wk", wk_h, wk), ("wv", wv_h, wv), ("wq", wq_h, wq),
                          ("wo", wo_h, wo)):
        wtoks[name] = wring.dma(lambda e, hbm=hbm, sb=sb: e.dma_start(
            sb[:], hbm.rearrange("(c p) n -> p c n", p=128)), wait=list(after))
    P.emit("sync", lambda e: e.dma_start(bo[:], bo_h), wait=list(after),
           inc=C, inc_amt=16)
    P.emit("sync", lambda e: e.dma_start(esel[:], esel_h), wait=list(after),
           inc=C, inc_amt=16)
    for p in range(2):
        P.emit("gpsimd", lambda e, p=p: e.memset(
            vaug[p].ap().rearrange("p (h e) -> p h e", h=16, e=65)[:, :, 64:65], 1.0),
            inc=ONE)

    nsub = NTOK // 128
    for s in range(nsub):
        p = s % 2
        for half in range(2):
            for c in range(CH):
                P.emit("tensor", lambda e, s=s, c=c, half=half: e.matmul(
                    ps_k[:, half * 512:(half + 1) * 512], _r_slice(R, c, s * 128, 128),
                    wk[:, c * 1024 + half * 512: c * 1024 + (half + 1) * 512],
                    start=(c == 0), stop=(c == CH - 1)),
                    wait=(([wtoks["wk"], (VEk, max(0, VEk.n - 1))]
                           + (list(after) if s == 0 else []))
                          if (c == 0 and half == 0) else []),
                    inc=(PEk if (c, half) == (CH - 1, 1) else ()))
        for half in range(2):
            for c in range(CH):
                P.emit("tensor", lambda e, s=s, c=c, half=half: e.matmul(
                    ps_v[:, half * 512:(half + 1) * 512], _r_slice(R, c, s * 128, 128),
                    wv[:, c * 1024 + half * 512: c * 1024 + (half + 1) * 512],
                    start=(c == 0), stop=(c == CH - 1)),
                    wait=([wtoks["wv"], (AEv, max(0, AEv.n - 1))]
                          if (c == 0 and half == 0) else []),
                    inc=(PEv if (c, half) == (CH - 1, 1) else ()))
        P.emit("vector", lambda e: e.tensor_scalar_min(mt[:], ps_k, 0.0),
               wait=[(PEk, None)], inc=MN)
        P.emit("scalar", lambda e: e.activation(et[:], mt[:], AF.Exp),
               wait=[(MN, None)], inc=EX)
        P.emit("vector", lambda e, p=p: e.scalar_tensor_tensor(
            kbf[p][:], ps_k, 0.0, et[:], ALU.max, ALU.add),
            wait=[(EX, None), (PEkv, max(0, PEkv.n - 1))], inc=VEk)
        P.emit("scalar", lambda e, p=p: e.copy(
            vaug[p].ap().rearrange("p (h e) -> p h e", h=16, e=65)[:, :, 0:64],
            ps_v.rearrange("p (h e) -> p h e", h=16, e=64)),
            wait=[(PEv, None), (ONE, 2), (PEkv, max(0, PEkv.n - 1))], inc=AEv)
        for h in range(16):
            P.emit("tensor", lambda e, p=p, h=h: e.matmul(
                ps_kv[h], kbf[p][:, h * 64:(h + 1) * 64],
                vaug[p][:, h * 65:(h + 1) * 65],
                start=True, stop=True),
                wait=([(VEk, None), (AEv, None), (KVA, None)] if h == 0 else []),
                inc=(PEkv if h == 15 else ()))
        psv = A2.ap()[0:64, :].rearrange("p (h e) -> p h e", h=16, e=128)[:, :, 0:65]
        if s == 0:
            P.emit("vector", lambda e, psv=psv: e.tensor_copy(
                kvacc.ap().rearrange("p (h e) -> p h e", h=16, e=65), psv),
                wait=[(PEkv, None)], inc=KVA)
        else:
            P.emit("vector", lambda e, psv=psv: e.tensor_add(
                kvacc.ap().rearrange("p (h e) -> p h e", h=16, e=65),
                kvacc.ap().rearrange("p (h e) -> p h e", h=16, e=65), psv),
                wait=[(PEkv, None)], inc=KVA)

    P.emit("vector", lambda e: e.tensor_copy(kvsb[:], kvacc[:]),
           wait=[(KVA, None)], inc=VS)
    P.emit("vector", lambda e: e.memset(ksbd[:], 0.0), wait=[(VS, None)], inc=VS)
    P.emit("vector", lambda e: e.memset(kvbd[:], 0.0), wait=[(VS, None)], inc=VS)
    for h in range(16):
        c, half = h // 2, h % 2
        P.emit("vector", lambda e, h=h, c=c, half=half: e.tensor_copy(
            ksbd.ap()[half * 64: half * 64 + 64, c * 16 + h: c * 16 + h + 1],
            kvsb[0:64, h * 65 + 64: h * 65 + 65]),
            wait=[(VS, None)], inc=VS)
    for m in range(CH):
        for half in range(2):
            h = 2 * m + half
            P.emit("vector", lambda e, m=m, half=half, h=h: e.tensor_copy(
                kvbd.ap()[half * 64: half * 64 + 64,
                          m * 128 + half * 64: m * 128 + half * 64 + 64],
                kvsb[0:64, h * 65: h * 65 + 64]),
                wait=[(VS, None)], inc=BD if (m, half) == (CH - 1, 1) else VS)

    ntile = NTOK // 512
    atoks = []
    for t in range(ntile):
        for m in range(CH):
            pq = m % 2
            for c in range(CH):
                P.emit("tensor", lambda e, t=t, m=m, c=c, pq=pq: e.matmul(
                    ps_q[pq], wq[:, c * 1024 + m * 128: c * 1024 + (m + 1) * 128],
                    _r_slice(R, c, t * 512, 512), start=(c == 0), stop=(c == CH - 1)),
                    wait=(([wtoks["wq"], (VEq, max(0, VEq.n - 1))]
                           + ([(VEk, None)] if (t, m) == (0, 0) else []))
                          if c == 0 else []),
                    inc=(PEq if c == CH - 1 else ()))
            P.emit("vector", lambda e, pq=pq: e.tensor_scalar_min(
                mt[:, 0:512], ps_q[pq], 0.0), wait=[(PEq, None), (VEq, None)], inc=MN)
            P.emit("scalar", lambda e: e.activation(et[:, 0:512], mt[:, 0:512], AF.Exp),
                   wait=[(MN, None)], inc=EX)
            P.emit("vector", lambda e, m=m, pq=pq: e.scalar_tensor_tensor(
                qbf[:, m * 512:(m + 1) * 512], ps_q[pq], 0.0, et[:, 0:512],
                ALU.max, ALU.add),
                wait=[(EX, None)] + ([(PEo, None)] if m == 0 else []), inc=VEq)
        for c in range(CH):
            P.emit("tensor", lambda e, c=c: e.matmul(
                ps_z, ksbd[:, c * 16:(c + 1) * 16], qbf[:, c * 512:(c + 1) * 512],
                start=(c == 0), stop=(c == CH - 1)),
                wait=([(BD, None), (VEq, None), (VEz, None)]
                      if c == 0 else []),
                inc=(PEz if c == CH - 1 else ()))
        P.emit("vector", lambda e: e.tensor_scalar_add(zs[:], ps_z, 1e-6),
               wait=[(PEz, None), (VEz, None)], inc=VS)
        P.emit("vector", lambda e: e.reciprocal(zs[:], zs[:]),
               wait=[(VS, None)], inc=VS)
        P.emit("vector", lambda e: e.tensor_copy(zib[:], zs[:]),
               wait=[(VS, None)], inc=VEz)
        for m in range(CH):
            P.emit("tensor", lambda e, m=m: e.matmul(
                ps_zb, esel[:, m * 128:(m + 1) * 128], zib[:],
                start=True, stop=True),
                wait=[(VEz, None), (VEzb, None), (C, 32)]
                     + ([(BD, None)] if m == 0 else []),
                inc=PEzb)
            P.emit("tensor", lambda e, m=m, t=t: e.matmul(
                ps_o[m % 2], kvbd[:, m * 128:(m + 1) * 128],
                qbf[:, m * 512:(m + 1) * 512], start=True, stop=True),
                wait=[(VEo, max(0, VEo.n - 1))]
                     + ([(AEv, None)] if (t, m) == (0, 0) else []),
                inc=PEo)
            P.emit("vector", lambda e: e.tensor_copy(zbsb[:], ps_zb),
                   wait=[(PEzb, None), (VS, None)], inc=VEzb)
            P.emit("vector", lambda e, m=m: e.tensor_mul(
                obf[:, m * 512:(m + 1) * 512], ps_o[m % 2], zbsb[:]),
                wait=[(PEo, None), (VEzb, None)]
                     + ([(PEw, None)] if m == 0 else []), inc=VEo)
        for mo in range(CH):
            pw = mo % 2
            for c in range(CH):
                P.emit("tensor", lambda e, mo=mo, c=c, pw=pw: e.matmul(
                    ps_w[pw], wo[:, c * 1024 + mo * 128: c * 1024 + (mo + 1) * 128],
                    obf[:, c * 512:(c + 1) * 512], start=(c == 0), stop=(c == CH - 1)),
                    wait=([wtoks["wo"], (VEo, None), (AEw, max(0, AEw.n - 1))]
                          + ([(BD, None)] if (t, mo) == (0, 0) else [])
                          if c == 0 else []),
                    inc=(PEw if c == CH - 1 else ()))
            P.emit("vector", lambda e, mo=mo, pw=pw, t=t: e.tensor_scalar_add(
                att[t % 2][:, mo * 512:(mo + 1) * 512], ps_w[pw], bo[:, mo:mo + 1]),
                wait=[(PEw, None), (C, 32)]
                     + ([atoks[t - 1]] if (t >= 1 and mo == 0) else []),
                inc=AEw)
        atoks.append(oring.dma(lambda e, t=t: e.dma_start(
            attn0.rearrange("(c p) n -> p c n", p=128)[:, :, t * 512:(t + 1) * 512],
            att[t % 2].ap().rearrange("p (c n) -> p c n", c=CH)),
            wait=[(AEw, None)]))
    P.emit("sync", lambda e: e.nop(), wait=oring.all_done())
    return oring.all_done() + [(AEw, AEw.n)]


import os
P2_NOPREV = os.environ.get("P2_NOPREV", "0") == "1"
P2_NOM0 = os.environ.get("P2_NOM0", "0") == "1"
P2_NOPREVMM = os.environ.get("P2_NOPREVMM", "0") == "1"
P2_NARROW = os.environ.get("P2_NARROW", "0") == "1"
P2_NOMID = os.environ.get("P2_NOMID", "0") == "1"
HEADS2 = 8
W1P = 5632
W2K = 2816


def emit_ph2_layer(P, io, l, after=(), sweeps=("A", "F"), dbg=None, astop=None):
    nw = NTOK // 512
    R = io.sbuf("R", (128, RCOLS), BF16)
    ident = io.sbuf("ident", (128, 128), BF16)
    pre = "q"

    wqkv_h = io.din(f"f_wqkv{l}", (D, 1536), F32)
    wo_h = io.din(f"f_wo{l}", (512, D), F32)
    w1p_h = io.din(f"f_w1p{l}", (D, W1P), F32)
    w2p_h = io.din(f"f_w2p{l}", (W2K, D), F32)
    b1a_h = io.din(f"f_b1a{l}", (128, 22), F32)
    b1g_h = io.din(f"f_b1g{l}", (128, 22), F32)
    lng_h = io.din(f"f_lng{l}", (128, CH), F32)
    lnb_h = io.din(f"f_lnb{l}", (128, CH), F32)
    b2_h = io.din(f"f_b2{l}", (128, CH), F32)
    qs_h = io.din(f"f_qs{l}", (1, 64), F32)
    ks_h = io.din(f"f_ks{l}", (1, 64), F32)
    tabs_h = {nm: io.din(f"tab_{nm}", (128, 2048), BF16)
              for nm in ("csq", "ssq", "csk", "ssk")}
    triu_h = io.din("tri_u", (128, 128), BF16)
    tril_h = io.din("tri_l", (128, 128), BF16)
    m640_h = io.din("mask640", (128, 640), BF16)
    ones8_h = io.din("ones8", (1, 128), BF16)
    ones1_h = io.din("ones1", (1, 128), BF16)
    onescol_h = io.din("onescol", (128, 1), BF16)
    w1bf = io.dint(f"w1bf{l}", (D, W1P), BF16)
    w2bf = io.dint(f"w2bf{l}", (W2K, D), BF16)

    tabs = {nm: io.sbuf(f"p2_{nm}", (128, 2048), BF16)
            for nm in ("csq", "ssq", "csk", "ssk")}
    m640 = io.sbuf("p2_m640", (128, 640), BF16)
    ones8 = io.sbuf("p2_ones8", (1, 128), BF16)
    ones1 = io.sbuf("p2_ones1", (1, 128), BF16)
    onescol = io.sbuf("p2_onescol", (128, 1), BF16)
    uni = io.sbuf("p2_wqkv", (128, CH * 1536), BF16)
    wqkv = uni[:, 0:12288]
    wosb = io.sbuf("p2_wo", (128, 4 * 1024), BF16)
    scr = io.sbuf("p2_scr", (128, 20480), BF16)
    q_fm = [scr[:, 0:2048], scr[:, 2048:4096]]
    k_fm = [scr[:, 4096:6144], scr[:, 6144:8192], scr[:, 8192:10240]]
    v_tm = [scr[:, 10240:12288], scr[:, 12288:14336], scr[:, 14336:16384]]
    o_bf = [scr[:, 16384:18432], scr[:, 18432:20480]]
    h3 = scr[:, 0:11264]
    h2n = uni[:, 0:4096]
    w1t = [uni[:, 4096:6144], uni[:, 6144:8192]]
    w2t = [uni[:, 8192:9216], uni[:, 9216:10240]]
    sqb = [io.sbuf(f"p2_sqb{i}", (128, 512), BF16) for i in range(2)]
    qs64 = io.sbuf("p2_qs64", (1, 64), F32)
    ks64 = io.sbuf("p2_ks64", (1, 64), F32)
    qs64b = io.sbuf("p2_qs64b", (1, 64), BF16)
    ks64b = io.sbuf("p2_ks64b", (1, 64), BF16)
    qsbc = io.sbuf("p2_qsbc", (128, 64), BF16)
    ksbc = io.sbuf("p2_ksbc", (128, 64), BF16)
    sq = [io.sbuf(f"p2_sq{i}", (128, 512), BF16) for i in range(2)]
    nrm2 = [io.sbuf(f"p2_nrm2{i}", (128, 8), F32) for i in range(2)]
    rn = [io.sbuf(f"p2_rn{i}", (128, 8), F32) for i in range(2)]
    qn = [io.sbuf(f"p2_qn{i}", (128, 512), F32) for i in range(2)]
    tmp = [io.sbuf(f"p2_tmp{i}", (128, 512), F32) for i in range(2)]
    t1f = [io.sbuf(f"p2_t1{i}", (128, 512), F32) for i in range(2)]
    qtm = [io.sbuf(f"p2_qtm{i}", (128, 512), BF16) for i in range(2)]
    s_sb = [io.sbuf(f"p2_ssb{i}", (128, 640), F32) for i in range(2)]
    p_bf = [io.sbuf(f"p2_pbf{i}", (128, 640), BF16) for i in range(2)]
    pT_sb = [io.sbuf(f"p2_pT{i}", (128, 640), BF16) for i in range(2)]
    mxc = [io.sbuf(f"p2_mx{i}", (128, 1), F32) for i in range(2)]
    dnc = [io.sbuf(f"p2_dn{i}", (128, 1), F32) for i in range(2)]
    b1a = io.sbuf("p2_b1a", (128, 22), F32)
    b1g = io.sbuf("p2_b1g", (128, 22), F32)
    lng = io.sbuf("p2_lng", (128, CH), F32)
    lnb = io.sbuf("p2_lnb", (128, CH), F32)
    b2c = io.sbuf("p2_b2", (128, CH), F32)
    mrow = io.sbuf("p2_mrow", (1, 512), F32)
    qrow = io.sbuf("p2_qrow", (1, 512), F32)
    vrow = io.sbuf("p2_vrow", (1, 512), F32)
    rrow = io.sbuf("p2_rrow", (1, 512), F32)
    mrrow = io.sbuf("p2_mrrow", (1, 512), F32)
    rrow_bf = io.sbuf("p2_rrowbf", (1, 512), BF16)
    mrrow_bf = io.sbuf("p2_mrrowbf", (1, 512), BF16)
    gg = [uni[:, 10240:10752], uni[:, 10752:11264]]
    nt1 = t1f[0]

    A1 = io.psum("p1_A1", (128, 2048), F32)
    A2 = io.psum("p1_A2", (128, 2048), F32)
    ps_pj = A1[:, 0:512]
    ps_T = A1.ap()[:, 1536:2048].bitcast(BF16)
    ps_S = A2[:, 0:640]
    ps_pT = A2.ap()[:, 1024:1536].bitcast(BF16)
    ps_oh = A2[0:64, 1536:2048]
    ps_w = [A1[:, 512:1024], A1[:, 1024:1536]]
    ps_sr = A2[0:1, 0:512]
    ps_qr = A2[0:1, 512:1024]
    ps_rbc = A2[:, 1024:1536]
    ps_mbc = A2[:, 1536:2048]
    ps_a = A1[:, 0:512]
    ps_g = A1[:, 512:1024]
    accs = [A1[:, 512 * i:512 * (i + 1)] for i in range(4)] + \
           [A2[:, 512 * i:512 * (i + 1)] for i in range(4)]

    ch = lambda nm: P.ch(pre + nm)
    VS = ch("VS")
    TBF = ch("TBF")
    SQA = ch("SQA");  RED = ch("RED");  LNE = ch("LNE");  APL = ch("APL")
    SWP = ch("SWP");  SXT = ch("SXT");  ADD = ch("ADD");  PET = ch("PET")
    TRD = ch("TRD");  QPF = ch("QPF");  KPF = ch("KPF");  VPF = ch("VPF")
    QKR = ch("QKR");  KVR = ch("KVR")
    SMM = ch("SMM");  VSm = ch("VSm");  AEx = ch("AEx")
    PTS = ch("PTS");  TPT = ch("TPT");  VPT = ch("VPT");  PEo = ch("PEo")
    VEo = ch("VEo");  PEw = ch("PEw");  RA = ch("RA")
    AEF = ch("AEF");  QRM = ch("QRM");  RWC = ch("RWC")
    PEb = ch("PEb");  VEn = ch("VEn");  PEa = ch("PEa");  AEg = ch("AEg")
    VEh = ch("VEh");  PEf = ch("PEf");  RAF = ch("RAF");  C2 = ch("C2")
    W2M = ch("W2M");  LNX = ch("LNX")
    QKP = ch("QKP")

    wring = DmaRing(P, "Lw", k=6, eng="gpsimd")
    cring = DmaRing(P, "Ls", k=4, eng="sync")
    fring = DmaRing(P, "Lf", k=4, eng="sync")

    tok_wqkv = wring.dma(lambda e: e.dma_start(
        wqkv[:], wqkv_h.rearrange("(c p) n -> p c n", p=128)),
        wait=[(QKP, None)] + list(after))
    tok_wo = wring.dma(lambda e: e.dma_start(
        wosb[:], wo_h.rearrange("(c p) n -> p c n", p=128)),
        wait=[(PEw, None)] + list(after))
    w1c_toks = [wring.dma(lambda e, j=j: e.dma_start(
        w1bf[:, j * 1408:(j + 1) * 1408], w1p_h[:, j * 1408:(j + 1) * 1408]))
        for j in range(4)]
    w2c_toks = [wring.dma(lambda e, j=j: e.dma_start(
        w2bf[j * 1408:(j + 1) * 1408, :], w2p_h[j * 1408:(j + 1) * 1408, :]))
        for j in range(2)]
    for hbm, sb in ((b1a_h, b1a), (b1g_h, b1g), (lng_h, lng), (lnb_h, lnb),
                    (b2_h, b2c), (qs_h, qs64), (ks_h, ks64),
                    (m640_h, m640), (ones8_h, ones8),
                    (ones1_h, ones1), (onescol_h, onescol)):
        P.emit("sync", lambda e, hbm=hbm, sb=sb: e.dma_start(sb[:], hbm),
               wait=[(RAF, None), (VSm, None)] + list(after), inc=C2, inc_amt=16)
    tab_toks = {nm: cring.dma(lambda e, nm=nm: e.dma_start(
        tabs[nm][:], tabs_h[nm]), wait=[(ADD, None)] + list(after))
        for nm in ("csq", "ssq", "csk", "ssk")}
    C2f = C2.n
    tbf_t = TBF.n + 1

    P.emit("vector", lambda e: e.tensor_copy(qs64b[:], qs64[:]),
           wait=[(C2, C2f)] + list(after), inc=VS)
    P.emit("vector", lambda e: e.tensor_copy(ks64b[:], ks64[:]), inc=VS)
    P.emit("tensor", lambda e: e.matmul(A2[:, 0:64], ones8[:], qs64b[:],
                                        start=True, stop=True),
           wait=[(VS, None)] + list(after), inc=QKP)
    P.emit("tensor", lambda e: e.matmul(A2[:, 64:128], ones1[:], ks64b[:],
                                        start=True, stop=True), inc=QKP)
    P.emit("vector", lambda e: e.tensor_copy(qsbc[:], A2[:, 0:64]),
           wait=[(QKP, None)], inc=VS)
    P.emit("vector", lambda e: e.tensor_copy(ksbc[:], A2[:, 64:128]), inc=VS)
    for nm, bc in (("csq", qsbc), ("ssq", qsbc), ("csk", ksbc), ("ssk", ksbc)):
        P.emit("vector", lambda e, nm=nm, bc=bc: e.tensor_tensor(
            tabs[nm].ap().rearrange("p (s f) -> p s f", s=32),
            tabs[nm].ap().rearrange("p (s f) -> p s f", s=32),
            bc.ap().rearrange("p (o f) -> p o f", o=1).broadcast_to([128, 32, 64]),
            ALU.mult),
            wait=[(VS, None), tab_toks[nm]],
            inc=(TBF if nm == "ssk" else VS))

    ecall = [0]
    bQPF = QPF.n
    cic = [0]

    def _qk_epilogue(nm, dpar, sub, gsub, guard_smm):
        i = ecall[0] % 2
        ecall[0] += 1
        ps = ps_pj
        cs = tabs["csq" if nm == "q" else "csk"]
        ssx = tabs["ssq" if nm == "q" else "ssk"]
        P.emit("scalar", lambda e, ps=ps, i=i: e.activation(
            sq[i][:], ps, AF.Square),
            wait=[(QKP, None), (RED, max(0, RED.n - 1)), (TBF, tbf_t)], inc=SQA)
        for h in range(8):
            P.emit("vector", lambda e, h=h, i=i: e.reduce_sum(
                nrm2[i][:, h:h + 1], sq[i][:, h * 64:(h + 1) * 64], axis=AX.X),
                wait=([(SQA, None), (LNE, max(0, LNE.n - 1))] if h == 0 else []),
                inc=(VS if h == 7 else ()))
        P.emit("vector", lambda e, i=i: e.tensor_scalar_max(
            nrm2[i][:], nrm2[i][:], 1e-24), wait=[(VS, None)], inc=RED)
        P.emit("scalar", lambda e, i=i: e.activation(rn[i][:], nrm2[i][:], AF.Ln),
               wait=[(RED, None), (APL, max(0, APL.n - 1))], inc=LNX)
        P.emit("scalar", lambda e, i=i: e.activation(
            rn[i][:], rn[i][:], AF.Exp, scale=-0.5),
            wait=[(LNX, None)], inc=LNE)
        for h in range(8):
            P.emit("vector", lambda e, h=h, ps=ps, i=i: e.tensor_scalar_mul(
                qn[i][:, h * 64:(h + 1) * 64], ps[:, h * 64:(h + 1) * 64],
                rn[i][:, h:h + 1]),
                wait=([(LNE, None), (SWP, max(0, SWP.n - 1))] if h == 0 else []),
                inc=(APL if h == 7 else VS))
        P.emit("vector", lambda e: e.engine_nop(), wait=[(APL, None)], inc=QPF)
        P.emit("gpsimd", lambda e, i=i: e.tensor_copy(
            tmp[i].ap().rearrange("p (h a j) -> p h a j", h=8, a=2, j=32),
            qn[i].ap().rearrange("p (h a j) -> p h a j", h=8, a=2, j=32)[:, :, ::-1, :]),
            wait=[(APL, None), (SXT, max(0, SXT.n - 1))], inc=SWP)
        P.emit("vector", lambda e, cs=cs, gsub=gsub, i=i: e.tensor_tensor(
            t1f[i].ap().rearrange("p (h f) -> p h f", h=8),
            qn[i].ap().rearrange("p (h f) -> p h f", h=8),
            cs.ap()[:, gsub * 64:gsub * 64 + 64]
              .rearrange("p (o f) -> p o f", o=1).broadcast_to([128, 8, 64]),
            ALU.mult), wait=[(APL, None)], inc=VS)
        P.emit("vector", lambda e, ssx=ssx, gsub=gsub, i=i: e.tensor_tensor(
            tmp[i].ap().rearrange("p (h f) -> p h f", h=8),
            tmp[i].ap().rearrange("p (h f) -> p h f", h=8),
            ssx.ap()[:, gsub * 64:gsub * 64 + 64]
              .rearrange("p (o f) -> p o f", o=1).broadcast_to([128, 8, 64]),
            ALU.mult), wait=[(SWP, None)], inc=SXT)
        P.emit("vector", lambda e, i=i: e.tensor_add(
            qtm[i][:], t1f[i][:], tmp[i][:]),
            wait=[(SXT, None), (PET, max(0, PET.n - 1))], inc=ADD)
        for j in range(4):
            P.emit("tensor", lambda e, j=j, i=i: e.transpose(
                ps_T[:, j * 128:(j + 1) * 128], qtm[i][:, j * 128:(j + 1) * 128],
                ident[:]),
                wait=([(ADD, None), (TRD, None)] if j == 0 else []),
                inc=(PET if j == 3 else ()))
        dst = q_fm[dpar] if nm == "q" else k_fm[dpar]
        P.emit("vector", lambda e, dst=dst, sub=sub: e.tensor_copy(
            dst.rearrange("p (hc t) -> p hc t", hc=4)[:, :, sub * 128:(sub + 1) * 128],
            ps_T[:, 0:512].rearrange("p (hc t) -> p hc t", hc=4)),
            wait=[(PET, None)] + ([(SMM, guard_smm)] if guard_smm else []),
            inc=TRD)

    def emit_chunk(w, sub, nm, guard):
        _S("q")
        ci = cic[0]
        cic[0] += 1
        t0 = w * 512 + sub * 128
        gsub = (t0 // 128) % 32
        col0 = {"q": 0, "k": 512, "v": 1024}[nm]
        for c in range(CH):
            P.emit("tensor", lambda e, c=c, t0=t0, col0=col0: e.matmul(
                ps_pj, _r_shift_slice(R, c, t0, 128),
                wqkv[:, c * 1536 + col0: c * 1536 + col0 + 512],
                start=(c == 0), stop=(c == CH - 1)),
                wait=([tok_wqkv, (TBF, tbf_t), (QPF, bQPF + ci)]
                      if c == 0 else []),
                inc=(QKP if c == CH - 1 else ()))
        if nm == "v":
            P.emit("vector", lambda e, w=w, sub=sub: e.tensor_copy(
                v_tm[w % 3][:, sub * 512:(sub + 1) * 512], ps_pj),
                wait=[(QKP, None)]
                     + ([(PEo, bPEo + guard)] if guard else []),
                inc=QPF)
        else:
            _qk_epilogue(nm, (w % 2) if nm == "q" else (w % 3), sub, gsub,
                         bSMM + guard if guard else 0)

    def attn_window(w):
        par, parp = w % 2, (w - 1) % 2
        for h in range(HEADS2):
            hc, hb = h // 2, 64 * (h % 2)
            for qc in range(4):
                if P2_NOPREV:
                    w = 0
                hx = (4 * h + qc) % 2
                ssb, pbf, pT, mx, dn = (s_sb[hx], p_bf[hx],
                                        pT_sb[hx], mxc[hx], dnc[hx])
                lo = 0 if w > 0 else 512 - 128 * qc
                if P2_NARROW:
                    lo = 512 - 128 * qc
                first = ([(QKR, None), (KVR, None), (C2, C2f)]
                         if (h, qc) == (0, 0) else [])
                prevw = [(VSm, None)]
                qsl = lambda q_fm=q_fm, hb=hb, hc=hc, qc=qc: q_fm[
                    hb:hb + 64, hc * 512 + qc * 128: hc * 512 + (qc + 1) * 128]
                if w > 0:
                    if P2_NOPREVMM:
                        P.emit("tensor", lambda e, hb=hb, hc=hc, qc=qc, par=par,
                               qsl=qsl: e.matmul(
                            ps_S[:, 0:512 - 128 * qc], qsl(),
                            k_fm[par][hb:hb + 64, hc * 512 + qc * 128: hc * 512 + 512],
                            start=True, stop=True), wait=first + prevw, inc=())
                    else:
                        P.emit("tensor", lambda e, hb=hb, hc=hc, qc=qc, parp=parp,
                               qsl=qsl: e.matmul(
                            ps_S[:, 0:512 - 128 * qc], qsl(),
                            k_fm[parp][hb:hb + 64, hc * 512 + qc * 128: hc * 512 + 512],
                            start=True, stop=True), wait=first + prevw, inc=())
                    first, prevw = [], []
                if 128 * qc > 0:
                    P.emit("tensor", lambda e, hb=hb, hc=hc, qc=qc, par=par,
                           qsl=qsl: e.matmul(
                        ps_S[:, 512 - 128 * qc:512], qsl(),
                        k_fm[par][hb:hb + 64, hc * 512: hc * 512 + 128 * qc],
                        start=True, stop=True), wait=first + prevw, inc=())
                    first, prevw = [], []
                P.emit("tensor", lambda e, hb=hb, hc=hc, qc=qc, par=par,
                       qsl=qsl: e.matmul(
                    ps_S[:, 512:640], qsl(),
                    k_fm[par][hb:hb + 64, hc * 512 + qc * 128: hc * 512 + (qc + 1) * 128],
                    start=True, stop=True), wait=first + prevw, inc=SMM)
                P.emit("vector", lambda e, ssb=ssb, lo=lo: e.tensor_tensor(
                    ssb[:, lo:640], ps_S[:, lo:640], m640[:, lo:640], ALU.add),
                    wait=[(SMM, None), (AEx, max(0, AEx.n - 1)),
                          (PTS, max(0, PTS.n - 1))], inc=VSm)
                P.emit("vector", lambda e, ssb=ssb, mx=mx, lo=lo: e.reduce_max(
                    mx[:, 0:1], ssb[:, lo:640], axis=AX.X),
                    wait=[(VSm, None)], inc=VS)
                P.emit("vector", lambda e, mx=mx: e.tensor_scalar_mul(
                    mx[:, 0:1], mx[:, 0:1], -1.0), wait=[(VS, None)], inc=VS)
                P.emit("scalar", lambda e, ssb=ssb, mx=mx, lo=lo: e.activation(
                    ssb[:, lo:640], ssb[:, lo:640], AF.Exp, bias=mx[:, 0:1]),
                    wait=[(VS, None), (PTS, max(0, PTS.n - 1))], inc=AEx)
                P.emit("vector", lambda e, ssb=ssb, dn=dn, lo=lo: e.reduce_sum(
                    dn[:, 0:1], ssb[:, lo:640], axis=AX.X),
                    wait=[(AEx, None), (PTS, max(0, PTS.n - 1))], inc=VS)
                P.emit("vector", lambda e, dn=dn: e.reciprocal(dn[:, 0:1], dn[:, 0:1]),
                       wait=[(VS, None)], inc=VS)
                P.emit("scalar", lambda e, ssb=ssb, pbf=pbf, dn=dn, lo=lo:
                       e.mul(
                    pbf[:, lo:640], ssb[:, lo:640], dn[:, 0:1]),
                    wait=[(VS, None), (TPT, max(0, TPT.n - 1))], inc=PTS)
                jlo = lo // 128
                if astop == "sm":
                    continue
                for j in range(jlo, 5):
                    P.emit("tensor", lambda e, pbf=pbf, j=j: e.transpose(
                        ps_pT[:, j * 128:(j + 1) * 128],
                        pbf[:, j * 128:(j + 1) * 128], ident[:]),
                        wait=([(PTS, None), (VPT, None)] if j == jlo else []),
                        inc=(TPT if j == 4 else ()))
                P.emit("vector", lambda e, pT=pT, jlo=jlo: e.tensor_copy(
                    pT[:, jlo * 128:640], ps_pT[:, jlo * 128:640]),
                    wait=[(TPT, None), (PEo, max(0, PEo.n - 1))], inc=VPT)
                if astop == "pt":
                    continue
                for j in range(jlo, 5):
                    kabs = qc + j
                    pv = parp if kabs < 4 else par
                    sb_ = kabs % 4
                    P.emit("tensor", lambda e, pv=pv, sb_=sb_, h=h, pT=pT, j=j,
                           qc=qc, jlo=jlo: e.matmul(
                        ps_oh[:, qc * 128:(qc + 1) * 128],
                        v_tm[pv][:, sb_ * 512 + h * 64: sb_ * 512 + (h + 1) * 64],
                        pT[:, j * 128:(j + 1) * 128],
                        start=(j == jlo), stop=(j == 4)),
                        wait=([(VPT, None)] + ([(VEo, None)] if qc == 0 else [])
                              if j == jlo else []),
                        inc=(PEo if j == 4 else ()))
            if astop is None:
                P.emit("vector", lambda e, hc=hc, hb=hb: e.tensor_copy(
                    o_bf[hb:hb + 64, hc * 512:(hc + 1) * 512], ps_oh[:, 0:512]),
                    wait=[(PEo, None), (PEw, None)], inc=VEo)

    def wo_window(w):
        for mo in range(CH):
            pw = mo % 2
            for kc in range(4):
                P.emit("tensor", lambda e, mo=mo, kc=kc, pw=pw: e.matmul(
                    ps_w[pw], wosb[:, kc * 1024 + mo * 128: kc * 1024 + (mo + 1) * 128],
                    o_bf[:, kc * 512:(kc + 1) * 512],
                    start=(kc == 0), stop=(kc == 3)),
                    wait=([tok_wo, (VEo, None), (QPF, None), (KPF, None),
                           (RA, max(0, RA.n - 1))] if kc == 0 else []),
                    inc=(PEw if kc == 3 else ()))
            P.emit("vector", lambda e, mo=mo, pw=pw, w=w: e.tensor_add(
                _r_slice(R, mo, w * 512, 512), _r_slice(R, mo, w * 512, 512),
                ps_w[pw]),
                wait=[(PEw, None)], inc=RA)

    _sc0 = P.scope

    def _S(suffix):
        if _sc0:
            P.scope = _sc0 + suffix

    amode = [s for s in sweeps if s in ("A", "QA", "Q")]
    amode = amode[0] if amode else None
    NI = nw * 32
    bSMM = SMM.n; bVSm = VSm.n; bAEx = AEx.n; bPTS = PTS.n; bTPT = TPT.n
    bVPT = VPT.n; bPEo = PEo.n; bVEo = VEo.n; bPEw = PEw.n; bRA = RA.n
    bQKR = QKR.n; bKVR = KVR.n

    def _dec(i):
        w = nw - 1 - i // 32
        h = (i % 32) // 4
        qc = i % 4
        return w, h // 2, 64 * (h % 2), qc, i % 2, (0 if w > 0 else 512 - 128 * qc)

    def st_scores(i):
        _S("a")
        w, hc, hb, qc, hx, lo = _dec(i)
        wi = i // 32
        par, parp = w % 3, (w - 1) % 3
        qf = q_fm[w % 2]
        first = [(VSm, bVSm + i)]
        if i % 32 == 0:
            first += [(QKR, bQKR + wi + 1), (KVR, bKVR + wi + 1)]
        if i == 0:
            first += [(C2, C2f)]
        qsl = lambda qf=qf, hb=hb, hc=hc, qc=qc: qf[
            hb:hb + 64, hc * 512 + qc * 128: hc * 512 + (qc + 1) * 128]
        if w > 0:
            P.emit("tensor", lambda e, hb=hb, hc=hc, qc=qc, parp=parp,
                   qsl=qsl: e.matmul(
                ps_S[:, 0:512 - 128 * qc], qsl(),
                k_fm[parp][hb:hb + 64, hc * 512 + qc * 128: hc * 512 + 512],
                start=True, stop=True), wait=first, inc=())
            first = []
        if qc > 0:
            P.emit("tensor", lambda e, hb=hb, hc=hc, qc=qc, par=par,
                   qsl=qsl: e.matmul(
                ps_S[:, 512 - 128 * qc:512], qsl(),
                k_fm[par][hb:hb + 64, hc * 512: hc * 512 + 128 * qc],
                start=True, stop=True), wait=first, inc=())
            first = []
        P.emit("tensor", lambda e, hb=hb, hc=hc, qc=qc, par=par,
               qsl=qsl: e.matmul(
            ps_S[:, 512:640], qsl(),
            k_fm[par][hb:hb + 64, hc * 512 + qc * 128: hc * 512 + (qc + 1) * 128],
            start=True, stop=True), wait=first, inc=SMM)

    def st_sm1(i):
        _S("a")
        w, hc, hb, qc, hx, lo = _dec(i)
        ssb, mx = s_sb[hx], mxc[hx]
        P.emit("vector", lambda e, ssb=ssb, lo=lo: e.tensor_tensor(
            ssb[:, lo:640], ps_S[:, lo:640], m640[:, lo:640], ALU.add),
            wait=[(SMM, bSMM + i + 1), (PTS, bPTS + max(0, i - 1))], inc=VSm)
        P.emit("vector", lambda e, ssb=ssb, mx=mx, lo=lo: e.reduce_max(
            mx[:, 0:1], ssb[:, lo:640], axis=AX.X), wait=[(VSm, None)], inc=VS)
        P.emit("vector", lambda e, mx=mx: e.tensor_scalar_mul(
            mx[:, 0:1], mx[:, 0:1], -1.0), wait=[(VS, None)], inc=VS)

    def st_exp(i):
        w, hc, hb, qc, hx, lo = _dec(i)
        ssb, mx = s_sb[hx], mxc[hx]
        P.emit("scalar", lambda e, ssb=ssb, mx=mx, lo=lo: e.activation(
            ssb[:, lo:640], ssb[:, lo:640], AF.Exp, bias=mx[:, 0:1]),
            wait=[(VS, None)], inc=AEx)

    def st_sm2(i):
        w, hc, hb, qc, hx, lo = _dec(i)
        ssb, dn = s_sb[hx], dnc[hx]
        P.emit("vector", lambda e, ssb=ssb, dn=dn, lo=lo: e.reduce_sum(
            dn[:, 0:1], ssb[:, lo:640], axis=AX.X),
            wait=[(AEx, bAEx + i + 1)], inc=VS)
        P.emit("vector", lambda e, dn=dn: e.reciprocal(dn[:, 0:1], dn[:, 0:1]),
               wait=[(VS, None)], inc=VS)

    def st_mul(i):
        w, hc, hb, qc, hx, lo = _dec(i)
        ssb, pbf, dn = s_sb[hx], p_bf[hx], dnc[hx]
        P.emit("scalar", lambda e, ssb=ssb, pbf=pbf, dn=dn, lo=lo: e.mul(
            pbf[:, lo:640], ssb[:, lo:640], dn[:, 0:1]),
            wait=[(VS, None), (TPT, bTPT + max(0, i - 1))], inc=PTS)

    def st_tp(i):
        _S("a")
        w, hc, hb, qc, hx, lo = _dec(i)
        pbf = p_bf[hx]
        jlo = lo // 128
        for j in range(jlo, 5):
            P.emit("tensor", lambda e, pbf=pbf, j=j: e.transpose(
                ps_pT[:, j * 128:(j + 1) * 128],
                pbf[:, j * 128:(j + 1) * 128], ident[:]),
                wait=([(PTS, bPTS + i + 1), (VPT, bVPT + i)] if j == jlo else []),
                inc=(TPT if j == 4 else ()))

    def st_pc(i):
        w, hc, hb, qc, hx, lo = _dec(i)
        pT = pT_sb[hx]
        jlo = lo // 128
        P.emit("vector", lambda e, pT=pT, jlo=jlo: e.tensor_copy(
            pT[:, jlo * 128:640], ps_pT[:, jlo * 128:640]),
            wait=[(TPT, bTPT + i + 1), (PEo, bPEo + max(0, i - 1))], inc=VPT)

    def st_pv(i):
        _S("a")
        w, hc, hb, qc, hx, lo = _dec(i)
        par, parp = w % 3, (w - 1) % 3
        pT = pT_sb[hx]
        h = (i % 32) // 4
        jlo = lo // 128
        for j in range(jlo, 5):
            kabs = qc + j
            pv = parp if kabs < 4 else par
            sb_ = kabs % 4
            P.emit("tensor", lambda e, pv=pv, sb_=sb_, h=h, pT=pT, j=j,
                   qc=qc: e.matmul(
                ps_oh[:, qc * 128:(qc + 1) * 128],
                v_tm[pv][:, sb_ * 512 + h * 64: sb_ * 512 + (h + 1) * 64],
                pT[:, j * 128:(j + 1) * 128],
                start=(j == jlo), stop=(j == 4)),
                wait=([(VPT, bVPT + i + 1), (VEo, bVEo + i // 4)]
                      if j == jlo else []),
                inc=(PEo if j == 4 else ()))

    def st_ev(i):
        w, hc, hb, qc, hx, lo = _dec(i)
        wi = i // 32
        P.emit("vector", lambda e, w=w, hc=hc, hb=hb: e.tensor_copy(
            o_bf[w % 2][hb:hb + 64, hc * 512:(hc + 1) * 512], ps_oh[:, 0:512]),
            wait=[(PEo, bPEo + i + 1)]
                 + ([(PEw, bPEw + 8 * max(0, wi - 1))] if amode == "A" else []),
            inc=VEo)

    def st_wo(w, mo):
        _S("w")
        wi = nw - 1 - w
        pw = mo % 2
        for kc in range(4):
            P.emit("tensor", lambda e, w=w, mo=mo, kc=kc, pw=pw: e.matmul(
                ps_w[pw], wosb[:, kc * 1024 + mo * 128: kc * 1024 + (mo + 1) * 128],
                o_bf[w % 2][:, kc * 512:(kc + 1) * 512],
                start=(kc == 0), stop=(kc == 3)),
                wait=([tok_wo, (VEo, bVEo + 8 * (wi + 1)),
                       (RA, bRA + 8 * wi + max(0, mo - 1))] if kc == 0 else []),
                inc=(PEw if kc == 3 else ()))
        P.emit("vector", lambda e, w=w, mo=mo, pw=pw: e.tensor_add(
            _r_slice(R, mo, w * 512, 512), _r_slice(R, mo, w * 512, 512),
            ps_w[pw]),
            wait=[(PEw, bPEw + 8 * wi + mo + 1)], inc=RA)

    if amode:
        for sub in range(4):
            for nm in ("q", "k", "v"):
                emit_chunk(nw - 1, sub, nm, 0)
        P.emit("vector", lambda e: e.engine_nop(), wait=[(TRD, None)], inc=QKR)
        if nw >= 2:
            for sub in range(4):
                for nm in ("k", "v"):
                    emit_chunk(nw - 2, sub, nm, 0)
        P.emit("vector", lambda e: e.engine_nop(),
               wait=[(TRD, None), (QPF, None)], inc=KVR)

    def _stretch_chunks(wi):
        w = nw - 1 - wi
        cl = []
        if w - 1 >= 0:
            for sub in range(4):
                cl.append(("c", w - 1, sub, "q"))
            cl.append(("qnop",))
        if w - 2 >= 0:
            for sub in range(4):
                cl.append(("c", w - 2, sub, "k"))
                cl.append(("c", w - 2, sub, "v"))
        if w - 1 >= 0:
            cl.append(("knop",))
        return cl

    def _emit_item(it, wi):
        if it[0] == "c":
            emit_chunk(it[1], it[2], it[3], wi * 32)
        elif it[0] == "qnop":
            P.emit("vector", lambda e: e.engine_nop(),
                   wait=[(TRD, None)], inc=QKR)
        else:
            P.emit("vector", lambda e: e.engine_nop(),
                   wait=[(TRD, None), (QPF, None)], inc=KVR)

    if amode == "Q":
        for wi in range(nw):
            for it in _stretch_chunks(wi):
                _emit_item(it, wi)
    elif amode in ("A", "QA"):
        sched = {}
        for cyc in range(NI + 13):
            wi = cyc // 32
            if cyc < NI and cyc % 32 == 0:
                cl = _stretch_chunks(wi)
                for j, it in enumerate(cl):
                    sched.setdefault(cyc + 3 + (j * 27) // max(1, len(cl)),
                                     []).append((it, wi))
            if 0 <= cyc - 3 < NI:
                st_tp(cyc - 3)
            if 0 <= cyc - 4 < NI:
                st_pv(cyc - 4)
            if 0 <= cyc - 1 < NI:
                st_sm1(cyc - 1)
                st_exp(cyc - 1)
            if 0 <= cyc - 2 < NI:
                st_sm2(cyc - 2)
                st_mul(cyc - 2)
            if 0 <= cyc - 3 < NI:
                st_pc(cyc - 3)
            if 0 <= cyc - 4 < NI and (cyc - 4) % 4 == 3:
                st_ev(cyc - 4)
            if amode == "A" and cyc >= 36 and 0 <= (cyc % 32) - 4 < 8:
                wprev = nw - cyc // 32
                if 0 <= wprev < nw:
                    st_wo(wprev, (cyc % 32) - 4)
            if cyc < NI:
                st_scores(cyc)
            for it, swi in sched.pop(cyc, []):
                _emit_item(it, swi)
        for it_list in sched.values():
            for it, swi in it_list:
                _emit_item(it, swi)

    for sti, st in enumerate(range(nw - 1, -1, -1) if "F" in sweeps else []):
        t0 = st * 512
        _S("n")
        for c in range(CH):
            P.emit("scalar", lambda e, c=c, t0=t0: e.activation(
                sqb[c % 2][:], _r_shift_slice(R, c, t0, 512), AF.Square),
                wait=[(RA, None), (QRM, max(0, QRM.n - 1))]
                     + (list(after) if (sti == 0 and c == 0) else []), inc=AEF)
            P.emit("tensor", lambda e, c=c, t0=t0: e.matmul(
                ps_sr, onescol[:], _r_shift_slice(R, c, t0, 512),
                start=(c == 0), stop=(c == CH - 1)),
                wait=([(C2, C2f), (RWC, None), (RAF, None), (RA, None),
                       (VEh, None)] if c == 0 else []), inc=())
            P.emit("tensor", lambda e, c=c: e.matmul(
                ps_qr, onescol[:], sqb[c % 2][:],
                start=(c == 0), stop=(c == CH - 1)),
                wait=[(AEF, None)], inc=QRM)
        P.emit("vector", lambda e: e.tensor_scalar_mul(mrow[:], ps_sr, 1.0 / D),
               wait=[(QRM, None)], inc=VS)
        P.emit("vector", lambda e: e.tensor_scalar_mul(qrow[:], ps_qr, 1.0 / D),
               inc=VS)
        P.emit("vector", lambda e: e.tensor_mul(vrow[:], mrow[:], mrow[:]),
               wait=[(VS, None)], inc=VS)
        P.emit("vector", lambda e: e.tensor_sub(vrow[:], qrow[:], vrow[:]),
               wait=[(VS, None)], inc=VS)
        P.emit("scalar", lambda e: e.activation(rrow[:], vrow[:], AF.Ln, bias=1e-5),
               wait=[(VS, None)], inc=LNX)
        P.emit("scalar", lambda e: e.activation(rrow[:], rrow[:], AF.Exp, scale=-0.5),
               wait=[(LNX, None)], inc=AEF)
        P.emit("vector", lambda e: e.tensor_mul(mrrow[:], mrow[:], rrow[:]),
               wait=[(AEF, None)], inc=VS)
        P.emit("vector", lambda e: e.tensor_copy(rrow_bf[:], rrow[:]), inc=VS)
        P.emit("vector", lambda e: e.tensor_copy(mrrow_bf[:], mrrow[:]),
               wait=[(VS, None)], inc=RWC)
        P.emit("tensor", lambda e: e.matmul(ps_rbc, ones1[:], rrow_bf[:],
                                            start=True, stop=True),
               wait=[(RWC, None), (VEn, None), (PEf, None), (RAF, None)], inc=())
        P.emit("tensor", lambda e: e.matmul(ps_mbc, ones1[:], mrrow_bf[:],
                                            start=True, stop=True), inc=PEb)
        for c in range(CH):
            P.emit("vector", lambda e, c=c, t0=t0: e.tensor_mul(
                nt1[:], _r_shift_slice(R, c, t0, 512), ps_rbc),
                wait=[(PEb, None), (VEn, None)]
                     + ([(PEa, None)] if c == 0 else []), inc=VS)
            P.emit("vector", lambda e: e.tensor_sub(nt1[:], nt1[:], ps_mbc),
                   wait=[(VS, None)], inc=VS)
            P.emit("vector", lambda e, c=c: e.tensor_scalar(
                h2n[:, c * 512:(c + 1) * 512], nt1[:], lng[:, c:c + 1],
                lnb[:, c:c + 1], ALU.mult, ALU.add),
                wait=[(VS, None)], inc=VEn)
        if dbg == "h2n" and st == 0:
            P.emit("sync", lambda e: e.dma_start(
                io.dout("dbg_h2n", (128, 4096), BF16), h2n[:]),
                wait=[(VEn, None)], inc=P.ch("DBG"), inc_amt=16)
        _S("1")
        for m in range(22):
            pw1 = m % 2
            tok = fring.dma(lambda e, m=m, pw1=pw1: e.dma_start(
                w1t[pw1][:], w1bf.rearrange("(c p) n -> p c n", p=128)
                [:, :, m * 256:(m + 1) * 256]),
                wait=[(PEa, max(0, PEa.n - 1))] + (w1c_toks if sti == 0 else [])
                     + ([(RA, None)] if (sti == 0 and m < 2) else []))
            for c in range(CH):
                P.emit("tensor", lambda e, c=c, pw1=pw1: e.matmul(
                    ps_a, w1t[pw1][:, c * 256: c * 256 + 128],
                    h2n[:, c * 512:(c + 1) * 512],
                    start=(c == 0), stop=(c == CH - 1)),
                    wait=([tok, (VEn, None), (VEh, None), (RWC, None),
                           (RAF, None)] if c == 0 else []), inc=())
            for c in range(CH):
                P.emit("tensor", lambda e, c=c, pw1=pw1: e.matmul(
                    ps_g, w1t[pw1][:, c * 256 + 128: c * 256 + 256],
                    h2n[:, c * 512:(c + 1) * 512],
                    start=(c == 0), stop=(c == CH - 1)),
                    wait=[(AEg, None)] if c == 0 else [],
                    inc=(PEa if c == CH - 1 else ()))
            P.emit("scalar", lambda e, m=m: e.activation(
                gg[m % 2][:], ps_g, AF.Gelu, bias=b1g[:, m:m + 1]),
                wait=[(PEa, None), (VEh, max(0, VEh.n - 1))], inc=AEg)
            P.emit("vector", lambda e, m=m: e.scalar_tensor_tensor(
                h3[:, m * 512:(m + 1) * 512], ps_a, b1a[:, m:m + 1], gg[m % 2][:],
                ALU.add, ALU.mult),
                wait=[(AEg, None)], inc=VEh)
        if dbg == "h3" and st == 0:
            P.emit("sync", lambda e: e.dma_start(
                io.dout("dbg_h3", (128, 11264), BF16), h3.rearrange("p n -> p n")),
                wait=[(VEh, None)], inc=P.ch("DBG"), inc_amt=16)
        _S("2")
        pef_marks = {}
        for kk in range(22):
            tok2 = fring.dma(lambda e, kk=kk: e.dma_start(
                w2t[kk % 2][:], w2bf[128 * kk:128 * (kk + 1), :]),
                wait=[(W2M, max(0, W2M.n - 1))]
                     + ([(PEf, None)] if kk == 0 else [])
                     + (w2c_toks if sti == 0 else [])
                     + ([(RA, None)] if (sti == 0 and kk < 2) else []))
            for mo in range(CH):
                P.emit("tensor", lambda e, mo=mo, kk=kk: e.matmul(
                    accs[mo], w2t[kk % 2][:, mo * 128:(mo + 1) * 128],
                    h3[:, kk * 512:(kk + 1) * 512],
                    start=(kk == 0), stop=(kk == 21)),
                    wait=(([tok2, (VEh, None), (AEg, None), (RAF, None),
                            (RWC, None), (VEn, None)] if kk == 0 else [tok2])
                          if mo == 0 else []),
                    inc=((PEf if kk == 21 else W2M) if mo == CH - 1 else ()))
            if kk == 21:
                for mo in range(CH):
                    pef_marks[mo] = PEf.n
        for mo in range(CH):
            P.emit("vector", lambda e, mo=mo, t0=t0: e.scalar_tensor_tensor(
                _r_slice(R, mo, t0, 512), accs[mo], b2c[:, mo:mo + 1],
                _r_slice(R, mo, t0, 512), ALU.add, ALU.add),
                wait=[(PEf, pef_marks[mo])], inc=RAF)
    return [(RAF, RAF.n)] if "F" in sweeps else [(RA, RA.n)]


def emit_ph3(P, io, after=()):
    R = io.sbuf("R", (128, RCOLS), BF16)
    x_t = io.din("x_t", (D, NTOK), F32)
    attn0 = io.dram["attn0"] if "attn0" in io.dram else \
        io.din("attn0_in", (D, NTOK), BF16)
    x1s = io.dout("x1s", (D, NTOK), F32)
    onescol = io.sbuf("p2_onescol", (128, 1), BF16)
    ones1 = io.sbuf("p2_ones1", (1, 128), BF16)

    xs = io.sbuf("p3_xs", (128, 4096), F32)
    as_ = io.sbuf("p3_as", (128, 4096), BF16)
    x1bf = io.sbuf("p3_x1bf", (128, 4096), BF16)
    sqbf = io.sbuf("p3_sqbf", (128, 4096), BF16)
    mrow = io.sbuf("p2_mrow", (1, 512), F32)
    qrow = io.sbuf("p2_qrow", (1, 512), F32)
    vrow = io.sbuf("p2_vrow", (1, 512), F32)
    rrow = io.sbuf("p2_rrow", (1, 512), F32)
    mrrow = io.sbuf("p2_mrrow", (1, 512), F32)
    rrow_bf = io.sbuf("p2_rrowbf", (1, 512), BF16)
    mrrow_bf = io.sbuf("p2_mrrowbf", (1, 512), BF16)
    nt1 = io.sbuf("p3_nt1", (128, 512), F32)

    A2 = io.psum("p1_A2", (128, 2048), F32)
    ps_sr = A2[0:1, 0:512]
    ps_qr = A2[0:1, 512:1024]
    ps_rbc = A2[:, 1024:1536]
    ps_mbc = A2[:, 1536:2048]

    ch = lambda nm: P.ch("q" + nm)
    VS, AEF, QRM, RWC, PEb, VEn, LNX = (ch(x) for x in
        ("VS", "AEF", "QRM", "RWC", "PEb", "VEn", "LNX"))
    C2 = ch("C2")
    XAD = P.ch("p3XAD")
    XBF = P.ch("p3XBF")
    ring = DmaRing(P, "Ls", k=4, eng="sync")
    oring = DmaRing(P, "Lo", k=2, eng="sync")
    otoks = []

    ntile = NTOK // 512
    for t in range(ntile):
        t0 = t * 512
        tok_x = ring.dma(lambda e, t0=t0: e.dma_start(
            xs[:], x_t.rearrange("(c p) n -> p c n", p=128)[:, :, t0:t0 + 512]),
            wait=[(XAD, None), (XBF, None), (VEn, None)]
                 + ([otoks[t - 1]] if t >= 1 else [])
                 + (list(after) if t == 0 else []))
        tok_a = ring.dma(lambda e, t0=t0: e.dma_start(
            as_[:], attn0.rearrange("(c p) n -> p c n", p=128)[:, :, t0:t0 + 512]),
            wait=[(XAD, None)])
        P.emit("vector", lambda e: e.tensor_add(xs[:], xs[:], as_[:]),
               wait=[tok_x, tok_a] + (list(after) if t == 0 else []), inc=VS)
        P.emit("vector", lambda e, t0=t0: e.tensor_add(
            xs.ap().rearrange("p (c n) -> p c n", c=CH),
            xs.ap().rearrange("p (c n) -> p c n", c=CH),
            R.ap().rearrange("p (c n) -> p c n", c=CH, n=RSTR)[:, :, 1 + t0:1 + t0 + 512]),
            wait=[(VS, None), (VEn, None)], inc=XAD)
        otoks.append(oring.dma(lambda e, t0=t0: e.dma_start(
            x1s.rearrange("(c p) n -> p c n", p=128)[:, :, t0:t0 + 512], xs[:]),
            wait=[(XAD, None)]))
        P.emit("scalar", lambda e: e.copy(x1bf[:], xs[:]),
               wait=[(XAD, None), (QRM, max(0, QRM.n - 8))], inc=XBF)
        P.emit("scalar", lambda e: e.activation(sqbf[:], xs[:], AF.Square), inc=XBF)
        for c in range(CH):
            P.emit("tensor", lambda e, c=c: e.matmul(
                ps_sr, onescol[:], x1bf[:, c * 512:(c + 1) * 512],
                start=(c == 0), stop=(c == CH - 1)),
                wait=([(XBF, None), (RWC, None), (VEn, None), (C2, None)]
                      if c == 0 else []),
                inc=())
            P.emit("tensor", lambda e, c=c: e.matmul(
                ps_qr, onescol[:], sqbf[:, c * 512:(c + 1) * 512],
                start=(c == 0), stop=(c == CH - 1)), inc=QRM)
        P.emit("vector", lambda e: e.tensor_scalar_mul(mrow[:], ps_sr, 1.0 / D),
               wait=[(QRM, None)], inc=VS)
        P.emit("vector", lambda e: e.tensor_scalar_mul(qrow[:], ps_qr, 1.0 / D),
               inc=VS)
        P.emit("vector", lambda e: e.tensor_mul(vrow[:], mrow[:], mrow[:]),
               wait=[(VS, None)], inc=VS)
        P.emit("vector", lambda e: e.tensor_sub(vrow[:], qrow[:], vrow[:]),
               wait=[(VS, None)], inc=VS)
        P.emit("scalar", lambda e: e.activation(rrow[:], vrow[:], AF.Ln, bias=1e-5),
               wait=[(VS, None)], inc=LNX)
        P.emit("scalar", lambda e: e.activation(rrow[:], rrow[:], AF.Exp, scale=-0.5),
               wait=[(LNX, None)], inc=AEF)
        P.emit("vector", lambda e: e.tensor_mul(mrrow[:], mrow[:], rrow[:]),
               wait=[(AEF, None)], inc=VS)
        P.emit("vector", lambda e: e.tensor_copy(rrow_bf[:], rrow[:]), inc=VS)
        P.emit("vector", lambda e: e.tensor_copy(mrrow_bf[:], mrrow[:]),
               wait=[(VS, None)], inc=RWC)
        P.emit("tensor", lambda e: e.matmul(ps_rbc, ones1[:], rrow_bf[:],
                                            start=True, stop=True),
               wait=[(RWC, None), (VEn, None), (C2, None)], inc=())
        P.emit("tensor", lambda e: e.matmul(ps_mbc, ones1[:], mrrow_bf[:],
                                            start=True, stop=True), inc=PEb)
        for c in range(CH):
            P.emit("vector", lambda e, c=c: e.tensor_mul(
                nt1[:], xs[:, c * 512:(c + 1) * 512], ps_rbc),
                wait=[(PEb, None), (VEn, None)], inc=VS)
            P.emit("vector", lambda e, c=c, t0=t0: e.tensor_sub(
                _r_slice(R, c, t0, 512), nt1[:], ps_mbc),
                wait=[(VS, None)], inc=VEn)
    P.emit("sync", lambda e: e.nop(), wait=oring.all_done() + [(VEn, None)])
    return [(P.ch("qVEn"), P.ch("qVEn").n)] + oring.all_done()


def emit_ph4(P, io, after=()):
    R = io.sbuf("R", (128, RCOLS), BF16)
    ident = io.sbuf("ident", (128, 128), BF16)
    m640 = io.sbuf("p2_m640", (128, 640), BF16)
    ones1 = io.sbuf("p2_ones1", (1, 128), BF16)
    nsel = NTOK // 4
    nseg = NTOK // 2048
    w_inT = io.din("l1winT", (D, 3072), F32)
    binqk = io.din("l1binqk", (128, 16), F32)
    binv = io.din("l1binv", (1, 1024), F32)
    w_outT = io.din("l1woutT", (D, D), F32)
    bout = io.din("l1bout", (128, CH), F32)
    attn1s = io.dout("attn1s", (D, nsel), BF16)

    wqt = [io.sbuf(f"p4_wqt{i}", (128, CH * 128), BF16) for i in range(2)]
    wvt = [io.sbuf(f"p4_wvt{i}", (128, CH * 512), BF16) for i in range(2)]
    wout = io.sbuf("p4_wout", (128, CH * 1024), BF16)
    binqk_s = io.sbuf("p4_binqk", (128, 16), F32)
    binv_s = io.sbuf("p4_binv", (1, 1024), F32)
    binv_bf = io.sbuf("p4_binvbf", (1, 1024), BF16)
    vbc = io.sbuf("p4_vbc", (128, 1024), BF16)
    bout_s = io.sbuf("p4_bout", (128, CH), F32)
    xsel = io.sbuf("p4_xsel", (128, 4096), BF16)
    qbf = io.sbuf("p4_qbf", (128, 4096), BF16)
    kbf = io.sbuf("p4_kbf", (128, 4096), BF16)
    vtm = io.sbuf("p4_vtm", (128, 4096), BF16)
    obf = io.sbuf("p4_obf", (128, 4096), BF16)
    att = io.sbuf("p4_att", (128, 4096), BF16)
    s_sb = [io.sbuf(f"p2_ssb{i}", (128, 640), F32) for i in range(2)]
    p_bf = [io.sbuf(f"p2_pbf{i}", (128, 640), BF16) for i in range(2)]
    pT_sb = [io.sbuf(f"p2_pT{i}", (128, 640), BF16) for i in range(2)]
    mxc = [io.sbuf(f"p2_mx{i}", (128, 1), F32) for i in range(2)]
    dnc = [io.sbuf(f"p2_dn{i}", (128, 1), F32) for i in range(2)]

    A1 = io.psum("p1_A1", (128, 2048), F32)
    A2 = io.psum("p1_A2", (128, 2048), F32)
    ps_q = [A1[:, 0:512], A1[:, 512:1024]]
    ps_v = [A1[:, 1024:1536], A1[:, 1536:2048]]
    ps_S = A2[:, 0:640]
    ps_pT = A2.ap()[:, 1024:1536].bitcast(BF16)
    ps_oh = A2[0:64, 1536:2048]
    ps_w = [A1[:, 0:512], A1[:, 512:1024]]

    ch = lambda nm: P.ch("q" + nm)
    VS = ch("VS")
    SMM = ch("SMM"); VSm = ch("VSm"); AEx = ch("AEx"); PTS = ch("PTS")
    TPT = ch("TPT"); VPT = ch("VPT"); PEo = ch("PEo"); VEo = ch("VEo")
    PEw = ch("PEw"); QKP = ch("QKP"); C2 = ch("C2")
    XSL = P.ch("p4XSL")
    QEV = P.ch("p4QEV")
    AEW = P.ch("p4AEW")
    VBC = P.ch("p4VBC")
    C4 = P.ch("p4C")
    wring = DmaRing(P, "Lw", k=6, eng="gpsimd")
    fring = DmaRing(P, "Lf", k=4, eng="sync")
    oring = DmaRing(P, "Lo", k=2, eng="sync")
    atoks = []
    P4Q = P.ch("p4PQ")
    P4V = P.ch("p4PV")

    tok_wout = wring.dma(lambda e: e.dma_start(
        wout[:], w_outT.rearrange("(c p) n -> p c n", p=128)), wait=list(after))
    winbf = io.dint("winbf", (D, 3072), BF16)
    win_casts = [wring.dma(lambda e, j=j: e.dma_start(
        winbf[j * 512:(j + 1) * 512, :], w_inT[j * 512:(j + 1) * 512, :]),
        wait=list(after)) for j in range(2)]
    for hbm, sb in ((binqk, binqk_s), (binv, binv_s), (bout, bout_s)):
        P.emit("sync", lambda e, hbm=hbm, sb=sb: e.dma_start(sb[:], hbm),
               wait=list(after), inc=C4, inc_amt=16)
    C4f = C4.n
    P.emit("vector", lambda e: e.tensor_copy(binv_bf[:], binv_s[:]),
           wait=[(C4, C4f)], inc=VS)
    P.emit("tensor", lambda e: e.matmul(A2[:, 0:512], ones1[:],
                                        binv_bf[:, 0:512], start=True, stop=True),
           wait=[(VS, None), (C2, None)] + list(after), inc=())
    P.emit("tensor", lambda e: e.matmul(A2[:, 512:1024], ones1[:],
                                        binv_bf[:, 512:1024], start=True, stop=True),
           inc=QKP)
    P.emit("vector", lambda e: e.tensor_copy(vbc[:], A2[:, 0:1024]),
           wait=[(QKP, None)], inc=VBC)

    for s in range(nseg):
        base = s * 2048
        for c in range(CH):
            P.emit("vector", lambda e, c=c, base=base: e.tensor_copy(
                xsel[:, c * 512:(c + 1) * 512],
                R.ap()[:, c * RSTR + 1 + base: c * RSTR + 1 + base + 2048][:, ::4]),
                wait=([(VBC, None)] if (s, c) == (0, 0) else [])
                     + ([(PEw, None)] if c == 0 else []),
                inc=(XSL if c == CH - 1 else ()))
        for qk in range(2):
            for mo in range(CH):
                pq = mo % 2
                tw = fring.dma(lambda e, mo=mo, qk=qk, pq=pq: e.dma_start(
                    wqt[pq][:], winbf.rearrange("(c p) n -> p c n", p=128)
                    [:, :, qk * 1024 + mo * 128: qk * 1024 + (mo + 1) * 128]),
                    wait=[(P4Q, max(0, P4Q.n - 1))]
                         + (win_casts if (s, qk, mo) == (0, 0, 0) else []))
                for c in range(CH):
                    P.emit("tensor", lambda e, c=c, pq=pq: e.matmul(
                        ps_q[pq], wqt[pq][:, c * 128:(c + 1) * 128],
                        xsel[:, c * 512:(c + 1) * 512],
                        start=(c == 0), stop=(c == CH - 1)),
                        wait=([tw, (XSL, None), (QEV, max(0, QEV.n - 1))]
                              if c == 0 else []),
                        inc=(P4Q if c == CH - 1 else ()))
                if qk == 0:
                    P.emit("vector", lambda e, mo=mo, pq=pq: e.tensor_scalar(
                        qbf[:, mo * 512:(mo + 1) * 512], ps_q[pq],
                        binqk_s[:, mo:mo + 1], 0.125, ALU.add, ALU.mult),
                        wait=[(P4Q, None), (C4, C4f)], inc=QEV)
                else:
                    P.emit("vector", lambda e, mo=mo, pq=pq: e.tensor_scalar_add(
                        kbf[:, mo * 512:(mo + 1) * 512], ps_q[pq],
                        binqk_s[:, 8 + mo:8 + mo + 1]),
                        wait=[(P4Q, None)], inc=QEV)
        for half in range(2):
            tv = fring.dma(lambda e, half=half: e.dma_start(
                wvt[half % 2][:], winbf.rearrange("(c p) n -> p c n", p=128)
                [:, :, 2048 + half * 512: 2048 + (half + 1) * 512]),
                wait=[(P4V, max(0, P4V.n - 4))]
                     + (win_casts if half == 0 else []))
            for sub in range(4):
                pv = sub % 2
                for c in range(CH):
                    P.emit("tensor", lambda e, sub=sub, half=half, c=c, pv=pv: e.matmul(
                        ps_v[pv],
                        xsel[:, c * 512 + sub * 128: c * 512 + sub * 128 + 128],
                        wvt[half % 2][:, c * 512:(c + 1) * 512],
                        start=(c == 0), stop=(c == CH - 1)),
                        wait=([tv, (QEV, max(0, QEV.n - 1))] if c == 0 else []),
                        inc=(P4V if c == CH - 1 else ()))
                P.emit("vector", lambda e, sub=sub, half=half, pv=pv: e.tensor_tensor(
                    vtm[:, sub * 1024 + half * 512: sub * 1024 + (half + 1) * 512],
                    ps_v[pv], vbc[:, half * 512:(half + 1) * 512], ALU.add),
                    wait=[(P4V, None), (VBC, 1)], inc=QEV)
        bSMM = SMM.n; bVSm = VSm.n; bAEx = AEx.n; bPTS = PTS.n; bTPT = TPT.n
        bVPT = VPT.n; bPEo = PEo.n; bVEo = VEo.n; bPEw = PEw.n; bAEW = AEW.n
        bQEV = QEV.n

        def _d4(i):
            h = i // 4
            return h // 2, 64 * (h % 2), i % 4, i % 2, 128 * (i % 4 + 1)

        def s4_scores(i, s=s):
            hc, hb, qc, hx, wdt = _d4(i)
            first = [(VSm, bVSm + i)] + ([(QEV, bQEV)] if i == 0 else [])
            P.emit("tensor", lambda e, hb=hb, hc=hc, qc=qc, wdt=wdt: e.matmul(
                ps_S[:, 0:wdt],
                qbf[hb:hb + 64, hc * 512 + qc * 128: hc * 512 + (qc + 1) * 128],
                kbf[hb:hb + 64, hc * 512: hc * 512 + wdt],
                start=True, stop=True),
                wait=first, inc=SMM)

        def s4_sm1(i, s=s):
            hc, hb, qc, hx, wdt = _d4(i)
            ssb, mx = s_sb[hx], mxc[hx]
            P.emit("vector", lambda e, ssb=ssb, qc=qc, wdt=wdt: e.tensor_tensor(
                ssb[:, 0:wdt], ps_S[:, 0:wdt],
                m640[:, 512 - qc * 128:640], ALU.add),
                wait=[(SMM, bSMM + i + 1), (PTS, bPTS + max(0, i - 1))]
                     + ([(C2, None)] if (s, i) == (0, 0) else []), inc=VSm)
            P.emit("vector", lambda e, ssb=ssb, mx=mx, wdt=wdt: e.reduce_max(
                mx[:, 0:1], ssb[:, 0:wdt], axis=AX.X),
                wait=[(VSm, None)], inc=VS)
            P.emit("vector", lambda e, mx=mx: e.tensor_scalar_mul(
                mx[:, 0:1], mx[:, 0:1], -1.0), wait=[(VS, None)], inc=VS)

        def s4_exp(i):
            hc, hb, qc, hx, wdt = _d4(i)
            ssb, mx = s_sb[hx], mxc[hx]
            P.emit("scalar", lambda e, ssb=ssb, mx=mx, wdt=wdt: e.activation(
                ssb[:, 0:wdt], ssb[:, 0:wdt], AF.Exp, bias=mx[:, 0:1]),
                wait=[(VS, None)], inc=AEx)

        def s4_sm2(i):
            hc, hb, qc, hx, wdt = _d4(i)
            ssb, dn = s_sb[hx], dnc[hx]
            P.emit("vector", lambda e, ssb=ssb, dn=dn, wdt=wdt: e.reduce_sum(
                dn[:, 0:1], ssb[:, 0:wdt], axis=AX.X),
                wait=[(AEx, bAEx + i + 1)], inc=VS)
            P.emit("vector", lambda e, dn=dn: e.reciprocal(dn[:, 0:1], dn[:, 0:1]),
                   wait=[(VS, None)], inc=VS)

        def s4_mul(i):
            hc, hb, qc, hx, wdt = _d4(i)
            ssb, pbf, dn = s_sb[hx], p_bf[hx], dnc[hx]
            P.emit("scalar", lambda e, ssb=ssb, pbf=pbf, dn=dn, wdt=wdt: e.mul(
                pbf[:, 0:wdt], ssb[:, 0:wdt], dn[:, 0:1]),
                wait=[(VS, None), (TPT, bTPT + max(0, i - 1))], inc=PTS)

        def s4_tp(i):
            hc, hb, qc, hx, wdt = _d4(i)
            pbf = p_bf[hx]
            for j in range(qc + 1):
                P.emit("tensor", lambda e, pbf=pbf, j=j: e.transpose(
                    ps_pT[:, j * 128:(j + 1) * 128],
                    pbf[:, j * 128:(j + 1) * 128], ident[:]),
                    wait=([(PTS, bPTS + i + 1), (VPT, bVPT + i)] if j == 0
                          else []),
                    inc=(TPT if j == qc else ()))

        def s4_pc(i):
            hc, hb, qc, hx, wdt = _d4(i)
            pT = pT_sb[hx]
            P.emit("vector", lambda e, pT=pT, wdt=wdt: e.tensor_copy(
                pT[:, 0:wdt], ps_pT[:, 0:wdt]),
                wait=[(TPT, bTPT + i + 1), (PEo, bPEo + max(0, i - 1))],
                inc=VPT)

        def s4_pv(i):
            hc, hb, qc, hx, wdt = _d4(i)
            pT = pT_sb[hx]
            h = i // 4
            for j in range(qc + 1):
                P.emit("tensor", lambda e, j=j, h=h, pT=pT, qc=qc: e.matmul(
                    ps_oh[:, qc * 128:(qc + 1) * 128],
                    vtm[:, j * 1024 + h * 64: j * 1024 + (h + 1) * 64],
                    pT[:, j * 128:(j + 1) * 128],
                    start=(j == 0), stop=(j == qc)),
                    wait=([(VPT, bVPT + i + 1), (VEo, bVEo + i // 4)]
                          if j == 0 else []),
                    inc=(PEo if j == qc else ()))

        def s4_ev(i):
            hc, hb, qc, hx, wdt = _d4(i)
            P.emit("vector", lambda e, hc=hc, hb=hb: e.tensor_copy(
                obf[hb:hb + 64, hc * 512:(hc + 1) * 512], ps_oh[:, 0:512]),
                wait=[(PEo, bPEo + i + 1)]
                     + ([(PEw, bPEw)] if bPEw else []), inc=VEo)

        def s4_wo(mo, s=s):
            pw = mo % 2
            for c in range(CH):
                P.emit("tensor", lambda e, mo=mo, c=c, pw=pw: e.matmul(
                    ps_w[pw], wout[:, c * 1024 + mo * 128: c * 1024 + (mo + 1) * 128],
                    obf[:, c * 512:(c + 1) * 512],
                    start=(c == 0), stop=(c == CH - 1)),
                    wait=([tok_wout, (VEo, bVEo + 16), (QEV, bQEV),
                           (AEW, bAEW + max(0, mo - 1))] if c == 0 else []),
                    inc=(PEw if c == CH - 1 else ()))
            P.emit("vector", lambda e, mo=mo, pw=pw: e.tensor_scalar_add(
                att[:, mo * 512:(mo + 1) * 512], ps_w[pw], bout_s[:, mo:mo + 1]),
                wait=[(PEw, bPEw + 8 * 0 + mo + 1), (C4, C4f)]
                     + ([atoks[s - 1]] if (s >= 1 and mo == 0) else []),
                inc=AEW)

        for cyc in range(64 + 13):
            if 0 <= cyc - 3 < 64:
                s4_tp(cyc - 3)
            if 0 <= cyc - 4 < 64:
                s4_pv(cyc - 4)
            if 0 <= cyc - 1 < 64:
                s4_sm1(cyc - 1)
                s4_exp(cyc - 1)
            if 0 <= cyc - 2 < 64:
                s4_sm2(cyc - 2)
                s4_mul(cyc - 2)
            if 0 <= cyc - 3 < 64:
                s4_pc(cyc - 3)
            if 0 <= cyc - 4 < 64 and (cyc - 4) % 4 == 3:
                s4_ev(cyc - 4)
            if 0 <= cyc - 68 < 8:
                s4_wo(cyc - 68)
            if cyc < 64:
                s4_scores(cyc)
        atoks.append(oring.dma(lambda e, s=s: e.dma_start(
            attn1s.rearrange("(c p) n -> p c n", p=128)[:, :, s * 512:(s + 1) * 512],
            att.ap().rearrange("p (c n) -> p c n", c=CH)),
            wait=[(AEW, None)]))
    P.emit("sync", lambda e: e.nop(),
           wait=oring.all_done() + ([(P.ch("DBG"), P.ch("DBG").n)]
                                    if "DBG" in P.chans else []))
    return oring.all_done()


def emit_ph5(P, io, after=(), attn1_after=()):
    R = io.sbuf("R", (128, RCOLS), BF16)
    nsel = NTOK // 4
    wg_h = io.din("l1wgT", (D, 3072), I32)
    wu_h = io.din("l1wuT", (D, 3072), I32)
    wd_h = io.din("l1wdT", (3072, D), I32)
    x1s = io.dram["x1s"] if "x1s" in io.dram else io.din("x1s_in", (D, NTOK), F32)
    attn1s = io.dram["attn1s"] if "attn1s" in io.dram else \
        io.din("attn1s_in", (D, nsel), BF16)
    y_t = io.dout("y_t", (D, NTOK), F32)
    wgbf = io.dint("wgbf", (D, 3072), BF16)
    wubf = io.dint("wubf", (D, 3072), BF16)
    wdbf = io.dint("wdbf", (3072, D), BF16)

    wgt = [io.sbuf(f"p5_wgt{i}", (128, CH * 128), BF16) for i in range(2)]
    wut = [io.sbuf(f"p5_wut{i}", (128, CH * 128), BF16) for i in range(2)]
    wdt_ = [io.sbuf(f"p5_wdt{i}", (128, 2048), BF16) for i in range(2)]
    hbf = io.sbuf("p5_hbf", (128, 24 * 512), BF16)
    sg = [io.sbuf(f"p5_sg{i}", (128, 512), BF16) for i in range(2)]
    x1t = io.sbuf("p5_x1t", (128, CH * 512), F32)
    attt = io.sbuf("p5_attt", (128, CH * 128), BF16)
    yt = io.sbuf("p5_yt", (128, CH * 512), F32)

    A1 = io.psum("p1_A1", (128, 2048), F32)
    A2 = io.psum("p1_A2", (128, 2048), F32)
    ps_g = [A1[:, 0:512], A1[:, 512:1024]]
    ps_u = [A1[:, 1024:1536], A1[:, 1536:2048]]
    accs = [A2[:, 512 * i:512 * (i + 1)] for i in range(4)] + \
           [A1[:, 512 * i:512 * (i + 1)] for i in range(4)]

    PEg = P.ch("p5PEg")
    PEu = P.ch("p5PEu")
    AEs = P.ch("p5AEs")
    VEh = P.ch("p5VEh")
    PEd = P.ch("p5PEd")
    DWM = P.ch("p5DWM")
    VEy = P.ch("p5VEy")
    wring = DmaRing(P, "Lw", k=6, eng="gpsimd")
    fring = DmaRing(P, "Lf", k=4, eng="sync")
    lring = DmaRing(P, "Ls", k=4, eng="sync")
    oring = DmaRing(P, "Lo", k=2, eng="sync")
    ytoks = []

    cast_toks = []
    for srcd, dstd, nr in ((wg_h, wgbf, D), (wu_h, wubf, D), (wd_h, wdbf, 3072)):
        for j in range(2):
            cast_toks.append(wring.dma(
                lambda e, srcd=srcd, dstd=dstd, j=j, nr=nr: e.dma_start(
                    dstd[j * (nr // 2):(j + 1) * (nr // 2), :],
                    srcd[j * (nr // 2):(j + 1) * (nr // 2), :]),
                wait=list(after)))

    ntile = NTOK // 512
    SCALE = 1.0 / 127.0
    for t in range(ntile):
        t0 = t * 512
        tok_x = lring.dma(lambda e, t0=t0: e.dma_start(
            x1t[:], x1s.rearrange("(c p) n -> p c n", p=128)[:, :, t0:t0 + 512]),
            wait=[(VEy, None)] + (list(after) if t == 0 else []))
        tok_a = lring.dma(lambda e, t=t: e.dma_start(
            attt[:], attn1s.rearrange("(c p) n -> p c n", p=128)
            [:, :, t * 128:(t + 1) * 128]),
            wait=[(VEy, None)] + (list(attn1_after) if t == 0 else []))
        for mo in range(24):
            pg = mo % 2
            tok_g = fring.dma(lambda e, mo=mo: e.dma_start(
                wgt[mo % 2][:], wgbf.rearrange("(c p) n -> p c n", p=128)
                [:, :, mo * 128:(mo + 1) * 128]),
                wait=[(PEg, max(0, PEg.n - 1))]
                     + (cast_toks if (t, mo) == (0, 0) else []))
            tok_u = fring.dma(lambda e, mo=mo: e.dma_start(
                wut[mo % 2][:], wubf.rearrange("(c p) n -> p c n", p=128)
                [:, :, mo * 128:(mo + 1) * 128]),
                wait=[(PEu, max(0, PEu.n - 1))]
                     + (cast_toks if (t, mo) == (0, 0) else []))
            for c in range(CH):
                P.emit("tensor", lambda e, mo=mo, c=c, pg=pg, t0=t0: e.matmul(
                    ps_g[pg], wgt[mo % 2][:, c * 128:(c + 1) * 128],
                    _r_slice(R, c, t0, 512),
                    start=(c == 0), stop=(c == CH - 1)),
                    wait=([tok_g, (AEs, max(0, AEs.n - 1))]
                          + ([(VEy, None)] if (t > 0 and mo == 0) else [])
                          + (list(after) if (t, mo) == (0, 0) else [])
                          if c == 0 else []),
                    inc=(PEg if c == CH - 1 else ()))
            for c in range(CH):
                P.emit("tensor", lambda e, mo=mo, c=c, pg=pg, t0=t0: e.matmul(
                    ps_u[pg], wut[mo % 2][:, c * 128:(c + 1) * 128],
                    _r_slice(R, c, t0, 512),
                    start=(c == 0), stop=(c == CH - 1)),
                    wait=[tok_u, (VEh, max(0, VEh.n - 1))] if c == 0 else [],
                    inc=(PEu if c == CH - 1 else ()))
            P.emit("scalar", lambda e, pg=pg, mo=mo: e.activation(
                sg[mo % 2][:], ps_g[pg], AF.Silu, scale=SCALE),
                wait=[(PEg, None), (VEh, max(0, VEh.n - 1))], inc=AEs)
            P.emit("vector", lambda e, mo=mo, pg=pg: e.scalar_tensor_tensor(
                hbf[:, mo * 512:(mo + 1) * 512],
                ps_u[pg], SCALE, sg[mo % 2][:], ALU.mult, ALU.mult),
                wait=[(PEu, None), (AEs, None)], inc=VEh)
        pdm = {}
        for kk in range(24):
            if kk % 2 == 0:
                tok_d = fring.dma(lambda e, kk=kk: e.dma_start(
                    wdt_[(kk // 2) % 2][:],
                    wdbf.rearrange("(j p) n -> p j n", p=128)
                    [:, kk:kk + 2, :]),
                    wait=[(DWM, max(0, DWM.n - 2))]
                         + ([(PEd, None)] if kk == 0 else [])
                         + (cast_toks if (t, kk) == (0, 0) else []))
            for md in range(CH):
                P.emit("tensor", lambda e, md=md, kk=kk: e.matmul(
                    accs[md],
                    wdt_[(kk // 2) % 2][:, (kk % 2) * 1024 + md * 128:
                                        (kk % 2) * 1024 + (md + 1) * 128],
                    hbf[:, kk * 512:(kk + 1) * 512],
                    start=(kk == 0), stop=(kk == 23)),
                    wait=(([tok_d, (VEh, None), (AEs, None), (VEy, None)]
                           if kk == 0 else [tok_d]) if md == 0 else []),
                    inc=(PEd if kk == 23 else (DWM if md == CH - 1 else ())))
            if kk == 23:
                for md in range(CH):
                    pdm[md] = PEd.n
        for md in range(CH):
            P.emit("vector", lambda e, md=md: e.scalar_tensor_tensor(
                yt[:, md * 512:(md + 1) * 512], accs[md], SCALE,
                x1t[:, md * 512:(md + 1) * 512], ALU.mult, ALU.add),
                wait=[(PEd, pdm[md]), tok_x]
                     + ([ytoks[t - 1]] if (t >= 1 and md == 0) else []),
                inc=VEy)
        for md in range(CH):
            P.emit("vector", lambda e, md=md: e.tensor_add(
                yt.ap()[:, md * 512: (md + 1) * 512][:, ::4],
                yt.ap()[:, md * 512: (md + 1) * 512][:, ::4],
                attt[:, md * 128:(md + 1) * 128]),
                wait=[(VEy, None), tok_a], inc=VEy)
        ytoks.append(oring.dma(lambda e, t0=t0: e.dma_start(
            y_t.rearrange("(c p) n -> p c n", p=128)[:, :, t0:t0 + 512],
            yt.ap().rearrange("p (c n) -> p c n", c=CH)),
            wait=[(VEy, None)]))
    P.emit("sync", lambda e: e.nop(), wait=oring.all_done())
    return oring.all_done()


def emit_load_r(P, io):
    r_in = io.din("r_in", (128, RCOLS), BF16)
    R = io.sbuf("R", (128, RCOLS), BF16)
    LD = P.ch("rLD")
    P.emit("sync", lambda e: e.dma_start(R[:], r_in), inc=LD, inc_amt=16)
    return (LD, 16)


def emit_store_r(P, io, wait_chs):
    r_out = io.dout("r_out", (128, RCOLS), BF16)
    R = io.sbuf("R", (128, RCOLS), BF16)
    SO = P.ch("rSO")
    P.emit("sync", lambda e: e.dma_start(r_out, R[:]),
           wait=[w if isinstance(w, tuple) else (w, None) for w in wait_chs],
           inc=SO, inc_amt=16)
    P.emit("sync", lambda e: e.nop(), wait=[(SO, None)])


def emit_finish(P, io, wait_chs):
    P.emit("sync", lambda e: e.nop(), wait=[(ch, None) for ch in wait_chs])


def set_ntok(n):
    global NTOK
    assert n % 512 == 0
    NTOK = n


def _col_chunks(vec):
    return np.ascontiguousarray(vec.reshape(CH, 128).T.astype(np.float32))


def _host_ph1(inputs):
    esel = np.zeros((16, 1024), bf16)
    for m in range(8):
        for pp in range(128):
            esel[2 * m + (pp >= 64), 128 * m + pp] = 1
    return {
        "l0Wq": np.ascontiguousarray(inputs["L0_attn_Wq"], dtype=np.float32),
        "l0Wk": np.ascontiguousarray(inputs["L0_attn_Wk"], dtype=np.float32),
        "l0Wv": np.ascontiguousarray(inputs["L0_attn_Wv"], dtype=np.float32),
        "l0Wo": np.ascontiguousarray(inputs["L0_attn_Wo"], dtype=np.float32),
        "l0bo": _col_chunks(np.asarray(inputs["L0_attn_bo"])),
        "l0esel": esel,
    }


def _host_ph2(inputs):
    HID = 2730
    w1 = np.asarray(inputs["L0_ffn_w1"], np.float32)
    b1 = np.asarray(inputs["L0_ffn_b1"], np.float32)
    w2 = np.asarray(inputs["L0_ffn_w2"], np.float32)
    out = {}
    w1p = np.zeros((2, D, W1P), np.float32)
    b1a = np.zeros((2, 128, 22), np.float32)
    b1g = np.zeros((2, 128, 22), np.float32)
    for m in range(22):
        na = min(128, HID - 128 * m)
        w1p[:, :, 256 * m:256 * m + na] = w1[:, :, 128 * m:128 * m + na]
        w1p[:, :, 256 * m + 128:256 * m + 128 + na] = \
            w1[:, :, HID + 128 * m:HID + 128 * m + na]
        b1a[:, 0:na, m] = b1[:, 128 * m:128 * m + na]
        b1g[:, 0:na, m] = b1[:, HID + 128 * m:HID + 128 * m + na]
    w2p = np.zeros((2, W2K, D), np.float32)
    w2p[:, 0:HID] = w2
    for l in range(2):
        out[f"f_wqkv{l}"] = np.ascontiguousarray(inputs["L0_ffn_wqkv"][l], dtype=np.float32)
        out[f"f_wo{l}"] = np.ascontiguousarray(inputs["L0_ffn_wo"][l], dtype=np.float32)
        out[f"f_w1p{l}"] = np.ascontiguousarray(w1p[l])
        out[f"f_w2p{l}"] = np.ascontiguousarray(w2p[l])
        out[f"f_b1a{l}"] = np.ascontiguousarray(b1a[l])
        out[f"f_b1g{l}"] = np.ascontiguousarray(b1g[l])
        out[f"f_lng{l}"] = _col_chunks(np.asarray(inputs["L0_ffn_ln_g"][l]))
        out[f"f_lnb{l}"] = _col_chunks(np.asarray(inputs["L0_ffn_ln_b"][l]))
        out[f"f_b2{l}"] = _col_chunks(np.asarray(inputs["L0_ffn_b2"][l]))
        out[f"f_qs{l}"] = np.asarray(inputs["L0_ffn_q_scale"][l], np.float32).reshape(1, 64)
        out[f"f_ks{l}"] = np.asarray(inputs["L0_ffn_k_scale"][l], np.float32).reshape(1, 64)
    n = NTOK
    half = np.arange(0, 64, 2, dtype=np.float64)
    inv_freq = 1.0 / 10000.0 ** (half / 64.0)
    t = np.arange(n, dtype=np.float64)
    freqs = t[:, None] * inv_freq[None, :]
    base = (half + 0.4 * 64.0) / (1.4 * 64.0)
    power = (t - n // 2) / 256.0
    scale = base[None, :] ** power[:, None]
    cos, sin = np.cos(freqs), np.sin(freqs)
    def pack(a):
        return np.ascontiguousarray(
            a.reshape(n // 128, 128, 64).transpose(1, 0, 2)
             .reshape(128, (n // 128) * 64).astype(bf16))
    cs_ = np.concatenate([cos * scale, cos * scale], -1)
    sx_ = np.concatenate([-sin * scale, sin * scale], -1)
    ck_ = np.concatenate([cos / scale, cos / scale], -1)
    sk_ = np.concatenate([-sin / scale, sin / scale], -1)
    zpad = np.zeros((128, 2048), bf16)
    for nm, a in (("csq", cs_), ("ssq", sx_), ("csk", ck_), ("ssk", sk_)):
        p = pack(a)
        z = zpad.copy(); z[:, 0:p.shape[1]] = p
        out[f"tab_{nm}"] = z
    lam = np.arange(128)[:, None]
    jj = np.arange(128)[None, :]
    out["tri_u"] = np.where(jj >= lam, 0.0, -1e30).astype(bf16)
    out["tri_l"] = np.where(jj <= lam, 0.0, -1e30).astype(bf16)
    m640 = np.zeros((128, 640), np.float32)
    m640[:, 0:128] = np.where(jj >= lam, 0.0, -1e30)
    m640[:, 512:640] = np.where(jj <= lam, 0.0, -1e30)
    out["mask640"] = m640.astype(bf16)
    out["ones8"] = np.full((1, 128), 8.0, bf16)
    out["ones1"] = np.ones((1, 128), bf16)
    out["onescol"] = np.ones((128, 1), bf16)
    return out


def _host_ph45(inputs):
    w_in = np.asarray(inputs["L1_attn_w_in"], np.float32)
    b_in = np.asarray(inputs["L1_attn_b_in"], np.float32)
    w_out = np.asarray(inputs["L1_attn_w_out"], np.float32)
    b_out = np.asarray(inputs["L1_attn_b_out"], np.float32)
    return {
        "l1winT": np.ascontiguousarray(w_in.T),
        "l1binqk": np.ascontiguousarray(b_in[:2048].reshape(16, 128).T),
        "l1binv": np.ascontiguousarray(b_in[2048:].reshape(1, 1024)),
        "l1woutT": np.ascontiguousarray(w_out.T),
        "l1bout": _col_chunks(b_out),
        "l1wgT": np.ascontiguousarray(np.asarray(inputs["L1_ffn_w_gate"], np.int32).T),
        "l1wuT": np.ascontiguousarray(np.asarray(inputs["L1_ffn_w_up"], np.int32).T),
        "l1wdT": np.ascontiguousarray(np.asarray(inputs["L1_ffn_w_down"], np.int32).T),
    }


def _host_consts():
    c = {}
    c["identin"] = np.eye(128, dtype=bf16)
    c["constsin"] = np.tile(np.asarray(CONST_VALS, np.float32), (128, 1))
    return c


def _prep_shared(inputs):
    sh = {}
    return sh


def _launch(phases, feed, n_cores=N_CORES, sim=False, trace=False):
    nc = bass.Bass("TRN2", target_bir_lowering=False, debug=False)
    io = IO(nc)
    P = Prog(nc)
    for ph in phases:
        ph(P, io)
    P.build()
    in_names = [n for n, k in io.kinds.items() if k == "in"]
    out_names = [n for n, k in io.kinds.items() if k == "out"]
    in_maps = []
    for core in range(n_cores):
        m = {}
        for n in in_names:
            arr = feed[n] if not isinstance(feed[n], list) else feed[n][core]
            m[n] = arr
        in_maps.append(m)
    if sim:
        from concourse.bass_interp import CoreSim

        class _R:
            pass
        nc.finalize()
        if sim != "race":
            nc.detect_race_conditions = False
        s = CoreSim(nc, require_finite=False)
        for n in in_names:
            s.tensor(n)[:] = in_maps[0][n]
        s.simulate(check_with_hw=False)
        r = _R()
        r.results = [{n: np.copy(s.tensor(n)) for n in out_names}]
        return r
    res = run_bass_kernel_spmd(nc, in_maps, core_ids=list(range(n_cores)),
                               trace=trace)
    return res


def _consts2(P, io):
    C2 = P.ch("qC2")
    for nm, shape in (("mask640", (128, 640)), ("ones8", (1, 128)),
                      ("ones1", (1, 128)), ("onescol", (128, 1))):
        hbm = io.din(nm, shape, BF16)
        sb = io.sbuf("p2_" + ("m640" if nm == "mask640" else nm), shape, BF16)
        P.emit("sync", lambda e, hbm=hbm, sb=sb: e.dma_start(sb[:], hbm),
               inc=C2, inc_amt=16)


def _feed_all(inputs):
    x = np.asarray(inputs["L0_attn_x"], dtype=np.float32)
    feed = {}
    feed["x_tm"] = [np.ascontiguousarray(x[b]) for b in range(N_CORES)]
    feed["x_t"] = [np.ascontiguousarray(x[b].T) for b in range(N_CORES)]
    feed.update(_host_consts())
    feed.update(_host_ph1(inputs))
    feed.update(_host_ph2(inputs))
    feed.update(_host_ph45(inputs))
    return feed


def kernel(**inputs):
    if os.environ.get("K_UNFUSED") == "1":
        return _kernel_unfused(**inputs)
    feed = _feed_all(inputs)
    nc = bass.Bass("TRN2", target_bir_lowering=False, debug=False)
    io = IO(nc, internal={"attn0", "x1s", "attn1s"}, use_arena=True)
    P = Prog(nc)
    emit_consts(P, io)
    emit_ph0(P, io)
    t0 = [(P.ch("p0EVT"), P.ch("p0EVT").n), (P.ch("p0G"), 1)]
    t1 = emit_ph1(P, io, after=t0)
    t2a = emit_ph2_layer(P, io, 0, after=t1)
    t2 = emit_ph2_layer(P, io, 1, after=t2a)
    t3 = emit_ph3(P, io, after=t2)
    t4 = emit_ph4(P, io, after=t3)
    emit_ph5(P, io, after=t4, attn1_after=t4)
    P.build()
    in_names = [n for n, k in io.kinds.items() if k == "in"]
    in_maps = []
    for core in range(N_CORES):
        m = {}
        for n in in_names:
            arr = feed[n] if not isinstance(feed[n], list) else feed[n][core]
            m[n] = arr
        in_maps.append(m)
    res = run_bass_kernel_spmd(nc, in_maps, core_ids=list(range(N_CORES)))
    y = np.stack([np.ascontiguousarray(res.results[b]["y_t"].T)
                  for b in range(N_CORES)])
    return y


def _kernel_unfused(**inputs):
    x = np.asarray(inputs["L0_attn_x"], dtype=np.float32)
    feed = {}
    feed["x_tm"] = [np.ascontiguousarray(x[b]) for b in range(N_CORES)]
    feed["x_t"] = [np.ascontiguousarray(x[b].T) for b in range(N_CORES)]
    feed.update(_host_consts())
    feed.update(_host_ph1(inputs))
    feed.update(_host_ph2(inputs))
    feed.update(_host_ph45(inputs))
    tok = {}

    def loadr(P, io):
        tok["t"] = [K_emit_load_r(P, io)]

    K_emit_load_r = emit_load_r

    def l1_ph0(P, io):
        emit_ph0(P, io)
        emit_store_r(P, io, [P.ch("p0EVT"), P.ch("p0G")])
    res = _launch([emit_consts, l1_ph0], feed)
    r = [res.results[b]["r_out"] for b in range(N_CORES)]

    feed["r_in"] = r
    def l2_ph1(P, io):
        emit_ph1(P, io, after=tok["t"])
    res = _launch([emit_consts, loadr, l2_ph1], feed)
    attn0 = [res.results[b]["attn0"] for b in range(N_CORES)]

    def l3_ph2(P, io):
        t0 = emit_ph2_layer(P, io, 0, after=tok["t"])
        t1 = emit_ph2_layer(P, io, 1, after=t0)
        emit_store_r(P, io, t1)
    res = _launch([emit_consts, loadr, l3_ph2], feed)
    r = [res.results[b]["r_out"] for b in range(N_CORES)]

    feed["r_in"] = r
    feed["attn0_in"] = attn0
    def l4_ph3(P, io):
        t = emit_ph3(P, io, after=tok["t"])
        emit_store_r(P, io, t)
    res = _launch([emit_consts, _consts2, loadr, l4_ph3], feed)
    r = [res.results[b]["r_out"] for b in range(N_CORES)]
    x1s = [res.results[b]["x1s"] for b in range(N_CORES)]

    feed["r_in"] = r
    def l5_ph4(P, io):
        emit_ph4(P, io, after=tok["t"])
    res = _launch([emit_consts, _consts2, loadr, l5_ph4], feed)
    attn1 = [res.results[b]["attn1s"] for b in range(N_CORES)]

    feed["x1s_in"] = x1s
    feed["attn1s_in"] = attn1
    def l6_ph5(P, io):
        emit_ph5(P, io, after=tok["t"], attn1_after=tok["t"])
    res = _launch([emit_consts, loadr, l6_ph5], feed)
    y = np.stack([np.ascontiguousarray(res.results[b]["y_t"].T)
                  for b in range(N_CORES)])
    return y


# version 07
# speedup vs baseline: 1.0744x; 1.0744x over previous
import contextlib
import numpy as np
import ml_dtypes
import concourse.bass as bass
import concourse.mybir as mybir
from concourse.bass_utils import run_bass_kernel_spmd

F32 = mybir.dt.float32
BF16 = mybir.dt.bfloat16
I32 = mybir.dt.int32
AF = mybir.ActivationFunctionType
ALU = mybir.AluOpType
AX = mybir.AxisListType
bf16 = ml_dtypes.bfloat16

N_CORES = 8
NTOK = 4096
D = 1024
CH = 8
RSTR = 4104
RCOLS = CH * RSTR

ENGS = ("sync", "tensor", "vector", "scalar", "gpsimd")


class _Chan:
    __slots__ = ("name", "n", "sem")

    def __init__(self, name):
        self.name = name
        self.n = 0
        self.sem = None


class Prog:

    def __init__(self, nc):
        self.nc = nc
        self.ops = {e: [] for e in ENGS}
        self.chans = {}
        self.scope = None

    def ch(self, name):
        if name not in self.chans:
            self.chans[name] = _Chan(name)
        return self.chans[name]

    def emit(self, eng, fn, wait=(), inc=(), inc_amt=1):
        waits = []
        for ch, v in wait:
            v = ch.n if v is None else v
            assert v <= ch.n, f"wait on future count of {ch.name}: {v} > {ch.n}"
            if v > 0:
                waits.append((ch, v))
        if isinstance(inc, _Chan):
            inc = (inc,)
        incs = []
        for ch in inc:
            ch.n += inc_amt
            incs.append((ch, inc_amt))
        self.ops[eng].append((waits, fn, incs, self.scope))

    def build(self):
        nc = self.nc
        with contextlib.ExitStack() as stack:
            for name in sorted(self.chans):
                ch = self.chans[name]
                ch.sem = stack.enter_context(nc.semaphore("ch_" + name))
            with nc.Block() as block:
                for eng in ENGS:
                    if not self.ops[eng]:
                        continue

                    ck = self.chans.get("CK")

                    def make(eng, ck=ck):
                        def emit_all(e):
                            last = {}
                            cur = [None, None]
                            if ck is not None and eng != "sync":
                                e.wait_ge(ck.sem, ck.n)
                                last = {ck: ck.n}
                            for waits, fn, incs, scope in self.ops[eng]:
                                if scope != cur[0]:
                                    if cur[1] is not None:
                                        nc.leave_named_scope(cur[0], cur[1], False)
                                        cur[1] = None
                                    cur[0] = scope
                                    if scope is not None:
                                        cur[1], _ = nc.enter_named_scope(
                                            scope, False)
                                for ch, v in waits:
                                    if last.get(ch, 0) >= v:
                                        continue
                                    e.wait_ge(ch.sem, v)
                                    last[ch] = v
                                inst = fn(e)
                                for ch, amt in incs:
                                    inst = inst.then_inc(ch.sem, amt)
                            if cur[1] is not None:
                                nc.leave_named_scope(cur[0], cur[1], False)
                            return None
                        return emit_all
                    getattr(block, eng)(make(eng))


class DmaRing:

    def __init__(self, P, name, k=4, eng="sync"):
        self.P, self.k, self.eng = P, k, eng
        self.chs = [P.ch(f"{name}{i}") for i in range(k)]
        self.i = 0

    def dma(self, fn, wait=()):
        ch = self.chs[self.i % self.k]
        w = list(wait)
        if ch.n > 0:
            w.append((ch, ch.n))
        self.P.emit(self.eng, fn, wait=w, inc=ch, inc_amt=16)
        self.i += 1
        return (ch, ch.n)

    def all_done(self):
        return [(ch, ch.n) for ch in self.chs]


class _View:
    __slots__ = ("_ap",)

    def __init__(self, ap):
        self._ap = ap

    def __getitem__(self, k):
        return self._ap[k]

    def ap(self):
        return self._ap

    def rearrange(self, *a, **k):
        return self._ap.rearrange(*a, **k)


ARENA_TAIL = {"p2_mrow", "p2_qrow", "p2_vrow", "p2_rrow", "p2_mrrow",
              "p2_rrowbf", "p2_mrrowbf", "p2_ssb0", "p2_ssb1", "p2_pbf0",
              "p2_pbf1", "p2_pT0", "p2_pT1", "p2_mx0", "p2_mx1",
              "p2_dn0", "p2_dn1"}
ARENA_REAL = {"p2_m640", "p2_ones8", "p2_ones1", "p2_onescol"}
ARENA_BYTES = 141 * 1024
ARENA_TAIL_BYTES = 26 * 1024
_DT_BYTES = {F32: 4, BF16: 2, I32: 4}


class IO:

    def __init__(self, nc, internal=(), use_arena=False):
        self.nc = nc
        self.dram = {}
        self.kinds = {}
        self.sb = {}
        self.internal = set(internal)
        self.use_arena = use_arena
        self.arena = None
        self.abump = {}

    def din(self, name, shape, dtype):
        if name not in self.dram:
            self.dram[name] = self.nc.dram_tensor(name, list(shape), dtype,
                                                  kind="ExternalInput").ap()
            self.kinds[name] = "in"
        return self.dram[name]

    def dout(self, name, shape, dtype):
        if name not in self.dram:
            kind = "Internal" if name in self.internal else "ExternalOutput"
            self.dram[name] = self.nc.dram_tensor(name, list(shape), dtype,
                                                  kind=kind).ap()
            self.kinds[name] = "internal" if name in self.internal else "out"
        return self.dram[name]

    def dint(self, name, shape, dtype):
        if name not in self.dram:
            self.dram[name] = self.nc.dram_tensor(name, list(shape), dtype,
                                                  kind="Internal").ap()
            self.kinds[name] = "internal"
        return self.dram[name]

    def sbuf(self, name, shape, dtype):
        if name in self.sb:
            return self.sb[name]
        arena_ok = (self.use_arena and name[0] == "p" and name[2] == "_"
                    and name not in ARENA_REAL)
        if not arena_ok:
            self.sb[name] = self.nc.alloc_sbuf_tensor(name, list(shape), dtype)
            return self.sb[name]
        if self.arena is None:
            self.arena = self.nc.alloc_sbuf_tensor(
                "ARENA", [128, ARENA_BYTES // 2], BF16)
        region = "tail" if name in ARENA_TAIL else name[:2]
        base = self.abump.get(region)
        if base is None:
            base = (ARENA_BYTES - ARENA_TAIL_BYTES) if region == "tail" else 0
            self.abump[region] = base
        nbytes = shape[1] * _DT_BYTES[dtype]
        nbytes = (nbytes + 63) & ~63
        limit = ARENA_BYTES if region == "tail" else ARENA_BYTES - ARENA_TAIL_BYTES
        assert base + nbytes <= limit, \
            f"arena overflow: {name} region {region} base {base} + {nbytes} > {limit}"
        self.abump[region] = base + nbytes
        col0 = base // 2
        view = self.arena.ap()[:, col0: col0 + shape[1] * _DT_BYTES[dtype] // 2]
        if dtype != BF16:
            view = view.bitcast(dtype)
        if shape[0] != 128:
            view = view[0:shape[0], :]
        self.sb[name] = _View(view)
        return self.sb[name]

    def psum(self, name, shape, dtype=F32):
        if name not in self.sb:
            self.sb[name] = self.nc.alloc_psum_tensor(name, list(shape), dtype)
        return self.sb[name]


def _r_slice(R, c, t0, n):
    return R[:, c * RSTR + 1 + t0: c * RSTR + 1 + t0 + n]


def _r_shift_slice(R, c, t0, n):
    off = 0 if c >= 4 else 1
    return R[:, c * RSTR + off + t0: c * RSTR + off + t0 + n]


CONST_VALS = (1e-5, 0.125, 1.0 / 127.0, 8.0, -1.0, 1e-6, 0.5)


def emit_consts(P, io):
    nc = P.nc
    identin = io.din("identin", (128, 128), BF16)
    ident = io.sbuf("ident", (128, 128), BF16)
    K = P.ch("CK")
    P.emit("sync", lambda e: e.dma_start(ident[:], identin), inc=K, inc_amt=16)
    constsin = io.din("constsin", (128, len(CONST_VALS)), F32)
    ct = io.sbuf("consts", (128, len(CONST_VALS)), F32)
    P.emit("sync", lambda e: e.dma_start(ct[:], constsin), inc=K, inc_amt=16)
    for i, v in enumerate(CONST_VALS):
        nc.const_aps.register(F32, float(v), ct[:, i:i + 1])


def emit_ph0(P, io):
    nc = P.nc
    x_tm = io.din("x_tm", (NTOK, D), F32)
    R = io.sbuf("R", (128, RCOLS), BF16)
    ident = io.sbuf("ident", (128, 128), BF16)

    xld = [io.sbuf(f"p0_x{p}", (128, D), F32) for p in range(2)]
    xn = [io.sbuf(f"p0_xn{p}", (128, D), BF16) for p in range(2)]
    dump = io.sbuf("p0_dump", (128, D), BF16)
    cs = io.sbuf("p0_cs", (128, 1), F32)
    cmean = io.sbuf("p0_cmean", (128, 1), F32)
    cq = io.sbuf("p0_cq", (128, 1), F32)
    cmsq = io.sbuf("p0_cmsq", (128, 1), F32)
    cm2 = io.sbuf("p0_cm2", (128, 1), F32)
    cvar = io.sbuf("p0_cvar", (128, 1), F32)
    cstd = io.sbuf("p0_cstd", (128, 1), F32)
    crstd = io.sbuf("p0_crstd", (128, 1), F32)
    A1 = io.psum("p1_A1", (128, 2048), F32)
    psT = [A1.ap()[:, 0:512].bitcast(BF16), A1.ap()[:, 512:1024].bitcast(BF16)]

    VT, SQ, VV, ST, TT, EVT, VS = (P.ch(x) for x in
                                   ("p0VT", "qSQA", "qRED", "qLNX", "p0TT",
                                    "p0EVT", "qVS"))
    ring = DmaRing(P, "Ls", k=4, eng="sync")

    P.emit("gpsimd", lambda e: e.memset(R[:], 0.0), inc=P.ch("p0G"))

    nsub = NTOK // 128
    for s in range(nsub):
        p = s % 2
        tok = ring.dma(lambda e, s=s, p=p: e.dma_start(
            xld[p][:], x_tm[s * 128:(s + 1) * 128, :]),
            wait=[(VT, max(0, s - 1))])
        P.emit("vector", lambda e, p=p: e.reduce_sum(cs[:, 0:1], xld[p][:], axis=AX.X),
               wait=[tok], inc=VS)
        P.emit("vector", lambda e: e.tensor_scalar_mul(cmean[:, 0:1], cs[:, 0:1], 1.0 / D),
               wait=[(VS, None)], inc=VS)
        P.emit("scalar", lambda e, p=p: e.activation(
            dump[:], xld[p][:], AF.Square),
            wait=[tok, (VT, max(0, s))], inc=SQ)
        P.emit("vector", lambda e: e.reduce_sum(cq[:, 0:1], dump[:], axis=AX.X),
               wait=[(SQ, None)], inc=VS)
        P.emit("vector", lambda e: e.tensor_scalar_mul(cmsq[:, 0:1], cq[:, 0:1], 1.0 / D),
               wait=[(VS, None)], inc=VS)
        P.emit("vector", lambda e: e.tensor_mul(cm2[:, 0:1], cmean[:, 0:1], cmean[:, 0:1]),
               inc=VS)
        P.emit("vector", lambda e: e.tensor_sub(cvar[:, 0:1], cmsq[:, 0:1], cm2[:, 0:1]),
               wait=[(VS, None)], inc=VV)
        P.emit("scalar", lambda e: e.activation(cstd[:, 0:1], cvar[:, 0:1], AF.Sqrt,
                                                bias=1e-5),
               wait=[(VV, None)], inc=ST)
        P.emit("vector", lambda e: e.reciprocal(crstd[:, 0:1], cstd[:, 0:1]),
               wait=[(ST, None)], inc=VS)
        P.emit("vector", lambda e, p=p: e.tensor_scalar(
            xn[p][:], xld[p][:], cmean[:, 0:1], crstd[:, 0:1], ALU.subtract, ALU.mult),
            wait=[(TT, max(0, s - 1)), (VS, None)], inc=VT)
        for c in range(CH):
            P.emit("tensor", lambda e, p=p, c=c: e.transpose(
                psT[p][:, c * 128:(c + 1) * 128], xn[p][:, c * 128:(c + 1) * 128],
                ident[:]),
                wait=([(VT, None), (EVT, max(0, s - 1))] if c == 0 else []),
                inc=(TT if c == CH - 1 else ()))
        P.emit("vector", lambda e, s=s, p=p: e.tensor_copy(
            R.ap().rearrange("p (c t) -> p c t", c=CH, t=RSTR)
             [:, :, 1 + s * 128: 1 + (s + 1) * 128],
            psT[p].rearrange("p (c t) -> p c t", c=CH, t=128)),
            wait=[(TT, None)] + ([(P.ch("p0G"), 1)] if s == 0 else []), inc=EVT)


def emit_ph1(P, io, after=()):
    R = io.sbuf("R", (128, RCOLS), BF16)
    wq_h = io.din("l0Wq", (D, D), F32)
    wk_h = io.din("l0Wk", (D, D), F32)
    wv_h = io.din("l0Wv", (D, D), F32)
    wo_h = io.din("l0Wo", (D, D), F32)
    bo_h = io.din("l0bo", (128, CH), F32)
    esel_h = io.din("l0esel", (16, 1024), BF16)
    attn0 = io.dout("attn0", (D, NTOK), BF16)

    bo = io.sbuf("p1_bo", (128, CH), F32)
    esel = io.sbuf("p1_esel", (16, 1024), BF16)

    kbf = [io.sbuf(f"p1_kbf{p}", (128, 1024), BF16) for p in range(2)]
    vaug = [io.sbuf(f"p1_vaug{p}", (128, 1040), BF16) for p in range(2)]
    mt = io.sbuf("p1_mt", (128, 1024), BF16)
    et = io.sbuf("p1_et", (128, 1024), BF16)
    kvsb = io.sbuf("p1_kvsb", (64, 1040), BF16)
    kvacc = io.sbuf("p1_kvacc", (64, 1040), F32)
    ksbd = io.sbuf("p1_ksbd", (128, 128), BF16)
    kvbd = io.sbuf("p1_kvbd", (128, 1024), BF16)
    qbf = io.sbuf("p1_qbf", (128, 4096), BF16)
    obf = io.sbuf("p1_obf", (128, 4096), BF16)
    zs = io.sbuf("p1_zs", (16, 512), F32)
    zib = io.sbuf("p1_zib", (16, 512), BF16)
    zbsb = io.sbuf("p1_zbsb", (128, 512), BF16)
    att = [io.sbuf("p1_att0", (128, CH * 512), BF16)] * 2
    wq = io.sbuf("p1_wq", (128, CH * 1024), BF16)
    wk = io.sbuf("p1_wk", (128, CH * 1024), BF16)
    wv = io.sbuf("p1_wv", (128, CH * 1024), BF16)
    wo = io.sbuf("p1_wo", (128, CH * 1024), BF16)

    A1 = io.psum("p1_A1", (128, 2048), F32)
    A2 = io.psum("p1_A2", (128, 2048), F32)
    ps_k = A1[:, 0:1024]
    ps_v = A1[:, 1024:2048]
    ps_kv = [A2[0:64, 128 * h: 128 * h + 65] for h in range(16)]
    ps_q = [A1[:, 0:512], A1[:, 512:1024]]
    ps_o = [A1[:, 1024:1536], A1[:, 1536:2048]]
    ps_z = A2[0:16, 0:512]
    ps_zb = A2[:, 512:1024]
    ps_w = [A2[:, 1024:1536], A2[:, 1536:2048]]

    wring = DmaRing(P, "Lw", k=6, eng="gpsimd")
    oring = DmaRing(P, "Lo", k=2, eng="sync")
    PEk = P.ch("p1PEk")
    PEv = P.ch("p1PEv")
    PEkv = P.ch("p1PEkv")
    KVA = P.ch("p1KVA")
    VEk = P.ch("p1VEk")
    AEv = P.ch("p1AEv")
    MN = P.ch("p1MN")
    EX = P.ch("qAEx")
    VS = P.ch("qVS")
    BD = P.ch("p1BD")
    PEq = P.ch("p1PEq")
    VEq = P.ch("p1VEq")
    PEz = P.ch("p1PEz")
    VEz = P.ch("p1VEz")
    PEo = P.ch("p1PEo")
    PEzb = P.ch("p1PEzb")
    VEo = P.ch("p1VEo")
    VEzb = P.ch("p1VEzb")
    PEw = P.ch("p1PEw")
    AEw = P.ch("p1AEw")
    ONE = P.ch("p1ONE")
    C = P.ch("p1C")

    wtoks = {}
    for name, hbm, sb in (("wk", wk_h, wk), ("wv", wv_h, wv), ("wq", wq_h, wq),
                          ("wo", wo_h, wo)):
        wtoks[name] = wring.dma(lambda e, hbm=hbm, sb=sb: e.dma_start(
            sb[:], hbm.rearrange("(c p) n -> p c n", p=128)))
    P.emit("sync", lambda e: e.dma_start(bo[:], bo_h), wait=list(after),
           inc=C, inc_amt=16)
    P.emit("sync", lambda e: e.dma_start(esel[:], esel_h), wait=list(after),
           inc=C, inc_amt=16)
    for p in range(2):
        P.emit("gpsimd", lambda e, p=p: e.memset(
            vaug[p].ap().rearrange("p (h e) -> p h e", h=16, e=65)[:, :, 64:65], 1.0),
            wait=(list(after) if p == 0 else []), inc=ONE)

    nsub = NTOK // 128
    for s in range(nsub):
        p = s % 2
        for half in range(2):
            for c in range(CH):
                P.emit("tensor", lambda e, s=s, c=c, half=half: e.matmul(
                    ps_k[:, half * 512:(half + 1) * 512], _r_slice(R, c, s * 128, 128),
                    wk[:, c * 1024 + half * 512: c * 1024 + (half + 1) * 512],
                    start=(c == 0), stop=(c == CH - 1)),
                    wait=(([wtoks["wk"], (VEk, max(0, VEk.n - 1))]
                           + (list(after) if s == 0 else []))
                          if (c == 0 and half == 0) else []),
                    inc=(PEk if (c, half) == (CH - 1, 1) else ()))
        for half in range(2):
            for c in range(CH):
                P.emit("tensor", lambda e, s=s, c=c, half=half: e.matmul(
                    ps_v[:, half * 512:(half + 1) * 512], _r_slice(R, c, s * 128, 128),
                    wv[:, c * 1024 + half * 512: c * 1024 + (half + 1) * 512],
                    start=(c == 0), stop=(c == CH - 1)),
                    wait=([wtoks["wv"], (AEv, max(0, AEv.n - 1))]
                          if (c == 0 and half == 0) else []),
                    inc=(PEv if (c, half) == (CH - 1, 1) else ()))
        P.emit("vector", lambda e: e.tensor_scalar_min(mt[:], ps_k, 0.0),
               wait=[(PEk, None)], inc=MN)
        P.emit("scalar", lambda e: e.activation(et[:], mt[:], AF.Exp),
               wait=[(MN, None)], inc=EX)
        P.emit("vector", lambda e, p=p: e.scalar_tensor_tensor(
            kbf[p][:], ps_k, 0.0, et[:], ALU.max, ALU.add),
            wait=[(EX, None), (PEkv, max(0, PEkv.n - 1))], inc=VEk)
        P.emit("scalar", lambda e, p=p: e.copy(
            vaug[p].ap().rearrange("p (h e) -> p h e", h=16, e=65)[:, :, 0:64],
            ps_v.rearrange("p (h e) -> p h e", h=16, e=64)),
            wait=[(PEv, None), (ONE, 2), (PEkv, max(0, PEkv.n - 1))], inc=AEv)
        for h in range(16):
            P.emit("tensor", lambda e, p=p, h=h: e.matmul(
                ps_kv[h], kbf[p][:, h * 64:(h + 1) * 64],
                vaug[p][:, h * 65:(h + 1) * 65],
                start=True, stop=True),
                wait=([(VEk, None), (AEv, None), (KVA, None)] if h == 0 else []),
                inc=(PEkv if h == 15 else ()))
        psv = A2.ap()[0:64, :].rearrange("p (h e) -> p h e", h=16, e=128)[:, :, 0:65]
        if s == 0:
            P.emit("vector", lambda e, psv=psv: e.tensor_copy(
                kvacc.ap().rearrange("p (h e) -> p h e", h=16, e=65), psv),
                wait=[(PEkv, None)], inc=KVA)
        else:
            P.emit("vector", lambda e, psv=psv: e.tensor_add(
                kvacc.ap().rearrange("p (h e) -> p h e", h=16, e=65),
                kvacc.ap().rearrange("p (h e) -> p h e", h=16, e=65), psv),
                wait=[(PEkv, None)], inc=KVA)

    P.emit("vector", lambda e: e.tensor_copy(kvsb[:], kvacc[:]),
           wait=[(KVA, None)], inc=VS)
    P.emit("vector", lambda e: e.memset(ksbd[:], 0.0), wait=[(VS, None)], inc=VS)
    P.emit("vector", lambda e: e.memset(kvbd[:], 0.0), wait=[(VS, None)], inc=VS)
    for h in range(16):
        c, half = h // 2, h % 2
        P.emit("vector", lambda e, h=h, c=c, half=half: e.tensor_copy(
            ksbd.ap()[half * 64: half * 64 + 64, c * 16 + h: c * 16 + h + 1],
            kvsb[0:64, h * 65 + 64: h * 65 + 65]),
            wait=[(VS, None)], inc=VS)
    for m in range(CH):
        for half in range(2):
            h = 2 * m + half
            P.emit("vector", lambda e, m=m, half=half, h=h: e.tensor_copy(
                kvbd.ap()[half * 64: half * 64 + 64,
                          m * 128 + half * 64: m * 128 + half * 64 + 64],
                kvsb[0:64, h * 65: h * 65 + 64]),
                wait=[(VS, None)], inc=BD if (m, half) == (CH - 1, 1) else VS)

    ntile = NTOK // 512
    atoks = []
    for t in range(ntile):
        for m in range(CH):
            pq = m % 2
            for c in range(CH):
                P.emit("tensor", lambda e, t=t, m=m, c=c, pq=pq: e.matmul(
                    ps_q[pq], wq[:, c * 1024 + m * 128: c * 1024 + (m + 1) * 128],
                    _r_slice(R, c, t * 512, 512), start=(c == 0), stop=(c == CH - 1)),
                    wait=(([wtoks["wq"], (VEq, max(0, VEq.n - 1))]
                           + ([(VEk, None)] if (t, m) == (0, 0) else []))
                          if c == 0 else []),
                    inc=(PEq if c == CH - 1 else ()))
            P.emit("vector", lambda e, pq=pq: e.tensor_scalar_min(
                mt[:, 0:512], ps_q[pq], 0.0), wait=[(PEq, None), (VEq, None)], inc=MN)
            P.emit("scalar", lambda e: e.activation(et[:, 0:512], mt[:, 0:512], AF.Exp),
                   wait=[(MN, None)], inc=EX)
            P.emit("vector", lambda e, m=m, pq=pq: e.scalar_tensor_tensor(
                qbf[:, m * 512:(m + 1) * 512], ps_q[pq], 0.0, et[:, 0:512],
                ALU.max, ALU.add),
                wait=[(EX, None)] + ([(PEo, None)] if m == 0 else []), inc=VEq)
        for c in range(CH):
            P.emit("tensor", lambda e, c=c: e.matmul(
                ps_z, ksbd[:, c * 16:(c + 1) * 16], qbf[:, c * 512:(c + 1) * 512],
                start=(c == 0), stop=(c == CH - 1)),
                wait=([(BD, None), (VEq, None), (VEz, None)]
                      if c == 0 else []),
                inc=(PEz if c == CH - 1 else ()))
        P.emit("vector", lambda e: e.tensor_scalar_add(zs[:], ps_z, 1e-6),
               wait=[(PEz, None), (VEz, None)], inc=VS)
        P.emit("vector", lambda e: e.reciprocal(zs[:], zs[:]),
               wait=[(VS, None)], inc=VS)
        P.emit("vector", lambda e: e.tensor_copy(zib[:], zs[:]),
               wait=[(VS, None)], inc=VEz)
        for m in range(CH):
            P.emit("tensor", lambda e, m=m: e.matmul(
                ps_zb, esel[:, m * 128:(m + 1) * 128], zib[:],
                start=True, stop=True),
                wait=[(VEz, None), (VEzb, None), (C, 32)]
                     + ([(BD, None)] if m == 0 else []),
                inc=PEzb)
            P.emit("tensor", lambda e, m=m, t=t: e.matmul(
                ps_o[m % 2], kvbd[:, m * 128:(m + 1) * 128],
                qbf[:, m * 512:(m + 1) * 512], start=True, stop=True),
                wait=[(VEo, max(0, VEo.n - 1))]
                     + ([(AEv, None)] if (t, m) == (0, 0) else []),
                inc=PEo)
            P.emit("vector", lambda e: e.tensor_copy(zbsb[:], ps_zb),
                   wait=[(PEzb, None), (VS, None)], inc=VEzb)
            P.emit("vector", lambda e, m=m: e.tensor_mul(
                obf[:, m * 512:(m + 1) * 512], ps_o[m % 2], zbsb[:]),
                wait=[(PEo, None), (VEzb, None)]
                     + ([(PEw, None)] if m == 0 else []), inc=VEo)
        for mo in range(CH):
            pw = mo % 2
            for c in range(CH):
                P.emit("tensor", lambda e, mo=mo, c=c, pw=pw: e.matmul(
                    ps_w[pw], wo[:, c * 1024 + mo * 128: c * 1024 + (mo + 1) * 128],
                    obf[:, c * 512:(c + 1) * 512], start=(c == 0), stop=(c == CH - 1)),
                    wait=([wtoks["wo"], (VEo, None), (AEw, max(0, AEw.n - 1))]
                          + ([(BD, None)] if (t, mo) == (0, 0) else [])
                          if c == 0 else []),
                    inc=(PEw if c == CH - 1 else ()))
            P.emit("vector", lambda e, mo=mo, pw=pw, t=t: e.tensor_scalar_add(
                att[t % 2][:, mo * 512:(mo + 1) * 512], ps_w[pw], bo[:, mo:mo + 1]),
                wait=[(PEw, None), (C, 32)]
                     + ([atoks[t - 1]] if (t >= 1 and mo == 0) else []),
                inc=AEw)
        atoks.append(oring.dma(lambda e, t=t: e.dma_start(
            attn0.rearrange("(c p) n -> p c n", p=128)[:, :, t * 512:(t + 1) * 512],
            att[t % 2].ap().rearrange("p (c n) -> p c n", c=CH)),
            wait=[(AEw, None)]))
    P.emit("sync", lambda e: e.nop(), wait=oring.all_done())
    return oring.all_done() + [(AEw, AEw.n)]


import os
P2_NOPREV = os.environ.get("P2_NOPREV", "0") == "1"
P2_NOM0 = os.environ.get("P2_NOM0", "0") == "1"
P2_NOPREVMM = os.environ.get("P2_NOPREVMM", "0") == "1"
P2_NARROW = os.environ.get("P2_NARROW", "0") == "1"
P2_NOMID = os.environ.get("P2_NOMID", "0") == "1"
HEADS2 = 8
W1P = 5632
W2K = 2816


def emit_ph2_layer(P, io, l, after=(), sweeps=("A", "F"), dbg=None, astop=None):
    nw = NTOK // 512
    R = io.sbuf("R", (128, RCOLS), BF16)
    ident = io.sbuf("ident", (128, 128), BF16)
    pre = "q"

    wqkv_h = io.din(f"f_wqkv{l}", (D, 1536), F32)
    wo_h = io.din(f"f_wo{l}", (512, D), F32)
    w1p_h = io.din(f"f_w1p{l}", (D, W1P), F32)
    w2p_h = io.din(f"f_w2p{l}", (W2K, D), F32)
    b1a_h = io.din(f"f_b1a{l}", (128, 22), F32)
    b1g_h = io.din(f"f_b1g{l}", (128, 22), F32)
    lng_h = io.din(f"f_lng{l}", (128, CH), F32)
    lnb_h = io.din(f"f_lnb{l}", (128, CH), F32)
    b2_h = io.din(f"f_b2{l}", (128, CH), F32)
    qs_h = io.din(f"f_qs{l}", (1, 64), F32)
    ks_h = io.din(f"f_ks{l}", (1, 64), F32)
    tabs_h = {nm: io.din(f"tab_{nm}", (128, 2048), BF16)
              for nm in ("csq", "ssq", "csk", "ssk")}
    triu_h = io.din("tri_u", (128, 128), BF16)
    tril_h = io.din("tri_l", (128, 128), BF16)
    m640_h = io.din("mask640", (128, 640), BF16)
    ones8_h = io.din("ones8", (1, 128), BF16)
    ones1_h = io.din("ones1", (1, 128), BF16)
    onescol_h = io.din("onescol", (128, 1), BF16)
    w1bf = io.dint(f"w1bf{l}", (D, W1P), BF16)
    w2bf = io.dint(f"w2bf{l}", (W2K, D), BF16)

    tabs = {nm: io.sbuf(f"p2_{nm}", (128, 2048), BF16)
            for nm in ("csq", "ssq", "csk", "ssk")}
    m640 = io.sbuf("p2_m640", (128, 640), BF16)
    ones8 = io.sbuf("p2_ones8", (1, 128), BF16)
    ones1 = io.sbuf("p2_ones1", (1, 128), BF16)
    onescol = io.sbuf("p2_onescol", (128, 1), BF16)
    uni = io.sbuf("p2_wqkv", (128, CH * 1536), BF16)
    wqkv = uni[:, 0:12288]
    wosb = io.sbuf("p2_wo", (128, 4 * 1024), BF16)
    scr = io.sbuf("p2_scr", (128, 20480), BF16)
    q_fm = [scr[:, 0:2048], scr[:, 2048:4096]]
    k_fm = [scr[:, 4096:6144], scr[:, 6144:8192], scr[:, 8192:10240]]
    v_tm = [scr[:, 10240:12288], scr[:, 12288:14336], scr[:, 14336:16384]]
    o_bf = [scr[:, 16384:18432], scr[:, 18432:20480]]
    h3 = scr[:, 0:11264]
    h2n = uni[:, 0:4096]
    w1t = [uni[:, 4096:6144], uni[:, 6144:8192]]
    w2t = [uni[:, 8192:9216], uni[:, 9216:10240]]
    sqb = [io.sbuf(f"p2_sqb{i}", (128, 512), BF16) for i in range(2)]
    qs64 = io.sbuf("p2_qs64", (1, 64), F32)
    ks64 = io.sbuf("p2_ks64", (1, 64), F32)
    qs64b = io.sbuf("p2_qs64b", (1, 64), BF16)
    ks64b = io.sbuf("p2_ks64b", (1, 64), BF16)
    qsbc = io.sbuf("p2_qsbc", (128, 64), BF16)
    ksbc = io.sbuf("p2_ksbc", (128, 64), BF16)
    sq = [io.sbuf(f"p2_sq{i}", (128, 512), BF16) for i in range(2)]
    nrm2 = [io.sbuf(f"p2_nrm2{i}", (128, 8), F32) for i in range(2)]
    rn = [io.sbuf(f"p2_rn{i}", (128, 8), F32) for i in range(2)]
    qn = [io.sbuf(f"p2_qn{i}", (128, 512), F32) for i in range(2)]
    tmp = [io.sbuf(f"p2_tmp{i}", (128, 512), F32) for i in range(2)]
    t1f = [io.sbuf(f"p2_t1{i}", (128, 512), F32) for i in range(2)]
    qtm = [io.sbuf(f"p2_qtm{i}", (128, 512), BF16) for i in range(2)]
    s_sb = [io.sbuf(f"p2_ssb{i}", (128, 640), F32) for i in range(2)]
    p_bf = [io.sbuf(f"p2_pbf{i}", (128, 640), BF16) for i in range(2)]
    pT_sb = [io.sbuf(f"p2_pT{i}", (128, 640), BF16) for i in range(2)]
    mxc = [io.sbuf(f"p2_mx{i}", (128, 1), F32) for i in range(2)]
    dnc = [io.sbuf(f"p2_dn{i}", (128, 1), F32) for i in range(2)]
    b1a = io.sbuf("p2_b1a", (128, 22), F32)
    b1g = io.sbuf("p2_b1g", (128, 22), F32)
    lng = io.sbuf("p2_lng", (128, CH), F32)
    lnb = io.sbuf("p2_lnb", (128, CH), F32)
    b2c = io.sbuf("p2_b2", (128, CH), F32)
    mrow = io.sbuf("p2_mrow", (1, 512), F32)
    qrow = io.sbuf("p2_qrow", (1, 512), F32)
    vrow = io.sbuf("p2_vrow", (1, 512), F32)
    rrow = io.sbuf("p2_rrow", (1, 512), F32)
    mrrow = io.sbuf("p2_mrrow", (1, 512), F32)
    rrow_bf = io.sbuf("p2_rrowbf", (1, 512), BF16)
    mrrow_bf = io.sbuf("p2_mrrowbf", (1, 512), BF16)
    gg = [uni[:, 10240:10752], uni[:, 10752:11264]]
    nt1 = t1f[0]

    A1 = io.psum("p1_A1", (128, 2048), F32)
    A2 = io.psum("p1_A2", (128, 2048), F32)
    ps_pj = A1[:, 0:512]
    ps_T = A1.ap()[:, 1536:2048].bitcast(BF16)
    ps_S = A2[:, 0:640]
    ps_pT = A2.ap()[:, 1024:1536].bitcast(BF16)
    ps_oh = A2[0:64, 1536:2048]
    ps_w = [A1[:, 512:1024], A1[:, 1024:1536]]
    ps_sr = A2[0:1, 0:512]
    ps_qr = A2[0:1, 512:1024]
    ps_rbc = A2[:, 1024:1536]
    ps_mbc = A2[:, 1536:2048]
    ps_a = A1[:, 0:512]
    ps_g = A1[:, 512:1024]
    accs = [A1[:, 512 * i:512 * (i + 1)] for i in range(4)] + \
           [A2[:, 512 * i:512 * (i + 1)] for i in range(4)]

    ch = lambda nm: P.ch(pre + nm)
    VS = ch("VS")
    TBF = ch("TBF")
    SQA = ch("SQA");  RED = ch("RED");  LNE = ch("LNE");  APL = ch("APL")
    SWP = ch("SWP");  SXT = ch("SXT");  ADD = ch("ADD");  PET = ch("PET")
    TRD = ch("TRD");  QPF = ch("QPF");  KPF = ch("KPF");  VPF = ch("VPF")
    QKR = ch("QKR");  KVR = ch("KVR")
    SMM = ch("SMM");  VSm = ch("VSm");  AEx = ch("AEx")
    PTS = ch("PTS");  TPT = ch("TPT");  VPT = ch("VPT");  PEo = ch("PEo")
    VEo = ch("VEo");  PEw = ch("PEw");  RA = ch("RA")
    AEF = ch("AEF");  QRM = ch("QRM");  RWC = ch("RWC")
    PEb = ch("PEb");  VEn = ch("VEn");  PEa = ch("PEa");  AEg = ch("AEg")
    VEh = ch("VEh");  PEf = ch("PEf");  RAF = ch("RAF");  C2 = ch("C2")
    W2M = ch("W2M");  LNX = ch("LNX")
    QKP = ch("QKP")

    wring = DmaRing(P, "Lw", k=6, eng="gpsimd")
    cring = DmaRing(P, "Ls", k=4, eng="sync")
    fring = DmaRing(P, "Lf", k=4, eng="sync")

    tok_wqkv = wring.dma(lambda e: e.dma_start(
        wqkv[:], wqkv_h.rearrange("(c p) n -> p c n", p=128)),
        wait=[(QKP, None)] + list(after))
    tok_wo = wring.dma(lambda e: e.dma_start(
        wosb[:], wo_h.rearrange("(c p) n -> p c n", p=128)),
        wait=[(PEw, None)] + list(after))
    w1c_toks = [wring.dma(lambda e, j=j: e.dma_start(
        w1bf[:, j * 1408:(j + 1) * 1408], w1p_h[:, j * 1408:(j + 1) * 1408]))
        for j in range(4)]
    w2c_toks = [wring.dma(lambda e, j=j: e.dma_start(
        w2bf[j * 1408:(j + 1) * 1408, :], w2p_h[j * 1408:(j + 1) * 1408, :]))
        for j in range(2)]
    for hbm, sb in ((b1a_h, b1a), (b1g_h, b1g), (lng_h, lng), (lnb_h, lnb),
                    (b2_h, b2c), (qs_h, qs64), (ks_h, ks64),
                    (m640_h, m640), (ones8_h, ones8),
                    (ones1_h, ones1), (onescol_h, onescol)):
        P.emit("sync", lambda e, hbm=hbm, sb=sb: e.dma_start(sb[:], hbm),
               wait=[(RAF, None), (VSm, None)] + list(after), inc=C2, inc_amt=16)
    tab_toks = {nm: cring.dma(lambda e, nm=nm: e.dma_start(
        tabs[nm][:], tabs_h[nm]), wait=[(ADD, None)] + list(after))
        for nm in ("csq", "ssq", "csk", "ssk")}
    C2f = C2.n
    tbf_t = TBF.n + 1

    P.emit("vector", lambda e: e.tensor_copy(qs64b[:], qs64[:]),
           wait=[(C2, C2f)] + list(after), inc=VS)
    P.emit("vector", lambda e: e.tensor_copy(ks64b[:], ks64[:]), inc=VS)
    P.emit("tensor", lambda e: e.matmul(A2[:, 0:64], ones8[:], qs64b[:],
                                        start=True, stop=True),
           wait=[(VS, None)] + list(after), inc=QKP)
    P.emit("tensor", lambda e: e.matmul(A2[:, 64:128], ones1[:], ks64b[:],
                                        start=True, stop=True), inc=QKP)
    P.emit("vector", lambda e: e.tensor_copy(qsbc[:], A2[:, 0:64]),
           wait=[(QKP, None)], inc=VS)
    P.emit("vector", lambda e: e.tensor_copy(ksbc[:], A2[:, 64:128]), inc=VS)
    for nm, bc in (("csq", qsbc), ("ssq", qsbc), ("csk", ksbc), ("ssk", ksbc)):
        P.emit("vector", lambda e, nm=nm, bc=bc: e.tensor_tensor(
            tabs[nm].ap().rearrange("p (s f) -> p s f", s=32),
            tabs[nm].ap().rearrange("p (s f) -> p s f", s=32),
            bc.ap().rearrange("p (o f) -> p o f", o=1).broadcast_to([128, 32, 64]),
            ALU.mult),
            wait=[(VS, None), tab_toks[nm]],
            inc=(TBF if nm == "ssk" else VS))

    ecall = [0]
    bQPF = QPF.n
    cic = [0]

    def _qk_epilogue(nm, dpar, sub, gsub, guard_smm):
        i = ecall[0] % 2
        ecall[0] += 1
        ps = ps_pj
        cs = tabs["csq" if nm == "q" else "csk"]
        ssx = tabs["ssq" if nm == "q" else "ssk"]
        P.emit("scalar", lambda e, ps=ps, i=i: e.activation(
            sq[i][:], ps, AF.Square),
            wait=[(QKP, None), (RED, max(0, RED.n - 1)), (TBF, tbf_t)], inc=SQA)
        for h in range(8):
            P.emit("vector", lambda e, h=h, i=i: e.reduce_sum(
                nrm2[i][:, h:h + 1], sq[i][:, h * 64:(h + 1) * 64], axis=AX.X),
                wait=([(SQA, None), (LNE, max(0, LNE.n - 1))] if h == 0 else []),
                inc=(VS if h == 7 else ()))
        P.emit("vector", lambda e, i=i: e.tensor_scalar_max(
            nrm2[i][:], nrm2[i][:], 1e-24), wait=[(VS, None)], inc=RED)
        P.emit("scalar", lambda e, i=i: e.activation(rn[i][:], nrm2[i][:], AF.Ln),
               wait=[(RED, None), (APL, max(0, APL.n - 1))], inc=LNX)
        P.emit("scalar", lambda e, i=i: e.activation(
            rn[i][:], rn[i][:], AF.Exp, scale=-0.5),
            wait=[(LNX, None)], inc=LNE)
        for h in range(8):
            P.emit("vector", lambda e, h=h, ps=ps, i=i: e.tensor_scalar_mul(
                qn[i][:, h * 64:(h + 1) * 64], ps[:, h * 64:(h + 1) * 64],
                rn[i][:, h:h + 1]),
                wait=([(LNE, None), (SWP, max(0, SWP.n - 1))] if h == 0 else []),
                inc=(APL if h == 7 else VS))
        P.emit("vector", lambda e: e.engine_nop(), wait=[(APL, None)], inc=QPF)
        P.emit("gpsimd", lambda e, i=i: e.tensor_copy(
            tmp[i].ap().rearrange("p (h a j) -> p h a j", h=8, a=2, j=32),
            qn[i].ap().rearrange("p (h a j) -> p h a j", h=8, a=2, j=32)[:, :, ::-1, :]),
            wait=[(APL, None), (SXT, max(0, SXT.n - 1))], inc=SWP)
        P.emit("vector", lambda e, cs=cs, gsub=gsub, i=i: e.tensor_tensor(
            t1f[i].ap().rearrange("p (h f) -> p h f", h=8),
            qn[i].ap().rearrange("p (h f) -> p h f", h=8),
            cs.ap()[:, gsub * 64:gsub * 64 + 64]
              .rearrange("p (o f) -> p o f", o=1).broadcast_to([128, 8, 64]),
            ALU.mult), wait=[(APL, None)], inc=VS)
        P.emit("vector", lambda e, ssx=ssx, gsub=gsub, i=i: e.tensor_tensor(
            tmp[i].ap().rearrange("p (h f) -> p h f", h=8),
            tmp[i].ap().rearrange("p (h f) -> p h f", h=8),
            ssx.ap()[:, gsub * 64:gsub * 64 + 64]
              .rearrange("p (o f) -> p o f", o=1).broadcast_to([128, 8, 64]),
            ALU.mult), wait=[(SWP, None)], inc=SXT)
        P.emit("vector", lambda e, i=i: e.tensor_add(
            qtm[i][:], t1f[i][:], tmp[i][:]),
            wait=[(SXT, None), (PET, max(0, PET.n - 1))], inc=ADD)
        for j in range(4):
            P.emit("tensor", lambda e, j=j, i=i: e.transpose(
                ps_T[:, j * 128:(j + 1) * 128], qtm[i][:, j * 128:(j + 1) * 128],
                ident[:]),
                wait=([(ADD, None), (TRD, None)] if j == 0 else []),
                inc=(PET if j == 3 else ()))
        dst = q_fm[dpar] if nm == "q" else k_fm[dpar]
        P.emit("vector", lambda e, dst=dst, sub=sub: e.tensor_copy(
            dst.rearrange("p (hc t) -> p hc t", hc=4)[:, :, sub * 128:(sub + 1) * 128],
            ps_T[:, 0:512].rearrange("p (hc t) -> p hc t", hc=4)),
            wait=[(PET, None)] + ([(SMM, guard_smm)] if guard_smm else []),
            inc=TRD)

    def emit_chunk(w, sub, nm, guard):
        _S("q")
        ci = cic[0]
        cic[0] += 1
        t0 = w * 512 + sub * 128
        gsub = (t0 // 128) % 32
        col0 = {"q": 0, "k": 512, "v": 1024}[nm]
        for c in range(CH):
            P.emit("tensor", lambda e, c=c, t0=t0, col0=col0: e.matmul(
                ps_pj, _r_shift_slice(R, c, t0, 128),
                wqkv[:, c * 1536 + col0: c * 1536 + col0 + 512],
                start=(c == 0), stop=(c == CH - 1)),
                wait=([tok_wqkv, (TBF, tbf_t), (QPF, bQPF + ci)]
                      if c == 0 else []),
                inc=(QKP if c == CH - 1 else ()))
        if nm == "v":
            P.emit("vector", lambda e, w=w, sub=sub: e.tensor_copy(
                v_tm[w % 3][:, sub * 512:(sub + 1) * 512], ps_pj),
                wait=[(QKP, None)]
                     + ([(PEo, bPEo + guard)] if guard else []),
                inc=QPF)
        else:
            _qk_epilogue(nm, (w % 2) if nm == "q" else (w % 3), sub, gsub,
                         bSMM + guard if guard else 0)

    def attn_window(w):
        par, parp = w % 2, (w - 1) % 2
        for h in range(HEADS2):
            hc, hb = h // 2, 64 * (h % 2)
            for qc in range(4):
                if P2_NOPREV:
                    w = 0
                hx = (4 * h + qc) % 2
                ssb, pbf, pT, mx, dn = (s_sb[hx], p_bf[hx],
                                        pT_sb[hx], mxc[hx], dnc[hx])
                lo = 0 if w > 0 else 512 - 128 * qc
                if P2_NARROW:
                    lo = 512 - 128 * qc
                first = ([(QKR, None), (KVR, None), (C2, C2f)]
                         if (h, qc) == (0, 0) else [])
                prevw = [(VSm, None)]
                qsl = lambda q_fm=q_fm, hb=hb, hc=hc, qc=qc: q_fm[
                    hb:hb + 64, hc * 512 + qc * 128: hc * 512 + (qc + 1) * 128]
                if w > 0:
                    if P2_NOPREVMM:
                        P.emit("tensor", lambda e, hb=hb, hc=hc, qc=qc, par=par,
                               qsl=qsl: e.matmul(
                            ps_S[:, 0:512 - 128 * qc], qsl(),
                            k_fm[par][hb:hb + 64, hc * 512 + qc * 128: hc * 512 + 512],
                            start=True, stop=True), wait=first + prevw, inc=())
                    else:
                        P.emit("tensor", lambda e, hb=hb, hc=hc, qc=qc, parp=parp,
                               qsl=qsl: e.matmul(
                            ps_S[:, 0:512 - 128 * qc], qsl(),
                            k_fm[parp][hb:hb + 64, hc * 512 + qc * 128: hc * 512 + 512],
                            start=True, stop=True), wait=first + prevw, inc=())
                    first, prevw = [], []
                if 128 * qc > 0:
                    P.emit("tensor", lambda e, hb=hb, hc=hc, qc=qc, par=par,
                           qsl=qsl: e.matmul(
                        ps_S[:, 512 - 128 * qc:512], qsl(),
                        k_fm[par][hb:hb + 64, hc * 512: hc * 512 + 128 * qc],
                        start=True, stop=True), wait=first + prevw, inc=())
                    first, prevw = [], []
                P.emit("tensor", lambda e, hb=hb, hc=hc, qc=qc, par=par,
                       qsl=qsl: e.matmul(
                    ps_S[:, 512:640], qsl(),
                    k_fm[par][hb:hb + 64, hc * 512 + qc * 128: hc * 512 + (qc + 1) * 128],
                    start=True, stop=True), wait=first + prevw, inc=SMM)
                P.emit("vector", lambda e, ssb=ssb, lo=lo: e.tensor_tensor(
                    ssb[:, lo:640], ps_S[:, lo:640], m640[:, lo:640], ALU.add),
                    wait=[(SMM, None), (AEx, max(0, AEx.n - 1)),
                          (PTS, max(0, PTS.n - 1))], inc=VSm)
                P.emit("vector", lambda e, ssb=ssb, mx=mx, lo=lo: e.reduce_max(
                    mx[:, 0:1], ssb[:, lo:640], axis=AX.X),
                    wait=[(VSm, None)], inc=VS)
                P.emit("vector", lambda e, mx=mx: e.tensor_scalar_mul(
                    mx[:, 0:1], mx[:, 0:1], -1.0), wait=[(VS, None)], inc=VS)
                P.emit("scalar", lambda e, ssb=ssb, mx=mx, lo=lo: e.activation(
                    ssb[:, lo:640], ssb[:, lo:640], AF.Exp, bias=mx[:, 0:1]),
                    wait=[(VS, None), (PTS, max(0, PTS.n - 1))], inc=AEx)
                P.emit("vector", lambda e, ssb=ssb, dn=dn, lo=lo: e.reduce_sum(
                    dn[:, 0:1], ssb[:, lo:640], axis=AX.X),
                    wait=[(AEx, None), (PTS, max(0, PTS.n - 1))], inc=VS)
                P.emit("vector", lambda e, dn=dn: e.reciprocal(dn[:, 0:1], dn[:, 0:1]),
                       wait=[(VS, None)], inc=VS)
                P.emit("scalar", lambda e, ssb=ssb, pbf=pbf, dn=dn, lo=lo:
                       e.mul(
                    pbf[:, lo:640], ssb[:, lo:640], dn[:, 0:1]),
                    wait=[(VS, None), (TPT, max(0, TPT.n - 1))], inc=PTS)
                jlo = lo // 128
                if astop == "sm":
                    continue
                for j in range(jlo, 5):
                    P.emit("tensor", lambda e, pbf=pbf, j=j: e.transpose(
                        ps_pT[:, j * 128:(j + 1) * 128],
                        pbf[:, j * 128:(j + 1) * 128], ident[:]),
                        wait=([(PTS, None), (VPT, None)] if j == jlo else []),
                        inc=(TPT if j == 4 else ()))
                P.emit("vector", lambda e, pT=pT, jlo=jlo: e.tensor_copy(
                    pT[:, jlo * 128:640], ps_pT[:, jlo * 128:640]),
                    wait=[(TPT, None), (PEo, max(0, PEo.n - 1))], inc=VPT)
                if astop == "pt":
                    continue
                for j in range(jlo, 5):
                    kabs = qc + j
                    pv = parp if kabs < 4 else par
                    sb_ = kabs % 4
                    P.emit("tensor", lambda e, pv=pv, sb_=sb_, h=h, pT=pT, j=j,
                           qc=qc, jlo=jlo: e.matmul(
                        ps_oh[:, qc * 128:(qc + 1) * 128],
                        v_tm[pv][:, sb_ * 512 + h * 64: sb_ * 512 + (h + 1) * 64],
                        pT[:, j * 128:(j + 1) * 128],
                        start=(j == jlo), stop=(j == 4)),
                        wait=([(VPT, None)] + ([(VEo, None)] if qc == 0 else [])
                              if j == jlo else []),
                        inc=(PEo if j == 4 else ()))
            if astop is None:
                P.emit("vector", lambda e, hc=hc, hb=hb: e.tensor_copy(
                    o_bf[hb:hb + 64, hc * 512:(hc + 1) * 512], ps_oh[:, 0:512]),
                    wait=[(PEo, None), (PEw, None)], inc=VEo)

    def wo_window(w):
        for mo in range(CH):
            pw = mo % 2
            for kc in range(4):
                P.emit("tensor", lambda e, mo=mo, kc=kc, pw=pw: e.matmul(
                    ps_w[pw], wosb[:, kc * 1024 + mo * 128: kc * 1024 + (mo + 1) * 128],
                    o_bf[:, kc * 512:(kc + 1) * 512],
                    start=(kc == 0), stop=(kc == 3)),
                    wait=([tok_wo, (VEo, None), (QPF, None), (KPF, None),
                           (RA, max(0, RA.n - 1))] if kc == 0 else []),
                    inc=(PEw if kc == 3 else ()))
            P.emit("vector", lambda e, mo=mo, pw=pw, w=w: e.tensor_add(
                _r_slice(R, mo, w * 512, 512), _r_slice(R, mo, w * 512, 512),
                ps_w[pw]),
                wait=[(PEw, None)], inc=RA)

    _sc0 = P.scope

    def _S(suffix):
        if _sc0:
            P.scope = _sc0 + suffix

    amode = [s for s in sweeps if s in ("A", "QA", "Q")]
    amode = amode[0] if amode else None
    NI = nw * 32
    bSMM = SMM.n; bVSm = VSm.n; bAEx = AEx.n; bPTS = PTS.n; bTPT = TPT.n
    bVPT = VPT.n; bPEo = PEo.n; bVEo = VEo.n; bPEw = PEw.n; bRA = RA.n
    bQKR = QKR.n; bKVR = KVR.n

    def _dec(i):
        w = nw - 1 - i // 32
        h = (i % 32) // 4
        qc = i % 4
        return w, h // 2, 64 * (h % 2), qc, i % 2, (0 if w > 0 else 512 - 128 * qc)

    def st_scores(i):
        _S("a")
        w, hc, hb, qc, hx, lo = _dec(i)
        wi = i // 32
        par, parp = w % 3, (w - 1) % 3
        qf = q_fm[w % 2]
        first = [(VSm, bVSm + i)]
        if i % 32 == 0:
            first += [(QKR, bQKR + wi + 1), (KVR, bKVR + wi + 1)]
        if i == 0:
            first += [(C2, C2f)]
        qsl = lambda qf=qf, hb=hb, hc=hc, qc=qc: qf[
            hb:hb + 64, hc * 512 + qc * 128: hc * 512 + (qc + 1) * 128]
        if w > 0:
            P.emit("tensor", lambda e, hb=hb, hc=hc, qc=qc, parp=parp,
                   qsl=qsl: e.matmul(
                ps_S[:, 0:512 - 128 * qc], qsl(),
                k_fm[parp][hb:hb + 64, hc * 512 + qc * 128: hc * 512 + 512],
                start=True, stop=True), wait=first, inc=())
            first = []
        if qc > 0:
            P.emit("tensor", lambda e, hb=hb, hc=hc, qc=qc, par=par,
                   qsl=qsl: e.matmul(
                ps_S[:, 512 - 128 * qc:512], qsl(),
                k_fm[par][hb:hb + 64, hc * 512: hc * 512 + 128 * qc],
                start=True, stop=True), wait=first, inc=())
            first = []
        P.emit("tensor", lambda e, hb=hb, hc=hc, qc=qc, par=par,
               qsl=qsl: e.matmul(
            ps_S[:, 512:640], qsl(),
            k_fm[par][hb:hb + 64, hc * 512 + qc * 128: hc * 512 + (qc + 1) * 128],
            start=True, stop=True), wait=first, inc=SMM)

    def st_sm1(i):
        _S("a")
        w, hc, hb, qc, hx, lo = _dec(i)
        ssb, mx = s_sb[hx], mxc[hx]
        P.emit("vector", lambda e, ssb=ssb, lo=lo: e.tensor_tensor(
            ssb[:, lo:640], ps_S[:, lo:640], m640[:, lo:640], ALU.add),
            wait=[(SMM, bSMM + i + 1), (PTS, bPTS + max(0, i - 1))], inc=VSm)
        P.emit("vector", lambda e, ssb=ssb, mx=mx, lo=lo: e.reduce_max(
            mx[:, 0:1], ssb[:, lo:640], axis=AX.X), wait=[(VSm, None)], inc=VS)
        P.emit("vector", lambda e, mx=mx: e.tensor_scalar_mul(
            mx[:, 0:1], mx[:, 0:1], -1.0), wait=[(VS, None)], inc=VS)

    def st_exp(i):
        w, hc, hb, qc, hx, lo = _dec(i)
        ssb, mx = s_sb[hx], mxc[hx]
        P.emit("scalar", lambda e, ssb=ssb, mx=mx, lo=lo: e.activation(
            ssb[:, lo:640], ssb[:, lo:640], AF.Exp, bias=mx[:, 0:1]),
            wait=[(VS, None)], inc=AEx)

    def st_sm2(i):
        w, hc, hb, qc, hx, lo = _dec(i)
        ssb, dn = s_sb[hx], dnc[hx]
        P.emit("vector", lambda e, ssb=ssb, dn=dn, lo=lo: e.reduce_sum(
            dn[:, 0:1], ssb[:, lo:640], axis=AX.X),
            wait=[(AEx, bAEx + i + 1)], inc=VS)
        P.emit("vector", lambda e, dn=dn: e.reciprocal(dn[:, 0:1], dn[:, 0:1]),
               wait=[(VS, None)], inc=VS)

    def st_mul(i):
        w, hc, hb, qc, hx, lo = _dec(i)
        ssb, pbf, dn = s_sb[hx], p_bf[hx], dnc[hx]
        P.emit("scalar", lambda e, ssb=ssb, pbf=pbf, dn=dn, lo=lo: e.mul(
            pbf[:, lo:640], ssb[:, lo:640], dn[:, 0:1]),
            wait=[(VS, None), (TPT, bTPT + max(0, i - 1))], inc=PTS)

    def st_tp(i):
        _S("a")
        w, hc, hb, qc, hx, lo = _dec(i)
        pbf = p_bf[hx]
        jlo = lo // 128
        for j in range(jlo, 5):
            P.emit("tensor", lambda e, pbf=pbf, j=j: e.transpose(
                ps_pT[:, j * 128:(j + 1) * 128],
                pbf[:, j * 128:(j + 1) * 128], ident[:]),
                wait=([(PTS, bPTS + i + 1), (VPT, bVPT + i)] if j == jlo else []),
                inc=(TPT if j == 4 else ()))

    def st_pc(i):
        w, hc, hb, qc, hx, lo = _dec(i)
        pT = pT_sb[hx]
        jlo = lo // 128
        P.emit("vector", lambda e, pT=pT, jlo=jlo: e.tensor_copy(
            pT[:, jlo * 128:640], ps_pT[:, jlo * 128:640]),
            wait=[(TPT, bTPT + i + 1), (PEo, bPEo + max(0, i - 1))], inc=VPT)

    def st_pv(i):
        _S("a")
        w, hc, hb, qc, hx, lo = _dec(i)
        par, parp = w % 3, (w - 1) % 3
        pT = pT_sb[hx]
        h = (i % 32) // 4
        jlo = lo // 128
        for j in range(jlo, 5):
            kabs = qc + j
            pv = parp if kabs < 4 else par
            sb_ = kabs % 4
            P.emit("tensor", lambda e, pv=pv, sb_=sb_, h=h, pT=pT, j=j,
                   qc=qc: e.matmul(
                ps_oh[:, qc * 128:(qc + 1) * 128],
                v_tm[pv][:, sb_ * 512 + h * 64: sb_ * 512 + (h + 1) * 64],
                pT[:, j * 128:(j + 1) * 128],
                start=(j == jlo), stop=(j == 4)),
                wait=([(VPT, bVPT + i + 1), (VEo, bVEo + i // 4)]
                      if j == jlo else []),
                inc=(PEo if j == 4 else ()))

    def st_ev(i):
        w, hc, hb, qc, hx, lo = _dec(i)
        wi = i // 32
        P.emit("vector", lambda e, w=w, hc=hc, hb=hb: e.tensor_copy(
            o_bf[w % 2][hb:hb + 64, hc * 512:(hc + 1) * 512], ps_oh[:, 0:512]),
            wait=[(PEo, bPEo + i + 1)]
                 + ([(PEw, bPEw + 8 * max(0, wi - 1))] if amode == "A" else []),
            inc=VEo)

    def st_wo(w, mo):
        _S("w")
        wi = nw - 1 - w
        pw = mo % 2
        for kc in range(4):
            P.emit("tensor", lambda e, w=w, mo=mo, kc=kc, pw=pw: e.matmul(
                ps_w[pw], wosb[:, kc * 1024 + mo * 128: kc * 1024 + (mo + 1) * 128],
                o_bf[w % 2][:, kc * 512:(kc + 1) * 512],
                start=(kc == 0), stop=(kc == 3)),
                wait=([tok_wo, (VEo, bVEo + 8 * (wi + 1)),
                       (RA, bRA + 8 * wi + max(0, mo - 1))] if kc == 0 else []),
                inc=(PEw if kc == 3 else ()))
        P.emit("vector", lambda e, w=w, mo=mo, pw=pw: e.tensor_add(
            _r_slice(R, mo, w * 512, 512), _r_slice(R, mo, w * 512, 512),
            ps_w[pw]),
            wait=[(PEw, bPEw + 8 * wi + mo + 1)], inc=RA)

    if amode:
        for sub in range(4):
            for nm in ("q", "k", "v"):
                emit_chunk(nw - 1, sub, nm, 0)
        P.emit("vector", lambda e: e.engine_nop(), wait=[(TRD, None)], inc=QKR)
        if nw >= 2:
            for sub in range(4):
                for nm in ("k", "v"):
                    emit_chunk(nw - 2, sub, nm, 0)
        P.emit("vector", lambda e: e.engine_nop(),
               wait=[(TRD, None), (QPF, None)], inc=KVR)

    def _stretch_chunks(wi):
        w = nw - 1 - wi
        cl = []
        if w - 1 >= 0:
            for sub in range(4):
                cl.append(("c", w - 1, sub, "q"))
            cl.append(("qnop",))
        if w - 2 >= 0:
            for sub in range(4):
                cl.append(("c", w - 2, sub, "k"))
                cl.append(("c", w - 2, sub, "v"))
        if w - 1 >= 0:
            cl.append(("knop",))
        return cl

    def _emit_item(it, wi):
        if it[0] == "c":
            emit_chunk(it[1], it[2], it[3], wi * 32)
        elif it[0] == "qnop":
            P.emit("vector", lambda e: e.engine_nop(),
                   wait=[(TRD, None)], inc=QKR)
        else:
            P.emit("vector", lambda e: e.engine_nop(),
                   wait=[(TRD, None), (QPF, None)], inc=KVR)

    if amode == "Q":
        for wi in range(nw):
            for it in _stretch_chunks(wi):
                _emit_item(it, wi)
    elif amode in ("A", "QA"):
        sched = {}
        for cyc in range(NI + 13):
            wi = cyc // 32
            if cyc < NI and cyc % 32 == 0:
                cl = _stretch_chunks(wi)
                for j, it in enumerate(cl):
                    sched.setdefault(cyc + 3 + (j * 27) // max(1, len(cl)),
                                     []).append((it, wi))
            if 0 <= cyc - 3 < NI:
                st_tp(cyc - 3)
            if 0 <= cyc - 4 < NI:
                st_pv(cyc - 4)
            if 0 <= cyc - 1 < NI:
                st_sm1(cyc - 1)
                st_exp(cyc - 1)
            if 0 <= cyc - 2 < NI:
                st_sm2(cyc - 2)
                st_mul(cyc - 2)
            if 0 <= cyc - 3 < NI:
                st_pc(cyc - 3)
            if 0 <= cyc - 4 < NI and (cyc - 4) % 4 == 3:
                st_ev(cyc - 4)
            if amode == "A" and cyc >= 36 and 0 <= (cyc % 32) - 4 < 8:
                wprev = nw - cyc // 32
                if 0 <= wprev < nw:
                    st_wo(wprev, (cyc % 32) - 4)
            if cyc < NI:
                st_scores(cyc)
            for it, swi in sched.pop(cyc, []):
                _emit_item(it, swi)
        for it_list in sched.values():
            for it, swi in it_list:
                _emit_item(it, swi)

    for sti, st in enumerate(range(nw - 1, -1, -1) if "F" in sweeps else []):
        t0 = st * 512
        _S("n")
        for c in range(CH):
            P.emit("scalar", lambda e, c=c, t0=t0: e.activation(
                sqb[c % 2][:], _r_shift_slice(R, c, t0, 512), AF.Square),
                wait=[(RA, None), (QRM, max(0, QRM.n - 1))]
                     + (list(after) if (sti == 0 and c == 0) else []), inc=AEF)
            P.emit("tensor", lambda e, c=c, t0=t0: e.matmul(
                ps_sr, onescol[:], _r_shift_slice(R, c, t0, 512),
                start=(c == 0), stop=(c == CH - 1)),
                wait=([(C2, C2f), (RWC, None), (RAF, None), (RA, None),
                       (VEh, None)] if c == 0 else []), inc=())
            P.emit("tensor", lambda e, c=c: e.matmul(
                ps_qr, onescol[:], sqb[c % 2][:],
                start=(c == 0), stop=(c == CH - 1)),
                wait=[(AEF, None)], inc=QRM)
        P.emit("vector", lambda e: e.tensor_scalar_mul(mrow[:], ps_sr, 1.0 / D),
               wait=[(QRM, None)], inc=VS)
        P.emit("vector", lambda e: e.tensor_scalar_mul(qrow[:], ps_qr, 1.0 / D),
               inc=VS)
        P.emit("vector", lambda e: e.tensor_mul(vrow[:], mrow[:], mrow[:]),
               wait=[(VS, None)], inc=VS)
        P.emit("vector", lambda e: e.tensor_sub(vrow[:], qrow[:], vrow[:]),
               wait=[(VS, None)], inc=VS)
        P.emit("scalar", lambda e: e.activation(rrow[:], vrow[:], AF.Ln, bias=1e-5),
               wait=[(VS, None)], inc=LNX)
        P.emit("scalar", lambda e: e.activation(rrow[:], rrow[:], AF.Exp, scale=-0.5),
               wait=[(LNX, None)], inc=AEF)
        P.emit("vector", lambda e: e.tensor_mul(mrrow[:], mrow[:], rrow[:]),
               wait=[(AEF, None)], inc=VS)
        P.emit("vector", lambda e: e.tensor_copy(rrow_bf[:], rrow[:]), inc=VS)
        P.emit("vector", lambda e: e.tensor_copy(mrrow_bf[:], mrrow[:]),
               wait=[(VS, None)], inc=RWC)
        P.emit("tensor", lambda e: e.matmul(ps_rbc, ones1[:], rrow_bf[:],
                                            start=True, stop=True),
               wait=[(RWC, None), (VEn, None), (PEf, None), (RAF, None)], inc=())
        P.emit("tensor", lambda e: e.matmul(ps_mbc, ones1[:], mrrow_bf[:],
                                            start=True, stop=True), inc=PEb)
        for c in range(CH):
            P.emit("vector", lambda e, c=c, t0=t0: e.tensor_mul(
                nt1[:], _r_shift_slice(R, c, t0, 512), ps_rbc),
                wait=[(PEb, None), (VEn, None)]
                     + ([(PEa, None)] if c == 0 else []), inc=VS)
            P.emit("vector", lambda e: e.tensor_sub(nt1[:], nt1[:], ps_mbc),
                   wait=[(VS, None)], inc=VS)
            P.emit("vector", lambda e, c=c: e.tensor_scalar(
                h2n[:, c * 512:(c + 1) * 512], nt1[:], lng[:, c:c + 1],
                lnb[:, c:c + 1], ALU.mult, ALU.add),
                wait=[(VS, None)], inc=VEn)
        if dbg == "h2n" and st == 0:
            P.emit("sync", lambda e: e.dma_start(
                io.dout("dbg_h2n", (128, 4096), BF16), h2n[:]),
                wait=[(VEn, None)], inc=P.ch("DBG"), inc_amt=16)
        _S("1")
        for m in range(22):
            pw1 = m % 2
            tok = fring.dma(lambda e, m=m, pw1=pw1: e.dma_start(
                w1t[pw1][:], w1bf.rearrange("(c p) n -> p c n", p=128)
                [:, :, m * 256:(m + 1) * 256]),
                wait=[(PEa, max(0, PEa.n - 1))] + (w1c_toks if sti == 0 else [])
                     + ([(RA, None)] if (sti == 0 and m < 2) else []))
            for c in range(CH):
                P.emit("tensor", lambda e, c=c, pw1=pw1: e.matmul(
                    ps_a, w1t[pw1][:, c * 256: c * 256 + 128],
                    h2n[:, c * 512:(c + 1) * 512],
                    start=(c == 0), stop=(c == CH - 1)),
                    wait=([tok, (VEn, None), (VEh, None), (RWC, None),
                           (RAF, None)] if c == 0 else []), inc=())
            for c in range(CH):
                P.emit("tensor", lambda e, c=c, pw1=pw1: e.matmul(
                    ps_g, w1t[pw1][:, c * 256 + 128: c * 256 + 256],
                    h2n[:, c * 512:(c + 1) * 512],
                    start=(c == 0), stop=(c == CH - 1)),
                    wait=[(AEg, None)] if c == 0 else [],
                    inc=(PEa if c == CH - 1 else ()))
            P.emit("scalar", lambda e, m=m: e.activation(
                gg[m % 2][:], ps_g, AF.Gelu, bias=b1g[:, m:m + 1]),
                wait=[(PEa, None), (VEh, max(0, VEh.n - 1))], inc=AEg)
            P.emit("vector", lambda e, m=m: e.scalar_tensor_tensor(
                h3[:, m * 512:(m + 1) * 512], ps_a, b1a[:, m:m + 1], gg[m % 2][:],
                ALU.add, ALU.mult),
                wait=[(AEg, None)], inc=VEh)
        if dbg == "h3" and st == 0:
            P.emit("sync", lambda e: e.dma_start(
                io.dout("dbg_h3", (128, 11264), BF16), h3.rearrange("p n -> p n")),
                wait=[(VEh, None)], inc=P.ch("DBG"), inc_amt=16)
        _S("2")
        pef_marks = {}
        for kk in range(22):
            tok2 = fring.dma(lambda e, kk=kk: e.dma_start(
                w2t[kk % 2][:], w2bf[128 * kk:128 * (kk + 1), :]),
                wait=[(W2M, max(0, W2M.n - 1))]
                     + ([(PEf, None)] if kk == 0 else [])
                     + (w2c_toks if sti == 0 else [])
                     + ([(RA, None)] if (sti == 0 and kk < 2) else []))
            for mo in range(CH):
                P.emit("tensor", lambda e, mo=mo, kk=kk: e.matmul(
                    accs[mo], w2t[kk % 2][:, mo * 128:(mo + 1) * 128],
                    h3[:, kk * 512:(kk + 1) * 512],
                    start=(kk == 0), stop=(kk == 21)),
                    wait=(([tok2, (VEh, None), (AEg, None), (RAF, None),
                            (RWC, None), (VEn, None)] if kk == 0 else [tok2])
                          if mo == 0 else []),
                    inc=((PEf if kk == 21 else W2M) if mo == CH - 1 else ()))
            if kk == 21:
                for mo in range(CH):
                    pef_marks[mo] = PEf.n
        for mo in range(CH):
            P.emit("vector", lambda e, mo=mo, t0=t0: e.scalar_tensor_tensor(
                _r_slice(R, mo, t0, 512), accs[mo], b2c[:, mo:mo + 1],
                _r_slice(R, mo, t0, 512), ALU.add, ALU.add),
                wait=[(PEf, pef_marks[mo])], inc=RAF)
    return [(RAF, RAF.n)] if "F" in sweeps else [(RA, RA.n)]


def emit_ph3(P, io, after=()):
    R = io.sbuf("R", (128, RCOLS), BF16)
    x_t = io.din("x_t", (D, NTOK), F32)
    attn0 = io.dram["attn0"] if "attn0" in io.dram else \
        io.din("attn0_in", (D, NTOK), BF16)
    x1s = io.dout("x1s", (D, NTOK), F32)
    onescol = io.sbuf("p2_onescol", (128, 1), BF16)
    ones1 = io.sbuf("p2_ones1", (1, 128), BF16)

    xs2 = [io.sbuf(f"p3_xs{i}", (128, 4096), F32) for i in range(2)]
    as2 = [io.sbuf(f"p3_as{i}", (128, 4096), BF16) for i in range(2)]
    x1bf2 = [io.sbuf(f"p3_x1bf{i}", (128, 4096), BF16) for i in range(2)]
    sqbf2 = [io.sbuf(f"p3_sqbf{i}", (128, 4096), BF16) for i in range(2)]
    mrow = io.sbuf("p2_mrow", (1, 512), F32)
    qrow = io.sbuf("p2_qrow", (1, 512), F32)
    vrow = io.sbuf("p2_vrow", (1, 512), F32)
    rrow = io.sbuf("p2_rrow", (1, 512), F32)
    mrrow = io.sbuf("p2_mrrow", (1, 512), F32)
    rrow_bf = io.sbuf("p2_rrowbf", (1, 512), BF16)
    mrrow_bf = io.sbuf("p2_mrrowbf", (1, 512), BF16)
    nt1 = io.sbuf("p3_nt1", (128, 512), F32)

    A2 = io.psum("p1_A2", (128, 2048), F32)
    ps_sr = A2[0:1, 0:512]
    ps_qr = A2[0:1, 512:1024]
    ps_rbc = A2[:, 1024:1536]
    ps_mbc = A2[:, 1536:2048]

    ch = lambda nm: P.ch("q" + nm)
    VS, AEF, QRM, RWC, PEb, VEn, LNX = (ch(x) for x in
        ("VS", "AEF", "QRM", "RWC", "PEb", "VEn", "LNX"))
    C2 = ch("C2")
    XAD = P.ch("p3XAD")
    XBF = P.ch("p3XBF")
    ring = DmaRing(P, "Ls", k=4, eng="sync")
    oring = DmaRing(P, "Lo", k=2, eng="sync")
    otoks = []

    ntile = NTOK // 512
    bVEn = P.ch("qVEn").n
    bXAD = XAD.n
    bQRM = QRM.n
    for t in range(ntile):
        t0 = t * 512
        tx = t % 2
        xs, as_, x1bf, sqbf = xs2[tx], as2[tx], x1bf2[tx], sqbf2[tx]
        tok_x = ring.dma(lambda e, xs=xs, t0=t0: e.dma_start(
            xs[:], x_t.rearrange("(c p) n -> p c n", p=128)[:, :, t0:t0 + 512]),
            wait=([(VEn, bVEn + 8 * (t - 1))] if t >= 2 else [])
                 + ([otoks[t - 2]] if t >= 2 else [])
                 + (list(after) if t == 0 else []))
        tok_a = ring.dma(lambda e, as_=as_, t0=t0: e.dma_start(
            as_[:], attn0.rearrange("(c p) n -> p c n", p=128)[:, :, t0:t0 + 512]),
            wait=([(XAD, bXAD + t - 1)] if t >= 2 else []))
        P.emit("vector", lambda e, xs=xs, as_=as_: e.tensor_add(
            xs[:], xs[:], as_[:]),
            wait=[tok_x, tok_a] + (list(after) if t == 0 else []), inc=VS)
        P.emit("vector", lambda e, xs=xs, t0=t0: e.tensor_add(
            xs.rearrange("p (c n) -> p c n", c=CH),
            xs.rearrange("p (c n) -> p c n", c=CH),
            R.ap().rearrange("p (c n) -> p c n", c=CH, n=RSTR)[:, :, 1 + t0:1 + t0 + 512]),
            wait=[(VS, None)], inc=XAD)
        otoks.append(oring.dma(lambda e, xs=xs, t0=t0: e.dma_start(
            x1s.rearrange("(c p) n -> p c n", p=128)[:, :, t0:t0 + 512], xs[:]),
            wait=[(XAD, None)]))
        P.emit("scalar", lambda e, xs=xs, x1bf=x1bf: e.copy(x1bf[:], xs[:]),
               wait=[(XAD, None), (QRM, bQRM + 8 * max(0, t - 1))], inc=XBF)
        P.emit("scalar", lambda e, xs=xs, sqbf=sqbf: e.activation(
            sqbf[:], xs[:], AF.Square), inc=XBF)
        for c in range(CH):
            P.emit("tensor", lambda e, c=c, x1bf=x1bf: e.matmul(
                ps_sr, onescol[:], x1bf[:, c * 512:(c + 1) * 512],
                start=(c == 0), stop=(c == CH - 1)),
                wait=([(XBF, None), (RWC, None), (C2, None)]
                      if c == 0 else []),
                inc=())
            P.emit("tensor", lambda e, c=c, sqbf=sqbf: e.matmul(
                ps_qr, onescol[:], sqbf[:, c * 512:(c + 1) * 512],
                start=(c == 0), stop=(c == CH - 1)), inc=QRM)
        P.emit("vector", lambda e: e.tensor_scalar_mul(mrow[:], ps_sr, 1.0 / D),
               wait=[(QRM, None)], inc=VS)
        P.emit("vector", lambda e: e.tensor_scalar_mul(qrow[:], ps_qr, 1.0 / D),
               inc=VS)
        P.emit("vector", lambda e: e.tensor_mul(vrow[:], mrow[:], mrow[:]),
               wait=[(VS, None)], inc=VS)
        P.emit("vector", lambda e: e.tensor_sub(vrow[:], qrow[:], vrow[:]),
               wait=[(VS, None)], inc=VS)
        P.emit("scalar", lambda e: e.activation(rrow[:], vrow[:], AF.Ln, bias=1e-5),
               wait=[(VS, None)], inc=LNX)
        P.emit("scalar", lambda e: e.activation(rrow[:], rrow[:], AF.Exp, scale=-0.5),
               wait=[(LNX, None)], inc=AEF)
        P.emit("vector", lambda e: e.tensor_mul(mrrow[:], mrow[:], rrow[:]),
               wait=[(AEF, None)], inc=VS)
        P.emit("vector", lambda e: e.tensor_copy(rrow_bf[:], rrow[:]), inc=VS)
        P.emit("vector", lambda e: e.tensor_copy(mrrow_bf[:], mrrow[:]),
               wait=[(VS, None)], inc=RWC)
        P.emit("tensor", lambda e: e.matmul(ps_rbc, ones1[:], rrow_bf[:],
                                            start=True, stop=True),
               wait=[(RWC, None), (VEn, None), (C2, None)], inc=())
        P.emit("tensor", lambda e: e.matmul(ps_mbc, ones1[:], mrrow_bf[:],
                                            start=True, stop=True), inc=PEb)
        for c in range(CH):
            P.emit("vector", lambda e, c=c, xs=xs: e.tensor_mul(
                nt1[:], xs[:, c * 512:(c + 1) * 512], ps_rbc),
                wait=[(PEb, None), (VEn, None)], inc=VS)
            P.emit("vector", lambda e, c=c, t0=t0: e.tensor_sub(
                _r_slice(R, c, t0, 512), nt1[:], ps_mbc),
                wait=[(VS, None)], inc=VEn)
    P.emit("sync", lambda e: e.nop(), wait=oring.all_done() + [(VEn, None)])
    return [(P.ch("qVEn"), P.ch("qVEn").n)] + oring.all_done()


def emit_ph4(P, io, after=()):
    R = io.sbuf("R", (128, RCOLS), BF16)
    ident = io.sbuf("ident", (128, 128), BF16)
    m640 = io.sbuf("p2_m640", (128, 640), BF16)
    ones1 = io.sbuf("p2_ones1", (1, 128), BF16)
    nsel = NTOK // 4
    nseg = NTOK // 2048
    w_inT = io.din("l1winT", (D, 3072), F32)
    binqk = io.din("l1binqk", (128, 16), F32)
    binv = io.din("l1binv", (1, 1024), F32)
    w_outT = io.din("l1woutT", (D, D), F32)
    bout = io.din("l1bout", (128, CH), F32)
    attn1s = io.dout("attn1s", (D, nsel), BF16)

    wqt = [io.sbuf(f"p4_wqt{i}", (128, CH * 128), BF16) for i in range(2)]
    wvt = [io.sbuf(f"p4_wvt{i}", (128, CH * 512), BF16) for i in range(2)]
    wout = io.sbuf("p4_wout", (128, CH * 1024), BF16)
    binqk_s = io.sbuf("p4_binqk", (128, 16), F32)
    binv_s = io.sbuf("p4_binv", (1, 1024), F32)
    binv_bf = io.sbuf("p4_binvbf", (1, 1024), BF16)
    vbc = io.sbuf("p4_vbc", (128, 1024), BF16)
    bout_s = io.sbuf("p4_bout", (128, CH), F32)
    xsel = io.sbuf("p4_xsel", (128, 4096), BF16)
    qbf = io.sbuf("p4_qbf", (128, 4096), BF16)
    kbf = io.sbuf("p4_kbf", (128, 4096), BF16)
    vtm = io.sbuf("p4_vtm", (128, 4096), BF16)
    obf = io.sbuf("p4_obf", (128, 4096), BF16)
    att = io.sbuf("p4_att", (128, 4096), BF16)
    s_sb = [io.sbuf(f"p2_ssb{i}", (128, 640), F32) for i in range(2)]
    p_bf = [io.sbuf(f"p2_pbf{i}", (128, 640), BF16) for i in range(2)]
    pT_sb = [io.sbuf(f"p2_pT{i}", (128, 640), BF16) for i in range(2)]
    mxc = [io.sbuf(f"p2_mx{i}", (128, 1), F32) for i in range(2)]
    dnc = [io.sbuf(f"p2_dn{i}", (128, 1), F32) for i in range(2)]

    A1 = io.psum("p1_A1", (128, 2048), F32)
    A2 = io.psum("p1_A2", (128, 2048), F32)
    ps_q = [A1[:, 0:512], A1[:, 512:1024]]
    ps_v = [A1[:, 1024:1536], A1[:, 1536:2048]]
    ps_S = A2[:, 0:640]
    ps_pT = A2.ap()[:, 1024:1536].bitcast(BF16)
    ps_oh = A2[0:64, 1536:2048]
    ps_w = [A1[:, 0:512], A1[:, 512:1024]]

    ch = lambda nm: P.ch("q" + nm)
    VS = ch("VS")
    SMM = ch("SMM"); VSm = ch("VSm"); AEx = ch("AEx"); PTS = ch("PTS")
    TPT = ch("TPT"); VPT = ch("VPT"); PEo = ch("PEo"); VEo = ch("VEo")
    PEw = ch("PEw"); QKP = ch("QKP"); C2 = ch("C2")
    XSL = P.ch("p4XSL")
    QEV = P.ch("p4QEV")
    AEW = P.ch("p4AEW")
    VBC = P.ch("p4VBC")
    C4 = P.ch("p4C")
    wring = DmaRing(P, "Lw", k=6, eng="gpsimd")
    fring = DmaRing(P, "Lf", k=4, eng="sync")
    oring = DmaRing(P, "Lo", k=2, eng="sync")
    atoks = []
    P4Q = P.ch("p4PQ")
    P4V = P.ch("p4PV")

    tok_wout = wring.dma(lambda e: e.dma_start(
        wout[:], w_outT.rearrange("(c p) n -> p c n", p=128)), wait=list(after))
    winbf = io.dint("winbf", (D, 3072), BF16)
    win_casts = [wring.dma(lambda e, j=j: e.dma_start(
        winbf[j * 512:(j + 1) * 512, :], w_inT[j * 512:(j + 1) * 512, :]),
        wait=list(after)) for j in range(2)]
    for hbm, sb in ((binqk, binqk_s), (binv, binv_s), (bout, bout_s)):
        P.emit("sync", lambda e, hbm=hbm, sb=sb: e.dma_start(sb[:], hbm),
               wait=list(after), inc=C4, inc_amt=16)
    C4f = C4.n
    P.emit("vector", lambda e: e.tensor_copy(binv_bf[:], binv_s[:]),
           wait=[(C4, C4f)], inc=VS)
    P.emit("tensor", lambda e: e.matmul(A2[:, 0:512], ones1[:],
                                        binv_bf[:, 0:512], start=True, stop=True),
           wait=[(VS, None), (C2, None)] + list(after), inc=())
    P.emit("tensor", lambda e: e.matmul(A2[:, 512:1024], ones1[:],
                                        binv_bf[:, 512:1024], start=True, stop=True),
           inc=QKP)
    P.emit("vector", lambda e: e.tensor_copy(vbc[:], A2[:, 0:1024]),
           wait=[(QKP, None)], inc=VBC)

    for s in range(nseg):
        base = s * 2048
        for c in range(CH):
            P.emit("vector", lambda e, c=c, base=base: e.tensor_copy(
                xsel[:, c * 512:(c + 1) * 512],
                R.ap()[:, c * RSTR + 1 + base: c * RSTR + 1 + base + 2048][:, ::4]),
                wait=([(VBC, None)] if (s, c) == (0, 0) else [])
                     + ([(PEw, None)] if c == 0 else []),
                inc=(XSL if c == CH - 1 else ()))
        for qk in range(2):
            for mo in range(CH):
                pq = mo % 2
                tw = fring.dma(lambda e, mo=mo, qk=qk, pq=pq: e.dma_start(
                    wqt[pq][:], winbf.rearrange("(c p) n -> p c n", p=128)
                    [:, :, qk * 1024 + mo * 128: qk * 1024 + (mo + 1) * 128]),
                    wait=[(P4Q, max(0, P4Q.n - 1))]
                         + (win_casts if (s, qk, mo) == (0, 0, 0) else []))
                for c in range(CH):
                    P.emit("tensor", lambda e, c=c, pq=pq: e.matmul(
                        ps_q[pq], wqt[pq][:, c * 128:(c + 1) * 128],
                        xsel[:, c * 512:(c + 1) * 512],
                        start=(c == 0), stop=(c == CH - 1)),
                        wait=([tw, (XSL, None), (QEV, max(0, QEV.n - 1))]
                              if c == 0 else []),
                        inc=(P4Q if c == CH - 1 else ()))
                if qk == 0:
                    P.emit("vector", lambda e, mo=mo, pq=pq: e.tensor_scalar(
                        qbf[:, mo * 512:(mo + 1) * 512], ps_q[pq],
                        binqk_s[:, mo:mo + 1], 0.125, ALU.add, ALU.mult),
                        wait=[(P4Q, None), (C4, C4f)], inc=QEV)
                else:
                    P.emit("vector", lambda e, mo=mo, pq=pq: e.tensor_scalar_add(
                        kbf[:, mo * 512:(mo + 1) * 512], ps_q[pq],
                        binqk_s[:, 8 + mo:8 + mo + 1]),
                        wait=[(P4Q, None)], inc=QEV)
        for half in range(2):
            tv = fring.dma(lambda e, half=half: e.dma_start(
                wvt[half % 2][:], winbf.rearrange("(c p) n -> p c n", p=128)
                [:, :, 2048 + half * 512: 2048 + (half + 1) * 512]),
                wait=[(P4V, max(0, P4V.n - 4))]
                     + (win_casts if half == 0 else []))
            for sub in range(4):
                pv = sub % 2
                for c in range(CH):
                    P.emit("tensor", lambda e, sub=sub, half=half, c=c, pv=pv: e.matmul(
                        ps_v[pv],
                        xsel[:, c * 512 + sub * 128: c * 512 + sub * 128 + 128],
                        wvt[half % 2][:, c * 512:(c + 1) * 512],
                        start=(c == 0), stop=(c == CH - 1)),
                        wait=([tv, (QEV, max(0, QEV.n - 1))] if c == 0 else []),
                        inc=(P4V if c == CH - 1 else ()))
                P.emit("vector", lambda e, sub=sub, half=half, pv=pv: e.tensor_tensor(
                    vtm[:, sub * 1024 + half * 512: sub * 1024 + (half + 1) * 512],
                    ps_v[pv], vbc[:, half * 512:(half + 1) * 512], ALU.add),
                    wait=[(P4V, None), (VBC, 1)], inc=QEV)
        bSMM = SMM.n; bVSm = VSm.n; bAEx = AEx.n; bPTS = PTS.n; bTPT = TPT.n
        bVPT = VPT.n; bPEo = PEo.n; bVEo = VEo.n; bPEw = PEw.n; bAEW = AEW.n
        bQEV = QEV.n

        def _d4(i):
            h = i // 4
            return h // 2, 64 * (h % 2), i % 4, i % 2, 128 * (i % 4 + 1)

        def s4_scores(i, s=s):
            hc, hb, qc, hx, wdt = _d4(i)
            first = [(VSm, bVSm + i)] + ([(QEV, bQEV)] if i == 0 else [])
            P.emit("tensor", lambda e, hb=hb, hc=hc, qc=qc, wdt=wdt: e.matmul(
                ps_S[:, 0:wdt],
                qbf[hb:hb + 64, hc * 512 + qc * 128: hc * 512 + (qc + 1) * 128],
                kbf[hb:hb + 64, hc * 512: hc * 512 + wdt],
                start=True, stop=True),
                wait=first, inc=SMM)

        def s4_sm1(i, s=s):
            hc, hb, qc, hx, wdt = _d4(i)
            ssb, mx = s_sb[hx], mxc[hx]
            P.emit("vector", lambda e, ssb=ssb, qc=qc, wdt=wdt: e.tensor_tensor(
                ssb[:, 0:wdt], ps_S[:, 0:wdt],
                m640[:, 512 - qc * 128:640], ALU.add),
                wait=[(SMM, bSMM + i + 1), (PTS, bPTS + max(0, i - 1))]
                     + ([(C2, None)] if (s, i) == (0, 0) else []), inc=VSm)
            P.emit("vector", lambda e, ssb=ssb, mx=mx, wdt=wdt: e.reduce_max(
                mx[:, 0:1], ssb[:, 0:wdt], axis=AX.X),
                wait=[(VSm, None)], inc=VS)
            P.emit("vector", lambda e, mx=mx: e.tensor_scalar_mul(
                mx[:, 0:1], mx[:, 0:1], -1.0), wait=[(VS, None)], inc=VS)

        def s4_exp(i):
            hc, hb, qc, hx, wdt = _d4(i)
            ssb, mx = s_sb[hx], mxc[hx]
            P.emit("scalar", lambda e, ssb=ssb, mx=mx, wdt=wdt: e.activation(
                ssb[:, 0:wdt], ssb[:, 0:wdt], AF.Exp, bias=mx[:, 0:1]),
                wait=[(VS, None)], inc=AEx)

        def s4_sm2(i):
            hc, hb, qc, hx, wdt = _d4(i)
            ssb, dn = s_sb[hx], dnc[hx]
            P.emit("vector", lambda e, ssb=ssb, dn=dn, wdt=wdt: e.reduce_sum(
                dn[:, 0:1], ssb[:, 0:wdt], axis=AX.X),
                wait=[(AEx, bAEx + i + 1)], inc=VS)
            P.emit("vector", lambda e, dn=dn: e.reciprocal(dn[:, 0:1], dn[:, 0:1]),
                   wait=[(VS, None)], inc=VS)

        def s4_mul(i):
            hc, hb, qc, hx, wdt = _d4(i)
            ssb, pbf, dn = s_sb[hx], p_bf[hx], dnc[hx]
            P.emit("scalar", lambda e, ssb=ssb, pbf=pbf, dn=dn, wdt=wdt: e.mul(
                pbf[:, 0:wdt], ssb[:, 0:wdt], dn[:, 0:1]),
                wait=[(VS, None), (TPT, bTPT + max(0, i - 1))], inc=PTS)

        def s4_tp(i):
            hc, hb, qc, hx, wdt = _d4(i)
            pbf = p_bf[hx]
            for j in range(qc + 1):
                P.emit("tensor", lambda e, pbf=pbf, j=j: e.transpose(
                    ps_pT[:, j * 128:(j + 1) * 128],
                    pbf[:, j * 128:(j + 1) * 128], ident[:]),
                    wait=([(PTS, bPTS + i + 1), (VPT, bVPT + i)] if j == 0
                          else []),
                    inc=(TPT if j == qc else ()))

        def s4_pc(i):
            hc, hb, qc, hx, wdt = _d4(i)
            pT = pT_sb[hx]
            P.emit("vector", lambda e, pT=pT, wdt=wdt: e.tensor_copy(
                pT[:, 0:wdt], ps_pT[:, 0:wdt]),
                wait=[(TPT, bTPT + i + 1), (PEo, bPEo + max(0, i - 1))],
                inc=VPT)

        def s4_pv(i):
            hc, hb, qc, hx, wdt = _d4(i)
            pT = pT_sb[hx]
            h = i // 4
            for j in range(qc + 1):
                P.emit("tensor", lambda e, j=j, h=h, pT=pT, qc=qc: e.matmul(
                    ps_oh[:, qc * 128:(qc + 1) * 128],
                    vtm[:, j * 1024 + h * 64: j * 1024 + (h + 1) * 64],
                    pT[:, j * 128:(j + 1) * 128],
                    start=(j == 0), stop=(j == qc)),
                    wait=([(VPT, bVPT + i + 1), (VEo, bVEo + i // 4)]
                          if j == 0 else []),
                    inc=(PEo if j == qc else ()))

        def s4_ev(i):
            hc, hb, qc, hx, wdt = _d4(i)
            P.emit("vector", lambda e, hc=hc, hb=hb: e.tensor_copy(
                obf[hb:hb + 64, hc * 512:(hc + 1) * 512], ps_oh[:, 0:512]),
                wait=[(PEo, bPEo + i + 1)]
                     + ([(PEw, bPEw)] if bPEw else []), inc=VEo)

        def s4_wo(mo, s=s):
            pw = mo % 2
            for c in range(CH):
                P.emit("tensor", lambda e, mo=mo, c=c, pw=pw: e.matmul(
                    ps_w[pw], wout[:, c * 1024 + mo * 128: c * 1024 + (mo + 1) * 128],
                    obf[:, c * 512:(c + 1) * 512],
                    start=(c == 0), stop=(c == CH - 1)),
                    wait=([tok_wout, (VEo, bVEo + 16), (QEV, bQEV),
                           (AEW, bAEW + max(0, mo - 1))] if c == 0 else []),
                    inc=(PEw if c == CH - 1 else ()))
            P.emit("vector", lambda e, mo=mo, pw=pw: e.tensor_scalar_add(
                att[:, mo * 512:(mo + 1) * 512], ps_w[pw], bout_s[:, mo:mo + 1]),
                wait=[(PEw, bPEw + 8 * 0 + mo + 1), (C4, C4f)]
                     + ([atoks[s - 1]] if (s >= 1 and mo == 0) else []),
                inc=AEW)

        for cyc in range(64 + 13):
            if 0 <= cyc - 3 < 64:
                s4_tp(cyc - 3)
            if 0 <= cyc - 4 < 64:
                s4_pv(cyc - 4)
            if 0 <= cyc - 1 < 64:
                s4_sm1(cyc - 1)
                s4_exp(cyc - 1)
            if 0 <= cyc - 2 < 64:
                s4_sm2(cyc - 2)
                s4_mul(cyc - 2)
            if 0 <= cyc - 3 < 64:
                s4_pc(cyc - 3)
            if 0 <= cyc - 4 < 64 and (cyc - 4) % 4 == 3:
                s4_ev(cyc - 4)
            if 0 <= cyc - 68 < 8:
                s4_wo(cyc - 68)
            if cyc < 64:
                s4_scores(cyc)
        atoks.append(oring.dma(lambda e, s=s: e.dma_start(
            attn1s.rearrange("(c p) n -> p c n", p=128)[:, :, s * 512:(s + 1) * 512],
            att.ap().rearrange("p (c n) -> p c n", c=CH)),
            wait=[(AEW, None)]))
    P.emit("sync", lambda e: e.nop(),
           wait=oring.all_done() + ([(P.ch("DBG"), P.ch("DBG").n)]
                                    if "DBG" in P.chans else []))
    return oring.all_done()


def emit_ph5(P, io, after=(), attn1_after=()):
    R = io.sbuf("R", (128, RCOLS), BF16)
    nsel = NTOK // 4
    wg_h = io.din("l1wgT", (D, 3072), I32)
    wu_h = io.din("l1wuT", (D, 3072), I32)
    wd_h = io.din("l1wdT", (3072, D), I32)
    x1s = io.dram["x1s"] if "x1s" in io.dram else io.din("x1s_in", (D, NTOK), F32)
    attn1s = io.dram["attn1s"] if "attn1s" in io.dram else \
        io.din("attn1s_in", (D, nsel), BF16)
    y_t = io.dout("y_t", (D, NTOK), F32)
    wgbf = io.dint("wgbf", (D, 3072), BF16)
    wubf = io.dint("wubf", (D, 3072), BF16)
    wdbf = io.dint("wdbf", (3072, D), BF16)

    wgt = [io.sbuf(f"p5_wgt{i}", (128, CH * 128), BF16) for i in range(2)]
    wut = [io.sbuf(f"p5_wut{i}", (128, CH * 128), BF16) for i in range(2)]
    wdt_ = [io.sbuf(f"p5_wdt{i}", (128, 2048), BF16) for i in range(2)]
    hbf = io.sbuf("p5_hbf", (128, 24 * 512), BF16)
    sg = [io.sbuf(f"p5_sg{i}", (128, 512), BF16) for i in range(2)]
    x1t = io.sbuf("p5_x1t", (128, CH * 512), F32)
    attt = io.sbuf("p5_attt", (128, CH * 128), BF16)
    yt = io.sbuf("p5_yt", (128, CH * 512), F32)

    A1 = io.psum("p1_A1", (128, 2048), F32)
    A2 = io.psum("p1_A2", (128, 2048), F32)
    ps_g = [A1[:, 0:512], A1[:, 512:1024]]
    ps_u = [A1[:, 1024:1536], A1[:, 1536:2048]]
    accs = [A2[:, 512 * i:512 * (i + 1)] for i in range(4)] + \
           [A1[:, 512 * i:512 * (i + 1)] for i in range(4)]

    PEg = P.ch("p5PEg")
    PEu = P.ch("p5PEu")
    AEs = P.ch("p5AEs")
    VEh = P.ch("p5VEh")
    PEd = P.ch("p5PEd")
    DWM = P.ch("p5DWM")
    VEy = P.ch("p5VEy")
    wring = DmaRing(P, "Lw", k=6, eng="gpsimd")
    fring = DmaRing(P, "Lf", k=4, eng="sync")
    lring = DmaRing(P, "Ls", k=4, eng="sync")
    oring = DmaRing(P, "Lo", k=2, eng="sync")
    ytoks = []

    cast_toks = []
    for srcd, dstd, nr in ((wg_h, wgbf, D), (wu_h, wubf, D), (wd_h, wdbf, 3072)):
        for j in range(2):
            cast_toks.append(wring.dma(
                lambda e, srcd=srcd, dstd=dstd, j=j, nr=nr: e.dma_start(
                    dstd[j * (nr // 2):(j + 1) * (nr // 2), :],
                    srcd[j * (nr // 2):(j + 1) * (nr // 2), :]),
                wait=list(after)))

    ntile = NTOK // 512
    SCALE = 1.0 / 127.0
    for t in range(ntile):
        t0 = t * 512
        tok_x = lring.dma(lambda e, t0=t0: e.dma_start(
            x1t[:], x1s.rearrange("(c p) n -> p c n", p=128)[:, :, t0:t0 + 512]),
            wait=[(VEy, None)] + (list(after) if t == 0 else []))
        tok_a = lring.dma(lambda e, t=t: e.dma_start(
            attt[:], attn1s.rearrange("(c p) n -> p c n", p=128)
            [:, :, t * 128:(t + 1) * 128]),
            wait=[(VEy, None)] + (list(attn1_after) if t == 0 else []))
        for mo in range(24):
            pg = mo % 2
            tok_g = fring.dma(lambda e, mo=mo: e.dma_start(
                wgt[mo % 2][:], wgbf.rearrange("(c p) n -> p c n", p=128)
                [:, :, mo * 128:(mo + 1) * 128]),
                wait=[(PEg, max(0, PEg.n - 1))]
                     + (cast_toks if (t, mo) == (0, 0) else []))
            tok_u = fring.dma(lambda e, mo=mo: e.dma_start(
                wut[mo % 2][:], wubf.rearrange("(c p) n -> p c n", p=128)
                [:, :, mo * 128:(mo + 1) * 128]),
                wait=[(PEu, max(0, PEu.n - 1))]
                     + (cast_toks if (t, mo) == (0, 0) else []))
            for c in range(CH):
                P.emit("tensor", lambda e, mo=mo, c=c, pg=pg, t0=t0: e.matmul(
                    ps_g[pg], wgt[mo % 2][:, c * 128:(c + 1) * 128],
                    _r_slice(R, c, t0, 512),
                    start=(c == 0), stop=(c == CH - 1)),
                    wait=([tok_g, (AEs, max(0, AEs.n - 1))]
                          + ([(VEy, None)] if (t > 0 and mo == 0) else [])
                          + (list(after) if (t, mo) == (0, 0) else [])
                          if c == 0 else []),
                    inc=(PEg if c == CH - 1 else ()))
            for c in range(CH):
                P.emit("tensor", lambda e, mo=mo, c=c, pg=pg, t0=t0: e.matmul(
                    ps_u[pg], wut[mo % 2][:, c * 128:(c + 1) * 128],
                    _r_slice(R, c, t0, 512),
                    start=(c == 0), stop=(c == CH - 1)),
                    wait=[tok_u, (VEh, max(0, VEh.n - 1))] if c == 0 else [],
                    inc=(PEu if c == CH - 1 else ()))
            P.emit("scalar", lambda e, pg=pg, mo=mo: e.activation(
                sg[mo % 2][:], ps_g[pg], AF.Silu, scale=SCALE),
                wait=[(PEg, None), (VEh, max(0, VEh.n - 1))], inc=AEs)
            P.emit("vector", lambda e, mo=mo, pg=pg: e.scalar_tensor_tensor(
                hbf[:, mo * 512:(mo + 1) * 512],
                ps_u[pg], SCALE, sg[mo % 2][:], ALU.mult, ALU.mult),
                wait=[(PEu, None), (AEs, None)], inc=VEh)
        pdm = {}
        for kk in range(24):
            if kk % 2 == 0:
                tok_d = fring.dma(lambda e, kk=kk: e.dma_start(
                    wdt_[(kk // 2) % 2][:],
                    wdbf.rearrange("(j p) n -> p j n", p=128)
                    [:, kk:kk + 2, :]),
                    wait=[(DWM, max(0, DWM.n - 2))]
                         + ([(PEd, None)] if kk == 0 else [])
                         + (cast_toks if (t, kk) == (0, 0) else []))
            for md in range(CH):
                P.emit("tensor", lambda e, md=md, kk=kk: e.matmul(
                    accs[md],
                    wdt_[(kk // 2) % 2][:, (kk % 2) * 1024 + md * 128:
                                        (kk % 2) * 1024 + (md + 1) * 128],
                    hbf[:, kk * 512:(kk + 1) * 512],
                    start=(kk == 0), stop=(kk == 23)),
                    wait=(([tok_d, (VEh, None), (AEs, None), (VEy, None)]
                           if kk == 0 else [tok_d]) if md == 0 else []),
                    inc=(PEd if kk == 23 else (DWM if md == CH - 1 else ())))
            if kk == 23:
                for md in range(CH):
                    pdm[md] = PEd.n
        for md in range(CH):
            P.emit("vector", lambda e, md=md: e.scalar_tensor_tensor(
                yt[:, md * 512:(md + 1) * 512], accs[md], SCALE,
                x1t[:, md * 512:(md + 1) * 512], ALU.mult, ALU.add),
                wait=[(PEd, pdm[md]), tok_x]
                     + ([ytoks[t - 1]] if (t >= 1 and md == 0) else []),
                inc=VEy)
        for md in range(CH):
            P.emit("vector", lambda e, md=md: e.tensor_add(
                yt.ap()[:, md * 512: (md + 1) * 512][:, ::4],
                yt.ap()[:, md * 512: (md + 1) * 512][:, ::4],
                attt[:, md * 128:(md + 1) * 128]),
                wait=[(VEy, None), tok_a], inc=VEy)
        ytoks.append(oring.dma(lambda e, t0=t0: e.dma_start(
            y_t.rearrange("(c p) n -> p c n", p=128)[:, :, t0:t0 + 512],
            yt.ap().rearrange("p (c n) -> p c n", c=CH)),
            wait=[(VEy, None)]))
    P.emit("sync", lambda e: e.nop(), wait=oring.all_done())
    return oring.all_done()


def emit_load_r(P, io):
    r_in = io.din("r_in", (128, RCOLS), BF16)
    R = io.sbuf("R", (128, RCOLS), BF16)
    LD = P.ch("rLD")
    P.emit("sync", lambda e: e.dma_start(R[:], r_in), inc=LD, inc_amt=16)
    return (LD, 16)


def emit_store_r(P, io, wait_chs):
    r_out = io.dout("r_out", (128, RCOLS), BF16)
    R = io.sbuf("R", (128, RCOLS), BF16)
    SO = P.ch("rSO")
    P.emit("sync", lambda e: e.dma_start(r_out, R[:]),
           wait=[w if isinstance(w, tuple) else (w, None) for w in wait_chs],
           inc=SO, inc_amt=16)
    P.emit("sync", lambda e: e.nop(), wait=[(SO, None)])


def emit_finish(P, io, wait_chs):
    P.emit("sync", lambda e: e.nop(), wait=[(ch, None) for ch in wait_chs])


def set_ntok(n):
    global NTOK
    assert n % 512 == 0
    NTOK = n


def _col_chunks(vec):
    return np.ascontiguousarray(vec.reshape(CH, 128).T.astype(np.float32))


def _host_ph1(inputs):
    esel = np.zeros((16, 1024), bf16)
    for m in range(8):
        for pp in range(128):
            esel[2 * m + (pp >= 64), 128 * m + pp] = 1
    return {
        "l0Wq": np.ascontiguousarray(inputs["L0_attn_Wq"], dtype=np.float32),
        "l0Wk": np.ascontiguousarray(inputs["L0_attn_Wk"], dtype=np.float32),
        "l0Wv": np.ascontiguousarray(inputs["L0_attn_Wv"], dtype=np.float32),
        "l0Wo": np.ascontiguousarray(inputs["L0_attn_Wo"], dtype=np.float32),
        "l0bo": _col_chunks(np.asarray(inputs["L0_attn_bo"])),
        "l0esel": esel,
    }


def _host_ph2(inputs):
    HID = 2730
    w1 = np.asarray(inputs["L0_ffn_w1"], np.float32)
    b1 = np.asarray(inputs["L0_ffn_b1"], np.float32)
    w2 = np.asarray(inputs["L0_ffn_w2"], np.float32)
    out = {}
    w1p = np.zeros((2, D, W1P), np.float32)
    b1a = np.zeros((2, 128, 22), np.float32)
    b1g = np.zeros((2, 128, 22), np.float32)
    for m in range(22):
        na = min(128, HID - 128 * m)
        w1p[:, :, 256 * m:256 * m + na] = w1[:, :, 128 * m:128 * m + na]
        w1p[:, :, 256 * m + 128:256 * m + 128 + na] = \
            w1[:, :, HID + 128 * m:HID + 128 * m + na]
        b1a[:, 0:na, m] = b1[:, 128 * m:128 * m + na]
        b1g[:, 0:na, m] = b1[:, HID + 128 * m:HID + 128 * m + na]
    w2p = np.zeros((2, W2K, D), np.float32)
    w2p[:, 0:HID] = w2
    for l in range(2):
        out[f"f_wqkv{l}"] = np.ascontiguousarray(inputs["L0_ffn_wqkv"][l], dtype=np.float32)
        out[f"f_wo{l}"] = np.ascontiguousarray(inputs["L0_ffn_wo"][l], dtype=np.float32)
        out[f"f_w1p{l}"] = np.ascontiguousarray(w1p[l])
        out[f"f_w2p{l}"] = np.ascontiguousarray(w2p[l])
        out[f"f_b1a{l}"] = np.ascontiguousarray(b1a[l])
        out[f"f_b1g{l}"] = np.ascontiguousarray(b1g[l])
        out[f"f_lng{l}"] = _col_chunks(np.asarray(inputs["L0_ffn_ln_g"][l]))
        out[f"f_lnb{l}"] = _col_chunks(np.asarray(inputs["L0_ffn_ln_b"][l]))
        out[f"f_b2{l}"] = _col_chunks(np.asarray(inputs["L0_ffn_b2"][l]))
        out[f"f_qs{l}"] = np.asarray(inputs["L0_ffn_q_scale"][l], np.float32).reshape(1, 64)
        out[f"f_ks{l}"] = np.asarray(inputs["L0_ffn_k_scale"][l], np.float32).reshape(1, 64)
    n = NTOK
    half = np.arange(0, 64, 2, dtype=np.float64)
    inv_freq = 1.0 / 10000.0 ** (half / 64.0)
    t = np.arange(n, dtype=np.float64)
    freqs = t[:, None] * inv_freq[None, :]
    base = (half + 0.4 * 64.0) / (1.4 * 64.0)
    power = (t - n // 2) / 256.0
    scale = base[None, :] ** power[:, None]
    cos, sin = np.cos(freqs), np.sin(freqs)
    def pack(a):
        return np.ascontiguousarray(
            a.reshape(n // 128, 128, 64).transpose(1, 0, 2)
             .reshape(128, (n // 128) * 64).astype(bf16))
    cs_ = np.concatenate([cos * scale, cos * scale], -1)
    sx_ = np.concatenate([-sin * scale, sin * scale], -1)
    ck_ = np.concatenate([cos / scale, cos / scale], -1)
    sk_ = np.concatenate([-sin / scale, sin / scale], -1)
    zpad = np.zeros((128, 2048), bf16)
    for nm, a in (("csq", cs_), ("ssq", sx_), ("csk", ck_), ("ssk", sk_)):
        p = pack(a)
        z = zpad.copy(); z[:, 0:p.shape[1]] = p
        out[f"tab_{nm}"] = z
    lam = np.arange(128)[:, None]
    jj = np.arange(128)[None, :]
    out["tri_u"] = np.where(jj >= lam, 0.0, -1e30).astype(bf16)
    out["tri_l"] = np.where(jj <= lam, 0.0, -1e30).astype(bf16)
    m640 = np.zeros((128, 640), np.float32)
    m640[:, 0:128] = np.where(jj >= lam, 0.0, -1e30)
    m640[:, 512:640] = np.where(jj <= lam, 0.0, -1e30)
    out["mask640"] = m640.astype(bf16)
    out["ones8"] = np.full((1, 128), 8.0, bf16)
    out["ones1"] = np.ones((1, 128), bf16)
    out["onescol"] = np.ones((128, 1), bf16)
    return out


def _host_ph45(inputs):
    w_in = np.asarray(inputs["L1_attn_w_in"], np.float32)
    b_in = np.asarray(inputs["L1_attn_b_in"], np.float32)
    w_out = np.asarray(inputs["L1_attn_w_out"], np.float32)
    b_out = np.asarray(inputs["L1_attn_b_out"], np.float32)
    return {
        "l1winT": np.ascontiguousarray(w_in.T),
        "l1binqk": np.ascontiguousarray(b_in[:2048].reshape(16, 128).T),
        "l1binv": np.ascontiguousarray(b_in[2048:].reshape(1, 1024)),
        "l1woutT": np.ascontiguousarray(w_out.T),
        "l1bout": _col_chunks(b_out),
        "l1wgT": np.ascontiguousarray(np.asarray(inputs["L1_ffn_w_gate"], np.int32).T),
        "l1wuT": np.ascontiguousarray(np.asarray(inputs["L1_ffn_w_up"], np.int32).T),
        "l1wdT": np.ascontiguousarray(np.asarray(inputs["L1_ffn_w_down"], np.int32).T),
    }


def _host_consts():
    c = {}
    c["identin"] = np.eye(128, dtype=bf16)
    c["constsin"] = np.tile(np.asarray(CONST_VALS, np.float32), (128, 1))
    return c


def _prep_shared(inputs):
    sh = {}
    return sh


def _launch(phases, feed, n_cores=N_CORES, sim=False, trace=False):
    nc = bass.Bass("TRN2", target_bir_lowering=False, debug=False)
    io = IO(nc)
    P = Prog(nc)
    for ph in phases:
        ph(P, io)
    P.build()
    in_names = [n for n, k in io.kinds.items() if k == "in"]
    out_names = [n for n, k in io.kinds.items() if k == "out"]
    in_maps = []
    for core in range(n_cores):
        m = {}
        for n in in_names:
            arr = feed[n] if not isinstance(feed[n], list) else feed[n][core]
            m[n] = arr
        in_maps.append(m)
    if sim:
        from concourse.bass_interp import CoreSim

        class _R:
            pass
        nc.finalize()
        if sim != "race":
            nc.detect_race_conditions = False
        s = CoreSim(nc, require_finite=False)
        for n in in_names:
            s.tensor(n)[:] = in_maps[0][n]
        s.simulate(check_with_hw=False)
        r = _R()
        r.results = [{n: np.copy(s.tensor(n)) for n in out_names}]
        return r
    res = run_bass_kernel_spmd(nc, in_maps, core_ids=list(range(n_cores)),
                               trace=trace)
    return res


def _consts2(P, io):
    C2 = P.ch("qC2")
    for nm, shape in (("mask640", (128, 640)), ("ones8", (1, 128)),
                      ("ones1", (1, 128)), ("onescol", (128, 1))):
        hbm = io.din(nm, shape, BF16)
        sb = io.sbuf("p2_" + ("m640" if nm == "mask640" else nm), shape, BF16)
        P.emit("sync", lambda e, hbm=hbm, sb=sb: e.dma_start(sb[:], hbm),
               inc=C2, inc_amt=16)


def _feed_all(inputs):
    x = np.asarray(inputs["L0_attn_x"], dtype=np.float32)
    feed = {}
    feed["x_tm"] = [np.ascontiguousarray(x[b]) for b in range(N_CORES)]
    feed["x_t"] = [np.ascontiguousarray(x[b].T) for b in range(N_CORES)]
    feed.update(_host_consts())
    feed.update(_host_ph1(inputs))
    feed.update(_host_ph2(inputs))
    feed.update(_host_ph45(inputs))
    return feed


def kernel(**inputs):
    if os.environ.get("K_UNFUSED") == "1":
        return _kernel_unfused(**inputs)
    feed = _feed_all(inputs)
    nc = bass.Bass("TRN2", target_bir_lowering=False, debug=False)
    io = IO(nc, internal={"attn0", "x1s", "attn1s"}, use_arena=True)
    P = Prog(nc)
    emit_consts(P, io)
    emit_ph0(P, io)
    t0 = [(P.ch("p0EVT"), P.ch("p0EVT").n), (P.ch("p0G"), 1)]
    t1 = emit_ph1(P, io, after=t0)
    t2a = emit_ph2_layer(P, io, 0, after=t1)
    t2 = emit_ph2_layer(P, io, 1, after=t2a)
    t3 = emit_ph3(P, io, after=t2)
    t4 = emit_ph4(P, io, after=t3)
    emit_ph5(P, io, after=t4, attn1_after=t4)
    P.build()
    in_names = [n for n, k in io.kinds.items() if k == "in"]
    in_maps = []
    for core in range(N_CORES):
        m = {}
        for n in in_names:
            arr = feed[n] if not isinstance(feed[n], list) else feed[n][core]
            m[n] = arr
        in_maps.append(m)
    res = run_bass_kernel_spmd(nc, in_maps, core_ids=list(range(N_CORES)))
    y = np.stack([np.ascontiguousarray(res.results[b]["y_t"].T)
                  for b in range(N_CORES)])
    return y


def _kernel_unfused(**inputs):
    x = np.asarray(inputs["L0_attn_x"], dtype=np.float32)
    feed = {}
    feed["x_tm"] = [np.ascontiguousarray(x[b]) for b in range(N_CORES)]
    feed["x_t"] = [np.ascontiguousarray(x[b].T) for b in range(N_CORES)]
    feed.update(_host_consts())
    feed.update(_host_ph1(inputs))
    feed.update(_host_ph2(inputs))
    feed.update(_host_ph45(inputs))
    tok = {}

    def loadr(P, io):
        tok["t"] = [K_emit_load_r(P, io)]

    K_emit_load_r = emit_load_r

    def l1_ph0(P, io):
        emit_ph0(P, io)
        emit_store_r(P, io, [P.ch("p0EVT"), P.ch("p0G")])
    res = _launch([emit_consts, l1_ph0], feed)
    r = [res.results[b]["r_out"] for b in range(N_CORES)]

    feed["r_in"] = r
    def l2_ph1(P, io):
        emit_ph1(P, io, after=tok["t"])
    res = _launch([emit_consts, loadr, l2_ph1], feed)
    attn0 = [res.results[b]["attn0"] for b in range(N_CORES)]

    def l3_ph2(P, io):
        t0 = emit_ph2_layer(P, io, 0, after=tok["t"])
        t1 = emit_ph2_layer(P, io, 1, after=t0)
        emit_store_r(P, io, t1)
    res = _launch([emit_consts, loadr, l3_ph2], feed)
    r = [res.results[b]["r_out"] for b in range(N_CORES)]

    feed["r_in"] = r
    feed["attn0_in"] = attn0
    def l4_ph3(P, io):
        t = emit_ph3(P, io, after=tok["t"])
        emit_store_r(P, io, t)
    res = _launch([emit_consts, _consts2, loadr, l4_ph3], feed)
    r = [res.results[b]["r_out"] for b in range(N_CORES)]
    x1s = [res.results[b]["x1s"] for b in range(N_CORES)]

    feed["r_in"] = r
    def l5_ph4(P, io):
        emit_ph4(P, io, after=tok["t"])
    res = _launch([emit_consts, _consts2, loadr, l5_ph4], feed)
    attn1 = [res.results[b]["attn1s"] for b in range(N_CORES)]

    feed["x1s_in"] = x1s
    feed["attn1s_in"] = attn1
    def l6_ph5(P, io):
        emit_ph5(P, io, after=tok["t"], attn1_after=tok["t"])
    res = _launch([emit_consts, loadr, l6_ph5], feed)
    y = np.stack([np.ascontiguousarray(res.results[b]["y_t"].T)
                  for b in range(N_CORES)])
    return y


# version 08
# speedup vs baseline: 1.0923x; 1.0166x over previous
import contextlib
import numpy as np
import ml_dtypes
import concourse.bass as bass
import concourse.mybir as mybir
from concourse.bass_utils import run_bass_kernel_spmd

F32 = mybir.dt.float32
BF16 = mybir.dt.bfloat16
I32 = mybir.dt.int32
AF = mybir.ActivationFunctionType
ALU = mybir.AluOpType
AX = mybir.AxisListType
bf16 = ml_dtypes.bfloat16

N_CORES = 8
NTOK = 4096
D = 1024
CH = 8
RSTR = 4104
RCOLS = CH * RSTR

ENGS = ("sync", "tensor", "vector", "scalar", "gpsimd")


class _Chan:
    __slots__ = ("name", "n", "sem")

    def __init__(self, name):
        self.name = name
        self.n = 0
        self.sem = None


class Prog:

    def __init__(self, nc):
        self.nc = nc
        self.ops = {e: [] for e in ENGS}
        self.chans = {}
        self.scope = None

    def ch(self, name):
        if name not in self.chans:
            self.chans[name] = _Chan(name)
        return self.chans[name]

    def emit(self, eng, fn, wait=(), inc=(), inc_amt=1):
        waits = []
        for ch, v in wait:
            v = ch.n if v is None else v
            assert v <= ch.n, f"wait on future count of {ch.name}: {v} > {ch.n}"
            if v > 0:
                waits.append((ch, v))
        if isinstance(inc, _Chan):
            inc = (inc,)
        incs = []
        for ch in inc:
            ch.n += inc_amt
            incs.append((ch, inc_amt))
        self.ops[eng].append((waits, fn, incs, self.scope))

    def build(self):
        nc = self.nc
        with contextlib.ExitStack() as stack:
            for name in sorted(self.chans):
                ch = self.chans[name]
                ch.sem = stack.enter_context(nc.semaphore("ch_" + name))
            with nc.Block() as block:
                for eng in ENGS:
                    if not self.ops[eng]:
                        continue

                    ck = self.chans.get("CK")

                    def make(eng, ck=ck):
                        def emit_all(e):
                            last = {}
                            cur = [None, None]
                            if ck is not None and eng != "sync":
                                e.wait_ge(ck.sem, ck.n)
                                last = {ck: ck.n}
                            for waits, fn, incs, scope in self.ops[eng]:
                                if scope != cur[0]:
                                    if cur[1] is not None:
                                        nc.leave_named_scope(cur[0], cur[1], False)
                                        cur[1] = None
                                    cur[0] = scope
                                    if scope is not None:
                                        cur[1], _ = nc.enter_named_scope(
                                            scope, False)
                                for ch, v in waits:
                                    if last.get(ch, 0) >= v:
                                        continue
                                    e.wait_ge(ch.sem, v)
                                    last[ch] = v
                                inst = fn(e)
                                for ch, amt in incs:
                                    inst = inst.then_inc(ch.sem, amt)
                            if cur[1] is not None:
                                nc.leave_named_scope(cur[0], cur[1], False)
                            return None
                        return emit_all
                    getattr(block, eng)(make(eng))


class DmaRing:

    def __init__(self, P, name, k=4, eng="sync"):
        self.P, self.k, self.eng = P, k, eng
        self.chs = [P.ch(f"{name}{i}") for i in range(k)]
        self.i = 0

    def dma(self, fn, wait=()):
        ch = self.chs[self.i % self.k]
        w = list(wait)
        if ch.n > 0:
            w.append((ch, ch.n))
        self.P.emit(self.eng, fn, wait=w, inc=ch, inc_amt=16)
        self.i += 1
        return (ch, ch.n)

    def all_done(self):
        return [(ch, ch.n) for ch in self.chs]


class _View:
    __slots__ = ("_ap",)

    def __init__(self, ap):
        self._ap = ap

    def __getitem__(self, k):
        return self._ap[k]

    def ap(self):
        return self._ap

    def rearrange(self, *a, **k):
        return self._ap.rearrange(*a, **k)


ARENA_TAIL = {"p2_mrow", "p2_qrow", "p2_vrow", "p2_rrow", "p2_mrrow",
              "p2_rrowbf", "p2_mrrowbf", "p2_ssb0", "p2_ssb1", "p2_pbf0",
              "p2_pbf1", "p2_pT0", "p2_pT1", "p2_mx0", "p2_mx1",
              "p2_dn0", "p2_dn1"}
ARENA_REAL = {"p2_m640", "p2_ones8", "p2_ones1", "p2_onescol"}
ARENA_BYTES = 141 * 1024
ARENA_TAIL_BYTES = 26 * 1024
_DT_BYTES = {F32: 4, BF16: 2, I32: 4}


class IO:

    def __init__(self, nc, internal=(), use_arena=False):
        self.nc = nc
        self.dram = {}
        self.kinds = {}
        self.sb = {}
        self.internal = set(internal)
        self.use_arena = use_arena
        self.arena = None
        self.abump = {}

    def din(self, name, shape, dtype):
        if name not in self.dram:
            self.dram[name] = self.nc.dram_tensor(name, list(shape), dtype,
                                                  kind="ExternalInput").ap()
            self.kinds[name] = "in"
        return self.dram[name]

    def dout(self, name, shape, dtype):
        if name not in self.dram:
            kind = "Internal" if name in self.internal else "ExternalOutput"
            self.dram[name] = self.nc.dram_tensor(name, list(shape), dtype,
                                                  kind=kind).ap()
            self.kinds[name] = "internal" if name in self.internal else "out"
        return self.dram[name]

    def dint(self, name, shape, dtype):
        if name not in self.dram:
            self.dram[name] = self.nc.dram_tensor(name, list(shape), dtype,
                                                  kind="Internal").ap()
            self.kinds[name] = "internal"
        return self.dram[name]

    def sbuf(self, name, shape, dtype):
        if name in self.sb:
            return self.sb[name]
        arena_ok = (self.use_arena and name[0] == "p" and name[2] == "_"
                    and name not in ARENA_REAL)
        if not arena_ok:
            self.sb[name] = self.nc.alloc_sbuf_tensor(name, list(shape), dtype)
            return self.sb[name]
        if self.arena is None:
            self.arena = self.nc.alloc_sbuf_tensor(
                "ARENA", [128, ARENA_BYTES // 2], BF16)
        region = "tail" if name in ARENA_TAIL else name[:2]
        base = self.abump.get(region)
        if base is None:
            base = (ARENA_BYTES - ARENA_TAIL_BYTES) if region == "tail" else 0
            self.abump[region] = base
        nbytes = shape[1] * _DT_BYTES[dtype]
        nbytes = (nbytes + 63) & ~63
        limit = ARENA_BYTES if region == "tail" else ARENA_BYTES - ARENA_TAIL_BYTES
        assert base + nbytes <= limit, \
            f"arena overflow: {name} region {region} base {base} + {nbytes} > {limit}"
        self.abump[region] = base + nbytes
        col0 = base // 2
        view = self.arena.ap()[:, col0: col0 + shape[1] * _DT_BYTES[dtype] // 2]
        if dtype != BF16:
            view = view.bitcast(dtype)
        if shape[0] != 128:
            view = view[0:shape[0], :]
        self.sb[name] = _View(view)
        return self.sb[name]

    def psum(self, name, shape, dtype=F32):
        if name not in self.sb:
            self.sb[name] = self.nc.alloc_psum_tensor(name, list(shape), dtype)
        return self.sb[name]


def _r_slice(R, c, t0, n):
    return R[:, c * RSTR + 1 + t0: c * RSTR + 1 + t0 + n]


def _r_shift_slice(R, c, t0, n):
    off = 0 if c >= 4 else 1
    return R[:, c * RSTR + off + t0: c * RSTR + off + t0 + n]


CONST_VALS = (1e-5, 0.125, 1.0 / 127.0, 8.0, -1.0, 1e-6, 0.5)


def emit_consts(P, io):
    nc = P.nc
    identin = io.din("identin", (128, 128), BF16)
    ident = io.sbuf("ident", (128, 128), BF16)
    K = P.ch("CK")
    P.emit("sync", lambda e: e.dma_start(ident[:], identin), inc=K, inc_amt=16)
    constsin = io.din("constsin", (128, len(CONST_VALS)), F32)
    ct = io.sbuf("consts", (128, len(CONST_VALS)), F32)
    P.emit("sync", lambda e: e.dma_start(ct[:], constsin), inc=K, inc_amt=16)
    for i, v in enumerate(CONST_VALS):
        nc.const_aps.register(F32, float(v), ct[:, i:i + 1])


def emit_ph0(P, io):
    nc = P.nc
    x_tm = io.din("x_tm", (NTOK, D), F32)
    R = io.sbuf("R", (128, RCOLS), BF16)
    ident = io.sbuf("ident", (128, 128), BF16)

    xld = [io.sbuf(f"p0_x{p}", (128, D), F32) for p in range(2)]
    xn = [io.sbuf(f"p0_xn{p}", (128, D), BF16) for p in range(2)]
    dump = io.sbuf("p0_dump", (128, D), BF16)
    cs = io.sbuf("p0_cs", (128, 1), F32)
    cmean = io.sbuf("p0_cmean", (128, 1), F32)
    cq = io.sbuf("p0_cq", (128, 1), F32)
    cmsq = io.sbuf("p0_cmsq", (128, 1), F32)
    cm2 = io.sbuf("p0_cm2", (128, 1), F32)
    cvar = io.sbuf("p0_cvar", (128, 1), F32)
    cstd = io.sbuf("p0_cstd", (128, 1), F32)
    crstd = io.sbuf("p0_crstd", (128, 1), F32)
    A1 = io.psum("p1_A1", (128, 2048), F32)
    psT = [A1.ap()[:, 0:512].bitcast(BF16), A1.ap()[:, 512:1024].bitcast(BF16)]

    VT, SQ, VV, ST, TT, EVT, VS = (P.ch(x) for x in
                                   ("p0VT", "qSQA", "qRED", "qLNX", "p0TT",
                                    "p0EVT", "qVS"))
    ring = DmaRing(P, "Ls", k=4, eng="sync")

    P.emit("gpsimd", lambda e: e.memset(R[:], 0.0), inc=P.ch("p0G"))

    nsub = NTOK // 128
    for s in range(nsub):
        p = s % 2
        tok = ring.dma(lambda e, s=s, p=p: e.dma_start(
            xld[p][:], x_tm[s * 128:(s + 1) * 128, :]),
            wait=[(VT, max(0, s - 1))])
        P.emit("vector", lambda e, p=p: e.reduce_sum(cs[:, 0:1], xld[p][:], axis=AX.X),
               wait=[tok], inc=VS)
        P.emit("vector", lambda e: e.tensor_scalar_mul(cmean[:, 0:1], cs[:, 0:1], 1.0 / D),
               wait=[(VS, None)], inc=VS)
        P.emit("scalar", lambda e, p=p: e.activation(
            dump[:], xld[p][:], AF.Square),
            wait=[tok, (VT, max(0, s))], inc=SQ)
        P.emit("vector", lambda e: e.reduce_sum(cq[:, 0:1], dump[:], axis=AX.X),
               wait=[(SQ, None)], inc=VS)
        P.emit("vector", lambda e: e.tensor_scalar_mul(cmsq[:, 0:1], cq[:, 0:1], 1.0 / D),
               wait=[(VS, None)], inc=VS)
        P.emit("vector", lambda e: e.tensor_mul(cm2[:, 0:1], cmean[:, 0:1], cmean[:, 0:1]),
               inc=VS)
        P.emit("vector", lambda e: e.tensor_sub(cvar[:, 0:1], cmsq[:, 0:1], cm2[:, 0:1]),
               wait=[(VS, None)], inc=VV)
        P.emit("scalar", lambda e: e.activation(cstd[:, 0:1], cvar[:, 0:1], AF.Sqrt,
                                                bias=1e-5),
               wait=[(VV, None)], inc=ST)
        P.emit("vector", lambda e: e.reciprocal(crstd[:, 0:1], cstd[:, 0:1]),
               wait=[(ST, None)], inc=VS)
        P.emit("vector", lambda e, p=p: e.tensor_scalar(
            xn[p][:], xld[p][:], cmean[:, 0:1], crstd[:, 0:1], ALU.subtract, ALU.mult),
            wait=[(TT, max(0, s - 1)), (VS, None)], inc=VT)
        for c in range(CH):
            P.emit("tensor", lambda e, p=p, c=c: e.transpose(
                psT[p][:, c * 128:(c + 1) * 128], xn[p][:, c * 128:(c + 1) * 128],
                ident[:]),
                wait=([(VT, None), (EVT, max(0, s - 1))] if c == 0 else []),
                inc=(TT if c == CH - 1 else ()))
        P.emit("vector", lambda e, s=s, p=p: e.tensor_copy(
            R.ap().rearrange("p (c t) -> p c t", c=CH, t=RSTR)
             [:, :, 1 + s * 128: 1 + (s + 1) * 128],
            psT[p].rearrange("p (c t) -> p c t", c=CH, t=128)),
            wait=[(TT, None)] + ([(P.ch("p0G"), 1)] if s == 0 else []), inc=EVT)


def emit_ph1(P, io, after=()):
    R = io.sbuf("R", (128, RCOLS), BF16)
    wq_h = io.din("l0Wq", (D, D), F32)
    wk_h = io.din("l0Wk", (D, D), F32)
    wv_h = io.din("l0Wv", (D, D), F32)
    wo_h = io.din("l0Wo", (D, D), F32)
    bo_h = io.din("l0bo", (128, CH), F32)
    esel_h = io.din("l0esel", (16, 1024), BF16)
    attn0 = io.dout("attn0", (D, NTOK), BF16)

    bo = io.sbuf("p1_bo", (128, CH), F32)
    esel = io.sbuf("p1_esel", (16, 1024), BF16)

    kbf = [io.sbuf(f"p1_kbf{p}", (128, 1024), BF16) for p in range(2)]
    vaug = [io.sbuf(f"p1_vaug{p}", (128, 1040), BF16) for p in range(2)]
    mt = io.sbuf("p1_mt", (128, 1024), BF16)
    et = io.sbuf("p1_et", (128, 1024), BF16)
    kvsb = io.sbuf("p1_kvsb", (64, 1040), BF16)
    kvacc = io.sbuf("p1_kvacc", (64, 1040), F32)
    ksbd = io.sbuf("p1_ksbd", (128, 128), BF16)
    kvbd = io.sbuf("p1_kvbd", (128, 1024), BF16)
    qbf = io.sbuf("p1_qbf", (128, 4096), BF16)
    obf = io.sbuf("p1_obf", (128, 4096), BF16)
    zs = io.sbuf("p1_zs", (16, 512), F32)
    zib = io.sbuf("p1_zib", (16, 512), BF16)
    zbsb = io.sbuf("p1_zbsb", (128, 512), BF16)
    att = [io.sbuf("p1_att0", (128, CH * 512), BF16)] * 2
    wq = io.sbuf("p1_wq", (128, CH * 1024), BF16)
    wk = io.sbuf("p1_wk", (128, CH * 1024), BF16)
    wv = io.sbuf("p1_wv", (128, CH * 1024), BF16)
    wo = io.sbuf("p1_wo", (128, CH * 1024), BF16)

    A1 = io.psum("p1_A1", (128, 2048), F32)
    A2 = io.psum("p1_A2", (128, 2048), F32)
    ps_k = A1[:, 0:1024]
    ps_v = A1[:, 1024:2048]
    ps_kv = [A2[0:64, 128 * h: 128 * h + 65] for h in range(16)]
    ps_q = [A1[:, 0:512], A1[:, 512:1024]]
    ps_o = [A1[:, 1024:1536], A1[:, 1536:2048]]
    ps_z = A2[0:16, 0:512]
    ps_zb = A2[:, 512:1024]
    ps_w = [A2[:, 1024:1536], A2[:, 1536:2048]]

    wring = DmaRing(P, "Lw", k=6, eng="gpsimd")
    oring = DmaRing(P, "Lo", k=2, eng="sync")
    PEk = P.ch("p1PEk")
    PEv = P.ch("p1PEv")
    PEkv = P.ch("p1PEkv")
    KVA = P.ch("p1KVA")
    VEk = P.ch("p1VEk")
    AEv = P.ch("p1AEv")
    MN = P.ch("p1MN")
    EX = P.ch("qAEx")
    VS = P.ch("qVS")
    BD = P.ch("p1BD")
    PEq = P.ch("p1PEq")
    VEq = P.ch("p1VEq")
    PEz = P.ch("p1PEz")
    VEz = P.ch("p1VEz")
    PEo = P.ch("p1PEo")
    PEzb = P.ch("p1PEzb")
    VEo = P.ch("p1VEo")
    VEzb = P.ch("p1VEzb")
    PEw = P.ch("p1PEw")
    AEw = P.ch("p1AEw")
    ONE = P.ch("p1ONE")
    C = P.ch("p1C")

    wtoks = {}
    for name, hbm, sb in (("wk", wk_h, wk), ("wv", wv_h, wv), ("wq", wq_h, wq),
                          ("wo", wo_h, wo)):
        wtoks[name] = wring.dma(lambda e, hbm=hbm, sb=sb: e.dma_start(
            sb[:], hbm.rearrange("(c p) n -> p c n", p=128)))
    P.emit("sync", lambda e: e.dma_start(bo[:], bo_h), wait=list(after),
           inc=C, inc_amt=16)
    P.emit("sync", lambda e: e.dma_start(esel[:], esel_h), wait=list(after),
           inc=C, inc_amt=16)
    for p in range(2):
        P.emit("gpsimd", lambda e, p=p: e.memset(
            vaug[p].ap().rearrange("p (h e) -> p h e", h=16, e=65)[:, :, 64:65], 1.0),
            wait=(list(after) if p == 0 else []), inc=ONE)

    nsub = NTOK // 128
    for s in range(nsub):
        p = s % 2
        for half in range(2):
            for c in range(CH):
                P.emit("tensor", lambda e, s=s, c=c, half=half: e.matmul(
                    ps_k[:, half * 512:(half + 1) * 512], _r_slice(R, c, s * 128, 128),
                    wk[:, c * 1024 + half * 512: c * 1024 + (half + 1) * 512],
                    start=(c == 0), stop=(c == CH - 1)),
                    wait=(([wtoks["wk"], (VEk, max(0, VEk.n - 1))]
                           + (list(after) if s == 0 else []))
                          if (c == 0 and half == 0) else []),
                    inc=(PEk if (c, half) == (CH - 1, 1) else ()))
        for half in range(2):
            for c in range(CH):
                P.emit("tensor", lambda e, s=s, c=c, half=half: e.matmul(
                    ps_v[:, half * 512:(half + 1) * 512], _r_slice(R, c, s * 128, 128),
                    wv[:, c * 1024 + half * 512: c * 1024 + (half + 1) * 512],
                    start=(c == 0), stop=(c == CH - 1)),
                    wait=([wtoks["wv"], (AEv, max(0, AEv.n - 1))]
                          if (c == 0 and half == 0) else []),
                    inc=(PEv if (c, half) == (CH - 1, 1) else ()))
        P.emit("vector", lambda e: e.tensor_scalar_min(mt[:], ps_k, 0.0),
               wait=[(PEk, None)], inc=MN)
        P.emit("scalar", lambda e: e.activation(et[:], mt[:], AF.Exp),
               wait=[(MN, None)], inc=EX)
        P.emit("vector", lambda e, p=p: e.scalar_tensor_tensor(
            kbf[p][:], ps_k, 0.0, et[:], ALU.max, ALU.add),
            wait=[(EX, None), (PEkv, max(0, PEkv.n - 1))], inc=VEk)
        P.emit("scalar", lambda e, p=p: e.copy(
            vaug[p].ap().rearrange("p (h e) -> p h e", h=16, e=65)[:, :, 0:64],
            ps_v.rearrange("p (h e) -> p h e", h=16, e=64)),
            wait=[(PEv, None), (ONE, 2), (PEkv, max(0, PEkv.n - 1))], inc=AEv)
        for h in range(16):
            P.emit("tensor", lambda e, p=p, h=h: e.matmul(
                ps_kv[h], kbf[p][:, h * 64:(h + 1) * 64],
                vaug[p][:, h * 65:(h + 1) * 65],
                start=True, stop=True),
                wait=([(VEk, None), (AEv, None), (KVA, None)] if h == 0 else []),
                inc=(PEkv if h == 15 else ()))
        psv = A2.ap()[0:64, :].rearrange("p (h e) -> p h e", h=16, e=128)[:, :, 0:65]
        if s == 0:
            P.emit("vector", lambda e, psv=psv: e.tensor_copy(
                kvacc.ap().rearrange("p (h e) -> p h e", h=16, e=65), psv),
                wait=[(PEkv, None)], inc=KVA)
        else:
            P.emit("vector", lambda e, psv=psv: e.tensor_add(
                kvacc.ap().rearrange("p (h e) -> p h e", h=16, e=65),
                kvacc.ap().rearrange("p (h e) -> p h e", h=16, e=65), psv),
                wait=[(PEkv, None)], inc=KVA)

    P.emit("vector", lambda e: e.tensor_copy(kvsb[:], kvacc[:]),
           wait=[(KVA, None)], inc=VS)
    P.emit("vector", lambda e: e.memset(ksbd[:], 0.0), wait=[(VS, None)], inc=VS)
    P.emit("vector", lambda e: e.memset(kvbd[:], 0.0), wait=[(VS, None)], inc=VS)
    for h in range(16):
        c, half = h // 2, h % 2
        P.emit("vector", lambda e, h=h, c=c, half=half: e.tensor_copy(
            ksbd.ap()[half * 64: half * 64 + 64, c * 16 + h: c * 16 + h + 1],
            kvsb[0:64, h * 65 + 64: h * 65 + 65]),
            wait=[(VS, None)], inc=VS)
    for m in range(CH):
        for half in range(2):
            h = 2 * m + half
            P.emit("vector", lambda e, m=m, half=half, h=h: e.tensor_copy(
                kvbd.ap()[half * 64: half * 64 + 64,
                          m * 128 + half * 64: m * 128 + half * 64 + 64],
                kvsb[0:64, h * 65: h * 65 + 64]),
                wait=[(VS, None)], inc=BD if (m, half) == (CH - 1, 1) else VS)

    ntile = NTOK // 512
    atoks = []
    for t in range(ntile):
        for m in range(CH):
            pq = m % 2
            for c in range(CH):
                P.emit("tensor", lambda e, t=t, m=m, c=c, pq=pq: e.matmul(
                    ps_q[pq], wq[:, c * 1024 + m * 128: c * 1024 + (m + 1) * 128],
                    _r_slice(R, c, t * 512, 512), start=(c == 0), stop=(c == CH - 1)),
                    wait=(([wtoks["wq"], (VEq, max(0, VEq.n - 1))]
                           + ([(VEk, None)] if (t, m) == (0, 0) else []))
                          if c == 0 else []),
                    inc=(PEq if c == CH - 1 else ()))
            P.emit("vector", lambda e, pq=pq: e.tensor_scalar_min(
                mt[:, 0:512], ps_q[pq], 0.0), wait=[(PEq, None), (VEq, None)], inc=MN)
            P.emit("scalar", lambda e: e.activation(et[:, 0:512], mt[:, 0:512], AF.Exp),
                   wait=[(MN, None)], inc=EX)
            P.emit("vector", lambda e, m=m, pq=pq: e.scalar_tensor_tensor(
                qbf[:, m * 512:(m + 1) * 512], ps_q[pq], 0.0, et[:, 0:512],
                ALU.max, ALU.add),
                wait=[(EX, None)] + ([(PEo, None)] if m == 0 else []), inc=VEq)
        for c in range(CH):
            P.emit("tensor", lambda e, c=c: e.matmul(
                ps_z, ksbd[:, c * 16:(c + 1) * 16], qbf[:, c * 512:(c + 1) * 512],
                start=(c == 0), stop=(c == CH - 1)),
                wait=([(BD, None), (VEq, None), (VEz, None)]
                      if c == 0 else []),
                inc=(PEz if c == CH - 1 else ()))
        P.emit("vector", lambda e: e.tensor_scalar_add(zs[:], ps_z, 1e-6),
               wait=[(PEz, None), (VEz, None)], inc=VS)
        P.emit("vector", lambda e: e.reciprocal(zs[:], zs[:]),
               wait=[(VS, None)], inc=VS)
        P.emit("vector", lambda e: e.tensor_copy(zib[:], zs[:]),
               wait=[(VS, None)], inc=VEz)
        for m in range(CH):
            P.emit("tensor", lambda e, m=m: e.matmul(
                ps_zb, esel[:, m * 128:(m + 1) * 128], zib[:],
                start=True, stop=True),
                wait=[(VEz, None), (VEzb, None), (C, 32)]
                     + ([(BD, None)] if m == 0 else []),
                inc=PEzb)
            P.emit("tensor", lambda e, m=m, t=t: e.matmul(
                ps_o[m % 2], kvbd[:, m * 128:(m + 1) * 128],
                qbf[:, m * 512:(m + 1) * 512], start=True, stop=True),
                wait=[(VEo, max(0, VEo.n - 1))]
                     + ([(AEv, None)] if (t, m) == (0, 0) else []),
                inc=PEo)
            P.emit("vector", lambda e: e.tensor_copy(zbsb[:], ps_zb),
                   wait=[(PEzb, None), (VS, None)], inc=VEzb)
            P.emit("vector", lambda e, m=m: e.tensor_mul(
                obf[:, m * 512:(m + 1) * 512], ps_o[m % 2], zbsb[:]),
                wait=[(PEo, None), (VEzb, None)]
                     + ([(PEw, None)] if m == 0 else []), inc=VEo)
        for mo in range(CH):
            pw = mo % 2
            for c in range(CH):
                P.emit("tensor", lambda e, mo=mo, c=c, pw=pw: e.matmul(
                    ps_w[pw], wo[:, c * 1024 + mo * 128: c * 1024 + (mo + 1) * 128],
                    obf[:, c * 512:(c + 1) * 512], start=(c == 0), stop=(c == CH - 1)),
                    wait=([wtoks["wo"], (VEo, None), (AEw, max(0, AEw.n - 1))]
                          + ([(BD, None)] if (t, mo) == (0, 0) else [])
                          if c == 0 else []),
                    inc=(PEw if c == CH - 1 else ()))
            P.emit("vector", lambda e, mo=mo, pw=pw, t=t: e.tensor_scalar_add(
                att[t % 2][:, mo * 512:(mo + 1) * 512], ps_w[pw], bo[:, mo:mo + 1]),
                wait=[(PEw, None), (C, 32)]
                     + ([atoks[t - 1]] if (t >= 1 and mo == 0) else []),
                inc=AEw)
        atoks.append(oring.dma(lambda e, t=t: e.dma_start(
            attn0.rearrange("(c p) n -> p c n", p=128)[:, :, t * 512:(t + 1) * 512],
            att[t % 2].ap().rearrange("p (c n) -> p c n", c=CH)),
            wait=[(AEw, None)]))
    P.emit("sync", lambda e: e.nop(), wait=oring.all_done())
    return oring.all_done() + [(AEw, AEw.n)]


import os
P2_NOPREV = os.environ.get("P2_NOPREV", "0") == "1"
P2_NOM0 = os.environ.get("P2_NOM0", "0") == "1"
P2_NOPREVMM = os.environ.get("P2_NOPREVMM", "0") == "1"
P2_NARROW = os.environ.get("P2_NARROW", "0") == "1"
P2_NOMID = os.environ.get("P2_NOMID", "0") == "1"
HEADS2 = 8
W1P = 5632
W2K = 2816


def emit_ph2_layer(P, io, l, after=(), sweeps=("A", "F"), dbg=None, astop=None):
    nw = NTOK // 512
    R = io.sbuf("R", (128, RCOLS), BF16)
    ident = io.sbuf("ident", (128, 128), BF16)
    pre = "q"

    wqkv_h = io.din(f"f_wqkv{l}", (D, 1536), F32)
    wo_h = io.din(f"f_wo{l}", (512, D), F32)
    w1p_h = io.din(f"f_w1p{l}", (D, W1P), F32)
    w2p_h = io.din(f"f_w2p{l}", (W2K, D), F32)
    b1a_h = io.din(f"f_b1a{l}", (128, 22), F32)
    b1g_h = io.din(f"f_b1g{l}", (128, 22), F32)
    lng_h = io.din(f"f_lng{l}", (128, CH), F32)
    lnb_h = io.din(f"f_lnb{l}", (128, CH), F32)
    b2_h = io.din(f"f_b2{l}", (128, CH), F32)
    qs_h = io.din(f"f_qs{l}", (1, 64), F32)
    ks_h = io.din(f"f_ks{l}", (1, 64), F32)
    tabs_h = {nm: io.din(f"tab_{nm}", (128, 2048), BF16)
              for nm in ("csq", "ssq", "csk", "ssk")}
    triu_h = io.din("tri_u", (128, 128), BF16)
    tril_h = io.din("tri_l", (128, 128), BF16)
    m640_h = io.din("mask640", (128, 640), BF16)
    ones8_h = io.din("ones8", (1, 128), BF16)
    ones1_h = io.din("ones1", (1, 128), BF16)
    onescol_h = io.din("onescol", (128, 1), BF16)
    w1bf = io.dint(f"w1bf{l}", (D, W1P), BF16)
    w2bf = io.dint(f"w2bf{l}", (W2K, D), BF16)

    tabs = {nm: io.sbuf(f"p2_{nm}", (128, 2048), BF16)
            for nm in ("csq", "ssq", "csk", "ssk")}
    m640 = io.sbuf("p2_m640", (128, 640), BF16)
    ones8 = io.sbuf("p2_ones8", (1, 128), BF16)
    ones1 = io.sbuf("p2_ones1", (1, 128), BF16)
    onescol = io.sbuf("p2_onescol", (128, 1), BF16)
    uni = io.sbuf("p2_wqkv", (128, CH * 1536), BF16)
    wqkv = uni[:, 0:12288]
    wosb = io.sbuf("p2_wo", (128, 4 * 1024), BF16)
    scr = io.sbuf("p2_scr", (128, 20480), BF16)
    q_fm = [scr[:, 0:2048], scr[:, 2048:4096]]
    k_fm = [scr[:, 4096:6144], scr[:, 6144:8192], scr[:, 8192:10240]]
    v_tm = [scr[:, 10240:12288], scr[:, 12288:14336], scr[:, 14336:16384]]
    o_bf = [scr[:, 16384:18432], scr[:, 18432:20480]]
    h3 = scr[:, 0:11264]
    h2n = uni[:, 0:4096]
    w1t = [uni[:, 4096:6144], uni[:, 6144:8192]]
    w2t = [uni[:, 8192:9216], uni[:, 9216:10240]]
    sqb = [io.sbuf(f"p2_sqb{i}", (128, 512), BF16) for i in range(2)]
    qs64 = io.sbuf("p2_qs64", (1, 64), F32)
    ks64 = io.sbuf("p2_ks64", (1, 64), F32)
    qs64b = io.sbuf("p2_qs64b", (1, 64), BF16)
    ks64b = io.sbuf("p2_ks64b", (1, 64), BF16)
    qsbc = io.sbuf("p2_qsbc", (128, 64), BF16)
    ksbc = io.sbuf("p2_ksbc", (128, 64), BF16)
    sq = [io.sbuf(f"p2_sq{i}", (128, 512), BF16) for i in range(2)]
    nrm2 = [io.sbuf(f"p2_nrm2{i}", (128, 8), F32) for i in range(2)]
    rn = [io.sbuf(f"p2_rn{i}", (128, 8), F32) for i in range(2)]
    qn = [io.sbuf(f"p2_qn{i}", (128, 512), F32) for i in range(2)]
    tmp = [io.sbuf(f"p2_tmp{i}", (128, 512), F32) for i in range(2)]
    t1f = [io.sbuf(f"p2_t1{i}", (128, 512), F32) for i in range(2)]
    qtm = [io.sbuf(f"p2_qtm{i}", (128, 512), BF16) for i in range(2)]
    s_sb = [io.sbuf(f"p2_ssb{i}", (128, 640), F32) for i in range(2)]
    p_bf = [io.sbuf(f"p2_pbf{i}", (128, 640), BF16) for i in range(2)]
    pT_sb = [io.sbuf(f"p2_pT{i}", (128, 640), BF16) for i in range(2)]
    mxc = [io.sbuf(f"p2_mx{i}", (128, 1), F32) for i in range(2)]
    dnc = [io.sbuf(f"p2_dn{i}", (128, 1), F32) for i in range(2)]
    b1a = io.sbuf("p2_b1a", (128, 22), F32)
    b1g = io.sbuf("p2_b1g", (128, 22), F32)
    lng = io.sbuf("p2_lng", (128, CH), F32)
    lnb = io.sbuf("p2_lnb", (128, CH), F32)
    b2c = io.sbuf("p2_b2", (128, CH), F32)
    mrow = io.sbuf("p2_mrow", (1, 512), F32)
    qrow = io.sbuf("p2_qrow", (1, 512), F32)
    vrow = io.sbuf("p2_vrow", (1, 512), F32)
    rrow = io.sbuf("p2_rrow", (1, 512), F32)
    mrrow = io.sbuf("p2_mrrow", (1, 512), F32)
    rrow_bf = io.sbuf("p2_rrowbf", (1, 512), BF16)
    mrrow_bf = io.sbuf("p2_mrrowbf", (1, 512), BF16)
    gg = [uni[:, 10240:10752], uni[:, 10752:11264]]
    nt1 = t1f[0]

    A1 = io.psum("p1_A1", (128, 2048), F32)
    A2 = io.psum("p1_A2", (128, 2048), F32)
    ps_pj = A1[:, 0:512]
    ps_T = A1.ap()[:, 1536:2048].bitcast(BF16)
    ps_S = A2[:, 0:640]
    ps_pT = A2.ap()[:, 1024:1536].bitcast(BF16)
    ps_oh = A2[0:64, 1536:2048]
    ps_w = [A1[:, 512:1024], A1[:, 1024:1536]]
    ps_sr = A2[0:1, 0:512]
    ps_qr = A2[0:1, 512:1024]
    ps_rbc = A2[:, 1024:1536]
    ps_mbc = A2[:, 1536:2048]
    ps_a = A1[:, 0:512]
    ps_g = A1[:, 512:1024]
    accs = [A1[:, 512 * i:512 * (i + 1)] for i in range(4)] + \
           [A2[:, 512 * i:512 * (i + 1)] for i in range(4)]

    ch = lambda nm: P.ch(pre + nm)
    VS = ch("VS")
    TBF = ch("TBF")
    SQA = ch("SQA");  RED = ch("RED");  LNE = ch("LNE");  APL = ch("APL")
    SWP = ch("SWP");  SXT = ch("SXT");  ADD = ch("ADD");  PET = ch("PET")
    TRD = ch("TRD");  QPF = ch("QPF");  KPF = ch("KPF");  VPF = ch("VPF")
    QKR = ch("QKR");  KVR = ch("KVR")
    SMM = ch("SMM");  VSm = ch("VSm");  AEx = ch("AEx")
    PTS = ch("PTS");  TPT = ch("TPT");  VPT = ch("VPT");  PEo = ch("PEo")
    VEo = ch("VEo");  PEw = ch("PEw");  RA = ch("RA")
    AEF = ch("AEF");  QRM = ch("QRM");  RWC = ch("RWC")
    PEb = ch("PEb");  VEn = ch("VEn");  PEa = ch("PEa");  AEg = ch("AEg")
    VEh = ch("VEh");  PEf = ch("PEf");  RAF = ch("RAF");  C2 = ch("C2")
    W2M = ch("W2M");  LNX = ch("LNX")
    QKP = ch("QKP")

    wring = DmaRing(P, "Lw", k=6, eng="gpsimd")
    cring = DmaRing(P, "Ls", k=4, eng="sync")
    fring = DmaRing(P, "Lf", k=4, eng="sync")

    tok_wqkv = wring.dma(lambda e: e.dma_start(
        wqkv[:], wqkv_h.rearrange("(c p) n -> p c n", p=128)),
        wait=[(QKP, None)] + list(after))
    tok_wo = wring.dma(lambda e: e.dma_start(
        wosb[:], wo_h.rearrange("(c p) n -> p c n", p=128)),
        wait=[(PEw, None)] + list(after))
    w1c_toks = [wring.dma(lambda e, j=j: e.dma_start(
        w1bf[:, j * 1408:(j + 1) * 1408], w1p_h[:, j * 1408:(j + 1) * 1408]))
        for j in range(4)]
    w2c_toks = [wring.dma(lambda e, j=j: e.dma_start(
        w2bf[j * 1408:(j + 1) * 1408, :], w2p_h[j * 1408:(j + 1) * 1408, :]))
        for j in range(2)]
    for hbm, sb in ((b1a_h, b1a), (b1g_h, b1g), (lng_h, lng), (lnb_h, lnb),
                    (b2_h, b2c), (qs_h, qs64), (ks_h, ks64),
                    (m640_h, m640), (ones8_h, ones8),
                    (ones1_h, ones1), (onescol_h, onescol)):
        P.emit("sync", lambda e, hbm=hbm, sb=sb: e.dma_start(sb[:], hbm),
               wait=[(RAF, None), (VSm, None)] + list(after), inc=C2, inc_amt=16)
    tab_toks = {nm: cring.dma(lambda e, nm=nm: e.dma_start(
        tabs[nm][:], tabs_h[nm]), wait=[(ADD, None)] + list(after))
        for nm in ("csq", "ssq", "csk", "ssk")}
    C2f = C2.n
    tbf_t = TBF.n + 1

    P.emit("vector", lambda e: e.tensor_copy(qs64b[:], qs64[:]),
           wait=[(C2, C2f)] + list(after), inc=VS)
    P.emit("vector", lambda e: e.tensor_copy(ks64b[:], ks64[:]), inc=VS)
    P.emit("tensor", lambda e: e.matmul(A2[:, 0:64], ones8[:], qs64b[:],
                                        start=True, stop=True),
           wait=[(VS, None)] + list(after), inc=QKP)
    P.emit("tensor", lambda e: e.matmul(A2[:, 64:128], ones1[:], ks64b[:],
                                        start=True, stop=True), inc=QKP)
    P.emit("vector", lambda e: e.tensor_copy(qsbc[:], A2[:, 0:64]),
           wait=[(QKP, None)], inc=VS)
    P.emit("vector", lambda e: e.tensor_copy(ksbc[:], A2[:, 64:128]), inc=VS)
    for nm, bc in (("csq", qsbc), ("ssq", qsbc), ("csk", ksbc), ("ssk", ksbc)):
        P.emit("vector", lambda e, nm=nm, bc=bc: e.tensor_tensor(
            tabs[nm].ap().rearrange("p (s f) -> p s f", s=32),
            tabs[nm].ap().rearrange("p (s f) -> p s f", s=32),
            bc.ap().rearrange("p (o f) -> p o f", o=1).broadcast_to([128, 32, 64]),
            ALU.mult),
            wait=[(VS, None), tab_toks[nm]],
            inc=(TBF if nm == "ssk" else VS))

    ecall = [0]
    bQPF = QPF.n
    cic = [0]

    def _qk_epilogue(nm, dpar, sub, gsub, guard_smm):
        i = ecall[0] % 2
        ecall[0] += 1
        ps = ps_pj
        cs = tabs["csq" if nm == "q" else "csk"]
        ssx = tabs["ssq" if nm == "q" else "ssk"]
        P.emit("scalar", lambda e, ps=ps, i=i: e.activation(
            sq[i][:], ps, AF.Square),
            wait=[(QKP, None), (RED, max(0, RED.n - 1)), (TBF, tbf_t)], inc=SQA)
        for h in range(8):
            P.emit("vector", lambda e, h=h, i=i: e.reduce_sum(
                nrm2[i][:, h:h + 1], sq[i][:, h * 64:(h + 1) * 64], axis=AX.X),
                wait=([(SQA, None), (LNE, max(0, LNE.n - 1))] if h == 0 else []),
                inc=(VS if h == 7 else ()))
        P.emit("vector", lambda e, i=i: e.tensor_scalar_max(
            nrm2[i][:], nrm2[i][:], 1e-24), wait=[(VS, None)], inc=RED)
        P.emit("scalar", lambda e, i=i: e.activation(rn[i][:], nrm2[i][:], AF.Ln),
               wait=[(RED, None), (APL, max(0, APL.n - 1))], inc=LNX)
        P.emit("scalar", lambda e, i=i: e.activation(
            rn[i][:], rn[i][:], AF.Exp, scale=-0.5),
            wait=[(LNX, None)], inc=LNE)
        for h in range(8):
            P.emit("vector", lambda e, h=h, ps=ps, i=i: e.tensor_scalar_mul(
                qn[i][:, h * 64:(h + 1) * 64], ps[:, h * 64:(h + 1) * 64],
                rn[i][:, h:h + 1]),
                wait=([(LNE, None), (SWP, max(0, SWP.n - 1))] if h == 0 else []),
                inc=(APL if h == 7 else VS))
        P.emit("vector", lambda e: e.engine_nop(), wait=[(APL, None)], inc=QPF)
        P.emit("gpsimd", lambda e, i=i: e.tensor_copy(
            tmp[i].ap().rearrange("p (h a j) -> p h a j", h=8, a=2, j=32),
            qn[i].ap().rearrange("p (h a j) -> p h a j", h=8, a=2, j=32)[:, :, ::-1, :]),
            wait=[(APL, None), (SXT, max(0, SXT.n - 1))], inc=SWP)
        P.emit("vector", lambda e, cs=cs, gsub=gsub, i=i: e.tensor_tensor(
            t1f[i].ap().rearrange("p (h f) -> p h f", h=8),
            qn[i].ap().rearrange("p (h f) -> p h f", h=8),
            cs.ap()[:, gsub * 64:gsub * 64 + 64]
              .rearrange("p (o f) -> p o f", o=1).broadcast_to([128, 8, 64]),
            ALU.mult), wait=[(APL, None)], inc=VS)
        P.emit("vector", lambda e, ssx=ssx, gsub=gsub, i=i: e.tensor_tensor(
            tmp[i].ap().rearrange("p (h f) -> p h f", h=8),
            tmp[i].ap().rearrange("p (h f) -> p h f", h=8),
            ssx.ap()[:, gsub * 64:gsub * 64 + 64]
              .rearrange("p (o f) -> p o f", o=1).broadcast_to([128, 8, 64]),
            ALU.mult), wait=[(SWP, None)], inc=SXT)
        P.emit("vector", lambda e, i=i: e.tensor_add(
            qtm[i][:], t1f[i][:], tmp[i][:]),
            wait=[(SXT, None), (PET, max(0, PET.n - 1))], inc=ADD)
        for j in range(4):
            P.emit("tensor", lambda e, j=j, i=i: e.transpose(
                ps_T[:, j * 128:(j + 1) * 128], qtm[i][:, j * 128:(j + 1) * 128],
                ident[:]),
                wait=([(ADD, None), (TRD, None)] if j == 0 else []),
                inc=(PET if j == 3 else ()))
        dst = q_fm[dpar] if nm == "q" else k_fm[dpar]
        P.emit("vector", lambda e, dst=dst, sub=sub: e.tensor_copy(
            dst.rearrange("p (hc t) -> p hc t", hc=4)[:, :, sub * 128:(sub + 1) * 128],
            ps_T[:, 0:512].rearrange("p (hc t) -> p hc t", hc=4)),
            wait=[(PET, None)] + ([(SMM, guard_smm)] if guard_smm else []),
            inc=TRD)

    def emit_chunk(w, sub, nm, guard):
        _S("q")
        ci = cic[0]
        cic[0] += 1
        t0 = w * 512 + sub * 128
        gsub = (t0 // 128) % 32
        col0 = {"q": 0, "k": 512, "v": 1024}[nm]
        for c in range(CH):
            P.emit("tensor", lambda e, c=c, t0=t0, col0=col0: e.matmul(
                ps_pj, _r_shift_slice(R, c, t0, 128),
                wqkv[:, c * 1536 + col0: c * 1536 + col0 + 512],
                start=(c == 0), stop=(c == CH - 1)),
                wait=([tok_wqkv, (TBF, tbf_t), (QPF, bQPF + ci)]
                      if c == 0 else []),
                inc=(QKP if c == CH - 1 else ()))
        if nm == "v":
            P.emit("vector", lambda e, w=w, sub=sub: e.tensor_copy(
                v_tm[w % 3][:, sub * 512:(sub + 1) * 512], ps_pj),
                wait=[(QKP, None)]
                     + ([(PEo, bPEo + guard)] if guard else []),
                inc=QPF)
        else:
            _qk_epilogue(nm, (w % 2) if nm == "q" else (w % 3), sub, gsub,
                         bSMM + guard if guard else 0)

    def attn_window(w):
        par, parp = w % 2, (w - 1) % 2
        for h in range(HEADS2):
            hc, hb = h // 2, 64 * (h % 2)
            for qc in range(4):
                if P2_NOPREV:
                    w = 0
                hx = (4 * h + qc) % 2
                ssb, pbf, pT, mx, dn = (s_sb[hx], p_bf[hx],
                                        pT_sb[hx], mxc[hx], dnc[hx])
                lo = 0 if w > 0 else 512 - 128 * qc
                if P2_NARROW:
                    lo = 512 - 128 * qc
                first = ([(QKR, None), (KVR, None), (C2, C2f)]
                         if (h, qc) == (0, 0) else [])
                prevw = [(VSm, None)]
                qsl = lambda q_fm=q_fm, hb=hb, hc=hc, qc=qc: q_fm[
                    hb:hb + 64, hc * 512 + qc * 128: hc * 512 + (qc + 1) * 128]
                if w > 0:
                    if P2_NOPREVMM:
                        P.emit("tensor", lambda e, hb=hb, hc=hc, qc=qc, par=par,
                               qsl=qsl: e.matmul(
                            ps_S[:, 0:512 - 128 * qc], qsl(),
                            k_fm[par][hb:hb + 64, hc * 512 + qc * 128: hc * 512 + 512],
                            start=True, stop=True), wait=first + prevw, inc=())
                    else:
                        P.emit("tensor", lambda e, hb=hb, hc=hc, qc=qc, parp=parp,
                               qsl=qsl: e.matmul(
                            ps_S[:, 0:512 - 128 * qc], qsl(),
                            k_fm[parp][hb:hb + 64, hc * 512 + qc * 128: hc * 512 + 512],
                            start=True, stop=True), wait=first + prevw, inc=())
                    first, prevw = [], []
                if 128 * qc > 0:
                    P.emit("tensor", lambda e, hb=hb, hc=hc, qc=qc, par=par,
                           qsl=qsl: e.matmul(
                        ps_S[:, 512 - 128 * qc:512], qsl(),
                        k_fm[par][hb:hb + 64, hc * 512: hc * 512 + 128 * qc],
                        start=True, stop=True), wait=first + prevw, inc=())
                    first, prevw = [], []
                P.emit("tensor", lambda e, hb=hb, hc=hc, qc=qc, par=par,
                       qsl=qsl: e.matmul(
                    ps_S[:, 512:640], qsl(),
                    k_fm[par][hb:hb + 64, hc * 512 + qc * 128: hc * 512 + (qc + 1) * 128],
                    start=True, stop=True), wait=first + prevw, inc=SMM)
                P.emit("vector", lambda e, ssb=ssb, lo=lo: e.tensor_tensor(
                    ssb[:, lo:640], ps_S[:, lo:640], m640[:, lo:640], ALU.add),
                    wait=[(SMM, None), (AEx, max(0, AEx.n - 1)),
                          (PTS, max(0, PTS.n - 1))], inc=VSm)
                P.emit("vector", lambda e, ssb=ssb, mx=mx, lo=lo: e.reduce_max(
                    mx[:, 0:1], ssb[:, lo:640], axis=AX.X),
                    wait=[(VSm, None)], inc=VS)
                P.emit("vector", lambda e, mx=mx: e.tensor_scalar_mul(
                    mx[:, 0:1], mx[:, 0:1], -1.0), wait=[(VS, None)], inc=VS)
                P.emit("scalar", lambda e, ssb=ssb, mx=mx, lo=lo: e.activation(
                    ssb[:, lo:640], ssb[:, lo:640], AF.Exp, bias=mx[:, 0:1]),
                    wait=[(VS, None), (PTS, max(0, PTS.n - 1))], inc=AEx)
                P.emit("vector", lambda e, ssb=ssb, dn=dn, lo=lo: e.reduce_sum(
                    dn[:, 0:1], ssb[:, lo:640], axis=AX.X),
                    wait=[(AEx, None), (PTS, max(0, PTS.n - 1))], inc=VS)
                P.emit("vector", lambda e, dn=dn: e.reciprocal(dn[:, 0:1], dn[:, 0:1]),
                       wait=[(VS, None)], inc=VS)
                P.emit("scalar", lambda e, ssb=ssb, pbf=pbf, dn=dn, lo=lo:
                       e.mul(
                    pbf[:, lo:640], ssb[:, lo:640], dn[:, 0:1]),
                    wait=[(VS, None), (TPT, max(0, TPT.n - 1))], inc=PTS)
                jlo = lo // 128
                if astop == "sm":
                    continue
                for j in range(jlo, 5):
                    P.emit("tensor", lambda e, pbf=pbf, j=j: e.transpose(
                        ps_pT[:, j * 128:(j + 1) * 128],
                        pbf[:, j * 128:(j + 1) * 128], ident[:]),
                        wait=([(PTS, None), (VPT, None)] if j == jlo else []),
                        inc=(TPT if j == 4 else ()))
                P.emit("vector", lambda e, pT=pT, jlo=jlo: e.tensor_copy(
                    pT[:, jlo * 128:640], ps_pT[:, jlo * 128:640]),
                    wait=[(TPT, None), (PEo, max(0, PEo.n - 1))], inc=VPT)
                if astop == "pt":
                    continue
                for j in range(jlo, 5):
                    kabs = qc + j
                    pv = parp if kabs < 4 else par
                    sb_ = kabs % 4
                    P.emit("tensor", lambda e, pv=pv, sb_=sb_, h=h, pT=pT, j=j,
                           qc=qc, jlo=jlo: e.matmul(
                        ps_oh[:, qc * 128:(qc + 1) * 128],
                        v_tm[pv][:, sb_ * 512 + h * 64: sb_ * 512 + (h + 1) * 64],
                        pT[:, j * 128:(j + 1) * 128],
                        start=(j == jlo), stop=(j == 4)),
                        wait=([(VPT, None)] + ([(VEo, None)] if qc == 0 else [])
                              if j == jlo else []),
                        inc=(PEo if j == 4 else ()))
            if astop is None:
                P.emit("vector", lambda e, hc=hc, hb=hb: e.tensor_copy(
                    o_bf[hb:hb + 64, hc * 512:(hc + 1) * 512], ps_oh[:, 0:512]),
                    wait=[(PEo, None), (PEw, None)], inc=VEo)

    def wo_window(w):
        for mo in range(CH):
            pw = mo % 2
            for kc in range(4):
                P.emit("tensor", lambda e, mo=mo, kc=kc, pw=pw: e.matmul(
                    ps_w[pw], wosb[:, kc * 1024 + mo * 128: kc * 1024 + (mo + 1) * 128],
                    o_bf[:, kc * 512:(kc + 1) * 512],
                    start=(kc == 0), stop=(kc == 3)),
                    wait=([tok_wo, (VEo, None), (QPF, None), (KPF, None),
                           (RA, max(0, RA.n - 1))] if kc == 0 else []),
                    inc=(PEw if kc == 3 else ()))
            P.emit("vector", lambda e, mo=mo, pw=pw, w=w: e.tensor_add(
                _r_slice(R, mo, w * 512, 512), _r_slice(R, mo, w * 512, 512),
                ps_w[pw]),
                wait=[(PEw, None)], inc=RA)

    _sc0 = P.scope

    def _S(suffix):
        if _sc0:
            P.scope = _sc0 + suffix

    amode = [s for s in sweeps if s in ("A", "QA", "Q")]
    amode = amode[0] if amode else None
    NI = nw * 32
    bSMM = SMM.n; bVSm = VSm.n; bAEx = AEx.n; bPTS = PTS.n; bTPT = TPT.n
    bVPT = VPT.n; bPEo = PEo.n; bVEo = VEo.n; bPEw = PEw.n; bRA = RA.n
    bQKR = QKR.n; bKVR = KVR.n

    def _dec(i):
        w = nw - 1 - i // 32
        h = (i % 32) // 4
        qc = i % 4
        return w, h // 2, 64 * (h % 2), qc, i % 2, (0 if w > 0 else 512 - 128 * qc)

    def st_scores(i):
        _S("a")
        w, hc, hb, qc, hx, lo = _dec(i)
        wi = i // 32
        par, parp = w % 3, (w - 1) % 3
        qf = q_fm[w % 2]
        first = [(VSm, bVSm + i)]
        if i % 32 == 0:
            first += [(QKR, bQKR + wi + 1), (KVR, bKVR + wi + 1)]
        if i == 0:
            first += [(C2, C2f)]
        qsl = lambda qf=qf, hb=hb, hc=hc, qc=qc: qf[
            hb:hb + 64, hc * 512 + qc * 128: hc * 512 + (qc + 1) * 128]
        if w > 0:
            P.emit("tensor", lambda e, hb=hb, hc=hc, qc=qc, parp=parp,
                   qsl=qsl: e.matmul(
                ps_S[:, 0:512 - 128 * qc], qsl(),
                k_fm[parp][hb:hb + 64, hc * 512 + qc * 128: hc * 512 + 512],
                start=True, stop=True), wait=first, inc=())
            first = []
        if qc > 0:
            P.emit("tensor", lambda e, hb=hb, hc=hc, qc=qc, par=par,
                   qsl=qsl: e.matmul(
                ps_S[:, 512 - 128 * qc:512], qsl(),
                k_fm[par][hb:hb + 64, hc * 512: hc * 512 + 128 * qc],
                start=True, stop=True), wait=first, inc=())
            first = []
        P.emit("tensor", lambda e, hb=hb, hc=hc, qc=qc, par=par,
               qsl=qsl: e.matmul(
            ps_S[:, 512:640], qsl(),
            k_fm[par][hb:hb + 64, hc * 512 + qc * 128: hc * 512 + (qc + 1) * 128],
            start=True, stop=True), wait=first, inc=SMM)

    def st_sm1(i):
        _S("a")
        w, hc, hb, qc, hx, lo = _dec(i)
        ssb, mx = s_sb[hx], mxc[hx]
        P.emit("vector", lambda e, ssb=ssb, lo=lo: e.tensor_tensor(
            ssb[:, lo:640], ps_S[:, lo:640], m640[:, lo:640], ALU.add),
            wait=[(SMM, bSMM + i + 1), (PTS, bPTS + max(0, i - 1))], inc=VSm)
        P.emit("vector", lambda e, ssb=ssb, mx=mx, lo=lo: e.reduce_max(
            mx[:, 0:1], ssb[:, lo:640], axis=AX.X), wait=[(VSm, None)], inc=VS)

    def st_exp(i):
        w, hc, hb, qc, hx, lo = _dec(i)
        ssb, mx = s_sb[hx], mxc[hx]
        P.emit("scalar", lambda e, mx=mx: e.mul(mx[:, 0:1], mx[:, 0:1], -1.0),
               wait=[(VS, None)], inc=())
        P.emit("scalar", lambda e, ssb=ssb, mx=mx, lo=lo: e.activation(
            ssb[:, lo:640], ssb[:, lo:640], AF.Exp, bias=mx[:, 0:1]),
            inc=AEx)

    def st_sm2(i):
        w, hc, hb, qc, hx, lo = _dec(i)
        ssb, dn = s_sb[hx], dnc[hx]
        P.emit("vector", lambda e, ssb=ssb, dn=dn, lo=lo: e.reduce_sum(
            dn[:, 0:1], ssb[:, lo:640], axis=AX.X),
            wait=[(AEx, bAEx + i + 1)], inc=VS)
        P.emit("vector", lambda e, dn=dn: e.reciprocal(dn[:, 0:1], dn[:, 0:1]),
               wait=[(VS, None)], inc=VS)

    def st_mul(i):
        w, hc, hb, qc, hx, lo = _dec(i)
        ssb, pbf, dn = s_sb[hx], p_bf[hx], dnc[hx]
        P.emit("scalar", lambda e, ssb=ssb, pbf=pbf, dn=dn, lo=lo: e.mul(
            pbf[:, lo:640], ssb[:, lo:640], dn[:, 0:1]),
            wait=[(VS, None), (TPT, bTPT + max(0, i - 1))], inc=PTS)

    def st_tp(i):
        _S("a")
        w, hc, hb, qc, hx, lo = _dec(i)
        pbf = p_bf[hx]
        jlo = lo // 128
        for j in range(jlo, 5):
            P.emit("tensor", lambda e, pbf=pbf, j=j: e.transpose(
                ps_pT[:, j * 128:(j + 1) * 128],
                pbf[:, j * 128:(j + 1) * 128], ident[:]),
                wait=([(PTS, bPTS + i + 1), (VPT, bVPT + i)] if j == jlo else []),
                inc=(TPT if j == 4 else ()))

    def st_pc(i):
        w, hc, hb, qc, hx, lo = _dec(i)
        pT = pT_sb[hx]
        jlo = lo // 128
        P.emit("vector", lambda e, pT=pT, jlo=jlo: e.tensor_copy(
            pT[:, jlo * 128:640], ps_pT[:, jlo * 128:640]),
            wait=[(TPT, bTPT + i + 1), (PEo, bPEo + max(0, i - 1))], inc=VPT)

    def st_pv(i):
        _S("a")
        w, hc, hb, qc, hx, lo = _dec(i)
        par, parp = w % 3, (w - 1) % 3
        pT = pT_sb[hx]
        h = (i % 32) // 4
        jlo = lo // 128
        for j in range(jlo, 5):
            kabs = qc + j
            pv = parp if kabs < 4 else par
            sb_ = kabs % 4
            P.emit("tensor", lambda e, pv=pv, sb_=sb_, h=h, pT=pT, j=j,
                   qc=qc: e.matmul(
                ps_oh[:, qc * 128:(qc + 1) * 128],
                v_tm[pv][:, sb_ * 512 + h * 64: sb_ * 512 + (h + 1) * 64],
                pT[:, j * 128:(j + 1) * 128],
                start=(j == jlo), stop=(j == 4)),
                wait=([(VPT, bVPT + i + 1), (VEo, bVEo + i // 4)]
                      if j == jlo else []),
                inc=(PEo if j == 4 else ()))

    def st_ev(i):
        w, hc, hb, qc, hx, lo = _dec(i)
        wi = i // 32
        P.emit("vector", lambda e, w=w, hc=hc, hb=hb: e.tensor_copy(
            o_bf[w % 2][hb:hb + 64, hc * 512:(hc + 1) * 512], ps_oh[:, 0:512]),
            wait=[(PEo, bPEo + i + 1)]
                 + ([(PEw, bPEw + 8 * max(0, wi - 1))] if amode == "A" else []),
            inc=VEo)

    def st_wo(w, mo):
        _S("w")
        wi = nw - 1 - w
        pw = mo % 2
        for kc in range(4):
            P.emit("tensor", lambda e, w=w, mo=mo, kc=kc, pw=pw: e.matmul(
                ps_w[pw], wosb[:, kc * 1024 + mo * 128: kc * 1024 + (mo + 1) * 128],
                o_bf[w % 2][:, kc * 512:(kc + 1) * 512],
                start=(kc == 0), stop=(kc == 3)),
                wait=([tok_wo, (VEo, bVEo + 8 * (wi + 1)),
                       (RA, bRA + 8 * wi + max(0, mo - 1))] if kc == 0 else []),
                inc=(PEw if kc == 3 else ()))
        P.emit("vector", lambda e, w=w, mo=mo, pw=pw: e.tensor_add(
            _r_slice(R, mo, w * 512, 512), _r_slice(R, mo, w * 512, 512),
            ps_w[pw]),
            wait=[(PEw, bPEw + 8 * wi + mo + 1)], inc=RA)

    if amode:
        for sub in range(4):
            for nm in ("q", "k", "v"):
                emit_chunk(nw - 1, sub, nm, 0)
        P.emit("vector", lambda e: e.engine_nop(), wait=[(TRD, None)], inc=QKR)
        if nw >= 2:
            for sub in range(4):
                for nm in ("k", "v"):
                    emit_chunk(nw - 2, sub, nm, 0)
        P.emit("vector", lambda e: e.engine_nop(),
               wait=[(TRD, None), (QPF, None)], inc=KVR)

    def _stretch_chunks(wi):
        w = nw - 1 - wi
        cl = []
        if w - 1 >= 0:
            for sub in range(4):
                cl.append(("c", w - 1, sub, "q"))
            cl.append(("qnop",))
        if w - 2 >= 0:
            for sub in range(4):
                cl.append(("c", w - 2, sub, "k"))
                cl.append(("c", w - 2, sub, "v"))
        if w - 1 >= 0:
            cl.append(("knop",))
        return cl

    def _emit_item(it, wi):
        if it[0] == "c":
            emit_chunk(it[1], it[2], it[3], wi * 32)
        elif it[0] == "qnop":
            P.emit("vector", lambda e: e.engine_nop(),
                   wait=[(TRD, None)], inc=QKR)
        else:
            P.emit("vector", lambda e: e.engine_nop(),
                   wait=[(TRD, None), (QPF, None)], inc=KVR)

    if amode == "Q":
        for wi in range(nw):
            for it in _stretch_chunks(wi):
                _emit_item(it, wi)
    elif amode in ("A", "QA"):
        sched = {}
        for cyc in range(NI + 13):
            wi = cyc // 32
            if cyc < NI and cyc % 32 == 0:
                cl = _stretch_chunks(wi)
                for j, it in enumerate(cl):
                    sched.setdefault(cyc + 3 + (j * 27) // max(1, len(cl)),
                                     []).append((it, wi))
            if 0 <= cyc - 3 < NI:
                st_tp(cyc - 3)
            if 0 <= cyc - 4 < NI:
                st_pv(cyc - 4)
            if 0 <= cyc - 1 < NI:
                st_sm1(cyc - 1)
                st_exp(cyc - 1)
            if 0 <= cyc - 2 < NI:
                st_sm2(cyc - 2)
                st_mul(cyc - 2)
            if 0 <= cyc - 3 < NI:
                st_pc(cyc - 3)
            if 0 <= cyc - 4 < NI and (cyc - 4) % 4 == 3:
                st_ev(cyc - 4)
            if amode == "A" and cyc >= 36 and 0 <= (cyc % 32) - 4 < 8:
                wprev = nw - cyc // 32
                if 0 <= wprev < nw:
                    st_wo(wprev, (cyc % 32) - 4)
            if cyc < NI:
                st_scores(cyc)
            for it, swi in sched.pop(cyc, []):
                _emit_item(it, swi)
        for it_list in sched.values():
            for it, swi in it_list:
                _emit_item(it, swi)

    for sti, st in enumerate(range(nw - 1, -1, -1) if "F" in sweeps else []):
        t0 = st * 512
        _S("n")
        for c in range(CH):
            P.emit("scalar", lambda e, c=c, t0=t0: e.activation(
                sqb[c % 2][:], _r_shift_slice(R, c, t0, 512), AF.Square),
                wait=[(RA, None), (QRM, max(0, QRM.n - 1))]
                     + (list(after) if (sti == 0 and c == 0) else []), inc=AEF)
            P.emit("tensor", lambda e, c=c, t0=t0: e.matmul(
                ps_sr, onescol[:], _r_shift_slice(R, c, t0, 512),
                start=(c == 0), stop=(c == CH - 1)),
                wait=([(C2, C2f), (RWC, None), (RAF, None), (RA, None),
                       (VEh, None)] if c == 0 else []), inc=())
            P.emit("tensor", lambda e, c=c: e.matmul(
                ps_qr, onescol[:], sqb[c % 2][:],
                start=(c == 0), stop=(c == CH - 1)),
                wait=[(AEF, None)], inc=QRM)
        P.emit("vector", lambda e: e.tensor_scalar_mul(mrow[:], ps_sr, 1.0 / D),
               wait=[(QRM, None)], inc=VS)
        P.emit("vector", lambda e: e.tensor_scalar_mul(qrow[:], ps_qr, 1.0 / D),
               inc=VS)
        P.emit("vector", lambda e: e.tensor_mul(vrow[:], mrow[:], mrow[:]),
               wait=[(VS, None)], inc=VS)
        P.emit("vector", lambda e: e.tensor_sub(vrow[:], qrow[:], vrow[:]),
               wait=[(VS, None)], inc=VS)
        P.emit("scalar", lambda e: e.activation(rrow[:], vrow[:], AF.Ln, bias=1e-5),
               wait=[(VS, None)], inc=LNX)
        P.emit("scalar", lambda e: e.activation(rrow[:], rrow[:], AF.Exp, scale=-0.5),
               wait=[(LNX, None)], inc=AEF)
        P.emit("vector", lambda e: e.tensor_mul(mrrow[:], mrow[:], rrow[:]),
               wait=[(AEF, None)], inc=VS)
        P.emit("vector", lambda e: e.tensor_copy(rrow_bf[:], rrow[:]), inc=VS)
        P.emit("vector", lambda e: e.tensor_copy(mrrow_bf[:], mrrow[:]),
               wait=[(VS, None)], inc=RWC)
        P.emit("tensor", lambda e: e.matmul(ps_rbc, ones1[:], rrow_bf[:],
                                            start=True, stop=True),
               wait=[(RWC, None), (VEn, None), (PEf, None), (RAF, None)], inc=())
        P.emit("tensor", lambda e: e.matmul(ps_mbc, ones1[:], mrrow_bf[:],
                                            start=True, stop=True), inc=PEb)
        for c in range(CH):
            P.emit("vector", lambda e, c=c, t0=t0: e.tensor_mul(
                nt1[:], _r_shift_slice(R, c, t0, 512), ps_rbc),
                wait=[(PEb, None), (VEn, None)]
                     + ([(PEa, None)] if c == 0 else []), inc=VS)
            P.emit("vector", lambda e: e.tensor_sub(nt1[:], nt1[:], ps_mbc),
                   wait=[(VS, None)], inc=VS)
            P.emit("vector", lambda e, c=c: e.tensor_scalar(
                h2n[:, c * 512:(c + 1) * 512], nt1[:], lng[:, c:c + 1],
                lnb[:, c:c + 1], ALU.mult, ALU.add),
                wait=[(VS, None)], inc=VEn)
        if dbg == "h2n" and st == 0:
            P.emit("sync", lambda e: e.dma_start(
                io.dout("dbg_h2n", (128, 4096), BF16), h2n[:]),
                wait=[(VEn, None)], inc=P.ch("DBG"), inc_amt=16)
        _S("1")
        for m in range(22):
            pw1 = m % 2
            tok = fring.dma(lambda e, m=m, pw1=pw1: e.dma_start(
                w1t[pw1][:], w1bf.rearrange("(c p) n -> p c n", p=128)
                [:, :, m * 256:(m + 1) * 256]),
                wait=[(PEa, max(0, PEa.n - 1))] + (w1c_toks if sti == 0 else [])
                     + ([(RA, None)] if (sti == 0 and m < 2) else []))
            for c in range(CH):
                P.emit("tensor", lambda e, c=c, pw1=pw1: e.matmul(
                    ps_a, w1t[pw1][:, c * 256: c * 256 + 128],
                    h2n[:, c * 512:(c + 1) * 512],
                    start=(c == 0), stop=(c == CH - 1)),
                    wait=([tok, (VEn, None), (VEh, None), (RWC, None),
                           (RAF, None)] if c == 0 else []), inc=())
            for c in range(CH):
                P.emit("tensor", lambda e, c=c, pw1=pw1: e.matmul(
                    ps_g, w1t[pw1][:, c * 256 + 128: c * 256 + 256],
                    h2n[:, c * 512:(c + 1) * 512],
                    start=(c == 0), stop=(c == CH - 1)),
                    wait=[(AEg, None)] if c == 0 else [],
                    inc=(PEa if c == CH - 1 else ()))
            P.emit("scalar", lambda e, m=m: e.activation(
                gg[m % 2][:], ps_g, AF.Gelu, bias=b1g[:, m:m + 1]),
                wait=[(PEa, None), (VEh, max(0, VEh.n - 1))], inc=AEg)
            P.emit("vector", lambda e, m=m: e.scalar_tensor_tensor(
                h3[:, m * 512:(m + 1) * 512], ps_a, b1a[:, m:m + 1], gg[m % 2][:],
                ALU.add, ALU.mult),
                wait=[(AEg, None)], inc=VEh)
        if dbg == "h3" and st == 0:
            P.emit("sync", lambda e: e.dma_start(
                io.dout("dbg_h3", (128, 11264), BF16), h3.rearrange("p n -> p n")),
                wait=[(VEh, None)], inc=P.ch("DBG"), inc_amt=16)
        _S("2")
        pef_marks = {}
        for kk in range(22):
            tok2 = fring.dma(lambda e, kk=kk: e.dma_start(
                w2t[kk % 2][:], w2bf[128 * kk:128 * (kk + 1), :]),
                wait=[(W2M, max(0, W2M.n - 1))]
                     + ([(PEf, None)] if kk == 0 else [])
                     + (w2c_toks if sti == 0 else [])
                     + ([(RA, None)] if (sti == 0 and kk < 2) else []))
            for mo in range(CH):
                P.emit("tensor", lambda e, mo=mo, kk=kk: e.matmul(
                    accs[mo], w2t[kk % 2][:, mo * 128:(mo + 1) * 128],
                    h3[:, kk * 512:(kk + 1) * 512],
                    start=(kk == 0), stop=(kk == 21)),
                    wait=(([tok2, (VEh, None), (AEg, None), (RAF, None),
                            (RWC, None), (VEn, None)] if kk == 0 else [tok2])
                          if mo == 0 else []),
                    inc=((PEf if kk == 21 else W2M) if mo == CH - 1 else ()))
            if kk == 21:
                for mo in range(CH):
                    pef_marks[mo] = PEf.n
        for mo in range(CH):
            P.emit("vector", lambda e, mo=mo, t0=t0: e.scalar_tensor_tensor(
                _r_slice(R, mo, t0, 512), accs[mo], b2c[:, mo:mo + 1],
                _r_slice(R, mo, t0, 512), ALU.add, ALU.add),
                wait=[(PEf, pef_marks[mo])], inc=RAF)
    return [(RAF, RAF.n)] if "F" in sweeps else [(RA, RA.n)]


def emit_ph3(P, io, after=()):
    R = io.sbuf("R", (128, RCOLS), BF16)
    x_t = io.din("x_t", (D, NTOK), F32)
    attn0 = io.dram["attn0"] if "attn0" in io.dram else \
        io.din("attn0_in", (D, NTOK), BF16)
    x1s = io.dout("x1s", (D, NTOK), F32)
    onescol = io.sbuf("p2_onescol", (128, 1), BF16)
    ones1 = io.sbuf("p2_ones1", (1, 128), BF16)

    xs2 = [io.sbuf(f"p3_xs{i}", (128, 4096), F32) for i in range(2)]
    as2 = [io.sbuf(f"p3_as{i}", (128, 4096), BF16) for i in range(2)]
    x1bf2 = [io.sbuf(f"p3_x1bf{i}", (128, 4096), BF16) for i in range(2)]
    sqbf2 = [io.sbuf(f"p3_sqbf{i}", (128, 4096), BF16) for i in range(2)]
    mrow = io.sbuf("p2_mrow", (1, 512), F32)
    qrow = io.sbuf("p2_qrow", (1, 512), F32)
    vrow = io.sbuf("p2_vrow", (1, 512), F32)
    rrow = io.sbuf("p2_rrow", (1, 512), F32)
    mrrow = io.sbuf("p2_mrrow", (1, 512), F32)
    rrow_bf = io.sbuf("p2_rrowbf", (1, 512), BF16)
    mrrow_bf = io.sbuf("p2_mrrowbf", (1, 512), BF16)
    nt1 = io.sbuf("p3_nt1", (128, 512), F32)

    A2 = io.psum("p1_A2", (128, 2048), F32)
    ps_sr = A2[0:1, 0:512]
    ps_qr = A2[0:1, 512:1024]
    ps_rbc = A2[:, 1024:1536]
    ps_mbc = A2[:, 1536:2048]

    ch = lambda nm: P.ch("q" + nm)
    VS, AEF, QRM, RWC, PEb, VEn, LNX = (ch(x) for x in
        ("VS", "AEF", "QRM", "RWC", "PEb", "VEn", "LNX"))
    C2 = ch("C2")
    XAD = P.ch("p3XAD")
    XBF = P.ch("p3XBF")
    ring = DmaRing(P, "Ls", k=4, eng="sync")
    oring = DmaRing(P, "Lo", k=2, eng="sync")
    otoks = []

    ntile = NTOK // 512
    bVEn = P.ch("qVEn").n
    bXAD = XAD.n
    bQRM = QRM.n
    for t in range(ntile):
        t0 = t * 512
        tx = t % 2
        xs, as_, x1bf, sqbf = xs2[tx], as2[tx], x1bf2[tx], sqbf2[tx]
        tok_x = ring.dma(lambda e, xs=xs, t0=t0: e.dma_start(
            xs[:], x_t.rearrange("(c p) n -> p c n", p=128)[:, :, t0:t0 + 512]),
            wait=([(VEn, bVEn + 8 * (t - 1))] if t >= 2 else [])
                 + ([otoks[t - 2]] if t >= 2 else [])
                 + (list(after) if t == 0 else []))
        tok_a = ring.dma(lambda e, as_=as_, t0=t0: e.dma_start(
            as_[:], attn0.rearrange("(c p) n -> p c n", p=128)[:, :, t0:t0 + 512]),
            wait=([(XAD, bXAD + t - 1)] if t >= 2 else []))
        P.emit("vector", lambda e, xs=xs, as_=as_: e.tensor_add(
            xs[:], xs[:], as_[:]),
            wait=[tok_x, tok_a] + (list(after) if t == 0 else []), inc=VS)
        P.emit("vector", lambda e, xs=xs, t0=t0: e.tensor_add(
            xs.rearrange("p (c n) -> p c n", c=CH),
            xs.rearrange("p (c n) -> p c n", c=CH),
            R.ap().rearrange("p (c n) -> p c n", c=CH, n=RSTR)[:, :, 1 + t0:1 + t0 + 512]),
            wait=[(VS, None)], inc=XAD)
        otoks.append(oring.dma(lambda e, xs=xs, t0=t0: e.dma_start(
            x1s.rearrange("(c p) n -> p c n", p=128)[:, :, t0:t0 + 512], xs[:]),
            wait=[(XAD, None)]))
        P.emit("scalar", lambda e, xs=xs, x1bf=x1bf: e.copy(x1bf[:], xs[:]),
               wait=[(XAD, None), (QRM, bQRM + 8 * max(0, t - 1))], inc=XBF)
        P.emit("scalar", lambda e, xs=xs, sqbf=sqbf: e.activation(
            sqbf[:], xs[:], AF.Square), inc=XBF)
        for c in range(CH):
            P.emit("tensor", lambda e, c=c, x1bf=x1bf: e.matmul(
                ps_sr, onescol[:], x1bf[:, c * 512:(c + 1) * 512],
                start=(c == 0), stop=(c == CH - 1)),
                wait=([(XBF, None), (RWC, None), (C2, None)]
                      if c == 0 else []),
                inc=())
            P.emit("tensor", lambda e, c=c, sqbf=sqbf: e.matmul(
                ps_qr, onescol[:], sqbf[:, c * 512:(c + 1) * 512],
                start=(c == 0), stop=(c == CH - 1)), inc=QRM)
        P.emit("vector", lambda e: e.tensor_scalar_mul(mrow[:], ps_sr, 1.0 / D),
               wait=[(QRM, None)], inc=VS)
        P.emit("vector", lambda e: e.tensor_scalar_mul(qrow[:], ps_qr, 1.0 / D),
               inc=VS)
        P.emit("vector", lambda e: e.tensor_mul(vrow[:], mrow[:], mrow[:]),
               wait=[(VS, None)], inc=VS)
        P.emit("vector", lambda e: e.tensor_sub(vrow[:], qrow[:], vrow[:]),
               wait=[(VS, None)], inc=VS)
        P.emit("scalar", lambda e: e.activation(rrow[:], vrow[:], AF.Ln, bias=1e-5),
               wait=[(VS, None)], inc=LNX)
        P.emit("scalar", lambda e: e.activation(rrow[:], rrow[:], AF.Exp, scale=-0.5),
               wait=[(LNX, None)], inc=AEF)
        P.emit("vector", lambda e: e.tensor_mul(mrrow[:], mrow[:], rrow[:]),
               wait=[(AEF, None)], inc=VS)
        P.emit("vector", lambda e: e.tensor_copy(rrow_bf[:], rrow[:]), inc=VS)
        P.emit("vector", lambda e: e.tensor_copy(mrrow_bf[:], mrrow[:]),
               wait=[(VS, None)], inc=RWC)
        P.emit("tensor", lambda e: e.matmul(ps_rbc, ones1[:], rrow_bf[:],
                                            start=True, stop=True),
               wait=[(RWC, None), (VEn, None), (C2, None)], inc=())
        P.emit("tensor", lambda e: e.matmul(ps_mbc, ones1[:], mrrow_bf[:],
                                            start=True, stop=True), inc=PEb)
        for c in range(CH):
            P.emit("vector", lambda e, c=c, xs=xs: e.tensor_mul(
                nt1[:], xs[:, c * 512:(c + 1) * 512], ps_rbc),
                wait=[(PEb, None), (VEn, None)], inc=VS)
            P.emit("vector", lambda e, c=c, t0=t0: e.tensor_sub(
                _r_slice(R, c, t0, 512), nt1[:], ps_mbc),
                wait=[(VS, None)], inc=VEn)
    P.emit("sync", lambda e: e.nop(), wait=oring.all_done() + [(VEn, None)])
    return [(P.ch("qVEn"), P.ch("qVEn").n)] + oring.all_done()


def emit_ph4(P, io, after=()):
    R = io.sbuf("R", (128, RCOLS), BF16)
    ident = io.sbuf("ident", (128, 128), BF16)
    m640 = io.sbuf("p2_m640", (128, 640), BF16)
    ones1 = io.sbuf("p2_ones1", (1, 128), BF16)
    nsel = NTOK // 4
    nseg = NTOK // 2048
    w_inT = io.din("l1winT", (D, 3072), F32)
    binqk = io.din("l1binqk", (128, 16), F32)
    binv = io.din("l1binv", (1, 1024), F32)
    w_outT = io.din("l1woutT", (D, D), F32)
    bout = io.din("l1bout", (128, CH), F32)
    attn1s = io.dout("attn1s", (D, nsel), BF16)

    wqt = [io.sbuf(f"p4_wqt{i}", (128, CH * 128), BF16) for i in range(2)]
    wvt = [io.sbuf(f"p4_wvt{i}", (128, CH * 512), BF16) for i in range(2)]
    wout = io.sbuf("p4_wout", (128, CH * 1024), BF16)
    binqk_s = io.sbuf("p4_binqk", (128, 16), F32)
    binv_s = io.sbuf("p4_binv", (1, 1024), F32)
    binv_bf = io.sbuf("p4_binvbf", (1, 1024), BF16)
    vbc = io.sbuf("p4_vbc", (128, 1024), BF16)
    bout_s = io.sbuf("p4_bout", (128, CH), F32)
    xsel = io.sbuf("p4_xsel", (128, 4096), BF16)
    qbf = io.sbuf("p4_qbf", (128, 4096), BF16)
    kbf = io.sbuf("p4_kbf", (128, 4096), BF16)
    vtm = io.sbuf("p4_vtm", (128, 4096), BF16)
    obf = io.sbuf("p4_obf", (128, 4096), BF16)
    att = io.sbuf("p4_att", (128, 4096), BF16)
    s_sb = [io.sbuf(f"p2_ssb{i}", (128, 640), F32) for i in range(2)]
    p_bf = [io.sbuf(f"p2_pbf{i}", (128, 640), BF16) for i in range(2)]
    pT_sb = [io.sbuf(f"p2_pT{i}", (128, 640), BF16) for i in range(2)]
    mxc = [io.sbuf(f"p2_mx{i}", (128, 1), F32) for i in range(2)]
    dnc = [io.sbuf(f"p2_dn{i}", (128, 1), F32) for i in range(2)]

    A1 = io.psum("p1_A1", (128, 2048), F32)
    A2 = io.psum("p1_A2", (128, 2048), F32)
    ps_q = [A1[:, 0:512], A1[:, 512:1024]]
    ps_v = [A1[:, 1024:1536], A1[:, 1536:2048]]
    ps_S = A2[:, 0:640]
    ps_pT = A2.ap()[:, 1024:1536].bitcast(BF16)
    ps_oh = A2[0:64, 1536:2048]
    ps_w = [A1[:, 0:512], A1[:, 512:1024]]

    ch = lambda nm: P.ch("q" + nm)
    VS = ch("VS")
    SMM = ch("SMM"); VSm = ch("VSm"); AEx = ch("AEx"); PTS = ch("PTS")
    TPT = ch("TPT"); VPT = ch("VPT"); PEo = ch("PEo"); VEo = ch("VEo")
    PEw = ch("PEw"); QKP = ch("QKP"); C2 = ch("C2")
    XSL = P.ch("p4XSL")
    QEV = P.ch("p4QEV")
    AEW = P.ch("p4AEW")
    VBC = P.ch("p4VBC")
    C4 = P.ch("p4C")
    wring = DmaRing(P, "Lw", k=6, eng="gpsimd")
    fring = DmaRing(P, "Lf", k=4, eng="sync")
    oring = DmaRing(P, "Lo", k=2, eng="sync")
    atoks = []
    P4Q = P.ch("p4PQ")
    P4V = P.ch("p4PV")

    tok_wout = wring.dma(lambda e: e.dma_start(
        wout[:], w_outT.rearrange("(c p) n -> p c n", p=128)), wait=list(after))
    winbf = io.dint("winbf", (D, 3072), BF16)
    win_casts = [wring.dma(lambda e, j=j: e.dma_start(
        winbf[j * 512:(j + 1) * 512, :], w_inT[j * 512:(j + 1) * 512, :]),
        wait=list(after)) for j in range(2)]
    for hbm, sb in ((binqk, binqk_s), (binv, binv_s), (bout, bout_s)):
        P.emit("sync", lambda e, hbm=hbm, sb=sb: e.dma_start(sb[:], hbm),
               wait=list(after), inc=C4, inc_amt=16)
    C4f = C4.n
    P.emit("vector", lambda e: e.tensor_copy(binv_bf[:], binv_s[:]),
           wait=[(C4, C4f)], inc=VS)
    P.emit("tensor", lambda e: e.matmul(A2[:, 0:512], ones1[:],
                                        binv_bf[:, 0:512], start=True, stop=True),
           wait=[(VS, None), (C2, None)] + list(after), inc=())
    P.emit("tensor", lambda e: e.matmul(A2[:, 512:1024], ones1[:],
                                        binv_bf[:, 512:1024], start=True, stop=True),
           inc=QKP)
    P.emit("vector", lambda e: e.tensor_copy(vbc[:], A2[:, 0:1024]),
           wait=[(QKP, None)], inc=VBC)

    for s in range(nseg):
        base = s * 2048
        for c in range(CH):
            P.emit("vector", lambda e, c=c, base=base: e.tensor_copy(
                xsel[:, c * 512:(c + 1) * 512],
                R.ap()[:, c * RSTR + 1 + base: c * RSTR + 1 + base + 2048][:, ::4]),
                wait=([(VBC, None)] if (s, c) == (0, 0) else [])
                     + ([(PEw, None)] if c == 0 else []),
                inc=(XSL if c == CH - 1 else ()))
        for qk in range(2):
            for mo in range(CH):
                pq = mo % 2
                tw = fring.dma(lambda e, mo=mo, qk=qk, pq=pq: e.dma_start(
                    wqt[pq][:], winbf.rearrange("(c p) n -> p c n", p=128)
                    [:, :, qk * 1024 + mo * 128: qk * 1024 + (mo + 1) * 128]),
                    wait=[(P4Q, max(0, P4Q.n - 1))]
                         + (win_casts if (s, qk, mo) == (0, 0, 0) else []))
                for c in range(CH):
                    P.emit("tensor", lambda e, c=c, pq=pq: e.matmul(
                        ps_q[pq], wqt[pq][:, c * 128:(c + 1) * 128],
                        xsel[:, c * 512:(c + 1) * 512],
                        start=(c == 0), stop=(c == CH - 1)),
                        wait=([tw, (XSL, None), (QEV, max(0, QEV.n - 1))]
                              if c == 0 else []),
                        inc=(P4Q if c == CH - 1 else ()))
                if qk == 0:
                    P.emit("vector", lambda e, mo=mo, pq=pq: e.tensor_scalar(
                        qbf[:, mo * 512:(mo + 1) * 512], ps_q[pq],
                        binqk_s[:, mo:mo + 1], 0.125, ALU.add, ALU.mult),
                        wait=[(P4Q, None), (C4, C4f)], inc=QEV)
                else:
                    P.emit("vector", lambda e, mo=mo, pq=pq: e.tensor_scalar_add(
                        kbf[:, mo * 512:(mo + 1) * 512], ps_q[pq],
                        binqk_s[:, 8 + mo:8 + mo + 1]),
                        wait=[(P4Q, None)], inc=QEV)
        for half in range(2):
            tv = fring.dma(lambda e, half=half: e.dma_start(
                wvt[half % 2][:], winbf.rearrange("(c p) n -> p c n", p=128)
                [:, :, 2048 + half * 512: 2048 + (half + 1) * 512]),
                wait=[(P4V, max(0, P4V.n - 4))]
                     + (win_casts if half == 0 else []))
            for sub in range(4):
                pv = sub % 2
                for c in range(CH):
                    P.emit("tensor", lambda e, sub=sub, half=half, c=c, pv=pv: e.matmul(
                        ps_v[pv],
                        xsel[:, c * 512 + sub * 128: c * 512 + sub * 128 + 128],
                        wvt[half % 2][:, c * 512:(c + 1) * 512],
                        start=(c == 0), stop=(c == CH - 1)),
                        wait=([tv, (QEV, max(0, QEV.n - 1))] if c == 0 else []),
                        inc=(P4V if c == CH - 1 else ()))
                P.emit("vector", lambda e, sub=sub, half=half, pv=pv: e.tensor_tensor(
                    vtm[:, sub * 1024 + half * 512: sub * 1024 + (half + 1) * 512],
                    ps_v[pv], vbc[:, half * 512:(half + 1) * 512], ALU.add),
                    wait=[(P4V, None), (VBC, 1)], inc=QEV)
        bSMM = SMM.n; bVSm = VSm.n; bAEx = AEx.n; bPTS = PTS.n; bTPT = TPT.n
        bVPT = VPT.n; bPEo = PEo.n; bVEo = VEo.n; bPEw = PEw.n; bAEW = AEW.n
        bQEV = QEV.n

        def _d4(i):
            h = i // 4
            return h // 2, 64 * (h % 2), i % 4, i % 2, 128 * (i % 4 + 1)

        def s4_scores(i, s=s):
            hc, hb, qc, hx, wdt = _d4(i)
            first = [(VSm, bVSm + i)] + ([(QEV, bQEV)] if i == 0 else [])
            P.emit("tensor", lambda e, hb=hb, hc=hc, qc=qc, wdt=wdt: e.matmul(
                ps_S[:, 0:wdt],
                qbf[hb:hb + 64, hc * 512 + qc * 128: hc * 512 + (qc + 1) * 128],
                kbf[hb:hb + 64, hc * 512: hc * 512 + wdt],
                start=True, stop=True),
                wait=first, inc=SMM)

        def s4_sm1(i, s=s):
            hc, hb, qc, hx, wdt = _d4(i)
            ssb, mx = s_sb[hx], mxc[hx]
            P.emit("vector", lambda e, ssb=ssb, qc=qc, wdt=wdt: e.tensor_tensor(
                ssb[:, 0:wdt], ps_S[:, 0:wdt],
                m640[:, 512 - qc * 128:640], ALU.add),
                wait=[(SMM, bSMM + i + 1), (PTS, bPTS + max(0, i - 1))]
                     + ([(C2, None)] if (s, i) == (0, 0) else []), inc=VSm)
            P.emit("vector", lambda e, ssb=ssb, mx=mx, wdt=wdt: e.reduce_max(
                mx[:, 0:1], ssb[:, 0:wdt], axis=AX.X),
                wait=[(VSm, None)], inc=VS)

        def s4_exp(i):
            hc, hb, qc, hx, wdt = _d4(i)
            ssb, mx = s_sb[hx], mxc[hx]
            P.emit("scalar", lambda e, mx=mx: e.mul(
                mx[:, 0:1], mx[:, 0:1], -1.0), wait=[(VS, None)], inc=())
            P.emit("scalar", lambda e, ssb=ssb, mx=mx, wdt=wdt: e.activation(
                ssb[:, 0:wdt], ssb[:, 0:wdt], AF.Exp, bias=mx[:, 0:1]),
                inc=AEx)

        def s4_sm2(i):
            hc, hb, qc, hx, wdt = _d4(i)
            ssb, dn = s_sb[hx], dnc[hx]
            P.emit("vector", lambda e, ssb=ssb, dn=dn, wdt=wdt: e.reduce_sum(
                dn[:, 0:1], ssb[:, 0:wdt], axis=AX.X),
                wait=[(AEx, bAEx + i + 1)], inc=VS)
            P.emit("vector", lambda e, dn=dn: e.reciprocal(dn[:, 0:1], dn[:, 0:1]),
                   wait=[(VS, None)], inc=VS)

        def s4_mul(i):
            hc, hb, qc, hx, wdt = _d4(i)
            ssb, pbf, dn = s_sb[hx], p_bf[hx], dnc[hx]
            P.emit("scalar", lambda e, ssb=ssb, pbf=pbf, dn=dn, wdt=wdt: e.mul(
                pbf[:, 0:wdt], ssb[:, 0:wdt], dn[:, 0:1]),
                wait=[(VS, None), (TPT, bTPT + max(0, i - 1))], inc=PTS)

        def s4_tp(i):
            hc, hb, qc, hx, wdt = _d4(i)
            pbf = p_bf[hx]
            for j in range(qc + 1):
                P.emit("tensor", lambda e, pbf=pbf, j=j: e.transpose(
                    ps_pT[:, j * 128:(j + 1) * 128],
                    pbf[:, j * 128:(j + 1) * 128], ident[:]),
                    wait=([(PTS, bPTS + i + 1), (VPT, bVPT + i)] if j == 0
                          else []),
                    inc=(TPT if j == qc else ()))

        def s4_pc(i):
            hc, hb, qc, hx, wdt = _d4(i)
            pT = pT_sb[hx]
            P.emit("vector", lambda e, pT=pT, wdt=wdt: e.tensor_copy(
                pT[:, 0:wdt], ps_pT[:, 0:wdt]),
                wait=[(TPT, bTPT + i + 1), (PEo, bPEo + max(0, i - 1))],
                inc=VPT)

        def s4_pv(i):
            hc, hb, qc, hx, wdt = _d4(i)
            pT = pT_sb[hx]
            h = i // 4
            for j in range(qc + 1):
                P.emit("tensor", lambda e, j=j, h=h, pT=pT, qc=qc: e.matmul(
                    ps_oh[:, qc * 128:(qc + 1) * 128],
                    vtm[:, j * 1024 + h * 64: j * 1024 + (h + 1) * 64],
                    pT[:, j * 128:(j + 1) * 128],
                    start=(j == 0), stop=(j == qc)),
                    wait=([(VPT, bVPT + i + 1), (VEo, bVEo + i // 4)]
                          if j == 0 else []),
                    inc=(PEo if j == qc else ()))

        def s4_ev(i):
            hc, hb, qc, hx, wdt = _d4(i)
            P.emit("vector", lambda e, hc=hc, hb=hb: e.tensor_copy(
                obf[hb:hb + 64, hc * 512:(hc + 1) * 512], ps_oh[:, 0:512]),
                wait=[(PEo, bPEo + i + 1)]
                     + ([(PEw, bPEw)] if bPEw else []), inc=VEo)

        def s4_wo(mo, s=s):
            pw = mo % 2
            for c in range(CH):
                P.emit("tensor", lambda e, mo=mo, c=c, pw=pw: e.matmul(
                    ps_w[pw], wout[:, c * 1024 + mo * 128: c * 1024 + (mo + 1) * 128],
                    obf[:, c * 512:(c + 1) * 512],
                    start=(c == 0), stop=(c == CH - 1)),
                    wait=([tok_wout, (VEo, bVEo + 16), (QEV, bQEV),
                           (AEW, bAEW + max(0, mo - 1))] if c == 0 else []),
                    inc=(PEw if c == CH - 1 else ()))
            P.emit("vector", lambda e, mo=mo, pw=pw: e.tensor_scalar_add(
                att[:, mo * 512:(mo + 1) * 512], ps_w[pw], bout_s[:, mo:mo + 1]),
                wait=[(PEw, bPEw + 8 * 0 + mo + 1), (C4, C4f)]
                     + ([atoks[s - 1]] if (s >= 1 and mo == 0) else []),
                inc=AEW)

        for cyc in range(64 + 13):
            if 0 <= cyc - 3 < 64:
                s4_tp(cyc - 3)
            if 0 <= cyc - 4 < 64:
                s4_pv(cyc - 4)
            if 0 <= cyc - 1 < 64:
                s4_sm1(cyc - 1)
                s4_exp(cyc - 1)
            if 0 <= cyc - 2 < 64:
                s4_sm2(cyc - 2)
                s4_mul(cyc - 2)
            if 0 <= cyc - 3 < 64:
                s4_pc(cyc - 3)
            if 0 <= cyc - 4 < 64 and (cyc - 4) % 4 == 3:
                s4_ev(cyc - 4)
            if 0 <= cyc - 68 < 8:
                s4_wo(cyc - 68)
            if cyc < 64:
                s4_scores(cyc)
        atoks.append(oring.dma(lambda e, s=s: e.dma_start(
            attn1s.rearrange("(c p) n -> p c n", p=128)[:, :, s * 512:(s + 1) * 512],
            att.ap().rearrange("p (c n) -> p c n", c=CH)),
            wait=[(AEW, None)]))
    P.emit("sync", lambda e: e.nop(),
           wait=oring.all_done() + ([(P.ch("DBG"), P.ch("DBG").n)]
                                    if "DBG" in P.chans else []))
    return oring.all_done()


def emit_ph5(P, io, after=(), attn1_after=()):
    R = io.sbuf("R", (128, RCOLS), BF16)
    nsel = NTOK // 4
    wg_h = io.din("l1wgT", (D, 3072), I32)
    wu_h = io.din("l1wuT", (D, 3072), I32)
    wd_h = io.din("l1wdT", (3072, D), I32)
    x1s = io.dram["x1s"] if "x1s" in io.dram else io.din("x1s_in", (D, NTOK), F32)
    attn1s = io.dram["attn1s"] if "attn1s" in io.dram else \
        io.din("attn1s_in", (D, nsel), BF16)
    y_t = io.dout("y_t", (D, NTOK), F32)
    wgbf = io.dint("wgbf", (D, 3072), BF16)
    wubf = io.dint("wubf", (D, 3072), BF16)
    wdbf = io.dint("wdbf", (3072, D), BF16)

    wgt = [io.sbuf(f"p5_wgt{i}", (128, CH * 128), BF16) for i in range(2)]
    wut = [io.sbuf(f"p5_wut{i}", (128, CH * 128), BF16) for i in range(2)]
    wdt_ = [io.sbuf(f"p5_wdt{i}", (128, 2048), BF16) for i in range(2)]
    hbf = io.sbuf("p5_hbf", (128, 24 * 512), BF16)
    sg = [io.sbuf(f"p5_sg{i}", (128, 512), BF16) for i in range(2)]
    x1t2 = [io.sbuf(f"p5_x1t{i}", (128, CH * 512), F32) for i in range(2)]
    attt2 = [io.sbuf(f"p5_attt{i}", (128, CH * 128), BF16) for i in range(2)]
    yt = io.sbuf("p5_yt", (128, CH * 512), F32)

    A1 = io.psum("p1_A1", (128, 2048), F32)
    A2 = io.psum("p1_A2", (128, 2048), F32)
    ps_g = [A1[:, 0:512], A1[:, 512:1024]]
    ps_u = [A1[:, 1024:1536], A1[:, 1536:2048]]
    accs = [A2[:, 512 * i:512 * (i + 1)] for i in range(4)] + \
           [A1[:, 512 * i:512 * (i + 1)] for i in range(4)]

    PEg = P.ch("p5PEg")
    PEu = P.ch("p5PEu")
    AEs = P.ch("p5AEs")
    VEh = P.ch("p5VEh")
    PEd = P.ch("p5PEd")
    DWM = P.ch("p5DWM")
    VEy = P.ch("p5VEy")
    wring = DmaRing(P, "Lw", k=6, eng="gpsimd")
    fring = DmaRing(P, "Lf", k=4, eng="sync")
    lring = DmaRing(P, "Ls", k=4, eng="sync")
    oring = DmaRing(P, "Lo", k=2, eng="sync")
    ytoks = []

    cast_toks = []
    for srcd, dstd, nr in ((wg_h, wgbf, D), (wu_h, wubf, D), (wd_h, wdbf, 3072)):
        for j in range(2):
            cast_toks.append(wring.dma(
                lambda e, srcd=srcd, dstd=dstd, j=j, nr=nr: e.dma_start(
                    dstd[j * (nr // 2):(j + 1) * (nr // 2), :],
                    srcd[j * (nr // 2):(j + 1) * (nr // 2), :]),
                wait=list(after)))

    ntile = NTOK // 512
    SCALE = 1.0 / 127.0
    bVEy = VEy.n
    xtoks = {}
    atoks5 = {}

    def _pf5(t):
        xtoks[t] = lring.dma(lambda e, t=t: e.dma_start(
            x1t2[t % 2][:],
            x1s.rearrange("(c p) n -> p c n", p=128)[:, :, t * 512:(t + 1) * 512]),
            wait=([(VEy, bVEy + 16 * (t - 2) + 8)] if t >= 2 else [])
                 + (list(after) if t == 0 else []))
        atoks5[t] = lring.dma(lambda e, t=t: e.dma_start(
            attt2[t % 2][:], attn1s.rearrange("(c p) n -> p c n", p=128)
            [:, :, t * 128:(t + 1) * 128]),
            wait=([(VEy, bVEy + 16 * (t - 1))] if t >= 2 else [])
                 + (list(attn1_after) if t <= 1 else []))

    for t in range(ntile):
        t0 = t * 512
        if t == 0:
            _pf5(0)
            _pf5(1) if ntile > 1 else None
        x1t = x1t2[t % 2]
        attt = attt2[t % 2]
        tok_x = xtoks[t]
        tok_a = atoks5[t]
        for mo in range(24):
            pg = mo % 2
            tok_g = fring.dma(lambda e, mo=mo: e.dma_start(
                wgt[mo % 2][:], wgbf.rearrange("(c p) n -> p c n", p=128)
                [:, :, mo * 128:(mo + 1) * 128]),
                wait=[(PEg, max(0, PEg.n - 1))]
                     + (cast_toks if (t, mo) == (0, 0) else []))
            tok_u = fring.dma(lambda e, mo=mo: e.dma_start(
                wut[mo % 2][:], wubf.rearrange("(c p) n -> p c n", p=128)
                [:, :, mo * 128:(mo + 1) * 128]),
                wait=[(PEu, max(0, PEu.n - 1))]
                     + (cast_toks if (t, mo) == (0, 0) else []))
            for c in range(CH):
                P.emit("tensor", lambda e, mo=mo, c=c, pg=pg, t0=t0: e.matmul(
                    ps_g[pg], wgt[mo % 2][:, c * 128:(c + 1) * 128],
                    _r_slice(R, c, t0, 512),
                    start=(c == 0), stop=(c == CH - 1)),
                    wait=([tok_g, (AEs, max(0, AEs.n - 1))]
                          + ([(VEy, bVEy + 16 * t - 8)]
                             if (t > 0 and mo == 0) else [])
                          + (list(after) if (t, mo) == (0, 0) else [])
                          if c == 0 else []),
                    inc=(PEg if c == CH - 1 else ()))
            for c in range(CH):
                P.emit("tensor", lambda e, mo=mo, c=c, pg=pg, t0=t0: e.matmul(
                    ps_u[pg], wut[mo % 2][:, c * 128:(c + 1) * 128],
                    _r_slice(R, c, t0, 512),
                    start=(c == 0), stop=(c == CH - 1)),
                    wait=[tok_u, (VEh, max(0, VEh.n - 1))] if c == 0 else [],
                    inc=(PEu if c == CH - 1 else ()))
            P.emit("scalar", lambda e, pg=pg, mo=mo: e.activation(
                sg[mo % 2][:], ps_g[pg], AF.Silu, scale=SCALE),
                wait=[(PEg, None), (VEh, max(0, VEh.n - 1))], inc=AEs)
            P.emit("vector", lambda e, mo=mo, pg=pg: e.scalar_tensor_tensor(
                hbf[:, mo * 512:(mo + 1) * 512],
                ps_u[pg], SCALE, sg[mo % 2][:], ALU.mult, ALU.mult),
                wait=[(PEu, None), (AEs, None)], inc=VEh)
        if 1 <= t < ntile - 1:
            _pf5(t + 1)
        pdm = {}
        for kk in range(24):
            if kk % 2 == 0:
                tok_d = fring.dma(lambda e, kk=kk: e.dma_start(
                    wdt_[(kk // 2) % 2][:],
                    wdbf.rearrange("(j p) n -> p j n", p=128)
                    [:, kk:kk + 2, :]),
                    wait=[(DWM, max(0, DWM.n - 2))]
                         + ([(PEd, None)] if kk == 0 else [])
                         + (cast_toks if (t, kk) == (0, 0) else []))
            for md in range(CH):
                P.emit("tensor", lambda e, md=md, kk=kk: e.matmul(
                    accs[md],
                    wdt_[(kk // 2) % 2][:, (kk % 2) * 1024 + md * 128:
                                        (kk % 2) * 1024 + (md + 1) * 128],
                    hbf[:, kk * 512:(kk + 1) * 512],
                    start=(kk == 0), stop=(kk == 23)),
                    wait=(([tok_d, (VEh, None), (AEs, None)]
                           + ([(VEy, bVEy + 16 * t - 8)] if t > 0 else [])
                           if kk == 0 else [tok_d]) if md == 0 else []),
                    inc=(PEd if kk == 23 else (DWM if md == CH - 1 else ())))
            if kk == 23:
                for md in range(CH):
                    pdm[md] = PEd.n
        for md in range(CH):
            P.emit("vector", lambda e, md=md, x1t=x1t: e.scalar_tensor_tensor(
                yt[:, md * 512:(md + 1) * 512], accs[md], SCALE,
                x1t[:, md * 512:(md + 1) * 512], ALU.mult, ALU.add),
                wait=[(PEd, pdm[md]), tok_x]
                     + ([ytoks[t - 1]] if (t >= 1 and md == 0) else []),
                inc=VEy)
        for md in range(CH):
            P.emit("vector", lambda e, md=md, attt=attt: e.tensor_add(
                yt.ap()[:, md * 512: (md + 1) * 512][:, ::4],
                yt.ap()[:, md * 512: (md + 1) * 512][:, ::4],
                attt[:, md * 128:(md + 1) * 128]),
                wait=[(VEy, None), tok_a], inc=VEy)
        ytoks.append(oring.dma(lambda e, t0=t0: e.dma_start(
            y_t.rearrange("(c p) n -> p c n", p=128)[:, :, t0:t0 + 512],
            yt.ap().rearrange("p (c n) -> p c n", c=CH)),
            wait=[(VEy, None)]))
    P.emit("sync", lambda e: e.nop(), wait=oring.all_done())
    return oring.all_done()


def emit_load_r(P, io):
    r_in = io.din("r_in", (128, RCOLS), BF16)
    R = io.sbuf("R", (128, RCOLS), BF16)
    LD = P.ch("rLD")
    P.emit("sync", lambda e: e.dma_start(R[:], r_in), inc=LD, inc_amt=16)
    return (LD, 16)


def emit_store_r(P, io, wait_chs):
    r_out = io.dout("r_out", (128, RCOLS), BF16)
    R = io.sbuf("R", (128, RCOLS), BF16)
    SO = P.ch("rSO")
    P.emit("sync", lambda e: e.dma_start(r_out, R[:]),
           wait=[w if isinstance(w, tuple) else (w, None) for w in wait_chs],
           inc=SO, inc_amt=16)
    P.emit("sync", lambda e: e.nop(), wait=[(SO, None)])


def emit_finish(P, io, wait_chs):
    P.emit("sync", lambda e: e.nop(), wait=[(ch, None) for ch in wait_chs])


def set_ntok(n):
    global NTOK
    assert n % 512 == 0
    NTOK = n


def _col_chunks(vec):
    return np.ascontiguousarray(vec.reshape(CH, 128).T.astype(np.float32))


def _host_ph1(inputs):
    esel = np.zeros((16, 1024), bf16)
    for m in range(8):
        for pp in range(128):
            esel[2 * m + (pp >= 64), 128 * m + pp] = 1
    return {
        "l0Wq": np.ascontiguousarray(inputs["L0_attn_Wq"], dtype=np.float32),
        "l0Wk": np.ascontiguousarray(inputs["L0_attn_Wk"], dtype=np.float32),
        "l0Wv": np.ascontiguousarray(inputs["L0_attn_Wv"], dtype=np.float32),
        "l0Wo": np.ascontiguousarray(inputs["L0_attn_Wo"], dtype=np.float32),
        "l0bo": _col_chunks(np.asarray(inputs["L0_attn_bo"])),
        "l0esel": esel,
    }


def _host_ph2(inputs):
    HID = 2730
    w1 = np.asarray(inputs["L0_ffn_w1"], np.float32)
    b1 = np.asarray(inputs["L0_ffn_b1"], np.float32)
    w2 = np.asarray(inputs["L0_ffn_w2"], np.float32)
    out = {}
    w1p = np.zeros((2, D, W1P), np.float32)
    b1a = np.zeros((2, 128, 22), np.float32)
    b1g = np.zeros((2, 128, 22), np.float32)
    for m in range(22):
        na = min(128, HID - 128 * m)
        w1p[:, :, 256 * m:256 * m + na] = w1[:, :, 128 * m:128 * m + na]
        w1p[:, :, 256 * m + 128:256 * m + 128 + na] = \
            w1[:, :, HID + 128 * m:HID + 128 * m + na]
        b1a[:, 0:na, m] = b1[:, 128 * m:128 * m + na]
        b1g[:, 0:na, m] = b1[:, HID + 128 * m:HID + 128 * m + na]
    w2p = np.zeros((2, W2K, D), np.float32)
    w2p[:, 0:HID] = w2
    for l in range(2):
        out[f"f_wqkv{l}"] = np.ascontiguousarray(inputs["L0_ffn_wqkv"][l], dtype=np.float32)
        out[f"f_wo{l}"] = np.ascontiguousarray(inputs["L0_ffn_wo"][l], dtype=np.float32)
        out[f"f_w1p{l}"] = np.ascontiguousarray(w1p[l])
        out[f"f_w2p{l}"] = np.ascontiguousarray(w2p[l])
        out[f"f_b1a{l}"] = np.ascontiguousarray(b1a[l])
        out[f"f_b1g{l}"] = np.ascontiguousarray(b1g[l])
        out[f"f_lng{l}"] = _col_chunks(np.asarray(inputs["L0_ffn_ln_g"][l]))
        out[f"f_lnb{l}"] = _col_chunks(np.asarray(inputs["L0_ffn_ln_b"][l]))
        out[f"f_b2{l}"] = _col_chunks(np.asarray(inputs["L0_ffn_b2"][l]))
        out[f"f_qs{l}"] = np.asarray(inputs["L0_ffn_q_scale"][l], np.float32).reshape(1, 64)
        out[f"f_ks{l}"] = np.asarray(inputs["L0_ffn_k_scale"][l], np.float32).reshape(1, 64)
    n = NTOK
    half = np.arange(0, 64, 2, dtype=np.float64)
    inv_freq = 1.0 / 10000.0 ** (half / 64.0)
    t = np.arange(n, dtype=np.float64)
    freqs = t[:, None] * inv_freq[None, :]
    base = (half + 0.4 * 64.0) / (1.4 * 64.0)
    power = (t - n // 2) / 256.0
    scale = base[None, :] ** power[:, None]
    cos, sin = np.cos(freqs), np.sin(freqs)
    def pack(a):
        return np.ascontiguousarray(
            a.reshape(n // 128, 128, 64).transpose(1, 0, 2)
             .reshape(128, (n // 128) * 64).astype(bf16))
    cs_ = np.concatenate([cos * scale, cos * scale], -1)
    sx_ = np.concatenate([-sin * scale, sin * scale], -1)
    ck_ = np.concatenate([cos / scale, cos / scale], -1)
    sk_ = np.concatenate([-sin / scale, sin / scale], -1)
    zpad = np.zeros((128, 2048), bf16)
    for nm, a in (("csq", cs_), ("ssq", sx_), ("csk", ck_), ("ssk", sk_)):
        p = pack(a)
        z = zpad.copy(); z[:, 0:p.shape[1]] = p
        out[f"tab_{nm}"] = z
    lam = np.arange(128)[:, None]
    jj = np.arange(128)[None, :]
    out["tri_u"] = np.where(jj >= lam, 0.0, -1e30).astype(bf16)
    out["tri_l"] = np.where(jj <= lam, 0.0, -1e30).astype(bf16)
    m640 = np.zeros((128, 640), np.float32)
    m640[:, 0:128] = np.where(jj >= lam, 0.0, -1e30)
    m640[:, 512:640] = np.where(jj <= lam, 0.0, -1e30)
    out["mask640"] = m640.astype(bf16)
    out["ones8"] = np.full((1, 128), 8.0, bf16)
    out["ones1"] = np.ones((1, 128), bf16)
    out["onescol"] = np.ones((128, 1), bf16)
    return out


def _host_ph45(inputs):
    w_in = np.asarray(inputs["L1_attn_w_in"], np.float32)
    b_in = np.asarray(inputs["L1_attn_b_in"], np.float32)
    w_out = np.asarray(inputs["L1_attn_w_out"], np.float32)
    b_out = np.asarray(inputs["L1_attn_b_out"], np.float32)
    return {
        "l1winT": np.ascontiguousarray(w_in.T),
        "l1binqk": np.ascontiguousarray(b_in[:2048].reshape(16, 128).T),
        "l1binv": np.ascontiguousarray(b_in[2048:].reshape(1, 1024)),
        "l1woutT": np.ascontiguousarray(w_out.T),
        "l1bout": _col_chunks(b_out),
        "l1wgT": np.ascontiguousarray(np.asarray(inputs["L1_ffn_w_gate"], np.int32).T),
        "l1wuT": np.ascontiguousarray(np.asarray(inputs["L1_ffn_w_up"], np.int32).T),
        "l1wdT": np.ascontiguousarray(np.asarray(inputs["L1_ffn_w_down"], np.int32).T),
    }


def _host_consts():
    c = {}
    c["identin"] = np.eye(128, dtype=bf16)
    c["constsin"] = np.tile(np.asarray(CONST_VALS, np.float32), (128, 1))
    return c


def _prep_shared(inputs):
    sh = {}
    return sh


def _launch(phases, feed, n_cores=N_CORES, sim=False, trace=False):
    nc = bass.Bass("TRN2", target_bir_lowering=False, debug=False)
    io = IO(nc)
    P = Prog(nc)
    for ph in phases:
        ph(P, io)
    P.build()
    in_names = [n for n, k in io.kinds.items() if k == "in"]
    out_names = [n for n, k in io.kinds.items() if k == "out"]
    in_maps = []
    for core in range(n_cores):
        m = {}
        for n in in_names:
            arr = feed[n] if not isinstance(feed[n], list) else feed[n][core]
            m[n] = arr
        in_maps.append(m)
    if sim:
        from concourse.bass_interp import CoreSim

        class _R:
            pass
        nc.finalize()
        if sim != "race":
            nc.detect_race_conditions = False
        s = CoreSim(nc, require_finite=False)
        for n in in_names:
            s.tensor(n)[:] = in_maps[0][n]
        s.simulate(check_with_hw=False)
        r = _R()
        r.results = [{n: np.copy(s.tensor(n)) for n in out_names}]
        return r
    res = run_bass_kernel_spmd(nc, in_maps, core_ids=list(range(n_cores)),
                               trace=trace)
    return res


def _consts2(P, io):
    C2 = P.ch("qC2")
    for nm, shape in (("mask640", (128, 640)), ("ones8", (1, 128)),
                      ("ones1", (1, 128)), ("onescol", (128, 1))):
        hbm = io.din(nm, shape, BF16)
        sb = io.sbuf("p2_" + ("m640" if nm == "mask640" else nm), shape, BF16)
        P.emit("sync", lambda e, hbm=hbm, sb=sb: e.dma_start(sb[:], hbm),
               inc=C2, inc_amt=16)


def _feed_all(inputs):
    x = np.asarray(inputs["L0_attn_x"], dtype=np.float32)
    feed = {}
    feed["x_tm"] = [np.ascontiguousarray(x[b]) for b in range(N_CORES)]
    feed["x_t"] = [np.ascontiguousarray(x[b].T) for b in range(N_CORES)]
    feed.update(_host_consts())
    feed.update(_host_ph1(inputs))
    feed.update(_host_ph2(inputs))
    feed.update(_host_ph45(inputs))
    return feed


def kernel(**inputs):
    if os.environ.get("K_UNFUSED") == "1":
        return _kernel_unfused(**inputs)
    feed = _feed_all(inputs)
    nc = bass.Bass("TRN2", target_bir_lowering=False, debug=False)
    io = IO(nc, internal={"attn0", "x1s", "attn1s"}, use_arena=True)
    P = Prog(nc)
    emit_consts(P, io)
    emit_ph0(P, io)
    t0 = [(P.ch("p0EVT"), P.ch("p0EVT").n), (P.ch("p0G"), 1)]
    t1 = emit_ph1(P, io, after=t0)
    t2a = emit_ph2_layer(P, io, 0, after=t1)
    t2 = emit_ph2_layer(P, io, 1, after=t2a)
    t3 = emit_ph3(P, io, after=t2)
    t4 = emit_ph4(P, io, after=t3)
    emit_ph5(P, io, after=t4, attn1_after=t4)
    P.build()
    in_names = [n for n, k in io.kinds.items() if k == "in"]
    in_maps = []
    for core in range(N_CORES):
        m = {}
        for n in in_names:
            arr = feed[n] if not isinstance(feed[n], list) else feed[n][core]
            m[n] = arr
        in_maps.append(m)
    res = run_bass_kernel_spmd(nc, in_maps, core_ids=list(range(N_CORES)))
    y = np.stack([np.ascontiguousarray(res.results[b]["y_t"].T)
                  for b in range(N_CORES)])
    return y


def _kernel_unfused(**inputs):
    x = np.asarray(inputs["L0_attn_x"], dtype=np.float32)
    feed = {}
    feed["x_tm"] = [np.ascontiguousarray(x[b]) for b in range(N_CORES)]
    feed["x_t"] = [np.ascontiguousarray(x[b].T) for b in range(N_CORES)]
    feed.update(_host_consts())
    feed.update(_host_ph1(inputs))
    feed.update(_host_ph2(inputs))
    feed.update(_host_ph45(inputs))
    tok = {}

    def loadr(P, io):
        tok["t"] = [K_emit_load_r(P, io)]

    K_emit_load_r = emit_load_r

    def l1_ph0(P, io):
        emit_ph0(P, io)
        emit_store_r(P, io, [P.ch("p0EVT"), P.ch("p0G")])
    res = _launch([emit_consts, l1_ph0], feed)
    r = [res.results[b]["r_out"] for b in range(N_CORES)]

    feed["r_in"] = r
    def l2_ph1(P, io):
        emit_ph1(P, io, after=tok["t"])
    res = _launch([emit_consts, loadr, l2_ph1], feed)
    attn0 = [res.results[b]["attn0"] for b in range(N_CORES)]

    def l3_ph2(P, io):
        t0 = emit_ph2_layer(P, io, 0, after=tok["t"])
        t1 = emit_ph2_layer(P, io, 1, after=t0)
        emit_store_r(P, io, t1)
    res = _launch([emit_consts, loadr, l3_ph2], feed)
    r = [res.results[b]["r_out"] for b in range(N_CORES)]

    feed["r_in"] = r
    feed["attn0_in"] = attn0
    def l4_ph3(P, io):
        t = emit_ph3(P, io, after=tok["t"])
        emit_store_r(P, io, t)
    res = _launch([emit_consts, _consts2, loadr, l4_ph3], feed)
    r = [res.results[b]["r_out"] for b in range(N_CORES)]
    x1s = [res.results[b]["x1s"] for b in range(N_CORES)]

    feed["r_in"] = r
    def l5_ph4(P, io):
        emit_ph4(P, io, after=tok["t"])
    res = _launch([emit_consts, _consts2, loadr, l5_ph4], feed)
    attn1 = [res.results[b]["attn1s"] for b in range(N_CORES)]

    feed["x1s_in"] = x1s
    feed["attn1s_in"] = attn1
    def l6_ph5(P, io):
        emit_ph5(P, io, after=tok["t"], attn1_after=tok["t"])
    res = _launch([emit_consts, loadr, l6_ph5], feed)
    y = np.stack([np.ascontiguousarray(res.results[b]["y_t"].T)
                  for b in range(N_CORES)])
    return y


# version 09
# speedup vs baseline: 1.1054x; 1.0119x over previous
import contextlib
import numpy as np
import ml_dtypes
import concourse.bass as bass
import concourse.mybir as mybir
from concourse.bass_utils import run_bass_kernel_spmd

F32 = mybir.dt.float32
BF16 = mybir.dt.bfloat16
I32 = mybir.dt.int32
AF = mybir.ActivationFunctionType
ALU = mybir.AluOpType
AX = mybir.AxisListType
bf16 = ml_dtypes.bfloat16

N_CORES = 8
NTOK = 4096
D = 1024
CH = 8
RSTR = 4104
RCOLS = CH * RSTR

ENGS = ("sync", "tensor", "vector", "scalar", "gpsimd")


class _Chan:
    __slots__ = ("name", "n", "sem")

    def __init__(self, name):
        self.name = name
        self.n = 0
        self.sem = None


class Prog:

    def __init__(self, nc):
        self.nc = nc
        self.ops = {e: [] for e in ENGS}
        self.chans = {}
        self.scope = None

    def ch(self, name):
        if name not in self.chans:
            self.chans[name] = _Chan(name)
        return self.chans[name]

    def emit(self, eng, fn, wait=(), inc=(), inc_amt=1):
        waits = []
        for ch, v in wait:
            v = ch.n if v is None else v
            assert v <= ch.n, f"wait on future count of {ch.name}: {v} > {ch.n}"
            if v > 0:
                waits.append((ch, v))
        if isinstance(inc, _Chan):
            inc = (inc,)
        incs = []
        for ch in inc:
            ch.n += inc_amt
            incs.append((ch, inc_amt))
        self.ops[eng].append((waits, fn, incs, self.scope))

    def build(self):
        nc = self.nc
        with contextlib.ExitStack() as stack:
            for name in sorted(self.chans):
                ch = self.chans[name]
                ch.sem = stack.enter_context(nc.semaphore("ch_" + name))
            with nc.Block() as block:
                for eng in ENGS:
                    if not self.ops[eng]:
                        continue

                    ck = self.chans.get("CK")

                    def make(eng, ck=ck):
                        def emit_all(e):
                            last = {}
                            cur = [None, None]
                            if ck is not None and eng != "sync":
                                e.wait_ge(ck.sem, ck.n)
                                last = {ck: ck.n}
                            for waits, fn, incs, scope in self.ops[eng]:
                                if scope != cur[0]:
                                    if cur[1] is not None:
                                        nc.leave_named_scope(cur[0], cur[1], False)
                                        cur[1] = None
                                    cur[0] = scope
                                    if scope is not None:
                                        cur[1], _ = nc.enter_named_scope(
                                            scope, False)
                                for ch, v in waits:
                                    if last.get(ch, 0) >= v:
                                        continue
                                    e.wait_ge(ch.sem, v)
                                    last[ch] = v
                                inst = fn(e)
                                for ch, amt in incs:
                                    inst = inst.then_inc(ch.sem, amt)
                            if cur[1] is not None:
                                nc.leave_named_scope(cur[0], cur[1], False)
                            return None
                        return emit_all
                    getattr(block, eng)(make(eng))


class DmaRing:

    def __init__(self, P, name, k=4, eng="sync"):
        self.P, self.k, self.eng = P, k, eng
        self.chs = [P.ch(f"{name}{i}") for i in range(k)]
        self.i = 0

    def dma(self, fn, wait=()):
        ch = self.chs[self.i % self.k]
        w = list(wait)
        if ch.n > 0:
            w.append((ch, ch.n))
        self.P.emit(self.eng, fn, wait=w, inc=ch, inc_amt=16)
        self.i += 1
        return (ch, ch.n)

    def all_done(self):
        return [(ch, ch.n) for ch in self.chs]


class _View:
    __slots__ = ("_ap",)

    def __init__(self, ap):
        self._ap = ap

    def __getitem__(self, k):
        return self._ap[k]

    def ap(self):
        return self._ap

    def rearrange(self, *a, **k):
        return self._ap.rearrange(*a, **k)


ARENA_TAIL = {"p2_mrow", "p2_qrow", "p2_vrow", "p2_rrow", "p2_mrrow",
              "p2_rrowbf", "p2_mrrowbf", "p2_ssb0", "p2_ssb1", "p2_pbf0",
              "p2_pbf1", "p2_pT0", "p2_pT1", "p2_mx0", "p2_mx1",
              "p2_dn0", "p2_dn1"}
ARENA_REAL = {"p2_m640", "p2_ones8", "p2_ones1", "p2_onescol"}
ARENA_BYTES = 141 * 1024
ARENA_TAIL_BYTES = 26 * 1024
_DT_BYTES = {F32: 4, BF16: 2, I32: 4}


class IO:

    def __init__(self, nc, internal=(), use_arena=False):
        self.nc = nc
        self.dram = {}
        self.kinds = {}
        self.sb = {}
        self.internal = set(internal)
        self.use_arena = use_arena
        self.arena = None
        self.abump = {}

    def din(self, name, shape, dtype):
        if name not in self.dram:
            self.dram[name] = self.nc.dram_tensor(name, list(shape), dtype,
                                                  kind="ExternalInput").ap()
            self.kinds[name] = "in"
        return self.dram[name]

    def dout(self, name, shape, dtype):
        if name not in self.dram:
            kind = "Internal" if name in self.internal else "ExternalOutput"
            self.dram[name] = self.nc.dram_tensor(name, list(shape), dtype,
                                                  kind=kind).ap()
            self.kinds[name] = "internal" if name in self.internal else "out"
        return self.dram[name]

    def dint(self, name, shape, dtype):
        if name not in self.dram:
            self.dram[name] = self.nc.dram_tensor(name, list(shape), dtype,
                                                  kind="Internal").ap()
            self.kinds[name] = "internal"
        return self.dram[name]

    def sbuf(self, name, shape, dtype):
        if name in self.sb:
            return self.sb[name]
        arena_ok = (self.use_arena and name[0] == "p" and name[2] == "_"
                    and name not in ARENA_REAL)
        if not arena_ok:
            self.sb[name] = self.nc.alloc_sbuf_tensor(name, list(shape), dtype)
            return self.sb[name]
        if self.arena is None:
            self.arena = self.nc.alloc_sbuf_tensor(
                "ARENA", [128, ARENA_BYTES // 2], BF16)
        region = "tail" if name in ARENA_TAIL else name[:2]
        base = self.abump.get(region)
        if base is None:
            base = (ARENA_BYTES - ARENA_TAIL_BYTES) if region == "tail" else 0
            self.abump[region] = base
        nbytes = shape[1] * _DT_BYTES[dtype]
        nbytes = (nbytes + 63) & ~63
        limit = ARENA_BYTES if region == "tail" else ARENA_BYTES - ARENA_TAIL_BYTES
        assert base + nbytes <= limit, \
            f"arena overflow: {name} region {region} base {base} + {nbytes} > {limit}"
        self.abump[region] = base + nbytes
        col0 = base // 2
        view = self.arena.ap()[:, col0: col0 + shape[1] * _DT_BYTES[dtype] // 2]
        if dtype != BF16:
            view = view.bitcast(dtype)
        if shape[0] != 128:
            view = view[0:shape[0], :]
        self.sb[name] = _View(view)
        return self.sb[name]

    def psum(self, name, shape, dtype=F32):
        if name not in self.sb:
            self.sb[name] = self.nc.alloc_psum_tensor(name, list(shape), dtype)
        return self.sb[name]


def _r_slice(R, c, t0, n):
    return R[:, c * RSTR + 1 + t0: c * RSTR + 1 + t0 + n]


def _r_shift_slice(R, c, t0, n):
    off = 0 if c >= 4 else 1
    return R[:, c * RSTR + off + t0: c * RSTR + off + t0 + n]


CONST_VALS = (1e-5, 0.125, 1.0 / 127.0, 8.0, -1.0, 1e-6, 0.5)


def emit_consts(P, io):
    nc = P.nc
    identin = io.din("identin", (128, 128), BF16)
    ident = io.sbuf("ident", (128, 128), BF16)
    K = P.ch("CK")
    P.emit("sync", lambda e: e.dma_start(ident[:], identin), inc=K, inc_amt=16)
    constsin = io.din("constsin", (128, len(CONST_VALS)), F32)
    ct = io.sbuf("consts", (128, len(CONST_VALS)), F32)
    P.emit("sync", lambda e: e.dma_start(ct[:], constsin), inc=K, inc_amt=16)
    for i, v in enumerate(CONST_VALS):
        nc.const_aps.register(F32, float(v), ct[:, i:i + 1])


def emit_ph0(P, io):
    nc = P.nc
    x_tm = io.din("x_tm", (NTOK, D), F32)
    R = io.sbuf("R", (128, RCOLS), BF16)
    ident = io.sbuf("ident", (128, 128), BF16)

    xld = [io.sbuf(f"p0_x{p}", (128, D), F32) for p in range(2)]
    xn = [io.sbuf(f"p0_xn{p}", (128, D), BF16) for p in range(2)]
    dump = io.sbuf("p0_dump", (128, D), BF16)
    cs = io.sbuf("p0_cs", (128, 1), F32)
    cmean = io.sbuf("p0_cmean", (128, 1), F32)
    cq = io.sbuf("p0_cq", (128, 1), F32)
    cmsq = io.sbuf("p0_cmsq", (128, 1), F32)
    cm2 = io.sbuf("p0_cm2", (128, 1), F32)
    cvar = io.sbuf("p0_cvar", (128, 1), F32)
    cstd = io.sbuf("p0_cstd", (128, 1), F32)
    crstd = io.sbuf("p0_crstd", (128, 1), F32)
    A1 = io.psum("p1_A1", (128, 2048), F32)
    psT = [A1.ap()[:, 0:512].bitcast(BF16), A1.ap()[:, 512:1024].bitcast(BF16)]

    VT, SQ, VV, ST, TT, EVT, VS = (P.ch(x) for x in
                                   ("p0VT", "qSQA", "qRED", "qLNX", "p0TT",
                                    "p0EVT", "qVS"))
    ring = DmaRing(P, "Ls", k=4, eng="sync")

    P.emit("gpsimd", lambda e: e.memset(R[:], 0.0), inc=P.ch("p0G"))

    nsub = NTOK // 128
    for s in range(nsub):
        p = s % 2
        tok = ring.dma(lambda e, s=s, p=p: e.dma_start(
            xld[p][:], x_tm[s * 128:(s + 1) * 128, :]),
            wait=[(VT, max(0, s - 1))])
        P.emit("vector", lambda e, p=p: e.reduce_sum(cs[:, 0:1], xld[p][:], axis=AX.X),
               wait=[tok], inc=VS)
        P.emit("vector", lambda e: e.tensor_scalar_mul(cmean[:, 0:1], cs[:, 0:1], 1.0 / D),
               wait=[(VS, None)], inc=VS)
        P.emit("scalar", lambda e, p=p: e.activation(
            dump[:], xld[p][:], AF.Square),
            wait=[tok, (VT, max(0, s))], inc=SQ)
        P.emit("vector", lambda e: e.reduce_sum(cq[:, 0:1], dump[:], axis=AX.X),
               wait=[(SQ, None)], inc=VS)
        P.emit("vector", lambda e: e.tensor_scalar_mul(cmsq[:, 0:1], cq[:, 0:1], 1.0 / D),
               wait=[(VS, None)], inc=VS)
        P.emit("vector", lambda e: e.tensor_mul(cm2[:, 0:1], cmean[:, 0:1], cmean[:, 0:1]),
               inc=VS)
        P.emit("vector", lambda e: e.tensor_sub(cvar[:, 0:1], cmsq[:, 0:1], cm2[:, 0:1]),
               wait=[(VS, None)], inc=VV)
        P.emit("scalar", lambda e: e.activation(cstd[:, 0:1], cvar[:, 0:1], AF.Sqrt,
                                                bias=1e-5),
               wait=[(VV, None)], inc=ST)
        P.emit("vector", lambda e: e.reciprocal(crstd[:, 0:1], cstd[:, 0:1]),
               wait=[(ST, None)], inc=VS)
        P.emit("vector", lambda e, p=p: e.tensor_scalar(
            xn[p][:], xld[p][:], cmean[:, 0:1], crstd[:, 0:1], ALU.subtract, ALU.mult),
            wait=[(TT, max(0, s - 1)), (VS, None)], inc=VT)
        for c in range(CH):
            P.emit("tensor", lambda e, p=p, c=c: e.transpose(
                psT[p][:, c * 128:(c + 1) * 128], xn[p][:, c * 128:(c + 1) * 128],
                ident[:]),
                wait=([(VT, None), (EVT, max(0, s - 1))] if c == 0 else []),
                inc=(TT if c == CH - 1 else ()))
        P.emit("vector", lambda e, s=s, p=p: e.tensor_copy(
            R.ap().rearrange("p (c t) -> p c t", c=CH, t=RSTR)
             [:, :, 1 + s * 128: 1 + (s + 1) * 128],
            psT[p].rearrange("p (c t) -> p c t", c=CH, t=128)),
            wait=[(TT, None)] + ([(P.ch("p0G"), 1)] if s == 0 else []), inc=EVT)


def emit_ph1(P, io, after=()):
    R = io.sbuf("R", (128, RCOLS), BF16)
    wq_h = io.din("l0Wq", (D, D), F32)
    wk_h = io.din("l0Wk", (D, D), F32)
    wv_h = io.din("l0Wv", (D, D), F32)
    wo_h = io.din("l0Wo", (D, D), F32)
    bo_h = io.din("l0bo", (128, CH), F32)
    esel_h = io.din("l0esel", (16, 1024), BF16)
    attn0 = io.dout("attn0", (D, NTOK), BF16)

    bo = io.sbuf("p1_bo", (128, CH), F32)
    esel = io.sbuf("p1_esel", (16, 1024), BF16)

    kbf = [io.sbuf(f"p1_kbf{p}", (128, 1024), BF16) for p in range(2)]
    vaug = [io.sbuf(f"p1_vaug{p}", (128, 1040), BF16) for p in range(2)]
    mt = io.sbuf("p1_mt", (128, 1024), BF16)
    et = io.sbuf("p1_et", (128, 1024), BF16)
    kvsb = io.sbuf("p1_kvsb", (64, 1040), BF16)
    kvacc = io.sbuf("p1_kvacc", (64, 1040), F32)
    ksbd = io.sbuf("p1_ksbd", (128, 128), BF16)
    kvbd = io.sbuf("p1_kvbd", (128, 1024), BF16)
    qbf = io.sbuf("p1_qbf", (128, 4096), BF16)
    obf = io.sbuf("p1_obf", (128, 4096), BF16)
    zs = io.sbuf("p1_zs", (16, 512), F32)
    zib = io.sbuf("p1_zib", (16, 512), BF16)
    zbsb = io.sbuf("p1_zbsb", (128, 512), BF16)
    att = [io.sbuf("p1_att0", (128, CH * 512), BF16)] * 2
    wq = io.sbuf("p1_wq", (128, CH * 1024), BF16)
    wk = io.sbuf("p1_wk", (128, CH * 1024), BF16)
    wv = io.sbuf("p1_wv", (128, CH * 1024), BF16)
    wo = io.sbuf("p1_wo", (128, CH * 1024), BF16)

    A1 = io.psum("p1_A1", (128, 2048), F32)
    A2 = io.psum("p1_A2", (128, 2048), F32)
    ps_k = A1[:, 0:1024]
    ps_v = A1[:, 1024:2048]
    ps_kv = [A2[0:64, 128 * h: 128 * h + 65] for h in range(16)]
    ps_q = [A1[:, 0:512], A1[:, 512:1024]]
    ps_o = [A1[:, 1024:1536], A1[:, 1536:2048]]
    ps_z = A2[0:16, 0:512]
    ps_zb = A2[:, 512:1024]
    ps_w = [A2[:, 1024:1536], A2[:, 1536:2048]]

    wring = DmaRing(P, "Lw", k=6, eng="gpsimd")
    oring = DmaRing(P, "Lo", k=2, eng="sync")
    PEk = P.ch("p1PEk")
    PEv = P.ch("p1PEv")
    PEkv = P.ch("p1PEkv")
    KVA = P.ch("p1KVA")
    VEk = P.ch("p1VEk")
    AEv = P.ch("p1AEv")
    MN = P.ch("p1MN")
    EX = P.ch("qAEx")
    VS = P.ch("qVS")
    BD = P.ch("p1BD")
    PEq = P.ch("p1PEq")
    VEq = P.ch("p1VEq")
    PEz = P.ch("p1PEz")
    VEz = P.ch("p1VEz")
    PEo = P.ch("p1PEo")
    PEzb = P.ch("p1PEzb")
    VEo = P.ch("p1VEo")
    VEzb = P.ch("p1VEzb")
    PEw = P.ch("p1PEw")
    AEw = P.ch("p1AEw")
    ONE = P.ch("p1ONE")
    C = P.ch("p1C")

    wtoks = {}
    for name, hbm, sb in (("wk", wk_h, wk), ("wv", wv_h, wv), ("wq", wq_h, wq),
                          ("wo", wo_h, wo)):
        wtoks[name] = wring.dma(lambda e, hbm=hbm, sb=sb: e.dma_start(
            sb[:], hbm.rearrange("(c p) n -> p c n", p=128)))
    P.emit("sync", lambda e: e.dma_start(bo[:], bo_h), wait=list(after),
           inc=C, inc_amt=16)
    P.emit("sync", lambda e: e.dma_start(esel[:], esel_h), wait=list(after),
           inc=C, inc_amt=16)
    for p in range(2):
        P.emit("gpsimd", lambda e, p=p: e.memset(
            vaug[p].ap().rearrange("p (h e) -> p h e", h=16, e=65)[:, :, 64:65], 1.0),
            wait=(list(after) if p == 0 else []), inc=ONE)

    nsub = NTOK // 128
    for s in range(nsub):
        p = s % 2
        for half in range(2):
            for c in range(CH):
                P.emit("tensor", lambda e, s=s, c=c, half=half: e.matmul(
                    ps_k[:, half * 512:(half + 1) * 512], _r_slice(R, c, s * 128, 128),
                    wk[:, c * 1024 + half * 512: c * 1024 + (half + 1) * 512],
                    start=(c == 0), stop=(c == CH - 1)),
                    wait=(([wtoks["wk"], (VEk, max(0, VEk.n - 1))]
                           + (list(after) if s == 0 else []))
                          if (c == 0 and half == 0) else []),
                    inc=(PEk if (c, half) == (CH - 1, 1) else ()))
        for half in range(2):
            for c in range(CH):
                P.emit("tensor", lambda e, s=s, c=c, half=half: e.matmul(
                    ps_v[:, half * 512:(half + 1) * 512], _r_slice(R, c, s * 128, 128),
                    wv[:, c * 1024 + half * 512: c * 1024 + (half + 1) * 512],
                    start=(c == 0), stop=(c == CH - 1)),
                    wait=([wtoks["wv"], (AEv, max(0, AEv.n - 1))]
                          if (c == 0 and half == 0) else []),
                    inc=(PEv if (c, half) == (CH - 1, 1) else ()))
        P.emit("vector", lambda e: e.tensor_scalar_min(mt[:], ps_k, 0.0),
               wait=[(PEk, None)], inc=MN)
        P.emit("scalar", lambda e: e.activation(et[:], mt[:], AF.Exp),
               wait=[(MN, None)], inc=EX)
        P.emit("vector", lambda e, p=p: e.scalar_tensor_tensor(
            kbf[p][:], ps_k, 0.0, et[:], ALU.max, ALU.add),
            wait=[(EX, None), (PEkv, max(0, PEkv.n - 1))], inc=VEk)
        P.emit("scalar", lambda e, p=p: e.copy(
            vaug[p].ap().rearrange("p (h e) -> p h e", h=16, e=65)[:, :, 0:64],
            ps_v.rearrange("p (h e) -> p h e", h=16, e=64)),
            wait=[(PEv, None), (ONE, 2), (PEkv, max(0, PEkv.n - 1))], inc=AEv)
        for h in range(16):
            P.emit("tensor", lambda e, p=p, h=h: e.matmul(
                ps_kv[h], kbf[p][:, h * 64:(h + 1) * 64],
                vaug[p][:, h * 65:(h + 1) * 65],
                start=True, stop=True),
                wait=([(VEk, None), (AEv, None), (KVA, None)] if h == 0 else []),
                inc=(PEkv if h == 15 else ()))
        psv = A2.ap()[0:64, :].rearrange("p (h e) -> p h e", h=16, e=128)[:, :, 0:65]
        if s == 0:
            P.emit("vector", lambda e, psv=psv: e.tensor_copy(
                kvacc.ap().rearrange("p (h e) -> p h e", h=16, e=65), psv),
                wait=[(PEkv, None)], inc=KVA)
        else:
            P.emit("vector", lambda e, psv=psv: e.tensor_add(
                kvacc.ap().rearrange("p (h e) -> p h e", h=16, e=65),
                kvacc.ap().rearrange("p (h e) -> p h e", h=16, e=65), psv),
                wait=[(PEkv, None)], inc=KVA)

    P.emit("vector", lambda e: e.tensor_copy(kvsb[:], kvacc[:]),
           wait=[(KVA, None)], inc=VS)
    P.emit("vector", lambda e: e.memset(ksbd[:], 0.0), wait=[(VS, None)], inc=VS)
    P.emit("vector", lambda e: e.memset(kvbd[:], 0.0), wait=[(VS, None)], inc=VS)
    for h in range(16):
        c, half = h // 2, h % 2
        P.emit("vector", lambda e, h=h, c=c, half=half: e.tensor_copy(
            ksbd.ap()[half * 64: half * 64 + 64, c * 16 + h: c * 16 + h + 1],
            kvsb[0:64, h * 65 + 64: h * 65 + 65]),
            wait=[(VS, None)], inc=VS)
    for m in range(CH):
        for half in range(2):
            h = 2 * m + half
            P.emit("vector", lambda e, m=m, half=half, h=h: e.tensor_copy(
                kvbd.ap()[half * 64: half * 64 + 64,
                          m * 128 + half * 64: m * 128 + half * 64 + 64],
                kvsb[0:64, h * 65: h * 65 + 64]),
                wait=[(VS, None)], inc=BD if (m, half) == (CH - 1, 1) else VS)

    ntile = NTOK // 512
    atoks = []
    for t in range(ntile):
        for m in range(CH):
            pq = m % 2
            for c in range(CH):
                P.emit("tensor", lambda e, t=t, m=m, c=c, pq=pq: e.matmul(
                    ps_q[pq], wq[:, c * 1024 + m * 128: c * 1024 + (m + 1) * 128],
                    _r_slice(R, c, t * 512, 512), start=(c == 0), stop=(c == CH - 1)),
                    wait=(([wtoks["wq"], (VEq, max(0, VEq.n - 1))]
                           + ([(VEk, None)] if (t, m) == (0, 0) else []))
                          if c == 0 else []),
                    inc=(PEq if c == CH - 1 else ()))
            P.emit("vector", lambda e, pq=pq: e.tensor_scalar_min(
                mt[:, 0:512], ps_q[pq], 0.0), wait=[(PEq, None), (VEq, None)], inc=MN)
            P.emit("scalar", lambda e: e.activation(et[:, 0:512], mt[:, 0:512], AF.Exp),
                   wait=[(MN, None)], inc=EX)
            P.emit("vector", lambda e, m=m, pq=pq: e.scalar_tensor_tensor(
                qbf[:, m * 512:(m + 1) * 512], ps_q[pq], 0.0, et[:, 0:512],
                ALU.max, ALU.add),
                wait=[(EX, None)] + ([(PEo, None)] if m == 0 else []), inc=VEq)
        for c in range(CH):
            P.emit("tensor", lambda e, c=c: e.matmul(
                ps_z, ksbd[:, c * 16:(c + 1) * 16], qbf[:, c * 512:(c + 1) * 512],
                start=(c == 0), stop=(c == CH - 1)),
                wait=([(BD, None), (VEq, None), (VEz, None)]
                      if c == 0 else []),
                inc=(PEz if c == CH - 1 else ()))
        P.emit("vector", lambda e: e.tensor_scalar_add(zs[:], ps_z, 1e-6),
               wait=[(PEz, None), (VEz, None)], inc=VS)
        P.emit("vector", lambda e: e.reciprocal(zs[:], zs[:]),
               wait=[(VS, None)], inc=VS)
        P.emit("vector", lambda e: e.tensor_copy(zib[:], zs[:]),
               wait=[(VS, None)], inc=VEz)
        for m in range(CH):
            P.emit("tensor", lambda e, m=m: e.matmul(
                ps_zb, esel[:, m * 128:(m + 1) * 128], zib[:],
                start=True, stop=True),
                wait=[(VEz, None), (VEzb, None), (C, 32)]
                     + ([(BD, None)] if m == 0 else []),
                inc=PEzb)
            P.emit("tensor", lambda e, m=m, t=t: e.matmul(
                ps_o[m % 2], kvbd[:, m * 128:(m + 1) * 128],
                qbf[:, m * 512:(m + 1) * 512], start=True, stop=True),
                wait=[(VEo, max(0, VEo.n - 1))]
                     + ([(AEv, None)] if (t, m) == (0, 0) else []),
                inc=PEo)
            P.emit("vector", lambda e: e.tensor_copy(zbsb[:], ps_zb),
                   wait=[(PEzb, None), (VS, None)], inc=VEzb)
            P.emit("vector", lambda e, m=m: e.tensor_mul(
                obf[:, m * 512:(m + 1) * 512], ps_o[m % 2], zbsb[:]),
                wait=[(PEo, None), (VEzb, None)]
                     + ([(PEw, None)] if m == 0 else []), inc=VEo)
        for mo in range(CH):
            pw = mo % 2
            for c in range(CH):
                P.emit("tensor", lambda e, mo=mo, c=c, pw=pw: e.matmul(
                    ps_w[pw], wo[:, c * 1024 + mo * 128: c * 1024 + (mo + 1) * 128],
                    obf[:, c * 512:(c + 1) * 512], start=(c == 0), stop=(c == CH - 1)),
                    wait=([wtoks["wo"], (VEo, None), (AEw, max(0, AEw.n - 1))]
                          + ([(BD, None)] if (t, mo) == (0, 0) else [])
                          if c == 0 else []),
                    inc=(PEw if c == CH - 1 else ()))
            P.emit("vector", lambda e, mo=mo, pw=pw, t=t: e.tensor_scalar_add(
                att[t % 2][:, mo * 512:(mo + 1) * 512], ps_w[pw], bo[:, mo:mo + 1]),
                wait=[(PEw, None), (C, 32)]
                     + ([atoks[t - 1]] if (t >= 1 and mo == 0) else []),
                inc=AEw)
        atoks.append(oring.dma(lambda e, t=t: e.dma_start(
            attn0.rearrange("(c p) n -> p c n", p=128)[:, :, t * 512:(t + 1) * 512],
            att[t % 2].ap().rearrange("p (c n) -> p c n", c=CH)),
            wait=[(AEw, None)]))
    P.emit("sync", lambda e: e.nop(), wait=oring.all_done())
    return oring.all_done() + [(AEw, AEw.n)]


import os
P2_NOPREV = os.environ.get("P2_NOPREV", "0") == "1"
P2_NOM0 = os.environ.get("P2_NOM0", "0") == "1"
P2_NOPREVMM = os.environ.get("P2_NOPREVMM", "0") == "1"
P2_NARROW = os.environ.get("P2_NARROW", "0") == "1"
P2_NOMID = os.environ.get("P2_NOMID", "0") == "1"
HEADS2 = 8
W1P = 5632
W2K = 2816


def emit_ph2_layer(P, io, l, after=(), sweeps=("A", "F"), dbg=None, astop=None):
    nw = NTOK // 512
    R = io.sbuf("R", (128, RCOLS), BF16)
    ident = io.sbuf("ident", (128, 128), BF16)
    pre = "q"

    wqkv_h = io.din(f"f_wqkv{l}", (D, 1536), F32)
    wo_h = io.din(f"f_wo{l}", (512, D), F32)
    w1p_h = io.din(f"f_w1p{l}", (D, W1P), F32)
    w2p_h = io.din(f"f_w2p{l}", (W2K, D), F32)
    b1a_h = io.din(f"f_b1a{l}", (128, 22), F32)
    b1g_h = io.din(f"f_b1g{l}", (128, 22), F32)
    lng_h = io.din(f"f_lng{l}", (128, CH), F32)
    lnb_h = io.din(f"f_lnb{l}", (128, CH), F32)
    b2_h = io.din(f"f_b2{l}", (128, CH), F32)
    qs_h = io.din(f"f_qs{l}", (1, 64), F32)
    ks_h = io.din(f"f_ks{l}", (1, 64), F32)
    tabs_h = {nm: io.din(f"tab_{nm}", (128, 2048), BF16)
              for nm in ("csq", "ssq", "csk", "ssk")}
    triu_h = io.din("tri_u", (128, 128), BF16)
    tril_h = io.din("tri_l", (128, 128), BF16)
    m640_h = io.din("mask640", (128, 640), BF16)
    ones8_h = io.din("ones8", (1, 128), BF16)
    ones1_h = io.din("ones1", (1, 128), BF16)
    onescol_h = io.din("onescol", (128, 1), BF16)
    w1bf = io.dint(f"w1bf{l}", (D, W1P), BF16)
    w2bf = io.dint(f"w2bf{l}", (W2K, D), BF16)

    tabs = {nm: io.sbuf(f"p2_{nm}", (128, 2048), BF16)
            for nm in ("csq", "ssq", "csk", "ssk")}
    m640 = io.sbuf("p2_m640", (128, 640), BF16)
    ones8 = io.sbuf("p2_ones8", (1, 128), BF16)
    ones1 = io.sbuf("p2_ones1", (1, 128), BF16)
    onescol = io.sbuf("p2_onescol", (128, 1), BF16)
    uni = io.sbuf("p2_wqkv", (128, CH * 1536), BF16)
    wqkv = uni[:, 0:12288]
    wosb = io.sbuf("p2_wo", (128, 4 * 1024), BF16)
    scr = io.sbuf("p2_scr", (128, 20480), BF16)
    q_fm = [scr[:, 0:2048], scr[:, 2048:4096]]
    k_fm = [scr[:, 4096:6144], scr[:, 6144:8192], scr[:, 8192:10240]]
    v_tm = [scr[:, 10240:12288], scr[:, 12288:14336], scr[:, 14336:16384]]
    o_bf = [scr[:, 16384:18432], scr[:, 18432:20480]]
    h3 = scr[:, 0:11264]
    h2n = uni[:, 0:4096]
    w1t = [uni[:, 4096:6144], uni[:, 6144:8192]]
    w2t = [uni[:, 8192:9216], uni[:, 9216:10240]]
    sqb = [io.sbuf(f"p2_sqb{i}", (128, 512), BF16) for i in range(2)]
    qs64 = io.sbuf("p2_qs64", (1, 64), F32)
    ks64 = io.sbuf("p2_ks64", (1, 64), F32)
    qs64b = io.sbuf("p2_qs64b", (1, 64), BF16)
    ks64b = io.sbuf("p2_ks64b", (1, 64), BF16)
    qsbc = io.sbuf("p2_qsbc", (128, 64), BF16)
    ksbc = io.sbuf("p2_ksbc", (128, 64), BF16)
    sq = [io.sbuf(f"p2_sq{i}", (128, 512), BF16) for i in range(2)]
    nrm2 = [io.sbuf(f"p2_nrm2{i}", (128, 8), F32) for i in range(2)]
    rn = [io.sbuf(f"p2_rn{i}", (128, 8), F32) for i in range(2)]
    qn = [io.sbuf(f"p2_qn{i}", (128, 512), F32) for i in range(2)]
    tmp = [io.sbuf(f"p2_tmp{i}", (128, 512), F32) for i in range(2)]
    t1f = [io.sbuf(f"p2_t1{i}", (128, 512), F32) for i in range(2)]
    qtm = [io.sbuf(f"p2_qtm{i}", (128, 512), BF16) for i in range(2)]
    s_sb = [io.sbuf(f"p2_ssb{i}", (128, 640), F32) for i in range(2)]
    p_bf = [io.sbuf(f"p2_pbf{i}", (128, 640), BF16) for i in range(2)]
    pT_sb = [io.sbuf(f"p2_pT{i}", (128, 640), BF16) for i in range(2)]
    mxc = [io.sbuf(f"p2_mx{i}", (128, 1), F32) for i in range(2)]
    dnc = [io.sbuf(f"p2_dn{i}", (128, 1), F32) for i in range(2)]
    b1a = io.sbuf("p2_b1a", (128, 22), F32)
    b1g = io.sbuf("p2_b1g", (128, 22), F32)
    lng = io.sbuf("p2_lng", (128, CH), F32)
    lnb = io.sbuf("p2_lnb", (128, CH), F32)
    b2c = io.sbuf("p2_b2", (128, CH), F32)
    mrow = io.sbuf("p2_mrow", (1, 512), F32)
    qrow = io.sbuf("p2_qrow", (1, 512), F32)
    vrow = io.sbuf("p2_vrow", (1, 512), F32)
    rrow = io.sbuf("p2_rrow", (1, 512), F32)
    mrrow = io.sbuf("p2_mrrow", (1, 512), F32)
    rrow_bf = io.sbuf("p2_rrowbf", (1, 512), BF16)
    mrrow_bf = io.sbuf("p2_mrrowbf", (1, 512), BF16)
    gg = [uni[:, 10240:10752], uni[:, 10752:11264]]
    nt1 = t1f[0]

    A1 = io.psum("p1_A1", (128, 2048), F32)
    A2 = io.psum("p1_A2", (128, 2048), F32)
    ps_pj = A1[:, 0:512]
    ps_T = A1.ap()[:, 1536:2048].bitcast(BF16)
    ps_S = A2[:, 0:640]
    ps_pT = A2.ap()[:, 1024:1536].bitcast(BF16)
    ps_oh = A2[0:64, 1536:2048]
    ps_w = [A1[:, 512:1024], A1[:, 1024:1536]]
    ps_sr = A2[0:1, 0:512]
    ps_qr = A2[0:1, 512:1024]
    ps_rbc = A2[:, 1024:1536]
    ps_mbc = A2[:, 1536:2048]
    ps_a = A1[:, 0:512]
    ps_g = A1[:, 512:1024]
    accs = [A1[:, 512 * i:512 * (i + 1)] for i in range(4)] + \
           [A2[:, 512 * i:512 * (i + 1)] for i in range(4)]

    ch = lambda nm: P.ch(pre + nm)
    VS = ch("VS")
    TBF = ch("TBF")
    SQA = ch("SQA");  RED = ch("RED");  LNE = ch("LNE");  APL = ch("APL")
    SWP = ch("SWP");  SXT = ch("SXT");  ADD = ch("ADD");  PET = ch("PET")
    TRD = ch("TRD");  QPF = ch("QPF");  KPF = ch("KPF");  VPF = ch("VPF")
    QKR = ch("QKR");  KVR = ch("KVR")
    SMM = ch("SMM");  VSm = ch("VSm");  AEx = ch("AEx")
    PTS = ch("PTS");  TPT = ch("TPT");  VPT = ch("VPT");  PEo = ch("PEo")
    VEo = ch("VEo");  PEw = ch("PEw");  RA = ch("RA")
    AEF = ch("AEF");  QRM = ch("QRM");  RWC = ch("RWC")
    PEb = ch("PEb");  VEn = ch("VEn");  PEa = ch("PEa");  AEg = ch("AEg")
    VEh = ch("VEh");  PEf = ch("PEf");  RAF = ch("RAF");  C2 = ch("C2")
    W2M = ch("W2M");  LNX = ch("LNX")
    QKP = ch("QKP")

    wring = DmaRing(P, "Lw", k=6, eng="gpsimd")
    cring = DmaRing(P, "Ls", k=4, eng="sync")
    fring = DmaRing(P, "Lf", k=4, eng="sync")

    tok_wqkv = wring.dma(lambda e: e.dma_start(
        wqkv[:], wqkv_h.rearrange("(c p) n -> p c n", p=128)),
        wait=[(QKP, None)] + list(after))
    tok_wo = wring.dma(lambda e: e.dma_start(
        wosb[:], wo_h.rearrange("(c p) n -> p c n", p=128)),
        wait=[(PEw, None)] + list(after))
    w1c_toks = [wring.dma(lambda e, j=j: e.dma_start(
        w1bf[:, j * 1408:(j + 1) * 1408], w1p_h[:, j * 1408:(j + 1) * 1408]))
        for j in range(4)]
    w2c_toks = [wring.dma(lambda e, j=j: e.dma_start(
        w2bf[j * 1408:(j + 1) * 1408, :], w2p_h[j * 1408:(j + 1) * 1408, :]))
        for j in range(2)]
    for hbm, sb in ((b1a_h, b1a), (b1g_h, b1g), (lng_h, lng), (lnb_h, lnb),
                    (b2_h, b2c), (qs_h, qs64), (ks_h, ks64),
                    (m640_h, m640), (ones8_h, ones8),
                    (ones1_h, ones1), (onescol_h, onescol)):
        P.emit("sync", lambda e, hbm=hbm, sb=sb: e.dma_start(sb[:], hbm),
               wait=[(RAF, None), (VSm, None)] + list(after), inc=C2, inc_amt=16)
    tab_toks = {nm: cring.dma(lambda e, nm=nm: e.dma_start(
        tabs[nm][:], tabs_h[nm]), wait=[(ADD, None)] + list(after))
        for nm in ("csq", "ssq", "csk", "ssk")}
    C2f = C2.n
    tbf_t = TBF.n + 1

    P.emit("vector", lambda e: e.tensor_copy(qs64b[:], qs64[:]),
           wait=[(C2, C2f)] + list(after), inc=VS)
    P.emit("vector", lambda e: e.tensor_copy(ks64b[:], ks64[:]), inc=VS)
    P.emit("tensor", lambda e: e.matmul(A2[:, 0:64], ones8[:], qs64b[:],
                                        start=True, stop=True),
           wait=[(VS, None)] + list(after), inc=QKP)
    P.emit("tensor", lambda e: e.matmul(A2[:, 64:128], ones1[:], ks64b[:],
                                        start=True, stop=True), inc=QKP)
    P.emit("vector", lambda e: e.tensor_copy(qsbc[:], A2[:, 0:64]),
           wait=[(QKP, None)], inc=VS)
    P.emit("vector", lambda e: e.tensor_copy(ksbc[:], A2[:, 64:128]), inc=VS)
    for nm, bc in (("csq", qsbc), ("ssq", qsbc), ("csk", ksbc), ("ssk", ksbc)):
        P.emit("vector", lambda e, nm=nm, bc=bc: e.tensor_tensor(
            tabs[nm].ap().rearrange("p (s f) -> p s f", s=32),
            tabs[nm].ap().rearrange("p (s f) -> p s f", s=32),
            bc.ap().rearrange("p (o f) -> p o f", o=1).broadcast_to([128, 32, 64]),
            ALU.mult),
            wait=[(VS, None), tab_toks[nm]],
            inc=(TBF if nm == "ssk" else VS))

    ecall = [0]
    bQPF = QPF.n
    cic = [0]

    def _qk_epilogue(nm, dpar, sub, gsub, guard_smm):
        i = ecall[0] % 2
        ecall[0] += 1
        ps = ps_pj
        cs = tabs["csq" if nm == "q" else "csk"]
        ssx = tabs["ssq" if nm == "q" else "ssk"]
        P.emit("scalar", lambda e, ps=ps, i=i: e.activation(
            sq[i][:], ps, AF.Square),
            wait=[(QKP, None), (RED, max(0, RED.n - 1)), (TBF, tbf_t)], inc=SQA)
        for h in range(8):
            P.emit("vector", lambda e, h=h, i=i: e.reduce_sum(
                nrm2[i][:, h:h + 1], sq[i][:, h * 64:(h + 1) * 64], axis=AX.X),
                wait=([(SQA, None), (LNE, max(0, LNE.n - 1))] if h == 0 else []),
                inc=(VS if h == 7 else ()))
        P.emit("vector", lambda e, i=i: e.tensor_scalar_max(
            nrm2[i][:], nrm2[i][:], 1e-24), wait=[(VS, None)], inc=RED)
        P.emit("scalar", lambda e, i=i: e.activation(rn[i][:], nrm2[i][:], AF.Ln),
               wait=[(RED, None), (APL, max(0, APL.n - 1))], inc=LNX)
        P.emit("scalar", lambda e, i=i: e.activation(
            rn[i][:], rn[i][:], AF.Exp, scale=-0.5),
            wait=[(LNX, None)], inc=LNE)
        for h in range(8):
            P.emit("vector", lambda e, h=h, ps=ps, i=i: e.tensor_scalar_mul(
                qn[i][:, h * 64:(h + 1) * 64], ps[:, h * 64:(h + 1) * 64],
                rn[i][:, h:h + 1]),
                wait=([(LNE, None), (SWP, max(0, SWP.n - 1))] if h == 0 else []),
                inc=(APL if h == 7 else VS))
        P.emit("vector", lambda e: e.engine_nop(), wait=[(APL, None)], inc=QPF)
        P.emit("gpsimd", lambda e, i=i: e.tensor_copy(
            tmp[i].ap().rearrange("p (h a j) -> p h a j", h=8, a=2, j=32),
            qn[i].ap().rearrange("p (h a j) -> p h a j", h=8, a=2, j=32)[:, :, ::-1, :]),
            wait=[(APL, None), (SXT, max(0, SXT.n - 1))], inc=SWP)
        P.emit("vector", lambda e, cs=cs, gsub=gsub, i=i: e.tensor_tensor(
            t1f[i].ap().rearrange("p (h f) -> p h f", h=8),
            qn[i].ap().rearrange("p (h f) -> p h f", h=8),
            cs.ap()[:, gsub * 64:gsub * 64 + 64]
              .rearrange("p (o f) -> p o f", o=1).broadcast_to([128, 8, 64]),
            ALU.mult), wait=[(APL, None)], inc=VS)
        P.emit("vector", lambda e, ssx=ssx, gsub=gsub, i=i: e.tensor_tensor(
            tmp[i].ap().rearrange("p (h f) -> p h f", h=8),
            tmp[i].ap().rearrange("p (h f) -> p h f", h=8),
            ssx.ap()[:, gsub * 64:gsub * 64 + 64]
              .rearrange("p (o f) -> p o f", o=1).broadcast_to([128, 8, 64]),
            ALU.mult), wait=[(SWP, None)], inc=SXT)
        P.emit("vector", lambda e, i=i: e.tensor_add(
            qtm[i][:], t1f[i][:], tmp[i][:]),
            wait=[(SXT, None), (PET, max(0, PET.n - 1))], inc=ADD)
        for j in range(4):
            P.emit("tensor", lambda e, j=j, i=i: e.transpose(
                ps_T[:, j * 128:(j + 1) * 128], qtm[i][:, j * 128:(j + 1) * 128],
                ident[:]),
                wait=([(ADD, None), (TRD, None)] if j == 0 else []),
                inc=(PET if j == 3 else ()))
        dst = q_fm[dpar] if nm == "q" else k_fm[dpar]
        P.emit("vector", lambda e, dst=dst, sub=sub: e.tensor_copy(
            dst.rearrange("p (hc t) -> p hc t", hc=4)[:, :, sub * 128:(sub + 1) * 128],
            ps_T[:, 0:512].rearrange("p (hc t) -> p hc t", hc=4)),
            wait=[(PET, None)] + ([(SMM, guard_smm)] if guard_smm else []),
            inc=TRD)

    def emit_chunk(w, sub, nm, guard):
        _S("q")
        ci = cic[0]
        cic[0] += 1
        t0 = w * 512 + sub * 128
        gsub = (t0 // 128) % 32
        col0 = {"q": 0, "k": 512, "v": 1024}[nm]
        for c in range(CH):
            P.emit("tensor", lambda e, c=c, t0=t0, col0=col0: e.matmul(
                ps_pj, _r_shift_slice(R, c, t0, 128),
                wqkv[:, c * 1536 + col0: c * 1536 + col0 + 512],
                start=(c == 0), stop=(c == CH - 1)),
                wait=([tok_wqkv, (TBF, tbf_t), (QPF, bQPF + ci)]
                      if c == 0 else []),
                inc=(QKP if c == CH - 1 else ()))
        if nm == "v":
            P.emit("vector", lambda e, w=w, sub=sub: e.tensor_copy(
                v_tm[w % 3][:, sub * 512:(sub + 1) * 512], ps_pj),
                wait=[(QKP, None)]
                     + ([(PEo, bPEo + guard)] if guard else []),
                inc=QPF)
        else:
            _qk_epilogue(nm, (w % 2) if nm == "q" else (w % 3), sub, gsub,
                         bSMM + guard if guard else 0)

    def attn_window(w):
        par, parp = w % 2, (w - 1) % 2
        for h in range(HEADS2):
            hc, hb = h // 2, 64 * (h % 2)
            for qc in range(4):
                if P2_NOPREV:
                    w = 0
                hx = (4 * h + qc) % 2
                ssb, pbf, pT, mx, dn = (s_sb[hx], p_bf[hx],
                                        pT_sb[hx], mxc[hx], dnc[hx])
                lo = 0 if w > 0 else 512 - 128 * qc
                if P2_NARROW:
                    lo = 512 - 128 * qc
                first = ([(QKR, None), (KVR, None), (C2, C2f)]
                         if (h, qc) == (0, 0) else [])
                prevw = [(VSm, None)]
                qsl = lambda q_fm=q_fm, hb=hb, hc=hc, qc=qc: q_fm[
                    hb:hb + 64, hc * 512 + qc * 128: hc * 512 + (qc + 1) * 128]
                if w > 0:
                    if P2_NOPREVMM:
                        P.emit("tensor", lambda e, hb=hb, hc=hc, qc=qc, par=par,
                               qsl=qsl: e.matmul(
                            ps_S[:, 0:512 - 128 * qc], qsl(),
                            k_fm[par][hb:hb + 64, hc * 512 + qc * 128: hc * 512 + 512],
                            start=True, stop=True), wait=first + prevw, inc=())
                    else:
                        P.emit("tensor", lambda e, hb=hb, hc=hc, qc=qc, parp=parp,
                               qsl=qsl: e.matmul(
                            ps_S[:, 0:512 - 128 * qc], qsl(),
                            k_fm[parp][hb:hb + 64, hc * 512 + qc * 128: hc * 512 + 512],
                            start=True, stop=True), wait=first + prevw, inc=())
                    first, prevw = [], []
                if 128 * qc > 0:
                    P.emit("tensor", lambda e, hb=hb, hc=hc, qc=qc, par=par,
                           qsl=qsl: e.matmul(
                        ps_S[:, 512 - 128 * qc:512], qsl(),
                        k_fm[par][hb:hb + 64, hc * 512: hc * 512 + 128 * qc],
                        start=True, stop=True), wait=first + prevw, inc=())
                    first, prevw = [], []
                P.emit("tensor", lambda e, hb=hb, hc=hc, qc=qc, par=par,
                       qsl=qsl: e.matmul(
                    ps_S[:, 512:640], qsl(),
                    k_fm[par][hb:hb + 64, hc * 512 + qc * 128: hc * 512 + (qc + 1) * 128],
                    start=True, stop=True), wait=first + prevw, inc=SMM)
                P.emit("vector", lambda e, ssb=ssb, lo=lo: e.tensor_tensor(
                    ssb[:, lo:640], ps_S[:, lo:640], m640[:, lo:640], ALU.add),
                    wait=[(SMM, None), (AEx, max(0, AEx.n - 1)),
                          (PTS, max(0, PTS.n - 1))], inc=VSm)
                P.emit("vector", lambda e, ssb=ssb, mx=mx, lo=lo: e.reduce_max(
                    mx[:, 0:1], ssb[:, lo:640], axis=AX.X),
                    wait=[(VSm, None)], inc=VS)
                P.emit("vector", lambda e, mx=mx: e.tensor_scalar_mul(
                    mx[:, 0:1], mx[:, 0:1], -1.0), wait=[(VS, None)], inc=VS)
                P.emit("scalar", lambda e, ssb=ssb, mx=mx, lo=lo: e.activation(
                    ssb[:, lo:640], ssb[:, lo:640], AF.Exp, bias=mx[:, 0:1]),
                    wait=[(VS, None), (PTS, max(0, PTS.n - 1))], inc=AEx)
                P.emit("vector", lambda e, ssb=ssb, dn=dn, lo=lo: e.reduce_sum(
                    dn[:, 0:1], ssb[:, lo:640], axis=AX.X),
                    wait=[(AEx, None), (PTS, max(0, PTS.n - 1))], inc=VS)
                P.emit("vector", lambda e, dn=dn: e.reciprocal(dn[:, 0:1], dn[:, 0:1]),
                       wait=[(VS, None)], inc=VS)
                P.emit("scalar", lambda e, ssb=ssb, pbf=pbf, dn=dn, lo=lo:
                       e.mul(
                    pbf[:, lo:640], ssb[:, lo:640], dn[:, 0:1]),
                    wait=[(VS, None), (TPT, max(0, TPT.n - 1))], inc=PTS)
                jlo = lo // 128
                if astop == "sm":
                    continue
                for j in range(jlo, 5):
                    P.emit("tensor", lambda e, pbf=pbf, j=j: e.transpose(
                        ps_pT[:, j * 128:(j + 1) * 128],
                        pbf[:, j * 128:(j + 1) * 128], ident[:]),
                        wait=([(PTS, None), (VPT, None)] if j == jlo else []),
                        inc=(TPT if j == 4 else ()))
                P.emit("vector", lambda e, pT=pT, jlo=jlo: e.tensor_copy(
                    pT[:, jlo * 128:640], ps_pT[:, jlo * 128:640]),
                    wait=[(TPT, None), (PEo, max(0, PEo.n - 1))], inc=VPT)
                if astop == "pt":
                    continue
                for j in range(jlo, 5):
                    kabs = qc + j
                    pv = parp if kabs < 4 else par
                    sb_ = kabs % 4
                    P.emit("tensor", lambda e, pv=pv, sb_=sb_, h=h, pT=pT, j=j,
                           qc=qc, jlo=jlo: e.matmul(
                        ps_oh[:, qc * 128:(qc + 1) * 128],
                        v_tm[pv][:, sb_ * 512 + h * 64: sb_ * 512 + (h + 1) * 64],
                        pT[:, j * 128:(j + 1) * 128],
                        start=(j == jlo), stop=(j == 4)),
                        wait=([(VPT, None)] + ([(VEo, None)] if qc == 0 else [])
                              if j == jlo else []),
                        inc=(PEo if j == 4 else ()))
            if astop is None:
                P.emit("vector", lambda e, hc=hc, hb=hb: e.tensor_copy(
                    o_bf[hb:hb + 64, hc * 512:(hc + 1) * 512], ps_oh[:, 0:512]),
                    wait=[(PEo, None), (PEw, None)], inc=VEo)

    def wo_window(w):
        for mo in range(CH):
            pw = mo % 2
            for kc in range(4):
                P.emit("tensor", lambda e, mo=mo, kc=kc, pw=pw: e.matmul(
                    ps_w[pw], wosb[:, kc * 1024 + mo * 128: kc * 1024 + (mo + 1) * 128],
                    o_bf[:, kc * 512:(kc + 1) * 512],
                    start=(kc == 0), stop=(kc == 3)),
                    wait=([tok_wo, (VEo, None), (QPF, None), (KPF, None),
                           (RA, max(0, RA.n - 1))] if kc == 0 else []),
                    inc=(PEw if kc == 3 else ()))
            P.emit("vector", lambda e, mo=mo, pw=pw, w=w: e.tensor_add(
                _r_slice(R, mo, w * 512, 512), _r_slice(R, mo, w * 512, 512),
                ps_w[pw]),
                wait=[(PEw, None)], inc=RA)

    _sc0 = P.scope

    def _S(suffix):
        if _sc0:
            P.scope = _sc0 + suffix

    amode = [s for s in sweeps if s in ("A", "QA", "Q")]
    amode = amode[0] if amode else None
    NI = nw * 32
    bSMM = SMM.n; bVSm = VSm.n; bAEx = AEx.n; bPTS = PTS.n; bTPT = TPT.n
    bVPT = VPT.n; bPEo = PEo.n; bVEo = VEo.n; bPEw = PEw.n; bRA = RA.n
    bQKR = QKR.n; bKVR = KVR.n

    def _dec(i):
        w = nw - 1 - i // 32
        h = (i % 32) // 4
        qc = i % 4
        return w, h // 2, 64 * (h % 2), qc, i % 2, (0 if w > 0 else 512 - 128 * qc)

    def st_scores(i):
        _S("a")
        w, hc, hb, qc, hx, lo = _dec(i)
        wi = i // 32
        par, parp = w % 3, (w - 1) % 3
        qf = q_fm[w % 2]
        first = [(VSm, bVSm + i)]
        if i % 32 == 0:
            first += [(QKR, bQKR + wi + 1), (KVR, bKVR + wi + 1)]
        if i == 0:
            first += [(C2, C2f)]
        qsl = lambda qf=qf, hb=hb, hc=hc, qc=qc: qf[
            hb:hb + 64, hc * 512 + qc * 128: hc * 512 + (qc + 1) * 128]
        if w > 0:
            P.emit("tensor", lambda e, hb=hb, hc=hc, qc=qc, parp=parp,
                   qsl=qsl: e.matmul(
                ps_S[:, 0:512 - 128 * qc], qsl(),
                k_fm[parp][hb:hb + 64, hc * 512 + qc * 128: hc * 512 + 512],
                start=True, stop=True), wait=first, inc=())
            first = []
        if qc > 0:
            P.emit("tensor", lambda e, hb=hb, hc=hc, qc=qc, par=par,
                   qsl=qsl: e.matmul(
                ps_S[:, 512 - 128 * qc:512], qsl(),
                k_fm[par][hb:hb + 64, hc * 512: hc * 512 + 128 * qc],
                start=True, stop=True), wait=first, inc=())
            first = []
        P.emit("tensor", lambda e, hb=hb, hc=hc, qc=qc, par=par,
               qsl=qsl: e.matmul(
            ps_S[:, 512:640], qsl(),
            k_fm[par][hb:hb + 64, hc * 512 + qc * 128: hc * 512 + (qc + 1) * 128],
            start=True, stop=True), wait=first, inc=SMM)

    def st_sm1(i):
        _S("a")
        w, hc, hb, qc, hx, lo = _dec(i)
        ssb, mx = s_sb[hx], mxc[hx]
        P.emit("vector", lambda e, ssb=ssb, lo=lo: e.tensor_tensor(
            ssb[:, lo:640], ps_S[:, lo:640], m640[:, lo:640], ALU.add),
            wait=[(SMM, bSMM + i + 1), (PTS, bPTS + max(0, i - 1))], inc=VSm)
        P.emit("vector", lambda e, ssb=ssb, mx=mx, lo=lo: e.reduce_max(
            mx[:, 0:1], ssb[:, lo:640], axis=AX.X), wait=[(VSm, None)], inc=VS)

    def st_exp(i):
        w, hc, hb, qc, hx, lo = _dec(i)
        ssb, mx = s_sb[hx], mxc[hx]
        P.emit("scalar", lambda e, mx=mx: e.mul(mx[:, 0:1], mx[:, 0:1], -1.0),
               wait=[(VS, None)], inc=())
        P.emit("scalar", lambda e, ssb=ssb, mx=mx, lo=lo: e.activation(
            ssb[:, lo:640], ssb[:, lo:640], AF.Exp, bias=mx[:, 0:1]),
            inc=AEx)

    def st_sm2(i):
        w, hc, hb, qc, hx, lo = _dec(i)
        ssb, dn = s_sb[hx], dnc[hx]
        P.emit("vector", lambda e, ssb=ssb, dn=dn, lo=lo: e.reduce_sum(
            dn[:, 0:1], ssb[:, lo:640], axis=AX.X),
            wait=[(AEx, bAEx + i + 1)], inc=VS)
        P.emit("vector", lambda e, dn=dn: e.reciprocal(dn[:, 0:1], dn[:, 0:1]),
               wait=[(VS, None)], inc=VS)

    def st_mul(i):
        w, hc, hb, qc, hx, lo = _dec(i)
        ssb, pbf, dn = s_sb[hx], p_bf[hx], dnc[hx]
        P.emit("scalar", lambda e, ssb=ssb, pbf=pbf, dn=dn, lo=lo: e.mul(
            pbf[:, lo:640], ssb[:, lo:640], dn[:, 0:1]),
            wait=[(VS, None), (TPT, bTPT + max(0, i - 1))], inc=PTS)

    def st_tp(i):
        _S("a")
        w, hc, hb, qc, hx, lo = _dec(i)
        pbf = p_bf[hx]
        jlo = lo // 128
        for j in range(jlo, 5):
            P.emit("tensor", lambda e, pbf=pbf, j=j: e.transpose(
                ps_pT[:, j * 128:(j + 1) * 128],
                pbf[:, j * 128:(j + 1) * 128], ident[:]),
                wait=([(PTS, bPTS + i + 1), (VPT, bVPT + i)] if j == jlo else []),
                inc=(TPT if j == 4 else ()))

    def st_pc(i):
        w, hc, hb, qc, hx, lo = _dec(i)
        pT = pT_sb[hx]
        jlo = lo // 128
        P.emit("vector", lambda e, pT=pT, jlo=jlo: e.tensor_copy(
            pT[:, jlo * 128:640], ps_pT[:, jlo * 128:640]),
            wait=[(TPT, bTPT + i + 1), (PEo, bPEo + max(0, i - 1))], inc=VPT)

    def st_pv(i):
        _S("a")
        w, hc, hb, qc, hx, lo = _dec(i)
        par, parp = w % 3, (w - 1) % 3
        pT = pT_sb[hx]
        h = (i % 32) // 4
        jlo = lo // 128
        for j in range(jlo, 5):
            kabs = qc + j
            pv = parp if kabs < 4 else par
            sb_ = kabs % 4
            P.emit("tensor", lambda e, pv=pv, sb_=sb_, h=h, pT=pT, j=j,
                   qc=qc: e.matmul(
                ps_oh[:, qc * 128:(qc + 1) * 128],
                v_tm[pv][:, sb_ * 512 + h * 64: sb_ * 512 + (h + 1) * 64],
                pT[:, j * 128:(j + 1) * 128],
                start=(j == jlo), stop=(j == 4)),
                wait=([(VPT, bVPT + i + 1), (VEo, bVEo + i // 4)]
                      if j == jlo else []),
                inc=(PEo if j == 4 else ()))

    def st_ev(i):
        w, hc, hb, qc, hx, lo = _dec(i)
        wi = i // 32
        P.emit("vector", lambda e, w=w, hc=hc, hb=hb: e.tensor_copy(
            o_bf[w % 2][hb:hb + 64, hc * 512:(hc + 1) * 512], ps_oh[:, 0:512]),
            wait=[(PEo, bPEo + i + 1)]
                 + ([(PEw, bPEw + 8 * max(0, wi - 1))] if amode == "A" else []),
            inc=VEo)

    def st_wo(w, mo):
        _S("w")
        wi = nw - 1 - w
        pw = mo % 2
        for kc in range(4):
            P.emit("tensor", lambda e, w=w, mo=mo, kc=kc, pw=pw: e.matmul(
                ps_w[pw], wosb[:, kc * 1024 + mo * 128: kc * 1024 + (mo + 1) * 128],
                o_bf[w % 2][:, kc * 512:(kc + 1) * 512],
                start=(kc == 0), stop=(kc == 3)),
                wait=([tok_wo, (VEo, bVEo + 8 * (wi + 1)),
                       (RA, bRA + 8 * wi + max(0, mo - 1))] if kc == 0 else []),
                inc=(PEw if kc == 3 else ()))
        P.emit("vector", lambda e, w=w, mo=mo, pw=pw: e.tensor_add(
            _r_slice(R, mo, w * 512, 512), _r_slice(R, mo, w * 512, 512),
            ps_w[pw]),
            wait=[(PEw, bPEw + 8 * wi + mo + 1)], inc=RA)

    if amode:
        for sub in range(4):
            for nm in ("q", "k", "v"):
                emit_chunk(nw - 1, sub, nm, 0)
        P.emit("vector", lambda e: e.engine_nop(), wait=[(TRD, None)], inc=QKR)
        if nw >= 2:
            for sub in range(4):
                for nm in ("k", "v"):
                    emit_chunk(nw - 2, sub, nm, 0)
        P.emit("vector", lambda e: e.engine_nop(),
               wait=[(TRD, None), (QPF, None)], inc=KVR)

    def _stretch_chunks(wi):
        w = nw - 1 - wi
        cl = []
        if w - 1 >= 0:
            for sub in range(4):
                cl.append(("c", w - 1, sub, "q"))
            cl.append(("qnop",))
        if w - 2 >= 0:
            for sub in range(4):
                cl.append(("c", w - 2, sub, "k"))
                cl.append(("c", w - 2, sub, "v"))
        if w - 1 >= 0:
            cl.append(("knop",))
        return cl

    def _emit_item(it, wi):
        if it[0] == "c":
            emit_chunk(it[1], it[2], it[3], wi * 32)
        elif it[0] == "qnop":
            P.emit("vector", lambda e: e.engine_nop(),
                   wait=[(TRD, None)], inc=QKR)
        else:
            P.emit("vector", lambda e: e.engine_nop(),
                   wait=[(TRD, None), (QPF, None)], inc=KVR)

    if amode == "Q":
        for wi in range(nw):
            for it in _stretch_chunks(wi):
                _emit_item(it, wi)
    elif amode in ("A", "QA"):
        sched = {}
        for cyc in range(NI + 13):
            wi = cyc // 32
            if cyc < NI and cyc % 32 == 0:
                cl = _stretch_chunks(wi)
                for j, it in enumerate(cl):
                    sched.setdefault(cyc + 3 + (j * 27) // max(1, len(cl)),
                                     []).append((it, wi))
            if 0 <= cyc - 3 < NI:
                st_tp(cyc - 3)
            if 0 <= cyc - 4 < NI:
                st_pv(cyc - 4)
            if 0 <= cyc - 1 < NI:
                st_sm1(cyc - 1)
                st_exp(cyc - 1)
            if 0 <= cyc - 2 < NI:
                st_sm2(cyc - 2)
                st_mul(cyc - 2)
            if 0 <= cyc - 3 < NI:
                st_pc(cyc - 3)
            if 0 <= cyc - 4 < NI and (cyc - 4) % 4 == 3:
                st_ev(cyc - 4)
            if amode == "A" and cyc >= 36 and 0 <= (cyc % 32) - 4 < 8:
                wprev = nw - cyc // 32
                if 0 <= wprev < nw:
                    st_wo(wprev, (cyc % 32) - 4)
            if cyc < NI:
                st_scores(cyc)
            for it, swi in sched.pop(cyc, []):
                _emit_item(it, swi)
        for it_list in sched.values():
            for it, swi in it_list:
                _emit_item(it, swi)

    if "F" in sweeps:
        bRWC = RWC.n; bVEn = VEn.n; bPEa = PEa.n; bRAF = RAF.n; bPEf = PEf.n
        sts = list(range(nw - 1, -1, -1))

        def f_stats_mm(j):
            _S("n")
            st = sts[j]
            t0 = st * 512
            for c in range(CH):
                P.emit("scalar", lambda e, c=c, t0=t0: e.activation(
                    sqb[c % 2][:], _r_shift_slice(R, c, t0, 512), AF.Square),
                    wait=[(RA, None), (QRM, max(0, QRM.n - 1))]
                         + (list(after) if (j == 0 and c == 0) else []),
                    inc=AEF)
                P.emit("tensor", lambda e, c=c, t0=t0: e.matmul(
                    ps_sr, onescol[:], _r_shift_slice(R, c, t0, 512),
                    start=(c == 0), stop=(c == CH - 1)),
                    wait=([(C2, C2f), (RWC, bRWC + j), (RAF, None),
                           (RA, None), (VEh, None)] if c == 0 else []), inc=())
                P.emit("tensor", lambda e, c=c: e.matmul(
                    ps_qr, onescol[:], sqb[c % 2][:],
                    start=(c == 0), stop=(c == CH - 1)),
                    wait=[(AEF, None)], inc=QRM)

        def f_rows(j):
            _S("n")
            P.emit("vector", lambda e: e.tensor_scalar_mul(mrow[:], ps_sr, 1.0 / D),
                   wait=[(QRM, None)], inc=VS)
            P.emit("vector", lambda e: e.tensor_scalar_mul(qrow[:], ps_qr, 1.0 / D),
                   inc=VS)
            P.emit("vector", lambda e: e.tensor_mul(vrow[:], mrow[:], mrow[:]),
                   wait=[(VS, None)], inc=VS)
            P.emit("vector", lambda e: e.tensor_sub(vrow[:], qrow[:], vrow[:]),
                   wait=[(VS, None)], inc=VS)
            P.emit("scalar", lambda e: e.activation(rrow[:], vrow[:], AF.Ln,
                                                    bias=1e-5),
                   wait=[(VS, None)], inc=LNX)
            P.emit("scalar", lambda e: e.activation(rrow[:], rrow[:], AF.Exp,
                                                    scale=-0.5),
                   wait=[(LNX, None)], inc=AEF)
            P.emit("vector", lambda e: e.tensor_mul(mrrow[:], mrow[:], rrow[:]),
                   wait=[(AEF, None)], inc=VS)
            P.emit("vector", lambda e: e.tensor_copy(rrow_bf[:], rrow[:]),
                   wait=[(PEb, None)], inc=VS)
            P.emit("vector", lambda e: e.tensor_copy(mrrow_bf[:], mrrow[:]),
                   wait=[(VS, None)], inc=RWC)

        def f_norm(j):
            _S("n")
            st = sts[j]
            t0 = st * 512
            P.emit("tensor", lambda e: e.matmul(ps_rbc, ones1[:], rrow_bf[:],
                                                start=True, stop=True),
                   wait=[(RWC, bRWC + j + 1), (VEn, bVEn + 8 * j),
                         (PEf, None), (RAF, None)], inc=())
            P.emit("tensor", lambda e: e.matmul(ps_mbc, ones1[:], mrrow_bf[:],
                                                start=True, stop=True), inc=PEb)
            for c in range(CH):
                P.emit("vector", lambda e, c=c, t0=t0: e.tensor_mul(
                    nt1[:], _r_shift_slice(R, c, t0, 512), ps_rbc),
                    wait=[(PEb, None)]
                         + ([(PEa, bPEa + 22 * j)] if c == 0 else []), inc=VS)
                P.emit("vector", lambda e: e.tensor_sub(nt1[:], nt1[:], ps_mbc),
                       wait=[(VS, None)], inc=VS)
                P.emit("vector", lambda e, c=c: e.tensor_scalar(
                    h2n[:, c * 512:(c + 1) * 512], nt1[:], lng[:, c:c + 1],
                    lnb[:, c:c + 1], ALU.mult, ALU.add),
                    wait=[(VS, None)], inc=VEn)

        f_stats_mm(0)
        f_rows(0)
        f_norm(0)
        for j, st in enumerate(sts):
            t0 = st * 512
            _S("1")
            for m in range(22):
                pw1 = m % 2
                tok = fring.dma(lambda e, m=m, pw1=pw1: e.dma_start(
                    w1t[pw1][:], w1bf.rearrange("(c p) n -> p c n", p=128)
                    [:, :, m * 256:(m + 1) * 256]),
                    wait=[(PEa, max(0, PEa.n - 1))] + (w1c_toks if j == 0 else [])
                         + ([(RA, None)] if (j == 0 and m < 2) else []))
                for c in range(CH):
                    P.emit("tensor", lambda e, c=c, pw1=pw1: e.matmul(
                        ps_a, w1t[pw1][:, c * 256: c * 256 + 128],
                        h2n[:, c * 512:(c + 1) * 512],
                        start=(c == 0), stop=(c == CH - 1)),
                        wait=(([tok, (VEh, None), (RWC, None), (RAF, None)]
                               if c == 0 else []) +
                              ([(VEn, bVEn + 8 * j + c + 1)] if m == 0 else [])),
                        inc=())
                for c in range(CH):
                    P.emit("tensor", lambda e, c=c, pw1=pw1: e.matmul(
                        ps_g, w1t[pw1][:, c * 256 + 128: c * 256 + 256],
                        h2n[:, c * 512:(c + 1) * 512],
                        start=(c == 0), stop=(c == CH - 1)),
                        wait=[(AEg, None)] if c == 0 else [],
                        inc=(PEa if c == CH - 1 else ()))
                P.emit("scalar", lambda e, m=m: e.activation(
                    gg[m % 2][:], ps_g, AF.Gelu, bias=b1g[:, m:m + 1]),
                    wait=[(PEa, None), (VEh, max(0, VEh.n - 1))], inc=AEg)
                P.emit("vector", lambda e, m=m: e.scalar_tensor_tensor(
                    h3[:, m * 512:(m + 1) * 512], ps_a, b1a[:, m:m + 1],
                    gg[m % 2][:], ALU.add, ALU.mult),
                    wait=[(AEg, None)], inc=VEh)
                if m == 14 and j + 1 < nw:
                    f_stats_mm(j + 1)
                if m == 18 and j + 1 < nw:
                    f_rows(j + 1)
            _S("2")
            pef_marks = {}
            for kk in range(22):
                tok2 = fring.dma(lambda e, kk=kk: e.dma_start(
                    w2t[kk % 2][:], w2bf[128 * kk:128 * (kk + 1), :]),
                    wait=[(W2M, max(0, W2M.n - 1))]
                         + ([(PEf, None)] if kk == 0 else [])
                         + (w2c_toks if j == 0 else [])
                         + ([(RA, None)] if (j == 0 and kk < 2) else []))
                for mo in range(CH):
                    xw = []
                    if kk == 0:
                        if mo == 0:
                            xw = [tok2, (VEh, None), (AEg, None), (RAF, None),
                                  (RWC, None), (VEn, None)]
                        elif mo == 4 and j + 1 < nw:
                            xw = [(RWC, bRWC + j + 2)]
                        elif mo == 6:
                            xw = [(VEn, bVEn + 8 * (j + 1))]
                    elif mo == 0:
                        xw = [tok2]
                    P.emit("tensor", lambda e, mo=mo, kk=kk: e.matmul(
                        accs[mo], w2t[kk % 2][:, mo * 128:(mo + 1) * 128],
                        h3[:, kk * 512:(kk + 1) * 512],
                        start=(kk == 0), stop=(kk == 21)),
                        wait=xw,
                        inc=((PEf if kk == 21 else W2M) if mo == CH - 1 else ()))
                if kk == 21:
                    for mo in range(CH):
                        pef_marks[mo] = PEf.n
            for mo in range(CH):
                P.emit("vector", lambda e, mo=mo, t0=t0: e.scalar_tensor_tensor(
                    _r_slice(R, mo, t0, 512), accs[mo], b2c[:, mo:mo + 1],
                    _r_slice(R, mo, t0, 512), ALU.add, ALU.add),
                    wait=[(PEf, pef_marks[mo])], inc=RAF)
            if j + 1 < nw:
                f_norm(j + 1)
    return [(RAF, RAF.n)] if "F" in sweeps else [(RA, RA.n)]


def emit_ph3(P, io, after=()):
    R = io.sbuf("R", (128, RCOLS), BF16)
    x_t = io.din("x_t", (D, NTOK), F32)
    attn0 = io.dram["attn0"] if "attn0" in io.dram else \
        io.din("attn0_in", (D, NTOK), BF16)
    x1s = io.dout("x1s", (D, NTOK), F32)
    onescol = io.sbuf("p2_onescol", (128, 1), BF16)
    ones1 = io.sbuf("p2_ones1", (1, 128), BF16)

    xs2 = [io.sbuf(f"p3_xs{i}", (128, 4096), F32) for i in range(2)]
    as2 = [io.sbuf(f"p3_as{i}", (128, 4096), BF16) for i in range(2)]
    x1bf2 = [io.sbuf(f"p3_x1bf{i}", (128, 4096), BF16) for i in range(2)]
    sqbf2 = [io.sbuf(f"p3_sqbf{i}", (128, 4096), BF16) for i in range(2)]
    mrow = io.sbuf("p2_mrow", (1, 512), F32)
    qrow = io.sbuf("p2_qrow", (1, 512), F32)
    vrow = io.sbuf("p2_vrow", (1, 512), F32)
    rrow = io.sbuf("p2_rrow", (1, 512), F32)
    mrrow = io.sbuf("p2_mrrow", (1, 512), F32)
    rrow_bf = io.sbuf("p2_rrowbf", (1, 512), BF16)
    mrrow_bf = io.sbuf("p2_mrrowbf", (1, 512), BF16)
    nt1 = io.sbuf("p3_nt1", (128, 512), F32)

    A2 = io.psum("p1_A2", (128, 2048), F32)
    ps_sr = A2[0:1, 0:512]
    ps_qr = A2[0:1, 512:1024]
    ps_rbc = A2[:, 1024:1536]
    ps_mbc = A2[:, 1536:2048]

    ch = lambda nm: P.ch("q" + nm)
    VS, AEF, QRM, RWC, PEb, VEn, LNX = (ch(x) for x in
        ("VS", "AEF", "QRM", "RWC", "PEb", "VEn", "LNX"))
    C2 = ch("C2")
    XAD = P.ch("p3XAD")
    XBF = P.ch("p3XBF")
    ring = DmaRing(P, "Ls", k=4, eng="sync")
    oring = DmaRing(P, "Lo", k=2, eng="sync")
    otoks = []

    ntile = NTOK // 512
    bVEn = P.ch("qVEn").n
    bXAD = XAD.n
    bQRM = QRM.n
    for t in range(ntile):
        t0 = t * 512
        tx = t % 2
        xs, as_, x1bf, sqbf = xs2[tx], as2[tx], x1bf2[tx], sqbf2[tx]
        tok_x = ring.dma(lambda e, xs=xs, t0=t0: e.dma_start(
            xs[:], x_t.rearrange("(c p) n -> p c n", p=128)[:, :, t0:t0 + 512]),
            wait=([(VEn, bVEn + 8 * (t - 1))] if t >= 2 else [])
                 + ([otoks[t - 2]] if t >= 2 else [])
                 + (list(after) if t == 0 else []))
        tok_a = ring.dma(lambda e, as_=as_, t0=t0: e.dma_start(
            as_[:], attn0.rearrange("(c p) n -> p c n", p=128)[:, :, t0:t0 + 512]),
            wait=([(XAD, bXAD + t - 1)] if t >= 2 else []))
        P.emit("vector", lambda e, xs=xs, as_=as_: e.tensor_add(
            xs[:], xs[:], as_[:]),
            wait=[tok_x, tok_a] + (list(after) if t == 0 else []), inc=VS)
        P.emit("vector", lambda e, xs=xs, t0=t0: e.tensor_add(
            xs.rearrange("p (c n) -> p c n", c=CH),
            xs.rearrange("p (c n) -> p c n", c=CH),
            R.ap().rearrange("p (c n) -> p c n", c=CH, n=RSTR)[:, :, 1 + t0:1 + t0 + 512]),
            wait=[(VS, None)], inc=XAD)
        otoks.append(oring.dma(lambda e, xs=xs, t0=t0: e.dma_start(
            x1s.rearrange("(c p) n -> p c n", p=128)[:, :, t0:t0 + 512], xs[:]),
            wait=[(XAD, None)]))
        P.emit("scalar", lambda e, xs=xs, x1bf=x1bf: e.copy(x1bf[:], xs[:]),
               wait=[(XAD, None), (QRM, bQRM + 8 * max(0, t - 1))], inc=XBF)
        P.emit("scalar", lambda e, xs=xs, sqbf=sqbf: e.activation(
            sqbf[:], xs[:], AF.Square), inc=XBF)
        for c in range(CH):
            P.emit("tensor", lambda e, c=c, x1bf=x1bf: e.matmul(
                ps_sr, onescol[:], x1bf[:, c * 512:(c + 1) * 512],
                start=(c == 0), stop=(c == CH - 1)),
                wait=([(XBF, None), (RWC, None), (C2, None)]
                      if c == 0 else []),
                inc=())
            P.emit("tensor", lambda e, c=c, sqbf=sqbf: e.matmul(
                ps_qr, onescol[:], sqbf[:, c * 512:(c + 1) * 512],
                start=(c == 0), stop=(c == CH - 1)), inc=QRM)
        P.emit("vector", lambda e: e.tensor_scalar_mul(mrow[:], ps_sr, 1.0 / D),
               wait=[(QRM, None)], inc=VS)
        P.emit("vector", lambda e: e.tensor_scalar_mul(qrow[:], ps_qr, 1.0 / D),
               inc=VS)
        P.emit("vector", lambda e: e.tensor_mul(vrow[:], mrow[:], mrow[:]),
               wait=[(VS, None)], inc=VS)
        P.emit("vector", lambda e: e.tensor_sub(vrow[:], qrow[:], vrow[:]),
               wait=[(VS, None)], inc=VS)
        P.emit("scalar", lambda e: e.activation(rrow[:], vrow[:], AF.Ln, bias=1e-5),
               wait=[(VS, None)], inc=LNX)
        P.emit("scalar", lambda e: e.activation(rrow[:], rrow[:], AF.Exp, scale=-0.5),
               wait=[(LNX, None)], inc=AEF)
        P.emit("vector", lambda e: e.tensor_mul(mrrow[:], mrow[:], rrow[:]),
               wait=[(AEF, None)], inc=VS)
        P.emit("vector", lambda e: e.tensor_copy(rrow_bf[:], rrow[:]), inc=VS)
        P.emit("vector", lambda e: e.tensor_copy(mrrow_bf[:], mrrow[:]),
               wait=[(VS, None)], inc=RWC)
        P.emit("tensor", lambda e: e.matmul(ps_rbc, ones1[:], rrow_bf[:],
                                            start=True, stop=True),
               wait=[(RWC, None), (VEn, None), (C2, None)], inc=())
        P.emit("tensor", lambda e: e.matmul(ps_mbc, ones1[:], mrrow_bf[:],
                                            start=True, stop=True), inc=PEb)
        for c in range(CH):
            P.emit("vector", lambda e, c=c, xs=xs: e.tensor_mul(
                nt1[:], xs[:, c * 512:(c + 1) * 512], ps_rbc),
                wait=[(PEb, None), (VEn, None)], inc=VS)
            P.emit("vector", lambda e, c=c, t0=t0: e.tensor_sub(
                _r_slice(R, c, t0, 512), nt1[:], ps_mbc),
                wait=[(VS, None)], inc=VEn)
    P.emit("sync", lambda e: e.nop(), wait=oring.all_done() + [(VEn, None)])
    return [(P.ch("qVEn"), P.ch("qVEn").n)] + oring.all_done()


def emit_ph4(P, io, after=()):
    R = io.sbuf("R", (128, RCOLS), BF16)
    ident = io.sbuf("ident", (128, 128), BF16)
    m640 = io.sbuf("p2_m640", (128, 640), BF16)
    ones1 = io.sbuf("p2_ones1", (1, 128), BF16)
    nsel = NTOK // 4
    nseg = NTOK // 2048
    w_inT = io.din("l1winT", (D, 3072), F32)
    binqk = io.din("l1binqk", (128, 16), F32)
    binv = io.din("l1binv", (1, 1024), F32)
    w_outT = io.din("l1woutT", (D, D), F32)
    bout = io.din("l1bout", (128, CH), F32)
    attn1s = io.dout("attn1s", (D, nsel), BF16)

    wqt = [io.sbuf(f"p4_wqt{i}", (128, CH * 128), BF16) for i in range(2)]
    wvt = [io.sbuf(f"p4_wvt{i}", (128, CH * 512), BF16) for i in range(2)]
    wout = io.sbuf("p4_wout", (128, CH * 1024), BF16)
    binqk_s = io.sbuf("p4_binqk", (128, 16), F32)
    binv_s = io.sbuf("p4_binv", (1, 1024), F32)
    binv_bf = io.sbuf("p4_binvbf", (1, 1024), BF16)
    vbc = io.sbuf("p4_vbc", (128, 1024), BF16)
    bout_s = io.sbuf("p4_bout", (128, CH), F32)
    xsel = io.sbuf("p4_xsel", (128, 4096), BF16)
    qbf = io.sbuf("p4_qbf", (128, 4096), BF16)
    kbf = io.sbuf("p4_kbf", (128, 4096), BF16)
    vtm = io.sbuf("p4_vtm", (128, 4096), BF16)
    obf = io.sbuf("p4_obf", (128, 4096), BF16)
    att = io.sbuf("p4_att", (128, 4096), BF16)
    s_sb = [io.sbuf(f"p2_ssb{i}", (128, 640), F32) for i in range(2)]
    p_bf = [io.sbuf(f"p2_pbf{i}", (128, 640), BF16) for i in range(2)]
    pT_sb = [io.sbuf(f"p2_pT{i}", (128, 640), BF16) for i in range(2)]
    mxc = [io.sbuf(f"p2_mx{i}", (128, 1), F32) for i in range(2)]
    dnc = [io.sbuf(f"p2_dn{i}", (128, 1), F32) for i in range(2)]

    A1 = io.psum("p1_A1", (128, 2048), F32)
    A2 = io.psum("p1_A2", (128, 2048), F32)
    ps_q = [A1[:, 0:512], A1[:, 512:1024]]
    ps_v = [A1[:, 1024:1536], A1[:, 1536:2048]]
    ps_S = A2[:, 0:640]
    ps_pT = A2.ap()[:, 1024:1536].bitcast(BF16)
    ps_oh = A2[0:64, 1536:2048]
    ps_w = [A1[:, 0:512], A1[:, 512:1024]]

    ch = lambda nm: P.ch("q" + nm)
    VS = ch("VS")
    SMM = ch("SMM"); VSm = ch("VSm"); AEx = ch("AEx"); PTS = ch("PTS")
    TPT = ch("TPT"); VPT = ch("VPT"); PEo = ch("PEo"); VEo = ch("VEo")
    PEw = ch("PEw"); QKP = ch("QKP"); C2 = ch("C2")
    XSL = P.ch("p4XSL")
    QEV = P.ch("p4QEV")
    AEW = P.ch("p4AEW")
    VBC = P.ch("p4VBC")
    C4 = P.ch("p4C")
    wring = DmaRing(P, "Lw", k=6, eng="gpsimd")
    fring = DmaRing(P, "Lf", k=4, eng="sync")
    oring = DmaRing(P, "Lo", k=2, eng="sync")
    atoks = []
    P4Q = P.ch("p4PQ")
    P4V = P.ch("p4PV")

    tok_wout = wring.dma(lambda e: e.dma_start(
        wout[:], w_outT.rearrange("(c p) n -> p c n", p=128)), wait=list(after))
    winbf = io.dint("winbf", (D, 3072), BF16)
    win_casts = [wring.dma(lambda e, j=j: e.dma_start(
        winbf[j * 512:(j + 1) * 512, :], w_inT[j * 512:(j + 1) * 512, :]),
        wait=list(after)) for j in range(2)]
    for hbm, sb in ((binqk, binqk_s), (binv, binv_s), (bout, bout_s)):
        P.emit("sync", lambda e, hbm=hbm, sb=sb: e.dma_start(sb[:], hbm),
               wait=list(after), inc=C4, inc_amt=16)
    C4f = C4.n
    P.emit("vector", lambda e: e.tensor_copy(binv_bf[:], binv_s[:]),
           wait=[(C4, C4f)], inc=VS)
    P.emit("tensor", lambda e: e.matmul(A2[:, 0:512], ones1[:],
                                        binv_bf[:, 0:512], start=True, stop=True),
           wait=[(VS, None), (C2, None)] + list(after), inc=())
    P.emit("tensor", lambda e: e.matmul(A2[:, 512:1024], ones1[:],
                                        binv_bf[:, 512:1024], start=True, stop=True),
           inc=QKP)
    P.emit("vector", lambda e: e.tensor_copy(vbc[:], A2[:, 0:1024]),
           wait=[(QKP, None)], inc=VBC)

    for s in range(nseg):
        base = s * 2048
        for c in range(CH):
            P.emit("vector", lambda e, c=c, base=base: e.tensor_copy(
                xsel[:, c * 512:(c + 1) * 512],
                R.ap()[:, c * RSTR + 1 + base: c * RSTR + 1 + base + 2048][:, ::4]),
                wait=([(VBC, None)] if (s, c) == (0, 0) else [])
                     + ([(PEw, None)] if c == 0 else []),
                inc=(XSL if c == CH - 1 else ()))
        for qk in range(2):
            for mo in range(CH):
                pq = mo % 2
                tw = fring.dma(lambda e, mo=mo, qk=qk, pq=pq: e.dma_start(
                    wqt[pq][:], winbf.rearrange("(c p) n -> p c n", p=128)
                    [:, :, qk * 1024 + mo * 128: qk * 1024 + (mo + 1) * 128]),
                    wait=[(P4Q, max(0, P4Q.n - 1))]
                         + (win_casts if (s, qk, mo) == (0, 0, 0) else []))
                for c in range(CH):
                    P.emit("tensor", lambda e, c=c, pq=pq: e.matmul(
                        ps_q[pq], wqt[pq][:, c * 128:(c + 1) * 128],
                        xsel[:, c * 512:(c + 1) * 512],
                        start=(c == 0), stop=(c == CH - 1)),
                        wait=([tw, (XSL, None), (QEV, max(0, QEV.n - 1))]
                              if c == 0 else []),
                        inc=(P4Q if c == CH - 1 else ()))
                if qk == 0:
                    P.emit("vector", lambda e, mo=mo, pq=pq: e.tensor_scalar(
                        qbf[:, mo * 512:(mo + 1) * 512], ps_q[pq],
                        binqk_s[:, mo:mo + 1], 0.125, ALU.add, ALU.mult),
                        wait=[(P4Q, None), (C4, C4f)], inc=QEV)
                else:
                    P.emit("vector", lambda e, mo=mo, pq=pq: e.tensor_scalar_add(
                        kbf[:, mo * 512:(mo + 1) * 512], ps_q[pq],
                        binqk_s[:, 8 + mo:8 + mo + 1]),
                        wait=[(P4Q, None)], inc=QEV)
        for half in range(2):
            tv = fring.dma(lambda e, half=half: e.dma_start(
                wvt[half % 2][:], winbf.rearrange("(c p) n -> p c n", p=128)
                [:, :, 2048 + half * 512: 2048 + (half + 1) * 512]),
                wait=[(P4V, max(0, P4V.n - 4))]
                     + (win_casts if half == 0 else []))
            for sub in range(4):
                pv = sub % 2
                for c in range(CH):
                    P.emit("tensor", lambda e, sub=sub, half=half, c=c, pv=pv: e.matmul(
                        ps_v[pv],
                        xsel[:, c * 512 + sub * 128: c * 512 + sub * 128 + 128],
                        wvt[half % 2][:, c * 512:(c + 1) * 512],
                        start=(c == 0), stop=(c == CH - 1)),
                        wait=([tv, (QEV, max(0, QEV.n - 1))] if c == 0 else []),
                        inc=(P4V if c == CH - 1 else ()))
                P.emit("vector", lambda e, sub=sub, half=half, pv=pv: e.tensor_tensor(
                    vtm[:, sub * 1024 + half * 512: sub * 1024 + (half + 1) * 512],
                    ps_v[pv], vbc[:, half * 512:(half + 1) * 512], ALU.add),
                    wait=[(P4V, None), (VBC, 1)], inc=QEV)
        bSMM = SMM.n; bVSm = VSm.n; bAEx = AEx.n; bPTS = PTS.n; bTPT = TPT.n
        bVPT = VPT.n; bPEo = PEo.n; bVEo = VEo.n; bPEw = PEw.n; bAEW = AEW.n
        bQEV = QEV.n

        def _d4(i):
            h = i // 4
            return h // 2, 64 * (h % 2), i % 4, i % 2, 128 * (i % 4 + 1)

        def s4_scores(i, s=s):
            hc, hb, qc, hx, wdt = _d4(i)
            first = [(VSm, bVSm + i)] + ([(QEV, bQEV)] if i == 0 else [])
            P.emit("tensor", lambda e, hb=hb, hc=hc, qc=qc, wdt=wdt: e.matmul(
                ps_S[:, 0:wdt],
                qbf[hb:hb + 64, hc * 512 + qc * 128: hc * 512 + (qc + 1) * 128],
                kbf[hb:hb + 64, hc * 512: hc * 512 + wdt],
                start=True, stop=True),
                wait=first, inc=SMM)

        def s4_sm1(i, s=s):
            hc, hb, qc, hx, wdt = _d4(i)
            ssb, mx = s_sb[hx], mxc[hx]
            P.emit("vector", lambda e, ssb=ssb, qc=qc, wdt=wdt: e.tensor_tensor(
                ssb[:, 0:wdt], ps_S[:, 0:wdt],
                m640[:, 512 - qc * 128:640], ALU.add),
                wait=[(SMM, bSMM + i + 1), (PTS, bPTS + max(0, i - 1))]
                     + ([(C2, None)] if (s, i) == (0, 0) else []), inc=VSm)
            P.emit("vector", lambda e, ssb=ssb, mx=mx, wdt=wdt: e.reduce_max(
                mx[:, 0:1], ssb[:, 0:wdt], axis=AX.X),
                wait=[(VSm, None)], inc=VS)

        def s4_exp(i):
            hc, hb, qc, hx, wdt = _d4(i)
            ssb, mx = s_sb[hx], mxc[hx]
            P.emit("scalar", lambda e, mx=mx: e.mul(
                mx[:, 0:1], mx[:, 0:1], -1.0), wait=[(VS, None)], inc=())
            P.emit("scalar", lambda e, ssb=ssb, mx=mx, wdt=wdt: e.activation(
                ssb[:, 0:wdt], ssb[:, 0:wdt], AF.Exp, bias=mx[:, 0:1]),
                inc=AEx)

        def s4_sm2(i):
            hc, hb, qc, hx, wdt = _d4(i)
            ssb, dn = s_sb[hx], dnc[hx]
            P.emit("vector", lambda e, ssb=ssb, dn=dn, wdt=wdt: e.reduce_sum(
                dn[:, 0:1], ssb[:, 0:wdt], axis=AX.X),
                wait=[(AEx, bAEx + i + 1)], inc=VS)
            P.emit("vector", lambda e, dn=dn: e.reciprocal(dn[:, 0:1], dn[:, 0:1]),
                   wait=[(VS, None)], inc=VS)

        def s4_mul(i):
            hc, hb, qc, hx, wdt = _d4(i)
            ssb, pbf, dn = s_sb[hx], p_bf[hx], dnc[hx]
            P.emit("scalar", lambda e, ssb=ssb, pbf=pbf, dn=dn, wdt=wdt: e.mul(
                pbf[:, 0:wdt], ssb[:, 0:wdt], dn[:, 0:1]),
                wait=[(VS, None), (TPT, bTPT + max(0, i - 1))], inc=PTS)

        def s4_tp(i):
            hc, hb, qc, hx, wdt = _d4(i)
            pbf = p_bf[hx]
            for j in range(qc + 1):
                P.emit("tensor", lambda e, pbf=pbf, j=j: e.transpose(
                    ps_pT[:, j * 128:(j + 1) * 128],
                    pbf[:, j * 128:(j + 1) * 128], ident[:]),
                    wait=([(PTS, bPTS + i + 1), (VPT, bVPT + i)] if j == 0
                          else []),
                    inc=(TPT if j == qc else ()))

        def s4_pc(i):
            hc, hb, qc, hx, wdt = _d4(i)
            pT = pT_sb[hx]
            P.emit("vector", lambda e, pT=pT, wdt=wdt: e.tensor_copy(
                pT[:, 0:wdt], ps_pT[:, 0:wdt]),
                wait=[(TPT, bTPT + i + 1), (PEo, bPEo + max(0, i - 1))],
                inc=VPT)

        def s4_pv(i):
            hc, hb, qc, hx, wdt = _d4(i)
            pT = pT_sb[hx]
            h = i // 4
            for j in range(qc + 1):
                P.emit("tensor", lambda e, j=j, h=h, pT=pT, qc=qc: e.matmul(
                    ps_oh[:, qc * 128:(qc + 1) * 128],
                    vtm[:, j * 1024 + h * 64: j * 1024 + (h + 1) * 64],
                    pT[:, j * 128:(j + 1) * 128],
                    start=(j == 0), stop=(j == qc)),
                    wait=([(VPT, bVPT + i + 1), (VEo, bVEo + i // 4)]
                          if j == 0 else []),
                    inc=(PEo if j == qc else ()))

        def s4_ev(i):
            hc, hb, qc, hx, wdt = _d4(i)
            P.emit("vector", lambda e, hc=hc, hb=hb: e.tensor_copy(
                obf[hb:hb + 64, hc * 512:(hc + 1) * 512], ps_oh[:, 0:512]),
                wait=[(PEo, bPEo + i + 1)]
                     + ([(PEw, bPEw)] if bPEw else []), inc=VEo)

        def s4_wo(mo, s=s):
            pw = mo % 2
            for c in range(CH):
                P.emit("tensor", lambda e, mo=mo, c=c, pw=pw: e.matmul(
                    ps_w[pw], wout[:, c * 1024 + mo * 128: c * 1024 + (mo + 1) * 128],
                    obf[:, c * 512:(c + 1) * 512],
                    start=(c == 0), stop=(c == CH - 1)),
                    wait=([tok_wout, (VEo, bVEo + 16), (QEV, bQEV),
                           (AEW, bAEW + max(0, mo - 1))] if c == 0 else []),
                    inc=(PEw if c == CH - 1 else ()))
            P.emit("vector", lambda e, mo=mo, pw=pw: e.tensor_scalar_add(
                att[:, mo * 512:(mo + 1) * 512], ps_w[pw], bout_s[:, mo:mo + 1]),
                wait=[(PEw, bPEw + 8 * 0 + mo + 1), (C4, C4f)]
                     + ([atoks[s - 1]] if (s >= 1 and mo == 0) else []),
                inc=AEW)

        for cyc in range(64 + 13):
            if 0 <= cyc - 3 < 64:
                s4_tp(cyc - 3)
            if 0 <= cyc - 4 < 64:
                s4_pv(cyc - 4)
            if 0 <= cyc - 1 < 64:
                s4_sm1(cyc - 1)
                s4_exp(cyc - 1)
            if 0 <= cyc - 2 < 64:
                s4_sm2(cyc - 2)
                s4_mul(cyc - 2)
            if 0 <= cyc - 3 < 64:
                s4_pc(cyc - 3)
            if 0 <= cyc - 4 < 64 and (cyc - 4) % 4 == 3:
                s4_ev(cyc - 4)
            if 0 <= cyc - 68 < 8:
                s4_wo(cyc - 68)
            if cyc < 64:
                s4_scores(cyc)
        atoks.append(oring.dma(lambda e, s=s: e.dma_start(
            attn1s.rearrange("(c p) n -> p c n", p=128)[:, :, s * 512:(s + 1) * 512],
            att.ap().rearrange("p (c n) -> p c n", c=CH)),
            wait=[(AEW, None)]))
    P.emit("sync", lambda e: e.nop(),
           wait=oring.all_done() + ([(P.ch("DBG"), P.ch("DBG").n)]
                                    if "DBG" in P.chans else []))
    return oring.all_done()


def emit_ph5(P, io, after=(), attn1_after=()):
    R = io.sbuf("R", (128, RCOLS), BF16)
    nsel = NTOK // 4
    wg_h = io.din("l1wgT", (D, 3072), I32)
    wu_h = io.din("l1wuT", (D, 3072), I32)
    wd_h = io.din("l1wdT", (3072, D), I32)
    x1s = io.dram["x1s"] if "x1s" in io.dram else io.din("x1s_in", (D, NTOK), F32)
    attn1s = io.dram["attn1s"] if "attn1s" in io.dram else \
        io.din("attn1s_in", (D, nsel), BF16)
    y_t = io.dout("y_t", (D, NTOK), F32)
    wgbf = io.dint("wgbf", (D, 3072), BF16)
    wubf = io.dint("wubf", (D, 3072), BF16)
    wdbf = io.dint("wdbf", (3072, D), BF16)

    wgt = [io.sbuf(f"p5_wgt{i}", (128, CH * 128), BF16) for i in range(2)]
    wut = [io.sbuf(f"p5_wut{i}", (128, CH * 128), BF16) for i in range(2)]
    wdt_ = [io.sbuf(f"p5_wdt{i}", (128, 2048), BF16) for i in range(2)]
    hbf = io.sbuf("p5_hbf", (128, 24 * 512), BF16)
    sg = [io.sbuf(f"p5_sg{i}", (128, 512), BF16) for i in range(2)]
    x1t2 = [io.sbuf(f"p5_x1t{i}", (128, CH * 512), F32) for i in range(2)]
    attt2 = [io.sbuf(f"p5_attt{i}", (128, CH * 128), BF16) for i in range(2)]
    yt = io.sbuf("p5_yt", (128, CH * 512), F32)

    A1 = io.psum("p1_A1", (128, 2048), F32)
    A2 = io.psum("p1_A2", (128, 2048), F32)
    ps_g = [A1[:, 0:512], A1[:, 512:1024]]
    ps_u = [A1[:, 1024:1536], A1[:, 1536:2048]]
    accs = [A2[:, 512 * i:512 * (i + 1)] for i in range(4)] + \
           [A1[:, 512 * i:512 * (i + 1)] for i in range(4)]

    PEg = P.ch("p5PEg")
    PEu = P.ch("p5PEu")
    AEs = P.ch("p5AEs")
    VEh = P.ch("p5VEh")
    PEd = P.ch("p5PEd")
    DWM = P.ch("p5DWM")
    VEy = P.ch("p5VEy")
    wring = DmaRing(P, "Lw", k=6, eng="gpsimd")
    fring = DmaRing(P, "Lf", k=4, eng="sync")
    lring = DmaRing(P, "Ls", k=4, eng="sync")
    oring = DmaRing(P, "Lo", k=2, eng="sync")
    ytoks = []

    cast_toks = []
    for srcd, dstd, nr in ((wg_h, wgbf, D), (wu_h, wubf, D), (wd_h, wdbf, 3072)):
        for j in range(2):
            cast_toks.append(wring.dma(
                lambda e, srcd=srcd, dstd=dstd, j=j, nr=nr: e.dma_start(
                    dstd[j * (nr // 2):(j + 1) * (nr // 2), :],
                    srcd[j * (nr // 2):(j + 1) * (nr // 2), :]),
                wait=list(after)))

    ntile = NTOK // 512
    SCALE = 1.0 / 127.0
    bVEy = VEy.n
    xtoks = {}
    atoks5 = {}

    def _pf5(t):
        xtoks[t] = lring.dma(lambda e, t=t: e.dma_start(
            x1t2[t % 2][:],
            x1s.rearrange("(c p) n -> p c n", p=128)[:, :, t * 512:(t + 1) * 512]),
            wait=([(VEy, bVEy + 16 * (t - 2) + 8)] if t >= 2 else [])
                 + (list(after) if t == 0 else []))
        atoks5[t] = lring.dma(lambda e, t=t: e.dma_start(
            attt2[t % 2][:], attn1s.rearrange("(c p) n -> p c n", p=128)
            [:, :, t * 128:(t + 1) * 128]),
            wait=([(VEy, bVEy + 16 * (t - 1))] if t >= 2 else [])
                 + (list(attn1_after) if t <= 1 else []))

    for t in range(ntile):
        t0 = t * 512
        if t == 0:
            _pf5(0)
            _pf5(1) if ntile > 1 else None
        x1t = x1t2[t % 2]
        attt = attt2[t % 2]
        tok_x = xtoks[t]
        tok_a = atoks5[t]
        for mo in range(24):
            pg = mo % 2
            tok_g = fring.dma(lambda e, mo=mo: e.dma_start(
                wgt[mo % 2][:], wgbf.rearrange("(c p) n -> p c n", p=128)
                [:, :, mo * 128:(mo + 1) * 128]),
                wait=[(PEg, max(0, PEg.n - 1))]
                     + (cast_toks if (t, mo) == (0, 0) else []))
            tok_u = fring.dma(lambda e, mo=mo: e.dma_start(
                wut[mo % 2][:], wubf.rearrange("(c p) n -> p c n", p=128)
                [:, :, mo * 128:(mo + 1) * 128]),
                wait=[(PEu, max(0, PEu.n - 1))]
                     + (cast_toks if (t, mo) == (0, 0) else []))
            for c in range(CH):
                P.emit("tensor", lambda e, mo=mo, c=c, pg=pg, t0=t0: e.matmul(
                    ps_g[pg], wgt[mo % 2][:, c * 128:(c + 1) * 128],
                    _r_slice(R, c, t0, 512),
                    start=(c == 0), stop=(c == CH - 1)),
                    wait=([tok_g, (AEs, max(0, AEs.n - 1))]
                          + ([(VEy, bVEy + 16 * t - 8)]
                             if (t > 0 and mo == 0) else [])
                          + (list(after) if (t, mo) == (0, 0) else [])
                          if c == 0 else []),
                    inc=(PEg if c == CH - 1 else ()))
            for c in range(CH):
                P.emit("tensor", lambda e, mo=mo, c=c, pg=pg, t0=t0: e.matmul(
                    ps_u[pg], wut[mo % 2][:, c * 128:(c + 1) * 128],
                    _r_slice(R, c, t0, 512),
                    start=(c == 0), stop=(c == CH - 1)),
                    wait=[tok_u, (VEh, max(0, VEh.n - 1))] if c == 0 else [],
                    inc=(PEu if c == CH - 1 else ()))
            P.emit("scalar", lambda e, pg=pg, mo=mo: e.activation(
                sg[mo % 2][:], ps_g[pg], AF.Silu, scale=SCALE),
                wait=[(PEg, None), (VEh, max(0, VEh.n - 1))], inc=AEs)
            P.emit("vector", lambda e, mo=mo, pg=pg: e.scalar_tensor_tensor(
                hbf[:, mo * 512:(mo + 1) * 512],
                ps_u[pg], SCALE, sg[mo % 2][:], ALU.mult, ALU.mult),
                wait=[(PEu, None), (AEs, None)], inc=VEh)
        if 1 <= t < ntile - 1:
            _pf5(t + 1)
        pdm = {}
        for kk in range(24):
            if kk % 2 == 0:
                tok_d = fring.dma(lambda e, kk=kk: e.dma_start(
                    wdt_[(kk // 2) % 2][:],
                    wdbf.rearrange("(j p) n -> p j n", p=128)
                    [:, kk:kk + 2, :]),
                    wait=[(DWM, max(0, DWM.n - 2))]
                         + ([(PEd, None)] if kk == 0 else [])
                         + (cast_toks if (t, kk) == (0, 0) else []))
            for md in range(CH):
                P.emit("tensor", lambda e, md=md, kk=kk: e.matmul(
                    accs[md],
                    wdt_[(kk // 2) % 2][:, (kk % 2) * 1024 + md * 128:
                                        (kk % 2) * 1024 + (md + 1) * 128],
                    hbf[:, kk * 512:(kk + 1) * 512],
                    start=(kk == 0), stop=(kk == 23)),
                    wait=(([tok_d, (VEh, None), (AEs, None)]
                           + ([(VEy, bVEy + 16 * t - 8)] if t > 0 else [])
                           if kk == 0 else [tok_d]) if md == 0 else []),
                    inc=(PEd if kk == 23 else (DWM if md == CH - 1 else ())))
            if kk == 23:
                for md in range(CH):
                    pdm[md] = PEd.n
        for md in range(CH):
            P.emit("vector", lambda e, md=md, x1t=x1t: e.scalar_tensor_tensor(
                yt[:, md * 512:(md + 1) * 512], accs[md], SCALE,
                x1t[:, md * 512:(md + 1) * 512], ALU.mult, ALU.add),
                wait=[(PEd, pdm[md]), tok_x]
                     + ([ytoks[t - 1]] if (t >= 1 and md == 0) else []),
                inc=VEy)
        for md in range(CH):
            P.emit("vector", lambda e, md=md, attt=attt: e.tensor_add(
                yt.ap()[:, md * 512: (md + 1) * 512][:, ::4],
                yt.ap()[:, md * 512: (md + 1) * 512][:, ::4],
                attt[:, md * 128:(md + 1) * 128]),
                wait=[(VEy, None), tok_a], inc=VEy)
        ytoks.append(oring.dma(lambda e, t0=t0: e.dma_start(
            y_t.rearrange("(c p) n -> p c n", p=128)[:, :, t0:t0 + 512],
            yt.ap().rearrange("p (c n) -> p c n", c=CH)),
            wait=[(VEy, None)]))
    P.emit("sync", lambda e: e.nop(), wait=oring.all_done())
    return oring.all_done()


def emit_load_r(P, io):
    r_in = io.din("r_in", (128, RCOLS), BF16)
    R = io.sbuf("R", (128, RCOLS), BF16)
    LD = P.ch("rLD")
    P.emit("sync", lambda e: e.dma_start(R[:], r_in), inc=LD, inc_amt=16)
    return (LD, 16)


def emit_store_r(P, io, wait_chs):
    r_out = io.dout("r_out", (128, RCOLS), BF16)
    R = io.sbuf("R", (128, RCOLS), BF16)
    SO = P.ch("rSO")
    P.emit("sync", lambda e: e.dma_start(r_out, R[:]),
           wait=[w if isinstance(w, tuple) else (w, None) for w in wait_chs],
           inc=SO, inc_amt=16)
    P.emit("sync", lambda e: e.nop(), wait=[(SO, None)])


def emit_finish(P, io, wait_chs):
    P.emit("sync", lambda e: e.nop(), wait=[(ch, None) for ch in wait_chs])


def set_ntok(n):
    global NTOK
    assert n % 512 == 0
    NTOK = n


def _col_chunks(vec):
    return np.ascontiguousarray(vec.reshape(CH, 128).T.astype(np.float32))


def _host_ph1(inputs):
    esel = np.zeros((16, 1024), bf16)
    for m in range(8):
        for pp in range(128):
            esel[2 * m + (pp >= 64), 128 * m + pp] = 1
    return {
        "l0Wq": np.ascontiguousarray(inputs["L0_attn_Wq"], dtype=np.float32),
        "l0Wk": np.ascontiguousarray(inputs["L0_attn_Wk"], dtype=np.float32),
        "l0Wv": np.ascontiguousarray(inputs["L0_attn_Wv"], dtype=np.float32),
        "l0Wo": np.ascontiguousarray(inputs["L0_attn_Wo"], dtype=np.float32),
        "l0bo": _col_chunks(np.asarray(inputs["L0_attn_bo"])),
        "l0esel": esel,
    }


def _host_ph2(inputs):
    HID = 2730
    w1 = np.asarray(inputs["L0_ffn_w1"], np.float32)
    b1 = np.asarray(inputs["L0_ffn_b1"], np.float32)
    w2 = np.asarray(inputs["L0_ffn_w2"], np.float32)
    out = {}
    w1p = np.zeros((2, D, W1P), np.float32)
    b1a = np.zeros((2, 128, 22), np.float32)
    b1g = np.zeros((2, 128, 22), np.float32)
    for m in range(22):
        na = min(128, HID - 128 * m)
        w1p[:, :, 256 * m:256 * m + na] = w1[:, :, 128 * m:128 * m + na]
        w1p[:, :, 256 * m + 128:256 * m + 128 + na] = \
            w1[:, :, HID + 128 * m:HID + 128 * m + na]
        b1a[:, 0:na, m] = b1[:, 128 * m:128 * m + na]
        b1g[:, 0:na, m] = b1[:, HID + 128 * m:HID + 128 * m + na]
    w2p = np.zeros((2, W2K, D), np.float32)
    w2p[:, 0:HID] = w2
    for l in range(2):
        out[f"f_wqkv{l}"] = np.ascontiguousarray(inputs["L0_ffn_wqkv"][l], dtype=np.float32)
        out[f"f_wo{l}"] = np.ascontiguousarray(inputs["L0_ffn_wo"][l], dtype=np.float32)
        out[f"f_w1p{l}"] = np.ascontiguousarray(w1p[l])
        out[f"f_w2p{l}"] = np.ascontiguousarray(w2p[l])
        out[f"f_b1a{l}"] = np.ascontiguousarray(b1a[l])
        out[f"f_b1g{l}"] = np.ascontiguousarray(b1g[l])
        out[f"f_lng{l}"] = _col_chunks(np.asarray(inputs["L0_ffn_ln_g"][l]))
        out[f"f_lnb{l}"] = _col_chunks(np.asarray(inputs["L0_ffn_ln_b"][l]))
        out[f"f_b2{l}"] = _col_chunks(np.asarray(inputs["L0_ffn_b2"][l]))
        out[f"f_qs{l}"] = np.asarray(inputs["L0_ffn_q_scale"][l], np.float32).reshape(1, 64)
        out[f"f_ks{l}"] = np.asarray(inputs["L0_ffn_k_scale"][l], np.float32).reshape(1, 64)
    n = NTOK
    half = np.arange(0, 64, 2, dtype=np.float64)
    inv_freq = 1.0 / 10000.0 ** (half / 64.0)
    t = np.arange(n, dtype=np.float64)
    freqs = t[:, None] * inv_freq[None, :]
    base = (half + 0.4 * 64.0) / (1.4 * 64.0)
    power = (t - n // 2) / 256.0
    scale = base[None, :] ** power[:, None]
    cos, sin = np.cos(freqs), np.sin(freqs)
    def pack(a):
        return np.ascontiguousarray(
            a.reshape(n // 128, 128, 64).transpose(1, 0, 2)
             .reshape(128, (n // 128) * 64).astype(bf16))
    cs_ = np.concatenate([cos * scale, cos * scale], -1)
    sx_ = np.concatenate([-sin * scale, sin * scale], -1)
    ck_ = np.concatenate([cos / scale, cos / scale], -1)
    sk_ = np.concatenate([-sin / scale, sin / scale], -1)
    zpad = np.zeros((128, 2048), bf16)
    for nm, a in (("csq", cs_), ("ssq", sx_), ("csk", ck_), ("ssk", sk_)):
        p = pack(a)
        z = zpad.copy(); z[:, 0:p.shape[1]] = p
        out[f"tab_{nm}"] = z
    lam = np.arange(128)[:, None]
    jj = np.arange(128)[None, :]
    out["tri_u"] = np.where(jj >= lam, 0.0, -1e30).astype(bf16)
    out["tri_l"] = np.where(jj <= lam, 0.0, -1e30).astype(bf16)
    m640 = np.zeros((128, 640), np.float32)
    m640[:, 0:128] = np.where(jj >= lam, 0.0, -1e30)
    m640[:, 512:640] = np.where(jj <= lam, 0.0, -1e30)
    out["mask640"] = m640.astype(bf16)
    out["ones8"] = np.full((1, 128), 8.0, bf16)
    out["ones1"] = np.ones((1, 128), bf16)
    out["onescol"] = np.ones((128, 1), bf16)
    return out


def _host_ph45(inputs):
    w_in = np.asarray(inputs["L1_attn_w_in"], np.float32)
    b_in = np.asarray(inputs["L1_attn_b_in"], np.float32)
    w_out = np.asarray(inputs["L1_attn_w_out"], np.float32)
    b_out = np.asarray(inputs["L1_attn_b_out"], np.float32)
    return {
        "l1winT": np.ascontiguousarray(w_in.T),
        "l1binqk": np.ascontiguousarray(b_in[:2048].reshape(16, 128).T),
        "l1binv": np.ascontiguousarray(b_in[2048:].reshape(1, 1024)),
        "l1woutT": np.ascontiguousarray(w_out.T),
        "l1bout": _col_chunks(b_out),
        "l1wgT": np.ascontiguousarray(np.asarray(inputs["L1_ffn_w_gate"], np.int32).T),
        "l1wuT": np.ascontiguousarray(np.asarray(inputs["L1_ffn_w_up"], np.int32).T),
        "l1wdT": np.ascontiguousarray(np.asarray(inputs["L1_ffn_w_down"], np.int32).T),
    }


def _host_consts():
    c = {}
    c["identin"] = np.eye(128, dtype=bf16)
    c["constsin"] = np.tile(np.asarray(CONST_VALS, np.float32), (128, 1))
    return c


def _prep_shared(inputs):
    sh = {}
    return sh


def _launch(phases, feed, n_cores=N_CORES, sim=False, trace=False):
    nc = bass.Bass("TRN2", target_bir_lowering=False, debug=False)
    io = IO(nc)
    P = Prog(nc)
    for ph in phases:
        ph(P, io)
    P.build()
    in_names = [n for n, k in io.kinds.items() if k == "in"]
    out_names = [n for n, k in io.kinds.items() if k == "out"]
    in_maps = []
    for core in range(n_cores):
        m = {}
        for n in in_names:
            arr = feed[n] if not isinstance(feed[n], list) else feed[n][core]
            m[n] = arr
        in_maps.append(m)
    if sim:
        from concourse.bass_interp import CoreSim

        class _R:
            pass
        nc.finalize()
        if sim != "race":
            nc.detect_race_conditions = False
        s = CoreSim(nc, require_finite=False)
        for n in in_names:
            s.tensor(n)[:] = in_maps[0][n]
        s.simulate(check_with_hw=False)
        r = _R()
        r.results = [{n: np.copy(s.tensor(n)) for n in out_names}]
        return r
    res = run_bass_kernel_spmd(nc, in_maps, core_ids=list(range(n_cores)),
                               trace=trace)
    return res


def _consts2(P, io):
    C2 = P.ch("qC2")
    for nm, shape in (("mask640", (128, 640)), ("ones8", (1, 128)),
                      ("ones1", (1, 128)), ("onescol", (128, 1))):
        hbm = io.din(nm, shape, BF16)
        sb = io.sbuf("p2_" + ("m640" if nm == "mask640" else nm), shape, BF16)
        P.emit("sync", lambda e, hbm=hbm, sb=sb: e.dma_start(sb[:], hbm),
               inc=C2, inc_amt=16)


def _feed_all(inputs):
    x = np.asarray(inputs["L0_attn_x"], dtype=np.float32)
    feed = {}
    feed["x_tm"] = [np.ascontiguousarray(x[b]) for b in range(N_CORES)]
    feed["x_t"] = [np.ascontiguousarray(x[b].T) for b in range(N_CORES)]
    feed.update(_host_consts())
    feed.update(_host_ph1(inputs))
    feed.update(_host_ph2(inputs))
    feed.update(_host_ph45(inputs))
    return feed


def kernel(**inputs):
    if os.environ.get("K_UNFUSED") == "1":
        return _kernel_unfused(**inputs)
    feed = _feed_all(inputs)
    nc = bass.Bass("TRN2", target_bir_lowering=False, debug=False)
    io = IO(nc, internal={"attn0", "x1s", "attn1s"}, use_arena=True)
    P = Prog(nc)
    emit_consts(P, io)
    emit_ph0(P, io)
    t0 = [(P.ch("p0EVT"), P.ch("p0EVT").n), (P.ch("p0G"), 1)]
    t1 = emit_ph1(P, io, after=t0)
    t2a = emit_ph2_layer(P, io, 0, after=t1)
    t2 = emit_ph2_layer(P, io, 1, after=t2a)
    t3 = emit_ph3(P, io, after=t2)
    t4 = emit_ph4(P, io, after=t3)
    emit_ph5(P, io, after=t4, attn1_after=t4)
    P.build()
    in_names = [n for n, k in io.kinds.items() if k == "in"]
    in_maps = []
    for core in range(N_CORES):
        m = {}
        for n in in_names:
            arr = feed[n] if not isinstance(feed[n], list) else feed[n][core]
            m[n] = arr
        in_maps.append(m)
    res = run_bass_kernel_spmd(nc, in_maps, core_ids=list(range(N_CORES)))
    y = np.stack([np.ascontiguousarray(res.results[b]["y_t"].T)
                  for b in range(N_CORES)])
    return y


def _kernel_unfused(**inputs):
    x = np.asarray(inputs["L0_attn_x"], dtype=np.float32)
    feed = {}
    feed["x_tm"] = [np.ascontiguousarray(x[b]) for b in range(N_CORES)]
    feed["x_t"] = [np.ascontiguousarray(x[b].T) for b in range(N_CORES)]
    feed.update(_host_consts())
    feed.update(_host_ph1(inputs))
    feed.update(_host_ph2(inputs))
    feed.update(_host_ph45(inputs))
    tok = {}

    def loadr(P, io):
        tok["t"] = [K_emit_load_r(P, io)]

    K_emit_load_r = emit_load_r

    def l1_ph0(P, io):
        emit_ph0(P, io)
        emit_store_r(P, io, [P.ch("p0EVT"), P.ch("p0G")])
    res = _launch([emit_consts, l1_ph0], feed)
    r = [res.results[b]["r_out"] for b in range(N_CORES)]

    feed["r_in"] = r
    def l2_ph1(P, io):
        emit_ph1(P, io, after=tok["t"])
    res = _launch([emit_consts, loadr, l2_ph1], feed)
    attn0 = [res.results[b]["attn0"] for b in range(N_CORES)]

    def l3_ph2(P, io):
        t0 = emit_ph2_layer(P, io, 0, after=tok["t"])
        t1 = emit_ph2_layer(P, io, 1, after=t0)
        emit_store_r(P, io, t1)
    res = _launch([emit_consts, loadr, l3_ph2], feed)
    r = [res.results[b]["r_out"] for b in range(N_CORES)]

    feed["r_in"] = r
    feed["attn0_in"] = attn0
    def l4_ph3(P, io):
        t = emit_ph3(P, io, after=tok["t"])
        emit_store_r(P, io, t)
    res = _launch([emit_consts, _consts2, loadr, l4_ph3], feed)
    r = [res.results[b]["r_out"] for b in range(N_CORES)]
    x1s = [res.results[b]["x1s"] for b in range(N_CORES)]

    feed["r_in"] = r
    def l5_ph4(P, io):
        emit_ph4(P, io, after=tok["t"])
    res = _launch([emit_consts, _consts2, loadr, l5_ph4], feed)
    attn1 = [res.results[b]["attn1s"] for b in range(N_CORES)]

    feed["x1s_in"] = x1s
    feed["attn1s_in"] = attn1
    def l6_ph5(P, io):
        emit_ph5(P, io, after=tok["t"], attn1_after=tok["t"])
    res = _launch([emit_consts, loadr, l6_ph5], feed)
    y = np.stack([np.ascontiguousarray(res.results[b]["y_t"].T)
                  for b in range(N_CORES)])
    return y
